# version 01
# baseline (speedup 1.0000x reference)
import numpy as np
import ml_dtypes
from contextlib import ExitStack

import concourse.bass as bass
import concourse.mybir as mybir
from concourse.bass_utils import run_bass_kernel_spmd

F32 = mybir.dt.float32
BF16 = mybir.dt.bfloat16
AF = mybir.ActivationFunctionType
ALU = mybir.AluOpType
AX = mybir.AxisListType

NCORES = 8
B, S, D = 2, 8192, 1024
L0_NEXP = 8
L0_NQ = 4
L0_SKIP_QKV = False
T = 2048
NB_F32 = np.float32
NB_BF16 = ml_dtypes.bfloat16
LN_EPS = 1e-5
PARANOID = False


class Ctx:
    def __init__(self, name):
        self.nc = bass.Bass("TRN2", target_bir_lowering=False)
        self.name = name
        self.stack = ExitStack()
        self.sems = {}
        self.inputs = {}
        self.outputs = {}

    def din(self, name, shape, dtype=F32):
        h = self.nc.dram_tensor(name, list(shape), dtype, kind="ExternalInput")
        self.inputs[name] = (tuple(shape), dtype)
        return h.ap()

    def dout(self, name, shape, dtype=F32):
        h = self.nc.dram_tensor(name, list(shape), dtype, kind="ExternalOutput")
        self.outputs[name] = (tuple(shape), dtype)
        return h.ap()

    def sb(self, name, shape, dtype=F32):
        return self.stack.enter_context(self.nc.sbuf_tensor(name, list(shape), dtype))

    def ps(self, name, shape, dtype=F32):
        return self.stack.enter_context(self.nc.psum_tensor(name, list(shape), dtype))

    def sem(self, name):
        if name not in self.sems:
            h = self.stack.enter_context(self.nc.semaphore(name))
            self.sems[name] = [h, 0]
        return self.sems[name][0]

    def inc(self, ins, name, amount=1):
        s = self.sems[name]
        ins.then_inc(s[0], amount)
        s[1] += amount
        return s[1]

    def val(self, name):
        return self.sems[name][1]

    def wait(self, eng, name, value=None):
        s = self.sems[name]
        eng.wait_ge(s[0], s[1] if value is None else value)

    def close(self):
        self.stack.close()

    def dma(self, eng, sem, out, in_):
        if PARANOID:
            self.wait(eng, sem)
        return self.inc(eng.dma_start(out=out, in_=in_), sem, 16)


class Chain:

    def __init__(self, c, eng, sem):
        self._c = c
        self._eng = eng
        self._sem = sem
        c.sem(sem)

    def __getattr__(self, name):
        if name.startswith("_"):
            raise AttributeError(name)
        f = getattr(self._eng, name)
        c, eng, sem = self._c, self._eng, self._sem

        def wrapped(*a, **kw):
            c.wait(eng, sem)
            ins = f(*a, **kw)
            c.inc(ins, sem)
            return ins

        return wrapped

    def wait_ge(self, semh, val):
        self._eng.wait_ge(semh, val)

    def raw(self):
        return self._eng


def cdiv(a, b):
    return (a + b - 1) // b


def np_t(a):
    return np.ascontiguousarray(np.swapaxes(np.asarray(a), -1, -2))


def to_np(a, dtype):
    arr = np.asarray(a)
    if arr.dtype == np.uint16 and dtype == NB_BF16:
        arr = arr.view(NB_BF16)
    return np.ascontiguousarray(arr.astype(dtype, copy=False))


def emit_fp4_quant(vector, dst_bf, src_f32, scr, nrows, ncols):
    nb = ncols // 16
    x3 = src_f32.rearrange("p (b k) -> p b k", k=16)
    mx = scr["mx"][0:nrows, 0:nb]
    vector.tensor_reduce(out=mx, in_=x3, axis=AX.X, op=ALU.max, apply_absolute_value=True)
    vector.tensor_scalar(out=mx, in0=mx, scalar1=6e-12, scalar2=None, op0=ALU.max)
    rinv = scr["rinv"][0:nrows, 0:nb]
    vector.reciprocal(out=rinv, in_=mx)
    vector.tensor_scalar(out=rinv, in0=rinv, scalar1=6.0, scalar2=None, op0=ALU.mult)
    xn = scr["xn"][0:nrows, 0:ncols]
    rb = rinv.rearrange("p (b o) -> p b o", o=1).broadcast_to((nrows, nb, 16))
    vector.tensor_tensor(out=xn.rearrange("p (b k) -> p b k", k=16), in0=x3, in1=rb, op=ALU.mult)
    v = scr["v"][0:nrows, 0:ncols]
    t = scr["t"][0:nrows, 0:ncols]
    iscr = scr["iscr"][0:nrows, 0:ncols]
    vector.tensor_scalar(out=t, in0=xn, scalar1=-1.0, scalar2=None, op0=ALU.mult)
    vector.tensor_tensor(out=v, in0=xn, in1=t, op=ALU.max)
    u = scr["u"][0:nrows, 0:ncols]
    vector.tensor_scalar(out=u, in0=v, scalar1=2.0, scalar2=2.0, op0=ALU.min, op1=ALU.mult)
    vector.tensor_copy(out=iscr, in_=u)
    vector.tensor_copy(out=u, in_=iscr)
    w = scr["w"][0:nrows, 0:ncols]
    vector.tensor_scalar(out=w, in0=v, scalar1=2.0, scalar2=4.0, op0=ALU.max, op1=ALU.min)
    vector.tensor_copy(out=iscr, in_=w)
    vector.tensor_copy(out=w, in_=iscr)
    hi = t
    vector.tensor_scalar(out=hi, in0=v, scalar1=5.0, scalar2=None, op0=ALU.is_gt)
    vector.scalar_tensor_tensor(out=u, in0=u, scalar=0.5, in1=w, op0=ALU.mult, op1=ALU.add)
    vector.scalar_tensor_tensor(out=u, in0=hi, scalar=2.0, in1=u, op0=ALU.mult, op1=ALU.add)
    vector.tensor_scalar(out=u, in0=u, scalar1=-2.0, scalar2=1.0 / 6.0, op0=ALU.add, op1=ALU.mult)
    sgp = w
    vector.tensor_scalar(out=sgp, in0=xn, scalar1=0.0, scalar2=None, op0=ALU.is_lt)
    vector.scalar_tensor_tensor(out=sgp, in0=xn, scalar=0.0, in1=sgp, op0=ALU.is_gt, op1=ALU.subtract)
    vector.tensor_tensor(out=u, in0=u, in1=sgp, op=ALU.mult)
    mxb = mx.rearrange("p (b o) -> p b o", o=1).broadcast_to((nrows, nb, 16))
    vector.tensor_tensor(out=dst_bf.rearrange("p (b k) -> p b k", k=16),
                         in0=u.rearrange("p (b k) -> p b k", k=16), in1=mxb, op=ALU.mult)


def build_prep():
    c = Ctx("prep")
    nc = c.nc

    wg = c.din("wg_sl", [384, 1024], F32)
    wu = c.din("wu_sl", [384, 1024], F32)
    wd = c.din("wd_sl", [128, 2816], F32)
    w1t = c.din("w1t", [1024, 4096], F32)
    w2t = c.din("w2t", [4096, 1024], F32)
    w1t_sl = c.din("w1t_sl", [128, 4096], F32)
    w2t_sl = c.din("w2t_sl", [512, 1024], F32)

    wgq = c.dout("wgq", [384, 1024], BF16)
    wuq = c.dout("wuq", [384, 1024], BF16)
    wdq = c.dout("wdq", [128, 2816], BF16)
    w1q_sl = c.dout("w1q_sl", [128, 4096], BF16)
    w2q_sl = c.dout("w2q_sl", [512, 1024], BF16)

    sin = c.sb("sin", [128, 8960], F32)
    sout = c.sb("sout", [128, 8960], BF16)
    scr = {k: c.sb("scr_" + k, [128, 2816], F32)
           for k in ["xn", "v", "u", "t", "w"]}
    scr["mx"] = c.sb("scr_mx", [128, 176], F32)
    scr["rinv"] = c.sb("scr_rinv", [128, 176], F32)
    scr["iscr"] = c.sb("scr_iscr", [128, 2816], mybir.dt.int32)
    bufA = c.sb("bufA", [128, 4096], F32)
    bufB = c.sb("bufB", [128, 4096], F32)
    qout = c.sb("qout", [128, 4096], BF16)
    qscr = c.sb("qscr", [128, 4096], F32)
    acc1 = c.sb("acc1", [128, 1], F32)
    acc2 = c.sb("acc2", [128, 1], F32)
    redcol = c.sb("redcol", [128, 1], F32)
    ones_col = c.sb("ones_col", [128, 1], F32)
    ones_row = c.sb("ones_row", [1, 128], F32)
    thr1 = c.sb("thr1", [128, 2], F32)
    thr2 = c.sb("thr2", [128, 2], F32)
    grow = c.sb("grow", [1, 2], F32)
    ps_s = c.ps("ps_s", [1, 1], F32)
    ps_b = c.ps("ps_b", [128, 1], F32)

    for s in ["ld", "la0", "la1", "pe", "dve", "st"]:
        c.sem(s)

    with nc.Block() as block:
        @block.sync
        def _(sync):
            c.dma(sync, "ld", sin[:, 0:3 * 1024].rearrange("p (a d) -> p a d", a=3), wg.rearrange("(a p) d -> p a d", p=128))
            c.dma(sync, "ld", sin[:, 3 * 1024:6 * 1024].rearrange("p (a d) -> p a d", a=3), wu.rearrange("(a p) d -> p a d", p=128))
            c.dma(sync, "ld", sin[:, 6 * 1024:6 * 1024 + 2816], wd)

        @block.gpsimd
        def _(gpsimd):
            gpsimd.memset(ones_col[:, :], 1.0)
            gpsimd.memset(ones_row[:, :], 1.0)

        @block.vector
        def _(vector):
            c.wait(vector, "ld")
            v = Chain(c, vector, "dvec")
            for j in range(6):
                emit_fp4_quant(v, sout[:, j * 1024:(j + 1) * 1024],
                               sin[:, j * 1024:(j + 1) * 1024], scr, 128, 1024)
            emit_fp4_quant(v, sout[:, 6 * 1024:6 * 1024 + 2816],
                           sin[:, 6 * 1024:6 * 1024 + 2816], scr, 128, 2816)

    with nc.Block() as block:
        @block.sync
        def _(sync):
            c.dma(sync, "st", wgq.rearrange("(a p) d -> p a d", p=128), sout[:, 0:3 * 1024].rearrange("p (a d) -> p a d", a=3))
            c.dma(sync, "st", wuq.rearrange("(a p) d -> p a d", p=128), sout[:, 3 * 1024:6 * 1024].rearrange("p (a d) -> p a d", a=3))
            c.dma(sync, "st", wdq, sout[:, 6 * 1024:6 * 1024 + 2816])

    def abs_sum_pass(src_dram, ntiles, ncols, acc):
        ld_base = {sm: c.val(sm) for sm in ("la0", "la1")}
        iter_done = []
        with nc.Block() as block:
            @block.vector
            def _(vector):
                v = Chain(c, vector, "dvec")
                for i in range(ntiles):
                    sm = "la0" if i % 2 == 0 else "la1"
                    vector.wait_ge(c.sems[sm][0], ld_base[sm] + 16 * (i // 2 + 1))
                    buf = (bufA if i % 2 == 0 else bufB)
                    v.tensor_reduce(out=redcol[:, :], in_=buf[:, 0:ncols], axis=AX.X,
                                    op=ALU.add, apply_absolute_value=True)
                    if i == 0:
                        v.tensor_copy(out=acc[:, :], in_=redcol[:, :])
                    else:
                        v.tensor_tensor(out=acc[:, :], in0=acc[:, :], in1=redcol[:, :], op=ALU.add)
                    iter_done.append(c.val("dvec"))

            @block.sync
            def _(sync):
                for i in range(ntiles):
                    if i >= 2:
                        sync.wait_ge(c.sems["dvec"][0], iter_done[i - 2])
                    c.dma(sync, "la0" if i % 2 == 0 else "la1",
                          (bufA if i % 2 == 0 else bufB)[:, 0:ncols], src_dram[i * 128:(i + 1) * 128, :])

    abs_sum_pass(w1t, 8, 4096, acc1)
    abs_sum_pass(w2t, 32, 1024, acc2)

    def gamma_thr(acc, numel, thr, col):
        pe_base = c.val("pe")
        grow_done = []
        with nc.Block() as block:
            @block.vector
            def _(vector):
                v = Chain(c, vector, "dvec")
                vector.wait_ge(c.sems["pe"][0], pe_base + 1)
                v.tensor_scalar(out=grow[0:1, col:col + 1], in0=ps_s[0:1, 0:1],
                                scalar1=1.0 / numel, scalar2=None, op0=ALU.mult)
                grow_done.append(c.val("dvec"))
                vector.wait_ge(c.sems["pe"][0], pe_base + 2)
                v.tensor_scalar(out=thr[:, 0:1], in0=ps_b[:, :], scalar1=1e-5, scalar2=0.5,
                                op0=ALU.add, op1=ALU.mult)
                v.tensor_scalar(out=thr[:, 1:2], in0=thr[:, 0:1], scalar1=-1.0, scalar2=None,
                                op0=ALU.mult)

            @block.tensor
            def _(tensor):
                c.inc(tensor.matmul(ps_s[:, :], ones_col[:, :], acc[:, :], start=True, stop=True), "pe", 1)
                tensor.wait_ge(c.sems["dvec"][0], grow_done[0])
                c.inc(tensor.matmul(ps_b[:, :], ones_row[:, :], grow[0:1, col:col + 1], start=True, stop=True), "pe", 1)

    gamma_thr(acc1, 4096 * 1024, thr1, 0)
    gamma_thr(acc2, 4096 * 1024, thr2, 1)

    def tern_quant(src_dram, dst_dram, ntiles, ncols, thr):
        ld_base = {sm: c.val(sm) for sm in ("la0", "la1")}
        st_base = c.val("st")
        iter_done = []
        with nc.Block() as block:
            @block.vector
            def _(vector):
                v = Chain(c, vector, "dvec")
                for i in range(ntiles):
                    sm = "la0" if i % 2 == 0 else "la1"
                    vector.wait_ge(c.sems[sm][0], ld_base[sm] + 16 * (i // 2 + 1))
                    vector.wait_ge(c.sems["st"][0], st_base + 16 * i)
                    buf = (bufA if i % 2 == 0 else bufB)
                    v.tensor_scalar(out=qscr[:, 0:ncols], in0=buf[:, 0:ncols],
                                    scalar1=thr[:, 1:2], scalar2=None, op0=ALU.is_lt)
                    v.scalar_tensor_tensor(out=qout[:, 0:ncols], in0=buf[:, 0:ncols],
                                           scalar=thr[:, 0:1], in1=qscr[:, 0:ncols],
                                           op0=ALU.is_gt, op1=ALU.subtract)
                    iter_done.append(c.val("dvec"))

            @block.sync
            def _(sync):
                for i in range(ntiles):
                    if i >= 2:
                        sync.wait_ge(c.sems["dvec"][0], iter_done[i - 2])
                    c.dma(sync, "la0" if i % 2 == 0 else "la1",
                          (bufA if i % 2 == 0 else bufB)[:, 0:ncols], src_dram[i * 128:(i + 1) * 128, :])

            @block.scalar
            def _(scalar):
                for i in range(ntiles):
                    scalar.wait_ge(c.sems["dvec"][0], iter_done[i])
                    c.dma(scalar, "st", dst_dram[i * 128:(i + 1) * 128, :], qout[:, 0:ncols])

    tern_quant(w1t_sl, w1q_sl, 1, 4096, thr1)
    tern_quant(w2t_sl, w2q_sl, 4, 1024, thr2)

    with nc.Block() as block:
        @block.sync
        def _(sync):
            c.wait(sync, "st")

    c.close()
    return c


def emit_ln_feat(c, nc, tiles_f32, DT, Tlen, prm, apply_fn):
    NCH = Tlen // 512
    prm["_bc_evac"] = {}
    prm["_rows_done"] = {}
    prm["_bc_pe"] = {}
    with nc.Block() as block:
        for ch in range(NCH):
            pe_ch = c.val("pe")
            sq_done = []

            @block.scalar
            def _(scalar, ch=ch, pe_ch=pe_ch, sq_done=sq_done):
                a = Chain(c, scalar, "actc")
                for i in range(DT):
                    if i >= 2:
                        scalar.wait_ge(c.sems["pe"][0], pe_ch + 2 * (i - 2) + 2)
                    elif ch > 0:
                        scalar.wait_ge(c.sems["pe"][0], pe_ch)
                    a.activation(out=(prm["sq0"] if i % 2 == 0 else prm["sq1"])[:, :],
                                 in_=tiles_f32(i)[:, ch * 512:(ch + 1) * 512], func=AF.Square)
                    sq_done.append(c.val("actc"))

            @block.tensor
            def _(tensor, ch=ch, sq_done=sq_done):
                if ch > 0:
                    tensor.wait_ge(c.sems["dvec"][0], prm["_rows_done"][ch - 1])
                for i in range(DT):
                    c.inc(tensor.matmul(prm["ps_row"][:, :], prm["ones_col_f32"][:, :],
                                        tiles_f32(i)[:, ch * 512:(ch + 1) * 512],
                                        start=(i == 0), stop=(i == DT - 1)), "pe", 1)
                    tensor.wait_ge(c.sems["actc"][0], sq_done[i])
                    c.inc(tensor.matmul(prm["ps_row2"][:, :], prm["ones_col_f32"][:, :],
                                        (prm["sq0"] if i % 2 == 0 else prm["sq1"])[:, :],
                                        start=(i == 0), stop=(i == DT - 1)), "pe", 1)

            rows_done = []

            @block.vector
            def _(vector, ch=ch, pe_ch=pe_ch, rows_done=rows_done):
                v = Chain(c, vector, "dvec")
                vector.wait_ge(c.sems["pe"][0], pe_ch + 2 * DT)
                v.tensor_scalar(out=prm["murow"][0:1, :], in0=prm["ps_row"][:, :],
                                scalar1=1.0 / (DT * 128), scalar2=None, op0=ALU.mult)
                v.tensor_scalar(out=prm["varrow"][0:1, :], in0=prm["ps_row2"][:, :],
                                scalar1=1.0 / (DT * 128), scalar2=None, op0=ALU.mult)
                v.tensor_tensor(out=prm["r2row"][0:1, :], in0=prm["murow"][0:1, :],
                                in1=prm["murow"][0:1, :], op=ALU.mult)
                v.tensor_tensor(out=prm["varrow"][0:1, :], in0=prm["varrow"][0:1, :],
                                in1=prm["r2row"][0:1, :], op=ALU.subtract)
                rows_done.append(c.val("dvec"))
                prm["_rows_done"][ch] = c.val("dvec")

            rstd_done = []

            @block.scalar
            def _(scalar, ch=ch, rows_done=rows_done, rstd_done=rstd_done):
                a = Chain(c, scalar, "actc")
                scalar.wait_ge(c.sems["dvec"][0], rows_done[0])
                if ch > 0:
                    scalar.wait_ge(c.sems["pe"][0], prm["_bc_pe"][ch - 1])
                a.activation(out=prm["rstdrow"][0:1, :], in_=prm["varrow"][0:1, :],
                             func=AF.Ln, bias=prm["epscol"][0:1, 0:1])
                a.activation(out=prm["rstdrow"][0:1, :], in_=prm["rstdrow"][0:1, :],
                             func=AF.Exp, scale=prm["nhcol"][0:1, 0:1])
                rstd_done.append(c.val("actc"))

            r2_done = []

            @block.vector
            def _(vector, ch=ch, rstd_done=rstd_done, r2_done=r2_done):
                v = Chain(c, vector, "dvec")
                vector.wait_ge(c.sems["actc"][0], rstd_done[0])
                v.scalar_tensor_tensor(out=prm["r2row"][0:1, :], in0=prm["murow"][0:1, :],
                                       scalar=-1.0, in1=prm["rstdrow"][0:1, :],
                                       op0=ALU.mult, op1=ALU.mult)
                r2_done.append(c.val("dvec"))

            bc_pe = []

            @block.tensor
            def _(tensor, ch=ch, r2_done=r2_done):
                tensor.wait_ge(c.sems["dvec"][0], r2_done[0])
                if ch > 0:
                    tensor.wait_ge(c.sems["dvec"][0], prm["_bc_evac"][ch - 1])
                c.inc(tensor.matmul(prm["bcA"][:, :], prm["ones_row_f32"][:, :],
                                    prm["rstdrow"][0:1, :], start=True, stop=True), "pe", 1)
                c.inc(tensor.matmul(prm["bcB"][:, :], prm["ones_row_f32"][:, :],
                                    prm["r2row"][0:1, :], start=True, stop=True), "pe", 1)
                bc_pe.append(c.val("pe"))
                prm["_bc_pe"][ch] = c.val("pe")

            @block.vector
            def _(vector, ch=ch, bc_pe=bc_pe):
                v = Chain(c, vector, "dvec")
                cs = slice(ch * 512, (ch + 1) * 512)
                vector.wait_ge(c.sems["pe"][0], bc_pe[0])
                v.tensor_copy(out=prm["r1b"][:, cs], in_=prm["bcA"][:, :])
                v.tensor_copy(out=prm["r2b"][:, cs], in_=prm["bcB"][:, :])
                prm["_bc_evac"][ch] = c.val("dvec")

        @block.vector
        def _(vector):
            v = Chain(c, vector, "dvec")
            apply_fn(v)


def build_l0():
    c = Ctx("l0")
    nc = c.nc
    DT = 8
    NCH = 4

    x_in = c.din("x_feat", [D, T], F32)
    wqkv = c.din("wqkv", [D, 3072], F32)
    bq_col_in = c.din("bqkv_col", [128, 24], F32)
    bv_in = c.din("bv_row", [1, 1024], F32)
    lng_in = c.din("lng_col", [128, 8], F32)
    lnb_in = c.din("lnb_col", [128, 8], F32)
    rw_in = c.din("rw_col", [128, 64], F32)
    eye_in = c.din("eye", [128, 128], F32)
    gup = c.din("gup_t", [8, D, 6144], F32)
    dwn = c.din("down_t", [8, 3072, D], F32)
    sgup = c.din("sgup_t", [D, 2048], F32)
    sdwn = c.din("sdwn_t", [1024, D], F32)

    q_out = c.dout("q0", [1024, T], BF16)
    k_out = c.dout("k0", [1024, T], BF16)
    v_out = c.dout("v0", [T, 1024], BF16)
    g_out = c.dout("g0", [D, T], F32)
    xn_out = c.dout("xn0", [D, T], BF16)
    comb_out = c.dout("comb_dbg", [8, T], BF16)
    fs_out = c.dout("facc_sh", [D, 512], F32)

    xs = c.sb("xs", [128, DT * T], F32)
    nrm = c.sb("nrm", [128, DT * T], BF16)
    rows4 = c.sb("rows4", [1, 2048], F32)
    murow = rows4[0:1, 0:512]
    varrow = rows4[0:1, 512:1024]
    rstdrow = rows4[0:1, 1024:1536]
    r2row = rows4[0:1, 1536:2048]
    ones_col_f32 = c.sb("ones_col_f32", [128, 1], F32)
    ones_row_f32 = c.sb("ones_row_f32", [1, 128], F32)
    ones_row_bf = c.sb("ones_row_bf", [1, 128], BF16)
    epscol = c.sb("epscol", [1, 1], F32)
    nhcol = c.sb("nhcol", [1, 1], F32)
    bq_col = c.sb("bq_col", [128, 24], F32)
    bv_row = c.sb("bv_rowb", [1, 1024], BF16)
    lng_c = c.sb("lng_c", [128, 8], F32)
    lnb_c = c.sb("lnb_c", [128, 8], F32)
    rwf = c.sb("rwf", [128, 64], F32)
    normf = c.sb("normf", [128, 1024], F32)
    idb = c.sb("idb", [128, 128], BF16)
    m1c = c.sb("m1c", [128, 1], F32)
    m2c = c.sb("m2c", [128, 1], F32)
    sumc = c.sb("sumc", [128, 1], F32)
    d12c = c.sb("d12c", [128, 1], F32)
    g1col = c.sb("g1col", [128, 1], F32)
    g2col = c.sb("g2col", [128, 1], F32)
    probs8 = c.sb("probs8", [128, 8], F32)
    tmp8 = c.sb("tmp8", [128, 8], F32)
    eq1 = c.sb("eq1", [128, 8], F32)
    eq2 = c.sb("eq2", [128, 8], F32)
    combt = c.sb("combt", [128, 8], BF16)
    comb_feat = c.sb("comb_feat", [8, T], BF16)
    comb_row_cur = c.sb("comb_row_cur", [1, T], BF16)

    wfA = c.sb("wfA", [128, 4096], F32)
    wfB = c.sb("wfB", [128, 4096], F32)
    wbA = c.sb("wbA", [128, 4096], BF16)
    wbB = c.sb("wbB", [128, 4096], BF16)
    sg = c.sb("sg", [128, 512], BF16)
    r1b = wfA[:, 0:2048]
    r2b = wfA[:, 2048:4096]
    tA = wfB[:, 0:2048]
    gstage = [wfA[:, 0:2048], wfB[:, 0:2048]]
    qk_stage = [nrm[:, 0:2048], nrm[:, 2048:4096]]
    vstage = [nrm[:, 4096:4608], nrm[:, 4608:5120]]
    sq0 = c.sb("sq0", [128, 512], F32)
    sq1 = c.sb("sq1", [128, 512], F32)

    pb = [c.ps(f"pb{i}", [128, 512], F32) for i in range(6)]

    for s in ["ld", "wl0", "wl1", "dl0", "dl1", "dl2", "pe", "actc", "dvec",
              "st", "st0", "st1", "sv0", "sv1", "sg0", "sg1", "cld"]:
        c.sem(s)

    def xt(i):
        return xs[:, i * T:(i + 1) * T]

    def nrmt(i):
        return nrm[:, i * T:(i + 1) * T]

    prm = dict(sq0=sq0, sq1=sq1, murow=murow, varrow=varrow, rstdrow=rstdrow,
               r2row=r2row, r1b=r1b, r2b=r2b, ones_col_f32=ones_col_f32,
               ones_row_f32=ones_row_f32, epscol=epscol, nhcol=nhcol)

    stackP = ExitStack()
    ps_row = stackP.enter_context(nc.psum_tensor("ps_row", [1, 512], F32))
    ps_row2 = stackP.enter_context(nc.psum_tensor("ps_row2", [1, 512], F32))
    prm["ps_row"] = ps_row
    prm["ps_row2"] = ps_row2
    prm["bcA"] = pb[4]
    prm["bcB"] = pb[5]

    xnb_stack = ExitStack()
    xnb = xnb_stack.enter_context(nc.sbuf_tensor("xnb", [128, DT * T], BF16))

    def xnbt(i):
        return xnb[:, i * T:(i + 1) * T]

    with nc.Block() as block:
        @block.sync
        def _(sync):
            for i in range(DT):
                c.dma(sync, "ld", xt(i), x_in[i * 128:(i + 1) * 128, :])
            c.dma(sync, "ld", bq_col[:, :], bq_col_in)
            c.dma(sync, "ld", lng_c[:, :], lng_in)
            c.dma(sync, "ld", lnb_c[:, :], lnb_in)
            c.dma(sync, "ld", rwf[:, :], rw_in)
            c.dma(sync, "ld", wfA[:, 64:192], eye_in)
            c.dma(sync, "ld", wfA[0:1, 192:1216], bv_in)

        @block.gpsimd
        def _(gpsimd):
            gpsimd.memset(ones_col_f32[:, :], 1.0)
            gpsimd.memset(ones_row_f32[:, :], 1.0)
            gpsimd.memset(ones_row_bf[:, :], 1.0)
            gpsimd.memset(epscol[:, :], LN_EPS)
            gpsimd.memset(nhcol[:, :], -0.5)

        @block.vector
        def _(vector):
            v = Chain(c, vector, "dvec")
            c.wait(vector, "ld")
            v.tensor_copy(out=idb[:, :], in_=wfA[:, 64:192])
            v.tensor_copy(out=bv_row[0:1, :], in_=wfA[0:1, 192:1216])

    def apply_ln1(v):
        for i in range(DT):
            v.tensor_tensor(out=xt(i), in0=xt(i), in1=r1b[:, :], op=ALU.mult)
            v.tensor_tensor(out=xt(i), in0=xt(i), in1=r2b[:, :], op=ALU.add)
            v.tensor_copy(out=xnbt(i), in_=xt(i))

    emit_ln_feat(c, nc, xt, DT, T, prm, apply_ln1)

    with nc.Block() as block:
        @block.sync
        def _(sync):
            c.wait(sync, "dvec")
            for i in range(DT):
                c.dma(sync, "st", xn_out[i * 128:(i + 1) * 128, :], xnbt(i))

    with nc.Block() as block:
        if L0_SKIP_QKV:
            pass
        cast_done = {}
        mm_done = {}
        mm_done_v = {}
        qk_pe = {}
        evac_done = {}
        store_done = {}
        qkv_wl = {}
        for blk in range(0 if L0_SKIP_QKV else 6):
            @block.sync
            def _(sync, blk=blk):
                if blk >= 2:
                    sync.wait_ge(c.sems["dvec"][0], cast_done[blk - 2])
                sm = "wl0" if blk % 2 == 0 else "wl1"
                c.dma(sync, sm, (wfA if blk % 2 == 0 else wfB)[:, :].rearrange("p (k m) -> p k m", k=8), wqkv[:, blk * 512:(blk + 1) * 512].rearrange("(k p) m -> p k m", p=128))
                qkv_wl[blk] = (sm, c.val(sm))

            @block.vector
            def _(vector, blk=blk):
                v = Chain(c, vector, "dvec")
                vector.wait_ge(c.sems[qkv_wl[blk][0]][0], qkv_wl[blk][1])
                if blk >= 2:
                    vector.wait_ge(c.sems["pe"][0], mm_done[blk - 2])
                v.tensor_copy(out=(wbA if blk % 2 == 0 else wbB)[:, :],
                              in_=(wfA if blk % 2 == 0 else wfB)[:, :])
                cast_done[blk] = c.val("dvec")

            wb_cur = wbA if blk % 2 == 0 else wbB
            if blk < 4:
                for m in range(4):
                    mg = blk * 4 + m
                    stg = qk_stage[mg % 2]

                    @block.tensor
                    def _(tensor, blk=blk, m=m, mg=mg, wb_cur=wb_cur):
                        tensor.wait_ge(c.sems["dvec"][0], cast_done[blk])
                        for ch in range(NCH):
                            bidx = (mg * 4 + ch) % 4
                            key = (mg * 4 + ch) - 4
                            if key >= 0:
                                tensor.wait_ge(c.sems["actc"][0], evac_done[key])
                            for k in range(8):
                                c.inc(tensor.matmul(
                                    pb[bidx][:, :],
                                    wb_cur[:, k * 512 + m * 128:k * 512 + (m + 1) * 128],
                                    xnbt(k)[:, ch * 512:(ch + 1) * 512],
                                    start=(k == 0), stop=(k == 7)), "pe", 1)
                            qk_pe[(mg, ch)] = c.val("pe")
                        mm_done[blk] = c.val("pe")

                    @block.scalar
                    def _(scalar, mg=mg, stg=stg):
                        a = Chain(c, scalar, "actc")
                        if mg >= 2:
                            scalar.wait_ge(c.sems[store_done[mg - 2][0]][0], store_done[mg - 2][1])
                        for ch in range(NCH):
                            bidx = (mg * 4 + ch) % 4
                            scalar.wait_ge(c.sems["pe"][0], qk_pe[(mg, ch)])
                            a.activation(out=stg[:, ch * 512:(ch + 1) * 512], in_=pb[bidx][:, :],
                                         func=AF.Identity, bias=bq_col[:, mg:mg + 1])
                            evac_done[(mg * 4 + ch)] = c.val("actc")

                    @block.sync
                    def _(sync, mg=mg, stg=stg):
                        sync.wait_ge(c.sems["actc"][0], evac_done[mg * 4 + 3])
                        dst = q_out if mg < 8 else k_out
                        r = mg if mg < 8 else mg - 8
                        sm = "st0" if mg % 2 == 0 else "st1"
                        c.dma(sync, sm, dst[r * 128:(r + 1) * 128, :], stg[:, :])
                        store_done[mg] = (sm, c.val(sm))
            else:
                for tt in range(16):
                    stg = vstage[tt % 2]

                    @block.tensor
                    def _(tensor, blk=blk, tt=tt, wb_cur=wb_cur):
                        tensor.wait_ge(c.sems["dvec"][0], cast_done[blk])
                        bidx = tt % 4
                        key = ("v", blk, tt - 4)
                        if tt >= 4:
                            tensor.wait_ge(c.sems["actc"][0], evac_done[key])
                        elif blk == 5:
                            tensor.wait_ge(c.sems["actc"][0], evac_done[("v", 4, 12 + tt)])
                        else:
                            tensor.wait_ge(c.sems["actc"][0], evac_done[60 + tt])
                        for k in range(8):
                            c.inc(tensor.matmul(
                                pb[bidx][:, :],
                                xnbt(k)[:, tt * 128:(tt + 1) * 128],
                                wb_cur[:, k * 512:(k + 1) * 512],
                                start=(k == 0), stop=False), "pe", 1)
                        c.inc(tensor.matmul(
                            pb[bidx][:, :], ones_row_bf[:, :],
                            bv_row[0:1, (blk - 4) * 512:(blk - 3) * 512],
                            start=False, stop=True), "pe", 1)
                        mm_done_v[(blk, tt)] = c.val("pe")

                    @block.scalar
                    def _(scalar, blk=blk, tt=tt, stg=stg):
                        a = Chain(c, scalar, "actc")
                        if tt >= 2:
                            sd = store_done[("v", blk, tt - 2)]
                            scalar.wait_ge(c.sems[sd[0]][0], sd[1])
                        elif blk == 5:
                            sd = store_done[("v", 4, 14 + tt)]
                            scalar.wait_ge(c.sems[sd[0]][0], sd[1])
                        scalar.wait_ge(c.sems["pe"][0], mm_done_v[(blk, tt)])
                        a.activation(out=stg[:, :], in_=pb[tt % 4][:, :], func=AF.Copy)
                        evac_done[("v", blk, tt)] = c.val("actc")

                    @block.sync
                    def _(sync, blk=blk, tt=tt, stg=stg):
                        sync.wait_ge(c.sems["actc"][0], evac_done[("v", blk, tt)])
                        sm = "sv0" if tt % 2 == 0 else "sv1"
                        c.dma(sync, sm, v_out[tt * 128:(tt + 1) * 128, (blk - 4) * 512:(blk - 3) * 512], stg[:, :])
                        store_done[("v", blk, tt)] = (sm, c.val(sm))

    st_parities = ["st0", "st1", "sv0", "sv1"]
    st_after_b3 = [(sm, c.val(sm)) for sm in st_parities]

    def apply_ln2(v):
        for sm, tv in st_after_b3:
            v.wait_ge(c.sems[sm][0], tv)
        for i in range(DT):
            v.tensor_tensor(out=tA[:, :], in0=xt(i), in1=r1b[:, :], op=ALU.mult)
            v.tensor_tensor(out=tA[:, :], in0=tA[:, :], in1=r2b[:, :], op=ALU.add)
            v.tensor_scalar(out=nrmt(i), in0=tA[:, :], scalar1=lng_c[:, i:i + 1],
                            scalar2=lnb_c[:, i:i + 1], op0=ALU.mult, op1=ALU.add)

    emit_ln_feat(c, nc, xt, DT, T, prm, apply_ln2)

    stackP.close()
    stackR = ExitStack()
    ps_t = stackR.enter_context(nc.psum_tensor("ps_t", [8, 128], BF16))

    with nc.Block() as block:
        exp_done = {}
        tr_done = {}
        ev_done = {}
        rt_pe = {}
        m1_done = {}
        d12_done = {}
        sig_done = {}
        comb_done = {}
        nf_done = {}
        for tt in range(16):
            @block.vector
            def _(vector, tt=tt):
                v = Chain(c, vector, "dvec")
                cs = slice(tt * 128, (tt + 1) * 128)
                if tt > 0:
                    vector.wait_ge(c.sems["pe"][0], rt_pe[tt - 1])
                for i in range(DT):
                    nf = normf[:, i * 128:(i + 1) * 128]
                    v.tensor_tensor(out=nf, in0=xt(i)[:, cs], in1=r1b[:, cs], op=ALU.mult)
                    v.tensor_tensor(out=nf, in0=nf, in1=r2b[:, cs], op=ALU.add)
                    v.tensor_scalar(out=nf, in0=nf, scalar1=lng_c[:, i:i + 1],
                                    scalar2=lnb_c[:, i:i + 1], op0=ALU.mult, op1=ALU.add)
                nf_done[tt] = c.val("dvec")

            @block.tensor
            def _(tensor, tt=tt):
                tensor.wait_ge(c.sems["dvec"][0], nf_done[tt])
                if tt > 0:
                    tensor.wait_ge(c.sems["actc"][0], exp_done[tt - 1])
                for k in range(8):
                    c.inc(tensor.matmul(pb[4][:, 0:8],
                                        normf[:, k * 128:(k + 1) * 128],
                                        rwf[:, k * 8:(k + 1) * 8],
                                        start=(k == 0), stop=(k == 7)), "pe", 1)
                rt_pe[tt] = c.val("pe")

            @block.vector
            def _(vector, tt=tt):
                v = Chain(c, vector, "dvec")
                vector.wait_ge(c.sems["pe"][0], rt_pe[tt])
                v.reduce_max(out=m1c[:, :], in_=pb[4][:, 0:8], axis=AX.X)
                v.tensor_scalar(out=m1c[:, :], in0=m1c[:, :], scalar1=-1.0, scalar2=None, op0=ALU.mult)
                m1_done[tt] = c.val("dvec")

            @block.scalar
            def _(scalar, tt=tt):
                a = Chain(c, scalar, "actc")
                scalar.wait_ge(c.sems["dvec"][0], m1_done[tt])
                a.activation(out=probs8[:, :], in_=pb[4][:, 0:8], func=AF.Exp, bias=m1c[:, 0:1])
                exp_done[tt] = c.val("actc")

            @block.vector
            def _(vector, tt=tt):
                v = Chain(c, vector, "dvec")
                vector.wait_ge(c.sems["actc"][0], exp_done[tt])
                v.reduce_sum(out=sumc[:, :], in_=probs8[:, :], axis=AX.X)
                v.reciprocal(out=sumc[:, :], in_=sumc[:, :])
                v.tensor_scalar(out=probs8[:, :], in0=probs8[:, :], scalar1=sumc[:, 0:1],
                                scalar2=None, op0=ALU.mult)
                v.reduce_max(out=m1c[:, :], in_=probs8[:, :], axis=AX.X)
                v.tensor_scalar(out=eq1[:, :], in0=probs8[:, :], scalar1=m1c[:, 0:1],
                                scalar2=None, op0=ALU.is_ge)
                v.scalar_tensor_tensor(out=tmp8[:, :], in0=eq1[:, :], scalar=-2.0, in1=probs8[:, :],
                                       op0=ALU.mult, op1=ALU.add)
                v.reduce_max(out=m2c[:, :], in_=tmp8[:, :], axis=AX.X)
                v.tensor_scalar(out=eq2[:, :], in0=tmp8[:, :], scalar1=m2c[:, 0:1],
                                scalar2=None, op0=ALU.is_ge)
                v.tensor_tensor(out=d12c[:, :], in0=m1c[:, :], in1=m2c[:, :], op=ALU.subtract)
                d12_done[tt] = c.val("dvec")

            @block.scalar
            def _(scalar, tt=tt):
                a = Chain(c, scalar, "actc")
                scalar.wait_ge(c.sems["dvec"][0], d12_done[tt])
                a.activation(out=g1col[:, :], in_=d12c[:, :], func=AF.Sigmoid)
                sig_done[tt] = c.val("actc")

            @block.vector
            def _(vector, tt=tt):
                v = Chain(c, vector, "dvec")
                vector.wait_ge(c.sems["actc"][0], sig_done[tt])
                v.tensor_scalar(out=g2col[:, :], in0=g1col[:, :], scalar1=-1.0, scalar2=1.0,
                                op0=ALU.mult, op1=ALU.add)
                v.tensor_scalar(out=eq1[:, :], in0=eq1[:, :], scalar1=g1col[:, 0:1],
                                scalar2=None, op0=ALU.mult)
                v.tensor_scalar(out=eq2[:, :], in0=eq2[:, :], scalar1=g2col[:, 0:1],
                                scalar2=None, op0=ALU.mult)
                v.tensor_tensor(out=combt[:, :], in0=eq1[:, :], in1=eq2[:, :], op=ALU.add)
                comb_done[tt] = c.val("dvec")

            @block.tensor
            def _(tensor, tt=tt):
                tensor.wait_ge(c.sems["dvec"][0], comb_done[tt])
                if tt > 0:
                    tensor.wait_ge(c.sems["dvec"][0], ev_done[tt - 1])
                c.inc(tensor.transpose(out=ps_t[:, :], in_=combt[:, :], identity=idb[:, :]), "pe", 1)
                tr_done[tt] = c.val("pe")

            @block.vector
            def _(vector, tt=tt):
                v = Chain(c, vector, "dvec")
                vector.wait_ge(c.sems["pe"][0], tr_done[tt])
                v.tensor_copy(out=comb_feat[0:8, tt * 128:(tt + 1) * 128], in_=ps_t[:, :])
                ev_done[tt] = c.val("dvec")

    comb_ev_done = c.val("dvec")
    stackR.close()
    pb6 = c.ps("pb6", [128, 512], F32)
    pb7 = c.ps("pb7", [128, 512], F32)
    pbank = pb + [pb6, pb7]
    xnb_stack.close()
    hbuf = c.sb("hbuf", [128, 24 * 512], BF16)
    wfg = [wfA[:, 0:1024], wfA[:, 1024:2048]]
    wfu = [wfA[:, 2048:3072], wfA[:, 3072:4096]]
    dsf = [wfB[:, j * 1024:(j + 1) * 1024] for j in range(3)]
    wbg = [wbA[:, 0:1024], wbA[:, 1024:2048]]
    wbu = [wbA[:, 2048:3072], wbA[:, 3072:4096]]
    dsb = [wbB[:, j * 1024:(j + 1) * 1024] for j in range(3)]
    combB = c.sb("combB", [128, T], BF16)

    def emit_ffn_expert(e, quarter, n_mp, n_ks, gup_ap, dwn_ap, gate_off, comb):
        t0 = quarter * 512
        with nc.Block() as block:
            gu_cast = {}
            g_pe = {}
            u_pe = {}
            silu_done = {}
            h_done = {}
            for mp in range(n_mp):
                @block.sync
                def _(sync, mp=mp):
                    if mp >= 2:
                        sync.wait_ge(c.sems["dvec"][0], gu_cast[mp - 2])
                    sm = "wl0" if mp % 2 == 0 else "wl1"
                    c.dma(sync, sm, wfg[mp % 2][:, :].rearrange("p (k m) -> p k m", k=8), gup_ap[:, mp * 128:(mp + 1) * 128].rearrange("(k p) m -> p k m", p=128))
                    c.dma(sync, sm, wfu[mp % 2][:, :].rearrange("p (k m) -> p k m", k=8), gup_ap[:, gate_off + mp * 128:gate_off + (mp + 1) * 128].rearrange(
                            "(k p) m -> p k m", p=128))
                    gu_wl[(e, quarter, mp)] = (sm, c.val(sm))

                @block.vector
                def _(vector, mp=mp):
                    v = Chain(c, vector, "dvec")
                    vector.wait_ge(c.sems[gu_wl[(e, quarter, mp)][0]][0], gu_wl[(e, quarter, mp)][1])
                    if mp >= 2:
                        vector.wait_ge(c.sems["pe"][0], u_pe[mp - 2])
                    v.tensor_copy(out=wbg[mp % 2][:, :], in_=wfg[mp % 2][:, :])
                    v.tensor_copy(out=wbu[mp % 2][:, :], in_=wfu[mp % 2][:, :])
                    gu_cast[mp] = c.val("dvec")

                @block.tensor
                def _(tensor, mp=mp):
                    tensor.wait_ge(c.sems["dvec"][0], gu_cast[mp])
                    if mp >= 1:
                        tensor.wait_ge(c.sems["actc"][0], silu_done[mp - 1])
                    for k in range(8):
                        c.inc(tensor.matmul(
                            pbank[0][:, :],
                            wbg[mp % 2][:, k * 128:(k + 1) * 128],
                            nrmt(k)[:, t0:t0 + 512],
                            start=(k == 0), stop=(k == 7)), "pe", 1)
                    g_pe[mp] = c.val("pe")
                    if mp >= 1:
                        tensor.wait_ge(c.sems["dvec"][0], h_done[mp - 1])
                    for k in range(8):
                        c.inc(tensor.matmul(
                            pbank[1][:, :],
                            wbu[mp % 2][:, k * 128:(k + 1) * 128],
                            nrmt(k)[:, t0:t0 + 512],
                            start=(k == 0), stop=(k == 7)), "pe", 1)
                    u_pe[mp] = c.val("pe")

                @block.scalar
                def _(scalar, mp=mp):
                    a = Chain(c, scalar, "actc")
                    scalar.wait_ge(c.sems["pe"][0], g_pe[mp])
                    if mp >= 1:
                        scalar.wait_ge(c.sems["dvec"][0], h_done[mp - 1])
                    a.activation(out=sg[:, :], in_=pbank[0][:, :], func=AF.Silu)
                    silu_done[mp] = c.val("actc")

                @block.vector
                def _(vector, mp=mp):
                    v = Chain(c, vector, "dvec")
                    vector.wait_ge(c.sems["actc"][0], silu_done[mp])
                    vector.wait_ge(c.sems["pe"][0], u_pe[mp])
                    hst = hbuf[:, mp * 512:(mp + 1) * 512]
                    v.tensor_tensor(out=hst, in0=sg[:, :], in1=pbank[1][:, :], op=ALU.mult)
                    if comb is not None:
                        v.tensor_tensor(out=hst, in0=hst, in1=comb[:, t0:t0 + 512], op=ALU.mult)
                    h_done[mp] = c.val("dvec")

            ds_cast = {}
            ds_mm = {}
            for k in range(n_ks):
                @block.sync
                def _(sync, k=k):
                    if k >= 3:
                        sync.wait_ge(c.sems["dvec"][0], ds_cast[k - 3])
                    sm = f"dl{k % 3}"
                    c.dma(sync, sm, dsf[k % 3][:, :], dwn_ap[k * 128:(k + 1) * 128, :])
                    ds_wl[(e, quarter, k)] = (sm, c.val(sm))

                @block.vector
                def _(vector, k=k):
                    v = Chain(c, vector, "dvec")
                    vector.wait_ge(c.sems[ds_wl[(e, quarter, k)][0]][0], ds_wl[(e, quarter, k)][1])
                    if k >= 3:
                        vector.wait_ge(c.sems["pe"][0], ds_mm[k - 3])
                    v.tensor_copy(out=dsb[k % 3][:, :], in_=dsf[k % 3][:, :])
                    ds_cast[k] = c.val("dvec")

                @block.tensor
                def _(tensor, k=k):
                    tensor.wait_ge(c.sems["dvec"][0], ds_cast[k])
                    if k == 0:
                        tensor.wait_ge(c.sems["actc"][0], silu_done[n_mp - 1])
                        tensor.wait_ge(c.sems["dvec"][0], h_done[n_mp - 1])
                    tensor.wait_ge(c.sems["dvec"][0], h_done[k])
                    for m in range(8):
                        c.inc(tensor.matmul(
                            pbank[m][:, :],
                            dsb[k % 3][:, m * 128:(m + 1) * 128],
                            hbuf[:, k * 512:(k + 1) * 512],
                            start=(k == 0), stop=(k == n_ks - 1)), "pe", 1)
                    ds_mm[k] = c.val("pe")

            @block.vector
            def _(vector):
                v = Chain(c, vector, "dvec")
                vector.wait_ge(c.sems["pe"][0], ds_mm[n_ks - 1])
                for m in range(8):
                    dst = xt(m)[:, t0:t0 + 512]
                    if e == "shared":
                        v.tensor_copy(out=dst, in_=pbank[m][:, :])
                    else:
                        v.tensor_tensor(out=dst, in0=dst, in1=pbank[m][:, :], op=ALU.add)

    gu_wl = {}
    ds_wl = {}

    for q in range(L0_NQ):
        emit_ffn_expert("shared", q, 8, 8, sgup, sdwn, 1024, None)

    with nc.Block() as block:
        @block.sync
        def _(sync):
            c.wait(sync, "dvec")
            for i in range(DT):
                c.dma(sync, "st", fs_out[i * 128:(i + 1) * 128, :], xt(i)[:, 0:512])

    with nc.Block() as block:
        @block.gpsimd
        def _(gpsimd):
            gpsimd.wait_ge(c.sems["st"][0], c.val("st"))

    for e in range(L0_NEXP):
        with nc.Block() as block:
            bc_pe = {}
            bc_ev = {}

            @block.sync
            def _(sync, e=e):
                sync.wait_ge(c.sems["dvec"][0], comb_ev_done)
                c.dma(sync, "cld", comb_row_cur[0:1, :], comb_feat[e:e + 1, :])

            combrow_ld = c.val("cld")

            for chh in range(4):
                @block.tensor
                def _(tensor, chh=chh, combrow_ld=combrow_ld):
                    if chh == 0:
                        tensor.wait_ge(c.sems["cld"][0], combrow_ld)
                    else:
                        tensor.wait_ge(c.sems["dvec"][0], bc_ev[chh - 1])
                    c.inc(tensor.matmul(pbank[4][:, :], ones_row_bf[:, :],
                                        comb_row_cur[0:1, chh * 512:(chh + 1) * 512],
                                        start=True, stop=True), "pe", 1)
                    bc_pe[chh] = c.val("pe")

                @block.vector
                def _(vector, chh=chh):
                    v = Chain(c, vector, "dvec")
                    vector.wait_ge(c.sems["pe"][0], bc_pe[chh])
                    v.tensor_copy(out=combB[:, chh * 512:(chh + 1) * 512], in_=pbank[4][:, :])
                    bc_ev[chh] = c.val("dvec")

        for q in range(L0_NQ):
            emit_ffn_expert(e, q, 24, 24, gup[e], dwn[e], 3072, combB)

    with nc.Block() as block:
        sil_done = {}
        gst_done = {}
        for i in range(DT):
            @block.scalar
            def _(scalar, i=i):
                a = Chain(c, scalar, "actc")
                if i >= 2:
                    scalar.wait_ge(c.sems[gst_done[i - 2][0]][0], gst_done[i - 2][1])
                a.activation(out=gstage[i % 2][:, :], in_=xt(i), func=AF.Silu)
                sil_done[i] = c.val("actc")

            @block.sync
            def _(sync, i=i):
                sync.wait_ge(c.sems["actc"][0], sil_done[i])
                sm = "sg0" if i % 2 == 0 else "sg1"
                c.dma(sync, sm, g_out[i * 128:(i + 1) * 128, :], gstage[i % 2][:, :])
                gst_done[i] = (sm, c.val(sm))

    with nc.Block() as block:
        @block.sync
        def _(sync):
            for sm in ["st", "st0", "st1", "sv0", "sv1", "sg0", "sg1"]:
                c.wait(sync, sm)

    c.close()
    return c


def build_attn():
    c = Ctx("attn")
    nc = c.nc
    NQ = S // 512
    NT = S // 128
    NPAIR = 2

    q_in = c.din("qh", [NPAIR, 128, S], BF16)
    k_in = c.din("kh", [NPAIR, 128, S], BF16)
    v_in = c.din("vh", [NPAIR, S, 128], BF16)
    tri_in = c.din("tri", [128, 128], F32)
    eye_in = c.din("eyef", [128, 128], F32)
    ans_out = c.dout("ansh", [NPAIR, 128, S], BF16)

    qb_ = [c.sb(f"qbuf{j}", [128, S], BF16) for j in range(2)]
    kb_ = [c.sb(f"kbuf{j}", [128, S], BF16) for j in range(2)]
    vb_ = [c.sb(f"vbuf{j}", [128, S], BF16) for j in range(2)]
    ansb = [c.sb(f"ansb{j}", [128, S], BF16) for j in range(2)]
    prob = [c.sb(f"prob{j}", [128, 512], BF16) for j in range(4)]
    tri = c.sb("tri_s", [128, 128], F32)
    eyef = c.sb("eyef_s", [128, 128], F32)
    onescol_bf = c.sb("onescol_bf", [128, 1], BF16)
    onesrow = c.sb("onesrow", [1, 128], F32)
    sel0 = c.sb("sel0", [1, 128], F32)
    sel1 = c.sb("sel1", [1, 128], F32)
    scl8 = c.sb("scl8", [128, 1], F32)
    nrm2 = c.sb("nrm2", [128, 128], F32)
    cmax = c.sb("cmax", [128, 4], F32)
    red4 = c.sb("red4", [4, 1], F32)
    brow = c.sb("brow", [1, 4], F32)
    bias_h = c.sb("bias_h", [128, 2], F32)
    recr = c.sb("recr", [1, 1024], F32)
    recipB = c.sb("recipB", [128, 512], F32)
    qq = [c.sb(f"qqs{j}", [128, 512], BF16) for j in range(2)]

    psA = [c.ps(f"psA{j}", [128, 512], F32) for j in range(2)]
    psB = [c.ps(f"psB{j}", [128, 512], F32) for j in range(2)]
    psC = c.ps("psC", [128, 512], F32)
    psD = c.ps("psD", [128, 512], F32)
    psE = c.ps("psE", [128, 512], F32)
    ps_n = c.ps("ps_n", [128, 128], F32)

    for s in ["lq0", "lq1", "pe", "actc", "dvec", "st0", "st1", "ld"]:
        c.sem(s)

    with nc.Block() as block:
        @block.sync
        def _(sync):
            c.dma(sync, "ld", tri[:, :], tri_in)
            c.dma(sync, "ld", eyef[:, :], eye_in)

        @block.gpsimd
        def _(gpsimd):
            gpsimd.memset(onescol_bf[:, :], 1.0)
            gpsimd.memset(onesrow[:, :], 1.0)
            gpsimd.memset(scl8[:, :], 0.125)
            gpsimd.memset(sel0[:, :], 0.0)
            gpsimd.memset(sel1[:, :], 0.0)
            gpsimd.memset(sel0[0:1, 0:64], 1.0)
            gpsimd.memset(sel1[0:1, 64:128], 1.0)

    ld_pair = {}
    store_done = {}

    def emit_loads(block, p):
        sm = f"lq{p % 2}"

        @block.sync
        def _(sync, p=p, sm=sm):
            c.dma(sync, sm, qb_[p % 2][:, :].rearrange("p (a s) -> p a s", a=16),
                  q_in[p].rearrange("p (a s) -> p a s", a=16))
            c.dma(sync, sm, kb_[p % 2][:, :].rearrange("p (a s) -> p a s", a=16),
                  k_in[p].rearrange("p (a s) -> p a s", a=16))
            c.dma(sync, sm, vb_[p % 2][:, :].rearrange("p (t d) -> p t d", d=128),
                  v_in[p].rearrange("(t p) d -> p t d", p=128))
            ld_pair[p] = c.val(sm)

    with nc.Block() as block:
        emit_loads(block, 0)

    def emit_pair(p):
        buf = p % 2
        Q, K, V, A = qb_[buf], kb_[buf], vb_[buf], ansb[buf]
        with nc.Block() as block:
            if p + 1 < NPAIR:
                emit_loads(block, p + 1)

            nb_dv = {}
            nb_pe = {}
            for si, SRC in enumerate([Q, K]):
                for ch in range(NQ):
                    @block.vector
                    def _(vector, si=si, ch=ch, SRC=SRC):
                        v = Chain(c, vector, "dvec")
                        if si == 0 and ch == 0:
                            vector.wait_ge(c.sems[f"lq{p % 2}"][0], ld_pair[p])
                            if p >= 2:
                                sd = store_done[p - 2]
                                vector.wait_ge(c.sems[sd[0]][0], sd[1])
                        key = (si, ch - 2) if ch >= 2 else ((si - 1, NQ - 2 + ch) if si > 0 else None)
                        if key is not None and key in nb_pe:
                            vector.wait_ge(c.sems["pe"][0], nb_pe[key])
                        v.tensor_tensor(out=qq[ch % 2][:, :], in0=SRC[:, ch * 512:(ch + 1) * 512],
                                        in1=SRC[:, ch * 512:(ch + 1) * 512], op=ALU.mult)
                        nb_dv[(si, ch)] = c.val("dvec")

                    @block.tensor
                    def _(tensor, si=si, ch=ch):
                        tensor.wait_ge(c.sems["dvec"][0], nb_dv[(si, ch)])
                        for t4 in range(4):
                            gt = ch * 4 + t4
                            for h in range(2):
                                col = h * 64 + gt % 64
                                c.inc(tensor.matmul(
                                    ps_n[:, col:col + 1],
                                    qq[ch % 2][h * 64:(h + 1) * 64, t4 * 128:(t4 + 1) * 128],
                                    onescol_bf[h * 64:(h + 1) * 64, :],
                                    start=(gt < 64), stop=True,
                                    tile_position=(h * 64, 0)), "pe", 1)
                        nb_pe[(si, ch)] = c.val("pe")

                @block.vector
                def _(vector, si=si):
                    v = Chain(c, vector, "dvec")
                    vector.wait_ge(c.sems["pe"][0], nb_pe[(si, NQ - 1)])
                    v.tensor_copy(out=nrm2[:, :], in_=ps_n[:, :])
                    v.reduce_max(out=cmax[:, 2 * si:2 * si + 1], in_=nrm2[:, 0:64], axis=AX.X)
                    v.reduce_max(out=cmax[:, 2 * si + 1:2 * si + 2], in_=nrm2[:, 64:128], axis=AX.X)
                    nb_dv[("red", si)] = c.val("dvec")


            @block.tensor
            def _(tensor):
                pass

            cb_pe = {}

            @block.tensor
            def _(tensor):
                tensor.wait_ge(c.sems["dvec"][0], nb_dv[("red", 1)])
                c.inc(tensor.transpose(out=ps_n[0:4, 0:128], in_=cmax[:, :], identity=eyef[:, :]),
                      "pe", 1)
                cb_pe[0] = c.val("pe")

            @block.vector
            def _(vector):
                v = Chain(c, vector, "dvec")
                vector.wait_ge(c.sems["pe"][0], cb_pe[0])
                v.reduce_max(out=red4[:, :], in_=ps_n[0:4, 0:128], axis=AX.X)
                nb_dv["red4"] = c.val("dvec")

            @block.tensor
            def _(tensor):
                tensor.wait_ge(c.sems["dvec"][0], nb_dv["red4"])
                c.inc(tensor.transpose(out=ps_n[0:1, 0:4], in_=red4[:, :], identity=eyef[0:4, 0:4]),
                      "pe", 1)
                cb_pe[1] = c.val("pe")

            @block.vector
            def _(vector):
                v = Chain(c, vector, "dvec")
                vector.wait_ge(c.sems["pe"][0], cb_pe[1])
                v.tensor_copy(out=brow[0:1, :], in_=ps_n[0:1, 0:4])
                v.tensor_tensor(out=brow[0:1, 0:2], in0=brow[0:1, 0:2], in1=brow[0:1, 2:4], op=ALU.add)
                v.tensor_scalar(out=brow[0:1, 0:2], in0=brow[0:1, 0:2], scalar1=-1.0 / 16.0,
                                scalar2=None, op0=ALU.mult)
                nb_dv["brow"] = c.val("dvec")

            @block.tensor
            def _(tensor):
                tensor.wait_ge(c.sems["dvec"][0], nb_dv["brow"])
                for h in range(2):
                    c.inc(tensor.matmul(psE[:, h:h + 1], onesrow[:, :], brow[0:1, h:h + 1],
                                        start=True, stop=True), "pe", 1)
                cb_pe[2] = c.val("pe")

            @block.vector
            def _(vector):
                v = Chain(c, vector, "dvec")
                vector.wait_ge(c.sems["pe"][0], cb_pe[2])
                v.tensor_copy(out=bias_h[:, :], in_=psE[:, 0:2])
                nb_dv["bias"] = c.val("dvec")

            sc_pe = {}
            exp_done = {}
            av_pe = {}
            msk_dv = {}
            ep_dv = {}
            for qb in range(NQ):
                qs = qb * 512
                n_kt = 4 * qb + 4
                for kt in range(n_kt):
                    j = kt - 4 * qb
                    col0 = max(0, j) * 128
                    pa, pb_ = psA[kt % 2], psB[kt % 2]

                    @block.tensor
                    def _(tensor, qb=qb, kt=kt, j=j, col0=col0, pa=pa, pb_=pb_):
                        if (qb, kt) == (0, 0):
                            tensor.wait_ge(c.sems["dvec"][0], nb_dv["bias"])
                        if kt >= 2:
                            tensor.wait_ge(c.sems["actc"][0], exp_done[(qb, kt - 2)])
                        elif qb > 0:
                            pk = 4 * (qb - 1) + 4 - 2 + kt
                            tensor.wait_ge(c.sems["actc"][0], exp_done[(qb - 1, pk)])
                        for h, pp in ((0, pa), (1, pb_)):
                            c.inc(tensor.matmul(
                                pp[:, col0:512],
                                K[h * 64:(h + 1) * 64, kt * 128:(kt + 1) * 128],
                                Q[h * 64:(h + 1) * 64, qs + col0:qs + 512],
                                start=True, stop=True, tile_position=(h * 64, 0)), "pe", 1)
                        sc_pe[(qb, kt)] = c.val("pe")

                    if j >= 0:
                        @block.vector
                        def _(vector, qb=qb, kt=kt, col0=col0, pa=pa, pb_=pb_):
                            v = Chain(c, vector, "dvec")
                            vector.wait_ge(c.sems["pe"][0], sc_pe[(qb, kt)])
                            v.tensor_tensor(out=pa[:, col0:col0 + 128], in0=pa[:, col0:col0 + 128],
                                            in1=tri[:, :], op=ALU.add)
                            v.tensor_tensor(out=pb_[:, col0:col0 + 128], in0=pb_[:, col0:col0 + 128],
                                            in1=tri[:, :], op=ALU.add)
                            msk_dv[(qb, kt)] = c.val("dvec")

                    @block.scalar
                    def _(scalar, qb=qb, kt=kt, j=j, col0=col0, pa=pa, pb_=pb_):
                        a = Chain(c, scalar, "actc")
                        if j >= 0:
                            scalar.wait_ge(c.sems["dvec"][0], msk_dv[(qb, kt)])
                        else:
                            scalar.wait_ge(c.sems["pe"][0], sc_pe[(qb, kt)])
                        if kt >= 2:
                            scalar.wait_ge(c.sems["pe"][0], av_pe[(qb, kt - 2)])
                        elif qb > 0:
                            pk = 4 * (qb - 1) + 4 - 2 + kt
                            scalar.wait_ge(c.sems["pe"][0], av_pe[(qb - 1, pk)])
                        for h, pp in ((0, pa), (1, pb_)):
                            a.activation(out=prob[2 * (kt % 2) + h][:, col0:512],
                                         in_=pp[:, col0:512], func=AF.Exp,
                                         scale=scl8[:, 0:1], bias=bias_h[:, h:h + 1])
                        exp_done[(qb, kt)] = c.val("actc")

                    @block.tensor
                    def _(tensor, qb=qb, kt=kt, col0=col0):
                        tensor.wait_ge(c.sems["actc"][0], exp_done[(qb, kt)])
                        if kt == 0 and qb > 0:
                            tensor.wait_ge(c.sems["dvec"][0], ep_dv[qb - 1])
                        for h in range(2):
                            c.inc(tensor.matmul(
                                psC[h * 64:(h + 1) * 64, col0:512],
                                V[:, kt * 128 + h * 64:kt * 128 + (h + 1) * 64],
                                prob[2 * (kt % 2) + h][:, col0:512],
                                start=(kt == 0), stop=(kt == n_kt - 1),
                                tile_position=(0, h * 64)), "pe", 1)
                        for h in range(2):
                            c.inc(tensor.matmul(
                                psD[h * 64:h * 64 + 1, col0:512],
                                onescol_bf[:, :],
                                prob[2 * (kt % 2) + h][:, col0:512],
                                start=(kt == 0), stop=(kt == n_kt - 1),
                                tile_position=(0, h * 64)), "pe", 1)
                        av_pe[(qb, kt)] = c.val("pe")

                @block.vector
                def _(vector, qb=qb, qs=qs, n_kt=n_kt):
                    v = Chain(c, vector, "dvec")
                    vector.wait_ge(c.sems["pe"][0], av_pe[(qb, n_kt - 1)])
                    v.reciprocal(out=recr[0:1, 0:512], in_=psD[0:1, :])
                    v.reciprocal(out=recr[0:1, 512:1024], in_=psD[64:65, :])
                    ep_dv[("r", qb)] = c.val("dvec")

                @block.tensor
                def _(tensor, qb=qb):
                    tensor.wait_ge(c.sems["dvec"][0], ep_dv[("r", qb)])
                    if qb > 0:
                        tensor.wait_ge(c.sems["dvec"][0], ep_dv[("e", qb - 1)])
                    c.inc(tensor.matmul(psE[:, :], sel0[:, :], recr[0:1, 0:512],
                                        start=True, stop=False), "pe", 1)
                    c.inc(tensor.matmul(psE[:, :], sel1[:, :], recr[0:1, 512:1024],
                                        start=False, stop=True), "pe", 1)
                    ep_dv[("b", qb)] = c.val("pe")

                @block.vector
                def _(vector, qb=qb, qs=qs):
                    v = Chain(c, vector, "dvec")
                    vector.wait_ge(c.sems["pe"][0], ep_dv[("b", qb)])
                    v.tensor_copy(out=recipB[:, :], in_=psE[:, :])
                    ep_dv[("e", qb)] = c.val("dvec")
                    v.tensor_tensor(out=A[:, qs:qs + 512], in0=psC[:, :], in1=recipB[:, :],
                                    op=ALU.mult)
                    ep_dv[qb] = c.val("dvec")

            @block.sync
            def _(sync, p=p):
                sync.wait_ge(c.sems["dvec"][0], ep_dv[NQ - 1])
                sm = f"st{p % 2}"
                c.dma(sync, sm, ans_out[p].rearrange("p (a s) -> p a s", a=16),
                      A[:, :].rearrange("p (a s) -> p a s", a=16))
                store_done[p] = (sm, c.val(sm))

    for p in range(NPAIR):
        emit_pair(p)

    with nc.Block() as block:
        @block.sync
        def _(sync):
            for sm in ["st0", "st1"]:
                c.wait(sync, sm)

    c.close()
    return c


def build_l1():
    c = Ctx("l1")
    nc = c.nc
    DT = 8
    NCH = 4

    x_in = c.din("x_feat", [D, T], F32)
    ans0_in = c.din("ans0", [D, T], BF16)
    wo0 = c.din("wo0", [D, D], F32)
    bo0_col = c.din("bo0_col", [128, 8], F32)
    g0_in = c.din("g0", [D, T], F32)
    wqkv1 = c.din("wqkv1", [D, 3072], F32)
    bqk1_col = c.din("bqk1_col", [128, 16], F32)
    bv1_row = c.din("bv1_row", [1, 1024], F32)
    dpw1 = c.din("dpw1", [D, 512], F32)
    dpb1_row = c.din("dpb1_row", [1, 512], F32)
    lng1_rep = c.din("lng1_rep", [128, 512], F32)
    lnb1_rep = c.din("lnb1_rep", [128, 512], F32)
    dpw2_rep = c.din("dpw2_rep", [128, 512], F32)
    dpb2_rep = c.din("dpb2_rep", [128, 1], F32)
    gw1_rep = c.din("gw1_rep", [128, 16], F32)
    gb1_rep = c.din("gb1_rep", [128, 16], F32)
    gw2_rep = c.din("gw2_rep", [128, 16], F32)
    gb2_rep = c.din("gb2_rep", [128, 1], F32)
    wgq_t = c.din("wgq_t", [D, 2816], BF16)
    wuq_t = c.din("wuq_t", [D, 2816], BF16)
    wdq_t = c.din("wdq_t", [2816, D], BF16)
    eyef_in = c.din("eyef", [128, 128], F32)
    eyeb_in = c.din("eyeb", [128, 128], BF16)

    x1_out = c.dout("x1", [D, T], F32)
    q_out = c.dout("q1", [D, T], BF16)
    k_out = c.dout("k1", [D, T], BF16)
    v_out = c.dout("v1g", [T, D], BF16)
    g_out = c.dout("g1", [D, T], F32)
    xn_out = c.dout("xn1", [D, T], BF16)

    xs = c.sb("xs", [128, DT * T], F32)
    xnb = c.sb("xnb", [128, DT * T], BF16)
    r1bK = c.sb("r1bK", [128, T], F32)
    r2bK = c.sb("r2bK", [128, T], F32)
    R1 = c.sb("R1", [128, 2048], F32)
    R2 = c.sb("R2", [128, 4096], F32)
    R3 = c.sb("R3", [128, 8192], BF16)
    R4 = c.sb("R4", [128, 16384], BF16)
    rows4 = R2[0:1, 0:2048]
    murow = rows4[0:1, 0:512]
    varrow = rows4[0:1, 512:1024]
    rstdrow = rows4[0:1, 1024:1536]
    r2row = rows4[0:1, 1536:2048]
    sq0 = c.sb("sq0", [128, 512], F32)
    sq1 = c.sb("sq1", [128, 512], F32)
    iscr = c.sb("iscr", [128, 512], mybir.dt.int32)
    ones_col_f32 = c.sb("ones_col_f32", [128, 1], F32)
    ones_row_f32 = c.sb("ones_row_f32", [1, 128], F32)
    ones_row_bf = c.sb("ones_row_bf", [1, 128], BF16)
    epscol = c.sb("epscol", [1, 1], F32)
    nhcol = c.sb("nhcol", [1, 1], F32)
    epscol_p = c.sb("epscol_p", [128, 1], F32)
    nhcol_p = c.sb("nhcol_p", [128, 1], F32)
    eyef = c.sb("eyef_s", [128, 128], F32)
    eyeb = c.sb("eyeb_s", [128, 128], BF16)
    bqk_c = c.sb("bqk_c", [128, 16], F32)
    bo0_c = c.sb("bo0_c", [128, 8], F32)
    bv_row = c.sb("bv_rowb", [1, 1024], BF16)
    dpb1_r = c.sb("dpb1_r", [1, 512], BF16)
    lng1_r = c.sb("lng1_r", [128, 512], F32)
    lnb1_r = c.sb("lnb1_r", [128, 512], F32)
    dpw2_r = c.sb("dpw2_r", [128, 512], F32)
    dpb2_r = c.sb("dpb2_r", [128, 1], F32)
    gw1_r = c.sb("gw1_r", [128, 16], F32)
    gb1_r = c.sb("gb1_r", [128, 16], F32)
    gw2_r = c.sb("gw2_r", [128, 16], F32)
    gb2_r = c.sb("gb2_r", [128, 1], F32)
    gateT = c.sb("gateT", [128, 16], F32)
    hsq = R1[:, 512:1024]
    hln = R1[:, 0:512]
    ghb = R1[:, 1024:1536]
    mucol = c.sb("mucol", [128, 4], F32)
    sgc = c.sb("sgc", [128, 512], BF16)
    scr = {"xn": R2[:, 0:512], "v": R2[:, 512:1024], "u": R2[:, 1024:1536],
           "t": R2[:, 1536:2048], "w": R2[:, 2048:2560], "iscr": iscr}
    scr["mx"] = c.sb("scr_mx", [128, 64], F32)
    scr["rinv"] = c.sb("scr_rinv", [128, 64], F32)
    xtokf = R2[:, 2560:3584]
    qkst = [R4[:, 0:2048], R4[:, 2048:4096]]
    vstage = [R4[:, 4096:4608], R4[:, 4608:5120]]
    ansring = [R4[:, 0:4096], R4[:, 4096:8192]]
    hq4 = xnb[:, 0:11264]
    htok = xnb[:, 11264:14080]
    xqf = xnb[:, 14080:15104]
    xtokq = xnb[:, 15104:16128]

    pb = [c.ps(f"pb{i}", [128, 512], F32) for i in range(5)]
    stackP = ExitStack()
    ps_row = stackP.enter_context(nc.psum_tensor("ps_row", [1, 512], F32))
    ps_row2 = stackP.enter_context(nc.psum_tensor("ps_row2", [1, 512], F32))

    for s in ["ld", "lg0", "lg1", "la0", "la1", "wl0", "wl1", "dl0", "dl1", "dl2",
              "dw0", "dw1", "dw2",
              "pe", "actc", "dvec", "st", "st0", "st1", "sv0", "sv1", "sg0", "sg1", "sx"]:
        c.sem(s)

    def xt(i):
        return xs[:, i * T:(i + 1) * T]

    def xnbt(i):
        return xnb[:, i * T:(i + 1) * T]

    prm = dict(sq0=sq0, sq1=sq1, murow=murow, varrow=varrow, rstdrow=rstdrow,
               r2row=r2row, r1b=r1bK, r2b=r2bK, ones_col_f32=ones_col_f32,
               ones_row_f32=ones_row_f32, epscol=epscol, nhcol=nhcol,
               ps_row=ps_row, ps_row2=ps_row2, bcA=pb[3], bcB=pb[4])

    with nc.Block() as block:
        @block.sync
        def _(sync):
            for i in range(DT):
                c.dma(sync, "ld", xt(i), x_in[i * 128:(i + 1) * 128, :])
            for dst, srcap in [(bqk_c, bqk1_col), (bo0_c, bo0_col),
                               (lng1_r, lng1_rep), (lnb1_r, lnb1_rep),
                               (dpw2_r, dpw2_rep), (dpb2_r, dpb2_rep),
                               (gw1_r, gw1_rep), (gb1_r, gb1_rep),
                               (gw2_r, gw2_rep), (gb2_r, gb2_rep),
                               (eyef, eyef_in), (eyeb, eyeb_in)]:
                c.dma(sync, "ld", dst[:, :], srcap)
            c.dma(sync, "ld", R2[0:1, 0:1024], bv1_row)
            c.dma(sync, "ld", R2[0:1, 1024:1536], dpb1_row)

        @block.gpsimd
        def _(gpsimd):
            gpsimd.memset(ones_col_f32[:, :], 1.0)
            gpsimd.memset(ones_row_f32[:, :], 1.0)
            gpsimd.memset(ones_row_bf[:, :], 1.0)
            gpsimd.memset(epscol[:, :], LN_EPS)
            gpsimd.memset(nhcol[:, :], -0.5)
            gpsimd.memset(epscol_p[:, :], LN_EPS)
            gpsimd.memset(nhcol_p[:, :], -0.5)

        @block.scalar
        def _(scalar):
            a = Chain(c, scalar, "actc")
            c.wait(scalar, "ld")
            a.activation(out=bv_row[0:1, :], in_=R2[0:1, 0:1024], func=AF.Copy)
            a.activation(out=dpb1_r[0:1, :], in_=R2[0:1, 1024:1536], func=AF.Copy)

    with nc.Block() as block:
        cast_dv = {}
        for j in range(8):
            @block.sync
            def _(sync, j=j):
                sm = "wl0" if j % 2 == 0 else "wl1"
                if j >= 2:
                    sync.wait_ge(c.sems["dvec"][0], cast_dv[j - 2])
                c.dma(sync, sm, R1[:, (j % 2) * 1024:(j % 2) * 1024 + 1024].rearrange(
                    "p (k m) -> p k m", k=8),
                    wo0[:, j * 128:(j + 1) * 128].rearrange("(k p) m -> p k m", p=128))
                cast_dv[("wl", j)] = (sm, c.val(sm))

            @block.vector
            def _(vector, j=j):
                v = Chain(c, vector, "dvec")
                sm, tv = cast_dv[("wl", j)]
                vector.wait_ge(c.sems[sm][0], tv)
                v.tensor_copy(out=R3[:, j * 1024:(j + 1) * 1024],
                              in_=R1[:, (j % 2) * 1024:(j % 2) * 1024 + 1024])
                cast_dv[j] = c.val("dvec")
    wo0_done = c.val("dvec")

    with nc.Block() as block:
        s0_pe = {}
        s0_ev = {}
        xu_dv = {}
        ans_ld = {}
        g0_ld = {}
        for qt in range(4):
            @block.sync
            def _(sync, qt=qt):
                sma = "la0" if qt % 2 == 0 else "la1"
                if qt >= 2:
                    sync.wait_ge(c.sems["pe"][0], s0_pe[(qt - 2, 7)])
                c.dma(sync, sma, ansring[qt % 2][:, :].rearrange("p (k s) -> p k s", k=8),
                      ans0_in.rearrange("(k p) t -> p k t", p=128)[:, :, qt * 512:(qt + 1) * 512])
                ans_ld[qt] = (sma, c.val(sma))
                for ihalf in range(2):
                    smg = "lg0" if ihalf == 0 else "lg1"
                    if qt >= 1:
                        sync.wait_ge(c.sems["dvec"][0], xu_dv[(qt - 1, ihalf * 4 + 3)])
                    c.dma(sync, smg, R2[:, ihalf * 2048:ihalf * 2048 + 2048].rearrange(
                        "p (k s) -> p k s", k=4),
                        g0_in.rearrange("(k p) t -> p k t", p=128)[:, ihalf * 4:(ihalf + 1) * 4,
                                                                  qt * 512:(qt + 1) * 512])
                    g0_ld[(qt, ihalf)] = (smg, c.val(smg))

            for i in range(DT):
                @block.tensor
                def _(tensor, qt=qt, i=i):
                    if i == 0:
                        if qt == 0:
                            tensor.wait_ge(c.sems["dvec"][0], wo0_done)
                        sm, tv = ans_ld[qt]
                        tensor.wait_ge(c.sems[sm][0], tv)
                    gidx = qt * 8 + i
                    bidx = gidx % 3
                    if gidx >= 3:
                        tensor.wait_ge(c.sems["actc"][0], s0_ev[gidx - 3])
                    for k in range(8):
                        c.inc(tensor.matmul(
                            pb[bidx][:, :],
                            R3[:, i * 1024 + k * 128:i * 1024 + (k + 1) * 128],
                            ansring[qt % 2][:, k * 512:(k + 1) * 512],
                            start=(k == 0), stop=(k == 7)), "pe", 1)
                    s0_pe[(qt, i)] = c.val("pe")

                @block.scalar
                def _(scalar, qt=qt, i=i):
                    a = Chain(c, scalar, "actc")
                    gidx = qt * 8 + i
                    scalar.wait_ge(c.sems["pe"][0], s0_pe[(qt, i)])
                    if gidx >= 1:
                        pq, pi = divmod(gidx - 1, 8)
                        scalar.wait_ge(c.sems["dvec"][0], xu_dv[(pq, pi)])
                    a.activation(out=sq0[:, :], in_=pb[gidx % 3][:, :],
                                 func=AF.Identity, bias=bo0_c[:, i:i + 1])
                    s0_ev[gidx] = c.val("actc")

                @block.vector
                def _(vector, qt=qt, i=i):
                    v = Chain(c, vector, "dvec")
                    gidx = qt * 8 + i
                    sm, tv = g0_ld[(qt, i // 4)]
                    vector.wait_ge(c.sems[sm][0], tv)
                    vector.wait_ge(c.sems["actc"][0], s0_ev[gidx])
                    gsl = R2[:, (i // 4) * 2048 + (i % 4) * 512:(i // 4) * 2048 + (i % 4 + 1) * 512]
                    cs = slice(qt * 512, (qt + 1) * 512)
                    v.tensor_tensor(out=sq1[:, :], in0=sq0[:, :], in1=gsl, op=ALU.mult)
                    v.tensor_tensor(out=xt(i)[:, cs], in0=xt(i)[:, cs], in1=sq1[:, :], op=ALU.add)
                    xu_dv[(qt, i)] = c.val("dvec")

        @block.sync
        def _(sync):
            c.wait(sync, "dvec")
            for i in range(DT):
                c.dma(sync, "sx", x1_out[i * 128:(i + 1) * 128, :], xt(i))

    def apply_ln1(v):
        for i in range(DT):
            for ch in range(NCH):
                cs = slice(ch * 512, (ch + 1) * 512)
                v.tensor_tensor(out=sq0[:, :], in0=xt(i)[:, cs], in1=r1bK[:, cs], op=ALU.mult)
                v.tensor_tensor(out=sq1[:, :], in0=sq0[:, :], in1=r2bK[:, cs], op=ALU.add)
                v.tensor_copy(out=xnbt(i)[:, cs], in_=sq1[:, :])

    emit_ln_feat(c, nc, xt, DT, T, prm, apply_ln1)
    stackP.close()
    ps_tf = c.ps("ps_tf", [128, 128], F32)
    ps_tb = c.ps("ps_tb", [128, 128], BF16)
    ps_tb2 = c.ps("ps_tb2", [128, 128], BF16)

    with nc.Block() as block:
        @block.sync
        def _(sync):
            c.wait(sync, "dvec")
            for i in range(DT):
                c.dma(sync, "st", xn_out[i * 128:(i + 1) * 128, :], xnbt(i))

    with nc.Block() as block:
        cast_dv = {}
        for j in range(4):
            @block.sync
            def _(sync, j=j):
                sm = "wl0" if j % 2 == 0 else "wl1"
                if j >= 2:
                    sync.wait_ge(c.sems["dvec"][0], cast_dv[j - 2])
                c.dma(sync, sm, R1[:, (j % 2) * 1024:(j % 2) * 1024 + 1024].rearrange(
                    "p (k m) -> p k m", k=8),
                    dpw1[:, j * 128:(j + 1) * 128].rearrange("(k p) m -> p k m", p=128))
                cast_dv[("wl", j)] = (sm, c.val(sm))

            @block.vector
            def _(vector, j=j):
                v = Chain(c, vector, "dvec")
                sm, tv = cast_dv[("wl", j)]
                vector.wait_ge(c.sems[sm][0], tv)
                v.tensor_copy(out=R3[:, j * 1024:(j + 1) * 1024],
                              in_=R1[:, (j % 2) * 1024:(j % 2) * 1024 + 1024])
                cast_dv[j] = c.val("dvec")
    dpw1_done = c.val("dvec")

    with nc.Block() as block:
        dr_pe = {}
        dr_dv = {}
        dr_ac = {}
        for tt in range(16):
            @block.tensor
            def _(tensor, tt=tt):
                if tt == 0:
                    tensor.wait_ge(c.sems["dvec"][0], dpw1_done)
                if tt >= 2:
                    tensor.wait_ge(c.sems["dvec"][0], dr_dv[(tt - 2, "done")])
                    tensor.wait_ge(c.sems["actc"][0], dr_ac[(tt - 2, "gate")])
                bidx = tt % 2
                for k in range(8):
                    rhs = bass.AP(R3.tensor, R3.offset + k * 128,
                                  [[R3.ap[0][0], 128], [1024, 4], [1, 128]]) if False else \
                        R3[:, 0:4096].rearrange("p (j m) -> p j m", j=4)[:, :, k * 128:(k + 1) * 128]
                    c.inc(tensor.matmul(
                        pb[bidx][:, :].rearrange("p (j m) -> p j m", j=4),
                        xnbt(k)[:, tt * 128:(tt + 1) * 128],
                        rhs,
                        start=(k == 0), stop=False), "pe", 1)
                c.inc(tensor.matmul(pb[bidx][:, :], ones_row_bf[:, :], dpb1_r[0:1, :],
                                    start=False, stop=True), "pe", 1)
                dr_pe[tt] = c.val("pe")

            @block.scalar
            def _(scalar, tt=tt):
                a = Chain(c, scalar, "actc")
                scalar.wait_ge(c.sems["pe"][0], dr_pe[tt])
                a.activation(out=hsq[:, :], in_=pb[tt % 2][:, :], func=AF.Square)
                dr_ac[(tt, "sq")] = c.val("actc")

            @block.vector
            def _(vector, tt=tt):
                v = Chain(c, vector, "dvec")
                vector.wait_ge(c.sems["actc"][0], dr_ac[(tt, "sq")])
                v.reduce_sum(out=mucol[:, 0:1], in_=pb[tt % 2][:, :], axis=AX.X)
                v.tensor_scalar(out=mucol[:, 0:1], in0=mucol[:, 0:1], scalar1=1.0 / 512,
                                scalar2=None, op0=ALU.mult)
                v.reduce_sum(out=mucol[:, 1:2], in_=hsq[:, :], axis=AX.X)
                v.tensor_scalar(out=mucol[:, 1:2], in0=mucol[:, 1:2], scalar1=1.0 / 512,
                                scalar2=None, op0=ALU.mult)
                v.tensor_tensor(out=mucol[:, 2:3], in0=mucol[:, 0:1], in1=mucol[:, 0:1], op=ALU.mult)
                v.tensor_tensor(out=mucol[:, 1:2], in0=mucol[:, 1:2], in1=mucol[:, 2:3], op=ALU.subtract)
                dr_dv[(tt, "muvar")] = c.val("dvec")

            @block.scalar
            def _(scalar, tt=tt):
                a = Chain(c, scalar, "actc")
                scalar.wait_ge(c.sems["dvec"][0], dr_dv[(tt, "muvar")])
                a.activation(out=mucol[:, 2:3], in_=mucol[:, 1:2], func=AF.Ln, bias=epscol_p[:, 0:1])
                a.activation(out=mucol[:, 2:3], in_=mucol[:, 2:3], func=AF.Exp, scale=nhcol_p[:, 0:1])
                dr_ac[(tt, "rstd")] = c.val("actc")

            @block.vector
            def _(vector, tt=tt):
                v = Chain(c, vector, "dvec")
                vector.wait_ge(c.sems["actc"][0], dr_ac[(tt, "rstd")])
                v.tensor_scalar(out=mucol[:, 3:4], in0=mucol[:, 0:1], scalar1=-1.0,
                                scalar2=None, op0=ALU.mult)
                v.tensor_scalar(out=hln[:, :], in0=pb[tt % 2][:, :], scalar1=mucol[:, 3:4],
                                scalar2=mucol[:, 2:3], op0=ALU.add, op1=ALU.mult)
                v.tensor_tensor(out=hln[:, :], in0=hln[:, :], in1=lng1_r[:, :], op=ALU.mult)
                v.tensor_tensor(out=hln[:, :], in0=hln[:, :], in1=lnb1_r[:, :], op=ALU.add)
                dr_dv[(tt, "done")] = c.val("dvec")

            @block.scalar
            def _(scalar, tt=tt):
                a = Chain(c, scalar, "actc")
                scalar.wait_ge(c.sems["dvec"][0], dr_dv[(tt, "done")])
                a.activation(out=ghb[:, :], in_=hln[:, :], func=AF.Gelu)
                dr_ac[(tt, "gelu")] = c.val("actc")

            @block.vector
            def _(vector, tt=tt):
                v = Chain(c, vector, "dvec")
                vector.wait_ge(c.sems["actc"][0], dr_ac[(tt, "gelu")])
                v.tensor_tensor(out=ghb[:, :], in0=ghb[:, :], in1=dpw2_r[:, :], op=ALU.mult)
                v.reduce_sum(out=mucol[:, 3:4], in_=ghb[:, :], axis=AX.X)
                v.tensor_tensor(out=mucol[:, 3:4], in0=mucol[:, 3:4], in1=dpb2_r[:, 0:1], op=ALU.add)
                v.tensor_scalar(out=hsq[:, 0:16], in0=gw1_r[:, :], scalar1=mucol[:, 3:4],
                                scalar2=None, op0=ALU.mult)
                v.tensor_tensor(out=hsq[:, 0:16], in0=hsq[:, 0:16], in1=gb1_r[:, :], op=ALU.add)
                dr_dv[(tt, "g16")] = c.val("dvec")

            @block.scalar
            def _(scalar, tt=tt):
                a = Chain(c, scalar, "actc")
                scalar.wait_ge(c.sems["dvec"][0], dr_dv[(tt, "g16")])
                a.activation(out=hsq[:, 16:32], in_=hsq[:, 0:16], func=AF.Gelu)
                dr_ac[(tt, "g16g")] = c.val("actc")

            @block.vector
            def _(vector, tt=tt):
                v = Chain(c, vector, "dvec")
                vector.wait_ge(c.sems["actc"][0], dr_ac[(tt, "g16g")])
                v.tensor_tensor(out=hsq[:, 16:32], in0=hsq[:, 16:32], in1=gw2_r[:, :], op=ALU.mult)
                v.reduce_sum(out=mucol[:, 2:3], in_=hsq[:, 16:32], axis=AX.X)
                v.tensor_tensor(out=mucol[:, 2:3], in0=mucol[:, 2:3], in1=gb2_r[:, 0:1], op=ALU.add)
                dr_dv[(tt, "tanhin")] = c.val("dvec")

            @block.scalar
            def _(scalar, tt=tt):
                a = Chain(c, scalar, "actc")
                scalar.wait_ge(c.sems["dvec"][0], dr_dv[(tt, "tanhin")])
                a.activation(out=gateT[:, tt:tt + 1], in_=mucol[:, 2:3], func=AF.Tanh)
                dr_ac[(tt, "gate")] = c.val("actc")

    gate_done = c.val("actc")

    with nc.Block() as block:
        cast_done = {}
        qk_pe = {}
        evac_done = {}
        store_done = {}
        wl_t = {}
        for mc in range(16):
            mg = mc

            @block.sync
            def _(sync, mc=mc):
                sm = "wl0" if mc % 2 == 0 else "wl1"
                if mc >= 2:
                    sync.wait_ge(c.sems["dvec"][0], cast_done[mc - 2])
                c.dma(sync, sm, R1[:, (mc % 2) * 1024:(mc % 2) * 1024 + 1024].rearrange(
                    "p (k m) -> p k m", k=8),
                    wqkv1[:, mc * 128:(mc + 1) * 128].rearrange("(k p) m -> p k m", p=128))
                wl_t[mc] = (sm, c.val(sm))

            @block.vector
            def _(vector, mc=mc):
                v = Chain(c, vector, "dvec")
                sm, tv = wl_t[mc]
                vector.wait_ge(c.sems[sm][0], tv)
                if mc >= 2:
                    vector.wait_ge(c.sems["pe"][0], qk_pe[(mc - 2, 3)])
                v.tensor_copy(out=R3[:, (mc % 2) * 1024:(mc % 2) * 1024 + 1024],
                              in_=R1[:, (mc % 2) * 1024:(mc % 2) * 1024 + 1024])
                cast_done[mc] = c.val("dvec")

            for ch in range(NCH):
                @block.tensor
                def _(tensor, mc=mc, mg=mg, ch=ch):
                    if ch == 0:
                        tensor.wait_ge(c.sems["dvec"][0], cast_done[mc])
                    gidx = mg * 4 + ch
                    bidx = gidx % 3
                    if gidx >= 3:
                        tensor.wait_ge(c.sems["actc"][0], evac_done[gidx - 3])
                    for k in range(8):
                        c.inc(tensor.matmul(
                            pb[bidx][:, :],
                            R3[:, (mc % 2) * 1024 + k * 128:(mc % 2) * 1024 + (k + 1) * 128],
                            xnbt(k)[:, ch * 512:(ch + 1) * 512],
                            start=(k == 0), stop=(k == 7)), "pe", 1)
                    qk_pe[(mc, ch)] = c.val("pe")

                @block.scalar
                def _(scalar, mc=mc, mg=mg, ch=ch):
                    a = Chain(c, scalar, "actc")
                    stg = qkst[mg % 2]
                    gidx = mg * 4 + ch
                    if ch == 0 and mg >= 2:
                        sd = store_done[mg - 2]
                        scalar.wait_ge(c.sems[sd[0]][0], sd[1])
                    scalar.wait_ge(c.sems["pe"][0], qk_pe[(mc, ch)])
                    a.activation(out=stg[:, ch * 512:(ch + 1) * 512], in_=pb[gidx % 3][:, :],
                                 func=AF.Identity, bias=bqk_c[:, mg:mg + 1])
                    evac_done[gidx] = c.val("actc")

            @block.sync
            def _(sync, mg=mg):
                stg = qkst[mg % 2]
                sync.wait_ge(c.sems["actc"][0], evac_done[mg * 4 + 3])
                dst = q_out if mg < 8 else k_out
                r = mg if mg < 8 else mg - 8
                sm = "st0" if mg % 2 == 0 else "st1"
                c.dma(sync, sm, dst[r * 128:(r + 1) * 128, :], stg[:, :])
                store_done[mg] = (sm, c.val(sm))

        vcast = {}
        for j in range(8):
            @block.sync
            def _(sync, j=j):
                sm = "wl0" if j % 2 == 0 else "wl1"
                if j == 0:
                    sync.wait_ge(c.sems["pe"][0], qk_pe[(15, 3)])
                c.dma(sync, sm, R1[:, (j % 2) * 1024:(j % 2) * 1024 + 1024].rearrange(
                    "p (k m) -> p k m", k=8),
                    wqkv1[:, 2048 + j * 128:2048 + (j + 1) * 128].rearrange("(k p) m -> p k m", p=128))
                wl_t[("v", j)] = (sm, c.val(sm))

            @block.vector
            def _(vector, j=j):
                v = Chain(c, vector, "dvec")
                sm, tv = wl_t[("v", j)]
                vector.wait_ge(c.sems[sm][0], tv)
                v.tensor_copy(out=R3[:, j * 1024:(j + 1) * 1024],
                              in_=R1[:, (j % 2) * 1024:(j % 2) * 1024 + 1024])
                vcast[j] = c.val("dvec")

        R3v = R3[:, :].rearrange("p (j x) -> p j x", j=8)
        mm_v = {}
        for tt in range(16):
            for vb in range(2):
                @block.tensor
                def _(tensor, tt=tt, vb=vb):
                    if (tt, vb) == (0, 0):
                        tensor.wait_ge(c.sems["dvec"][0], vcast[7])
                        tensor.wait_ge(c.sems["actc"][0], gate_done)
                    gidx = tt * 2 + vb
                    bidx = gidx % 3
                    if gidx >= 3:
                        tensor.wait_ge(c.sems["actc"][0], evac_done[("v", gidx - 3)])
                    elif gidx < 3:
                        tensor.wait_ge(c.sems["actc"][0], evac_done[61 + gidx])
                    for k in range(8):
                        c.inc(tensor.matmul(
                            pb[bidx][:, :],
                            xnbt(k)[:, tt * 128:(tt + 1) * 128],
                            R3v[:, vb * 4:(vb + 1) * 4, k * 128:(k + 1) * 128],
                            start=(k == 0), stop=False), "pe", 1)
                    c.inc(tensor.matmul(
                        pb[bidx][:, :], ones_row_bf[:, :],
                        bv_row[0:1, vb * 512:(vb + 1) * 512],
                        start=False, stop=True), "pe", 1)
                    mm_v[gidx] = c.val("pe")

                @block.scalar
                def _(scalar, tt=tt, vb=vb):
                    a = Chain(c, scalar, "actc")
                    gidx = tt * 2 + vb
                    stg = vstage[gidx % 2]
                    if gidx >= 2:
                        sd = store_done[("v", gidx - 2)]
                        scalar.wait_ge(c.sems[sd[0]][0], sd[1])
                    scalar.wait_ge(c.sems["pe"][0], mm_v[gidx])
                    a.activation(out=stg[:, :], in_=pb[gidx % 3][:, :], func=AF.Identity,
                                 scale=gateT[:, tt:tt + 1])
                    evac_done[("v", gidx)] = c.val("actc")

                @block.sync
                def _(sync, tt=tt, vb=vb):
                    gidx = tt * 2 + vb
                    sync.wait_ge(c.sems["actc"][0], evac_done[("v", gidx)])
                    sm = "sv0" if gidx % 2 == 0 else "sv1"
                    c.dma(sync, sm, v_out[tt * 128:(tt + 1) * 128, vb * 512:(vb + 1) * 512],
                          vstage[gidx % 2][:, :])
                    store_done[("v", gidx)] = (sm, c.val(sm))

    OC = [(0, 512), (512, 1024), (1024, 1536), (1536, 2048), (2048, 2560), (2560, 2816)]
    xqf4 = xnb[:, 11264:15360]
    hchf = R2[:, 3584:4096]
    g1st = [R1[:, 0:512], R1[:, 512:1024]]
    hqc = c.sb("hqc", [128, 512], BF16)
    wg_slots = [R4[:, 0:4096], R4[:, 4096:8192]]
    wu_slots = [R4[:, 8192:12288], R4[:, 12288:16384]]

    with nc.Block() as block:
        xq_dv = {}
        xq_pe = {}
        wld = {}
        gu_pe = {}
        sil_ac = {}
        h_dv = {}
        hq_pe = {}
        dn_pe = {}
        dn_ld = {}
        g1_ac = {}
        g1_sd = {}
        for grp in range(4):
            for t4 in range(4):
                tt = grp * 4 + t4

                @block.vector
                def _(vector, tt=tt, t4=t4, grp=grp):
                    v = Chain(c, vector, "dvec")
                    if tt == 0:
                        for sm in ["st0", "st1", "sv0", "sv1"]:
                            vector.wait_ge(c.sems[sm][0], c.val(sm))
                    if grp > 0 and t4 == 0:
                        vector.wait_ge(c.sems["pe"][0], dn_pe[(grp - 1, 1, 21)])
                    cs = slice(tt * 128, (tt + 1) * 128)
                    stg = R2[:, 0:1024]
                    for k in range(8):
                        v.tensor_tensor(out=sq0[:, 0:128], in0=xt(k)[:, cs],
                                        in1=r1bK[:, cs], op=ALU.mult)
                        v.tensor_tensor(out=stg[:, k * 128:(k + 1) * 128],
                                        in0=sq0[:, 0:128], in1=r2bK[:, cs], op=ALU.add)
                    xq_dv[(tt, "rc")] = c.val("dvec")

                for k in range(8):
                    @block.tensor
                    def _(tensor, tt=tt, k=k):
                        tensor.wait_ge(c.sems["dvec"][0], xq_dv[(tt, "rc")])
                        if (tt, k) != (0, 0):
                            pk = (tt, k - 1) if k > 0 else (tt - 1, 7)
                            tensor.wait_ge(c.sems["dvec"][0], xq_dv[("ev", pk)])
                        c.inc(tensor.transpose(out=ps_tf[:, :],
                                               in_=R2[:, k * 128:(k + 1) * 128],
                                               identity=eyef[:, :]), "pe", 1)
                        xq_pe[(tt, k)] = c.val("pe")

                    @block.vector
                    def _(vector, tt=tt, k=k):
                        v = Chain(c, vector, "dvec")
                        vector.wait_ge(c.sems["pe"][0], xq_pe[(tt, k)])
                        v.tensor_copy(out=xtokf[:, k * 128:(k + 1) * 128], in_=ps_tf[:, :])
                        xq_dv[("ev", (tt, k))] = c.val("dvec")

                @block.vector
                def _(vector, tt=tt, t4=t4):
                    v = Chain(c, vector, "dvec")
                    emit_fp4_quant(v, xtokq[:, 0:512], xtokf[:, 0:512], scr, 128, 512)
                    emit_fp4_quant(v, xtokq[:, 512:1024], xtokf[:, 512:1024], scr, 128, 512)
                    xq_dv[(tt, "q")] = c.val("dvec")

                for k in range(8):
                    @block.tensor
                    def _(tensor, tt=tt, t4=t4, k=k):
                        tensor.wait_ge(c.sems["dvec"][0], xq_dv[(tt, "q")])
                        if (tt, k) != (0, 0):
                            pk = (tt, k - 1) if k > 0 else (tt - 1, 7)
                            tensor.wait_ge(c.sems["dvec"][0], xq_dv[("evb", pk)])
                        c.inc(tensor.transpose(out=(ps_tb if k % 2 == 0 else ps_tb2)[:, :],
                                               in_=xtokq[:, k * 128:(k + 1) * 128],
                                               identity=eyeb[:, :]), "pe", 1)
                        xq_pe[("b", tt, k)] = c.val("pe")

                    @block.vector
                    def _(vector, tt=tt, t4=t4, k=k):
                        v = Chain(c, vector, "dvec")
                        vector.wait_ge(c.sems["pe"][0], xq_pe[("b", tt, k)])
                        v.tensor_copy(out=xqf4[:, t4 * 1024 + k * 128:t4 * 1024 + (k + 1) * 128],
                                      in_=(ps_tb if k % 2 == 0 else ps_tb2)[:, :])
                        xq_dv[("evb", (tt, k))] = c.val("dvec")

            for oci, (o0, o1) in enumerate(OC):
                ow = o1 - o0
                nkk = ow // 128

                @block.sync
                def _(sync, grp=grp, oci=oci, o0=o0, o1=o1, ow=ow):
                    smg = "dl0" if oci % 2 == 0 else "dl1"
                    gi = grp * 6 + oci
                    if gi >= 2:
                        sync.wait_ge(c.sems["pe"][0], wld[("done", gi - 2)])
                    c.dma(sync, smg, wg_slots[oci % 2][:, 0:ow * 8].rearrange(
                        "p (k s) -> p k s", k=8),
                        wgq_t[:, o0:o1].rearrange("(k p) s -> p k s", p=128))
                    c.dma(sync, smg, wu_slots[oci % 2][:, 0:ow * 8].rearrange(
                        "p (k s) -> p k s", k=8),
                        wuq_t[:, o0:o1].rearrange("(k p) s -> p k s", p=128))
                    wld[(grp, oci)] = (smg, c.val(smg))

                for t4 in range(4):
                    tt = grp * 4 + t4
                    u = oci * 4 + t4

                    @block.tensor
                    def _(tensor, grp=grp, oci=oci, t4=t4, o0=o0, ow=ow, u=u):
                        if t4 == 0:
                            sm, tv = wld[(grp, oci)]
                            tensor.wait_ge(c.sems[sm][0], tv)
                        if oci == 0 and t4 == 0:
                            tensor.wait_ge(c.sems["dvec"][0], xq_dv[("evb", (grp * 4 + 3, 7))])
                            if grp > 0:
                                tensor.wait_ge(c.sems["actc"][0], g1_ac[(grp - 1, 1, 3)])
                        if u >= 1:
                            tensor.wait_ge(c.sems["actc"][0], sil_ac[(grp, u - 1)])
                            tensor.wait_ge(c.sems["dvec"][0], h_dv[(grp, u - 1)])
                        for k in range(8):
                            c.inc(tensor.matmul(
                                pb[0][:, 0:ow],
                                xqf4[:, t4 * 1024 + k * 128:t4 * 1024 + (k + 1) * 128],
                                wg_slots[oci % 2][:, k * ow:(k + 1) * ow],
                                start=(k == 0), stop=(k == 7)), "pe", 1)
                        for k in range(8):
                            c.inc(tensor.matmul(
                                pb[1][:, 0:ow],
                                xqf4[:, t4 * 1024 + k * 128:t4 * 1024 + (k + 1) * 128],
                                wu_slots[oci % 2][:, k * ow:(k + 1) * ow],
                                start=(k == 0), stop=(k == 7)), "pe", 1)
                        gu_pe[(grp, oci, t4)] = c.val("pe")
                        wld[("done", grp * 6 + oci)] = c.val("pe")

                    @block.scalar
                    def _(scalar, grp=grp, oci=oci, t4=t4, ow=ow, u=u):
                        a = Chain(c, scalar, "actc")
                        scalar.wait_ge(c.sems["pe"][0], gu_pe[(grp, oci, t4)])
                        if u >= 1:
                            scalar.wait_ge(c.sems["dvec"][0], h_dv[(grp, u - 1)])
                        a.activation(out=sgc[:, 0:ow], in_=pb[0][:, 0:ow], func=AF.Silu)
                        sil_ac[(grp, u)] = c.val("actc")

                    @block.vector
                    def _(vector, grp=grp, oci=oci, t4=t4, o0=o0, ow=ow, u=u):
                        v = Chain(c, vector, "dvec")
                        vector.wait_ge(c.sems["actc"][0], sil_ac[(grp, u)])
                        v.tensor_tensor(out=hchf[:, 0:ow], in0=sgc[:, 0:ow],
                                        in1=pb[1][:, 0:ow], op=ALU.mult)
                        emit_fp4_quant(v, hqc[:, 0:ow], hchf[:, 0:ow], scr, 128, ow)
                        h_dv[(grp, u)] = c.val("dvec")

                    for jj in range(nkk):
                        kk = o0 // 128 + jj
                        pt = ps_tb if (kk + t4) % 2 == 0 else ps_tb2

                        @block.tensor
                        def _(tensor, grp=grp, t4=t4, u=u, jj=jj, kk=kk, pt=pt):
                            tensor.wait_ge(c.sems["dvec"][0], h_dv[(grp, u)])
                            key = ("tbprev", id(pt))
                            if key in hq_pe:
                                tensor.wait_ge(c.sems["dvec"][0], hq_pe[key])
                            c.inc(tensor.transpose(out=pt[:, :],
                                                   in_=hqc[:, jj * 128:(jj + 1) * 128],
                                                   identity=eyeb[:, :]), "pe", 1)
                            hq_pe[(grp, kk, t4)] = c.val("pe")

                        @block.vector
                        def _(vector, grp=grp, t4=t4, jj=jj, kk=kk, pt=pt):
                            v = Chain(c, vector, "dvec")
                            vector.wait_ge(c.sems["pe"][0], hq_pe[(grp, kk, t4)])
                            v.tensor_copy(out=hq4[:, kk * 512 + t4 * 128:kk * 512 + (t4 + 1) * 128],
                                          in_=pt[:, :])
                            hq_pe[("tbprev", id(pt))] = c.val("dvec")
                            hq_pe[("ev", grp, kk, t4)] = c.val("dvec")

            for pass_ in range(2):
                for k in range(22):
                    @block.sync
                    def _(sync, grp=grp, pass_=pass_, k=k):
                        sm = f"dw{k % 3}"
                        gi = (grp * 2 + pass_) * 22 + k
                        if (pass_, k) > (0, 2) or grp > 0:
                            pk = (pass_, k - 3) if k >= 3 else (pass_ - 1, 19 + k)
                            if pk[0] < 0:
                                if grp > 0:
                                    sync.wait_ge(c.sems["pe"][0], dn_pe[(grp - 1, 1, 19 + k)])
                            else:
                                sync.wait_ge(c.sems["pe"][0], dn_pe[(grp, pk[0], pk[1])])
                        c.dma(sync, sm, R3[:, (k % 3) * 1024:(k % 3 + 1) * 1024],
                              wdq_t[k * 128:(k + 1) * 128, :])
                        dn_ld[(grp, pass_, k)] = (sm, c.val(sm))

                    @block.tensor
                    def _(tensor, grp=grp, pass_=pass_, k=k):
                        sm, tv = dn_ld[(grp, pass_, k)]
                        tensor.wait_ge(c.sems[sm][0], tv)
                        if pass_ == 0:
                            tensor.wait_ge(c.sems["dvec"][0], hq_pe[("ev", grp, k, 3)])
                        if k == 0:
                            if pass_ == 0:
                                tensor.wait_ge(c.sems["actc"][0], sil_ac[(grp, 23)])
                                tensor.wait_ge(c.sems["dvec"][0], h_dv[(grp, 23)])
                            else:
                                tensor.wait_ge(c.sems["actc"][0], g1_ac[(grp, 0, 3)])
                        for mi in range(4):
                            m = pass_ * 4 + mi
                            c.inc(tensor.matmul(
                                pb[mi][:, :],
                                R3[:, (k % 3) * 1024 + m * 128:(k % 3) * 1024 + (m + 1) * 128],
                                hq4[:, k * 512:(k + 1) * 512],
                                start=(k == 0), stop=(k == 21)), "pe", 1)
                        dn_pe[(grp, pass_, k)] = c.val("pe")

                for mi in range(4):
                    m = pass_ * 4 + mi

                    @block.scalar
                    def _(scalar, grp=grp, pass_=pass_, mi=mi, m=m):
                        a = Chain(c, scalar, "actc")
                        scalar.wait_ge(c.sems["pe"][0], dn_pe[(grp, pass_, 21)])
                        gsl = g1st[mi % 2]
                        key = (grp * 2 + pass_) * 4 + mi - 2
                        if key >= 0:
                            sd = g1_sd[key]
                            scalar.wait_ge(c.sems[sd[0]][0], sd[1])
                        a.activation(out=gsl[:, :], in_=pb[mi][:, :], func=AF.Silu)
                        g1_ac[(grp, pass_, mi)] = c.val("actc")

                    @block.sync
                    def _(sync, grp=grp, pass_=pass_, mi=mi, m=m):
                        sync.wait_ge(c.sems["actc"][0], g1_ac[(grp, pass_, mi)])
                        sm = "sg0" if mi % 2 == 0 else "sg1"
                        c.dma(sync, sm, g_out[m * 128:(m + 1) * 128, grp * 512:(grp + 1) * 512],
                              g1st[mi % 2][:, :])
                        g1_sd[(grp * 2 + pass_) * 4 + mi] = (sm, c.val(sm))

    with nc.Block() as block:
        @block.sync
        def _(sync):
            for sm in ["st", "st0", "st1", "sv0", "sv1", "sg0", "sg1", "sx"]:
                c.wait(sync, sm)

    c.close()
    return c


def build_l2a():
    c = Ctx("l2a")
    nc = c.nc
    DT = 8
    NCH = 4
    ED = 2048

    x_in = c.din("x1_feat", [D, T], F32)
    ans_in = c.din("ans1", [D, T], BF16)
    wo1 = c.din("wo1", [D, D], F32)
    bo1_col = c.din("bo1_col", [128, 8], F32)
    g1_in = c.din("g1", [D, T], F32)
    win_t = c.din("win_t", [D, 2 * ED], F32)
    wdt_row_in = c.din("wdt_row", [1, ED], F32)
    A_in = c.din("A_", [ED, 16], F32)
    Bm_in = c.din("Bm_", [ED, 16], F32)
    convw_col = c.din("convw_col", [128, 48], F32)
    gw1_t = c.din("gw1_t", [D, D], F32)
    gb1_col = c.din("gb1_col", [128, 8], F32)

    x2_out = c.dout("x2", [D, T], F32)
    xn_out = c.dout("xn2", [D, T], BF16)
    xc_out = c.dout("xconv", [ED, T], BF16)
    xs_out = c.dout("xssm", [ED, T], BF16)
    ml_out = c.dout("mloc", [16, T], F32)
    ap_out = c.dout("aprod", [16, T], F32)
    pe_out = c.dout("pedge", [ED, 2], F32)
    hs_out = c.dout("hsum", [128, 8], F32)

    xs = c.sb("xs", [128, DT * T], F32)
    xnb = c.sb("xnb", [128, DT * T], BF16)
    R1 = c.sb("R1", [128, 2048], F32)
    R2 = c.sb("R2", [128, 4096], F32)
    R3 = c.sb("R3", [128, 8192], BF16)
    R4 = c.sb("R4", [128, 16384], BF16)
    rows4 = R2[0:1, 0:2048]
    murow = rows4[0:1, 0:512]
    varrow = rows4[0:1, 512:1024]
    rstdrow = rows4[0:1, 1024:1536]
    r2row = rows4[0:1, 1536:2048]
    sq0 = c.sb("sq0", [128, 512], F32)
    sq1 = c.sb("sq1", [128, 512], F32)
    r1b = R1[:, 0:2048]
    r2b = c.sb("r2b", [128, 2048], F32)
    ones_col_f32 = c.sb("ones_col_f32", [128, 1], F32)
    ones_row_f32 = c.sb("ones_row_f32", [1, 128], F32)
    ones_col_bf = c.sb("ones_col_bf", [128, 1], BF16)
    epscol = c.sb("epscol", [1, 1], F32)
    nhcol = c.sb("nhcol", [1, 1], F32)
    bo1_c = c.sb("bo1_c", [128, 8], F32)
    gb1_c = c.sb("gb1_c", [128, 8], F32)
    cw_c = c.sb("cw_c", [128, 48], F32)
    wdt_r = c.sb("wdt_r", [1, ED], F32)
    Ab = c.sb("Ab", [128, 16 * 16], BF16)
    Bb = c.sb("Bb", [128, 16 * 16], BF16)
    pedge_s = c.sb("pedge_s", [128, 2 * 16], F32)
    hsum = c.sb("hsum_s", [128, 8], F32)
    dtrow = xs[0:1, 0:2048]
    a_s = xs[0:16, 2048:4096]
    u_s = xs[0:16, 4096:6144]
    ones16 = xs[0:16, 6144:8192]
    mres = xs[0:16, 8192:10240]
    apres = xs[0:16, 10240:12288]
    hrel = xs[:, 12288:12800]
    pct = R4[:, 0:2048]
    cout = R4[:, 2048:4096]
    dtvb = R4[:, 4096:4608]
    ub = R4[:, 4608:5120]
    cacc = R2[:, 0:2048]

    pb = [c.ps(f"pb{i}", [128, 512], F32) for i in range(4)]
    stackP = ExitStack()
    ps_row = stackP.enter_context(nc.psum_tensor("ps_row", [1, 512], F32))
    ps_row2 = stackP.enter_context(nc.psum_tensor("ps_row2", [1, 512], F32))

    for s in ["ld", "lg0", "lg1", "la0", "la1", "wl0", "wl1", "xr0", "xr1",
              "pe", "actc", "dvec", "st", "sc0", "sc1", "ss0", "ss1", "sx"]:
        c.sem(s)

    def xt(i):
        return xs[:, i * T:(i + 1) * T]

    def xnbt(i):
        return xnb[:, i * T:(i + 1) * T]

    prm = dict(sq0=sq0, sq1=sq1, murow=murow, varrow=varrow, rstdrow=rstdrow,
               r2row=r2row, r1b=r1b, r2b=r2b, ones_col_f32=ones_col_f32,
               ones_row_f32=ones_row_f32, epscol=epscol, nhcol=nhcol,
               ps_row=ps_row, ps_row2=ps_row2, bcA=pb[2], bcB=pb[3])

    with nc.Block() as block:
        @block.sync
        def _(sync):
            for i in range(DT):
                c.dma(sync, "ld", xt(i), x_in[i * 128:(i + 1) * 128, :])
            for dst, srcap in [(bo1_c[:, :], bo1_col), (gb1_c[:, :], gb1_col),
                               (cw_c[:, :], convw_col), (wdt_r[:, :], wdt_row_in)]:
                c.dma(sync, "ld", dst, srcap)
            for m in range(16):
                c.dma(sync, "ld", R2[:, m * 16:(m + 1) * 16], A_in[m * 128:(m + 1) * 128, :])
            for m in range(16):
                c.dma(sync, "ld", R2[:, 256 + m * 16:256 + (m + 1) * 16],
                      Bm_in[m * 128:(m + 1) * 128, :])

        @block.gpsimd
        def _(gpsimd):
            gpsimd.memset(ones_col_f32[:, :], 1.0)
            gpsimd.memset(ones_row_f32[:, :], 1.0)
            gpsimd.memset(ones_col_bf[:, :], 1.0)
            gpsimd.memset(epscol[:, :], LN_EPS)
            gpsimd.memset(nhcol[:, :], -0.5)

        @block.vector
        def _(vector):
            v = Chain(c, vector, "dvec")
            c.wait(vector, "ld")
            v.tensor_copy(out=Ab[:, :], in_=R2[:, 0:256])
            v.tensor_copy(out=Bb[:, :], in_=R2[:, 256:512])

    with nc.Block() as block:
        cast_dv = {}
        for j in range(8):
            @block.sync
            def _(sync, j=j):
                sm = "wl0" if j % 2 == 0 else "wl1"
                if j >= 2:
                    sync.wait_ge(c.sems["dvec"][0], cast_dv[j - 2])
                c.dma(sync, sm, R1[:, (j % 2) * 1024:(j % 2) * 1024 + 1024].rearrange(
                    "p (k m) -> p k m", k=8),
                    wo1[:, j * 128:(j + 1) * 128].rearrange("(k p) m -> p k m", p=128))
                cast_dv[("wl", j)] = (sm, c.val(sm))

            @block.vector
            def _(vector, j=j):
                v = Chain(c, vector, "dvec")
                sm, tv = cast_dv[("wl", j)]
                vector.wait_ge(c.sems[sm][0], tv)
                v.tensor_copy(out=R3[:, j * 1024:(j + 1) * 1024],
                              in_=R1[:, (j % 2) * 1024:(j % 2) * 1024 + 1024])
                cast_dv[j] = c.val("dvec")
    wo1_done = c.val("dvec")

    ansring = [R4[:, 0:4096], R4[:, 4096:8192]]
    with nc.Block() as block:
        s_pe = {}
        s_ev = {}
        xu_dv = {}
        ans_ld = {}
        g_ld = {}
        for qt in range(4):
            @block.sync
            def _(sync, qt=qt):
                sma = "la0" if qt % 2 == 0 else "la1"
                if qt >= 2:
                    sync.wait_ge(c.sems["pe"][0], s_pe[(qt - 2, 7)])
                c.dma(sync, sma, ansring[qt % 2][:, :].rearrange("p (k s) -> p k s", k=8),
                      ans_in.rearrange("(k p) t -> p k t", p=128)[:, :, qt * 512:(qt + 1) * 512])
                ans_ld[qt] = (sma, c.val(sma))
                for ihalf in range(2):
                    smg = "lg0" if ihalf == 0 else "lg1"
                    if qt >= 1:
                        sync.wait_ge(c.sems["dvec"][0], xu_dv[(qt - 1, ihalf * 4 + 3)])
                    c.dma(sync, smg, R2[:, ihalf * 2048:ihalf * 2048 + 2048].rearrange(
                        "p (k s) -> p k s", k=4),
                        g1_in.rearrange("(k p) t -> p k t", p=128)[:, ihalf * 4:(ihalf + 1) * 4,
                                                                  qt * 512:(qt + 1) * 512])
                    g_ld[(qt, ihalf)] = (smg, c.val(smg))

            for i in range(DT):
                @block.tensor
                def _(tensor, qt=qt, i=i):
                    if i == 0:
                        if qt == 0:
                            tensor.wait_ge(c.sems["dvec"][0], wo1_done)
                        sm, tv = ans_ld[qt]
                        tensor.wait_ge(c.sems[sm][0], tv)
                    gidx = qt * 8 + i
                    bidx = gidx % 2
                    if gidx >= 2:
                        tensor.wait_ge(c.sems["actc"][0], s_ev[gidx - 2])
                    for k in range(8):
                        c.inc(tensor.matmul(
                            pb[bidx][:, :],
                            R3[:, i * 1024 + k * 128:i * 1024 + (k + 1) * 128],
                            ansring[qt % 2][:, k * 512:(k + 1) * 512],
                            start=(k == 0), stop=(k == 7)), "pe", 1)
                    s_pe[(qt, i)] = c.val("pe")

                @block.scalar
                def _(scalar, qt=qt, i=i):
                    a = Chain(c, scalar, "actc")
                    gidx = qt * 8 + i
                    scalar.wait_ge(c.sems["pe"][0], s_pe[(qt, i)])
                    if gidx >= 1:
                        pq, pi = divmod(gidx - 1, 8)
                        scalar.wait_ge(c.sems["dvec"][0], xu_dv[(pq, pi)])
                    a.activation(out=sq0[:, :], in_=pb[gidx % 2][:, :],
                                 func=AF.Identity, bias=bo1_c[:, i:i + 1])
                    s_ev[gidx] = c.val("actc")

                @block.vector
                def _(vector, qt=qt, i=i):
                    v = Chain(c, vector, "dvec")
                    gidx = qt * 8 + i
                    sm, tv = g_ld[(qt, i // 4)]
                    vector.wait_ge(c.sems[sm][0], tv)
                    vector.wait_ge(c.sems["actc"][0], s_ev[gidx])
                    gsl = R2[:, (i // 4) * 2048 + (i % 4) * 512:(i // 4) * 2048 + (i % 4 + 1) * 512]
                    cs = slice(qt * 512, (qt + 1) * 512)
                    v.tensor_tensor(out=sq1[:, :], in0=sq0[:, :], in1=gsl, op=ALU.mult)
                    v.tensor_tensor(out=xt(i)[:, cs], in0=xt(i)[:, cs], in1=sq1[:, :], op=ALU.add)
                    xu_dv[(qt, i)] = c.val("dvec")

        @block.sync
        def _(sync):
            c.wait(sync, "dvec")
            for i in range(DT):
                c.dma(sync, "sx", x2_out[i * 128:(i + 1) * 128, :], xt(i))

    def apply_ln(v):
        for i in range(DT):
            for ch in range(NCH):
                cs = slice(ch * 512, (ch + 1) * 512)
                v.tensor_tensor(out=sq0[:, :], in0=xt(i)[:, cs], in1=r1b[:, cs], op=ALU.mult)
                v.tensor_tensor(out=sq1[:, :], in0=sq0[:, :], in1=r2b[:, cs], op=ALU.add)
                v.tensor_copy(out=xnbt(i)[:, cs], in_=sq1[:, :])

    emit_ln_feat(c, nc, xt, DT, T, prm, apply_ln)
    stackP.close()
    ps_dt = [c.ps(f"ps_dt{j}", [1, 512], F32) for j in range(4)]

    with nc.Block() as block:
        @block.sync
        def _(sync):
            c.wait(sync, "dvec")
            for i in range(DT):
                c.dma(sync, "st", xn_out[i * 128:(i + 1) * 128, :], xnbt(i))

    with nc.Block() as block:
        @block.gpsimd
        def _(gpsimd):
            c.wait(gpsimd, "sx")
            gpsimd.memset(ones16, 1.0)

        wl_t = {}
        cast_done = {}
        pj_pe = {}
        pj_ev = {}
        conv_dv = {}
        st_done = {}
        for mc in range(32):
            is_ssm = mc >= 16

            @block.sync
            def _(sync, mc=mc):
                sm = "wl0" if mc % 2 == 0 else "wl1"
                if mc >= 2:
                    sync.wait_ge(c.sems["dvec"][0], cast_done[mc - 2])
                c.dma(sync, sm, R1[:, (mc % 2) * 1024:(mc % 2) * 1024 + 1024].rearrange(
                    "p (k m) -> p k m", k=8),
                    win_t[:, mc * 128:(mc + 1) * 128].rearrange("(k p) m -> p k m", p=128))
                wl_t[mc] = (sm, c.val(sm))

            @block.vector
            def _(vector, mc=mc):
                v = Chain(c, vector, "dvec")
                sm, tv = wl_t[mc]
                vector.wait_ge(c.sems[sm][0], tv)
                if mc >= 2:
                    vector.wait_ge(c.sems["pe"][0], pj_pe[(mc - 2, 3)])
                dst = R3[:, (mc % 2) * 1024:(mc % 2) * 1024 + 1024]
                v.tensor_copy(out=dst, in_=R1[:, (mc % 2) * 1024:(mc % 2) * 1024 + 1024])
                cast_done[mc] = c.val("dvec")

            for ch in range(NCH):
                @block.tensor
                def _(tensor, mc=mc, ch=ch):
                    if ch == 0:
                        tensor.wait_ge(c.sems["dvec"][0], cast_done[mc])
                    gidx = mc * 4 + ch
                    bidx = gidx % 2
                    if gidx >= 2:
                        tensor.wait_ge(c.sems["actc"][0], pj_ev[gidx - 2])
                    for k in range(8):
                        c.inc(tensor.matmul(
                            pb[bidx][:, :],
                            R3[:, (mc % 2) * 1024 + k * 128:(mc % 2) * 1024 + (k + 1) * 128],
                            xnbt(k)[:, ch * 512:(ch + 1) * 512],
                            start=(k == 0), stop=(k == 7)), "pe", 1)
                    pj_pe[(mc, ch)] = c.val("pe")

                @block.scalar
                def _(scalar, mc=mc, ch=ch):
                    a = Chain(c, scalar, "actc")
                    gidx = mc * 4 + ch
                    scalar.wait_ge(c.sems["pe"][0], pj_pe[(mc, ch)])
                    if ch == 0 and mc >= 1:
                        if mc - 1 < 16:
                            scalar.wait_ge(c.sems["dvec"][0], conv_dv[mc - 1])
                        else:
                            scalar.wait_ge(c.sems["pe"][0], pj_pe[("dt", mc - 1)])
                            sd = st_done[("s", mc - 1)]
                            scalar.wait_ge(c.sems[sd[0]][0], sd[1])
                    a.activation(out=pct[:, ch * 512:(ch + 1) * 512], in_=pb[gidx % 2][:, :],
                                 func=AF.Copy)
                    pj_ev[gidx] = c.val("actc")

            if not is_ssm:
                @block.vector
                def _(vector, mc=mc):
                    v = Chain(c, vector, "dvec")
                    vector.wait_ge(c.sems["actc"][0], pj_ev[mc * 4 + 3])
                    if mc >= 1:
                        sd = st_done[("c", mc - 1)]
                        vector.wait_ge(c.sems[sd[0]][0], sd[1])
                    w0 = cw_c[:, 0 * 16 + mc:0 * 16 + mc + 1]
                    w1 = cw_c[:, 1 * 16 + mc:1 * 16 + mc + 1]
                    w2 = cw_c[:, 2 * 16 + mc:2 * 16 + mc + 1]
                    v.tensor_scalar(out=cacc[:, 0:T], in0=pct[:, :], scalar1=w1,
                                    scalar2=None, op0=ALU.mult)
                    v.scalar_tensor_tensor(out=cacc[:, 1:T], in0=pct[:, 0:T - 1], scalar=w0,
                                           in1=cacc[:, 1:T], op0=ALU.mult, op1=ALU.add)
                    v.scalar_tensor_tensor(out=cacc[:, 0:T - 1], in0=pct[:, 1:T], scalar=w2,
                                           in1=cacc[:, 0:T - 1], op0=ALU.mult, op1=ALU.add)
                    v.tensor_copy(out=cout[:, :], in_=cacc[:, 0:T])
                    v.tensor_copy(out=pedge_s[:, 2 * mc:2 * mc + 1], in_=pct[:, 0:1])
                    v.tensor_copy(out=pedge_s[:, 2 * mc + 1:2 * mc + 2], in_=pct[:, T - 1:T])
                    conv_dv[mc] = c.val("dvec")

                @block.sync
                def _(sync, mc=mc):
                    sync.wait_ge(c.sems["dvec"][0], conv_dv[mc])
                    sm = "sc0" if mc % 2 == 0 else "sc1"
                    c.dma(sync, sm, xc_out[mc * 128:(mc + 1) * 128, :], cout[:, :])
                    st_done[("c", mc)] = (sm, c.val(sm))
            else:
                ms = mc - 16

                @block.tensor
                def _(tensor, mc=mc, ms=ms):
                    tensor.wait_ge(c.sems["actc"][0], pj_ev[mc * 4 + 3])
                    for ch in range(NCH):
                        c.inc(tensor.matmul(
                            ps_dt[ch][:, :], ones_col_bf[:, :],
                            pct[:, ch * 512:(ch + 1) * 512],
                            start=(ms == 0), stop=(ms == 15)), "pe", 1)
                    pj_pe[("dt", mc)] = c.val("pe")

                @block.sync
                def _(sync, mc=mc, ms=ms):
                    sync.wait_ge(c.sems["pe"][0], pj_pe[("dt", mc)])
                    sm = "ss0" if mc % 2 == 0 else "ss1"
                    c.dma(sync, sm, xs_out[ms * 128:(ms + 1) * 128, :], pct[:, :])
                    st_done[("s", mc)] = (sm, c.val(sm))

        @block.vector
        def _(vector):
            v = Chain(c, vector, "dvec")
            vector.wait_ge(c.sems["pe"][0], pj_pe[("dt", 31)])
            c.wait(vector, "sx")
            for ch in range(NCH):
                v.tensor_scalar(out=dtrow[0:1, ch * 512:(ch + 1) * 512], in0=ps_dt[ch][:, :],
                                scalar1=1.0 / ED, scalar2=None, op0=ALU.mult)

        @block.sync
        def _(sync):
            c.wait(sync, "dvec")
            for m in range(16):
                c.dma(sync, "st", pe_out[m * 128:(m + 1) * 128, :], pedge_s[:, 2 * m:2 * m + 2])

    with nc.Block() as block:
        xr_ld = {}
        dtv_pe = {}
        dtv_dv = {}
        au_pe = {}
        for ch in range(NCH):
            for m in range(16):
                @block.sync
                def _(sync, ch=ch, m=m):
                    g = ch * 16 + m
                    sm = "xr0" if g % 2 == 0 else "xr1"
                    if g == 0:
                        c.wait(sync, "ss0")
                        c.wait(sync, "ss1")
                    if g >= 2:
                        sync.wait_ge(c.sems["dvec"][0], dtv_dv[g - 2])
                    c.dma(sync, sm, R3[:, (g % 2) * 512:(g % 2) * 512 + 512],
                          xs_out[m * 128:(m + 1) * 128, ch * 512:(ch + 1) * 512])
                    xr_ld[g] = (sm, c.val(sm))

                @block.tensor
                def _(tensor, ch=ch, m=m):
                    g = ch * 16 + m
                    if g >= 1:
                        tensor.wait_ge(c.sems["dvec"][0], dtv_dv[g - 1])
                    c.inc(tensor.matmul(pb[2][:, :], wdt_r[0:1, m * 128:(m + 1) * 128],
                                        dtrow[0:1, ch * 512:(ch + 1) * 512],
                                        start=True, stop=True), "pe", 1)
                    dtv_pe[g] = c.val("pe")

                @block.vector
                def _(vector, ch=ch, m=m):
                    v = Chain(c, vector, "dvec")
                    g = ch * 16 + m
                    sm, tv = xr_ld[g]
                    vector.wait_ge(c.sems[sm][0], tv)
                    vector.wait_ge(c.sems["pe"][0], dtv_pe[g])
                    if g >= 1:
                        vector.wait_ge(c.sems["pe"][0], au_pe[g - 1])
                    v.tensor_copy(out=dtvb[:, :], in_=pb[2][:, :])
                    v.tensor_tensor(out=ub[:, :], in0=R3[:, (g % 2) * 512:(g % 2) * 512 + 512],
                                    in1=dtvb[:, :], op=ALU.mult)
                    dtv_dv[g] = c.val("dvec")

                @block.tensor
                def _(tensor, ch=ch, m=m):
                    g = ch * 16 + m
                    tensor.wait_ge(c.sems["dvec"][0], dtv_dv[g])
                    if m == 0 and ch >= 1:
                        tensor.wait_ge(c.sems["dvec"][0], dtv_dv[("evac", ch - 1)])
                    c.inc(tensor.matmul(pb[0][0:16, :], Ab[:, m * 16:(m + 1) * 16],
                                        dtvb[:, :], start=(m == 0), stop=(m == 15)), "pe", 1)
                    c.inc(tensor.matmul(pb[1][0:16, :], Bb[:, m * 16:(m + 1) * 16],
                                        ub[:, :], start=(m == 0), stop=(m == 15)), "pe", 1)
                    au_pe[g] = c.val("pe")

            @block.vector
            def _(vector, ch=ch):
                v = Chain(c, vector, "dvec")
                vector.wait_ge(c.sems["pe"][0], au_pe[ch * 16 + 15])
                cs = slice(ch * 512, (ch + 1) * 512)
                v.tensor_copy(out=a_s[:, cs], in_=pb[0][0:16, :])
                v.tensor_copy(out=u_s[:, cs], in_=pb[1][0:16, :])
                dtv_dv[("evac", ch)] = c.val("dvec")

        @block.vector
        def _(vector):
            v = Chain(c, vector, "dvec")
            v.tensor_tensor_scan(out=mres, data0=a_s, data1=u_s,
                                 initial=0.0, op0=ALU.mult, op1=ALU.add)
            v.tensor_tensor_scan(out=apres, data0=a_s, data1=ones16,
                                 initial=1.0, op0=ALU.mult, op1=ALU.mult)

        @block.sync
        def _(sync):
            c.wait(sync, "dvec")
            c.dma(sync, "st", ml_out[0:16, :], mres)
            c.dma(sync, "st", ap_out[0:16, :], apres)

    with nc.Block() as block:
        wl_t2 = {}
        cast2 = {}
        gp_pe = {}
        gp_ev = {}
        hs_dv = {}
        for mc in range(8):
            @block.sync
            def _(sync, mc=mc):
                sm = "wl0" if mc % 2 == 0 else "wl1"
                if mc >= 2:
                    sync.wait_ge(c.sems["dvec"][0], cast2[mc - 2])
                c.dma(sync, sm, R1[:, (mc % 2) * 1024:(mc % 2) * 1024 + 1024].rearrange(
                    "p (k m) -> p k m", k=8),
                    gw1_t[:, mc * 128:(mc + 1) * 128].rearrange("(k p) m -> p k m", p=128))
                wl_t2[mc] = (sm, c.val(sm))

            @block.vector
            def _(vector, mc=mc):
                v = Chain(c, vector, "dvec")
                sm, tv = wl_t2[mc]
                vector.wait_ge(c.sems[sm][0], tv)
                if mc >= 2:
                    vector.wait_ge(c.sems["pe"][0], gp_pe[(mc - 2, 3)])
                v.tensor_copy(out=R3[:, (mc % 2) * 1024:(mc % 2) * 1024 + 1024],
                              in_=R1[:, (mc % 2) * 1024:(mc % 2) * 1024 + 1024])
                cast2[mc] = c.val("dvec")

            for ch in range(NCH):
                @block.tensor
                def _(tensor, mc=mc, ch=ch):
                    if ch == 0:
                        tensor.wait_ge(c.sems["dvec"][0], cast2[mc])
                    gidx = mc * 4 + ch
                    bidx = gidx % 2
                    if gidx >= 2:
                        tensor.wait_ge(c.sems["actc"][0], gp_ev[gidx - 2])
                    for k in range(8):
                        c.inc(tensor.matmul(
                            pb[bidx][:, :],
                            R3[:, (mc % 2) * 1024 + k * 128:(mc % 2) * 1024 + (k + 1) * 128],
                            xnbt(k)[:, ch * 512:(ch + 1) * 512],
                            start=(k == 0), stop=(k == 7)), "pe", 1)
                    gp_pe[(mc, ch)] = c.val("pe")

                @block.scalar
                def _(scalar, mc=mc, ch=ch):
                    a = Chain(c, scalar, "actc")
                    gidx = mc * 4 + ch
                    scalar.wait_ge(c.sems["pe"][0], gp_pe[(mc, ch)])
                    if gidx >= 1:
                        scalar.wait_ge(c.sems["dvec"][0], hs_dv[gidx - 1])
                    a.activation(out=hrel, in_=pb[gidx % 2][:, :],
                                 func=AF.Relu, bias=gb1_c[:, mc:mc + 1])
                    gp_ev[gidx] = c.val("actc")

                @block.vector
                def _(vector, mc=mc, ch=ch):
                    v = Chain(c, vector, "dvec")
                    gidx = mc * 4 + ch
                    vector.wait_ge(c.sems["actc"][0], gp_ev[gidx])
                    v.reduce_sum(out=sq0[:, 0:1], in_=hrel, axis=AX.X)
                    if ch == 0:
                        v.tensor_copy(out=hsum[:, mc:mc + 1], in_=sq0[:, 0:1])
                    else:
                        v.tensor_tensor(out=hsum[:, mc:mc + 1], in0=hsum[:, mc:mc + 1],
                                        in1=sq0[:, 0:1], op=ALU.add)
                    hs_dv[gidx] = c.val("dvec")

        @block.sync
        def _(sync):
            c.wait(sync, "dvec")
            c.dma(sync, "st", hs_out, hsum[:, :])

    with nc.Block() as block:
        @block.sync
        def _(sync):
            for sm in ["st", "sc0", "sc1", "ss0", "ss1", "sx"]:
                c.wait(sync, sm)

    c.close()
    return c


def build_l2b():
    c = Ctx("l2b")
    nc = c.nc
    NCH = 4
    ED = 2048

    xn_in = c.din("xn2", [D, T], BF16)
    xc_in = c.din("xconv", [ED, T], BF16)
    xs_in = c.din("xssm", [ED, T], BF16)
    ml_in = c.din("mloc", [16, T], F32)
    ap_in = c.din("aprod", [16, T], F32)
    me3_in = c.din("mend3", [16, 3], F32)
    ae3_in = c.din("aend3", [16, 3], F32)
    halo_in = c.din("halo_col", [128, 32], F32)
    convw_col = c.din("convw_col", [128, 48], F32)
    Cm_in = c.din("Cm_", [16, ED], F32)
    dv_in = c.din("dv_col", [128, 16], F32)
    wout_t = c.din("wout_t", [ED, D], F32)
    hs_in = c.din("hsums_all", [128, 32], F32)
    gw2_in = c.din("gw2_col", [128, 64], F32)
    gb2_in = c.din("gb2_row", [1, 8], F32)
    tw1_in = c.din("tw1q", [8, D, 1024], F32)
    tw2_in = c.din("tw2q", [8, 1024, D], F32)
    w1q_in = c.din("w1tq", [D, 1024], F32)
    w2q_in = c.din("w2tq", [1024, D], F32)
    tb1_in = c.din("Tb1_", [8, 4096], F32)
    b1_in = c.din("b1_row", [1, 4096], F32)
    tb2_in = c.din("Tb2_", [8, 1024], F32)
    b2_in = c.din("b2_row", [1, 1024], F32)
    eyeb_in = c.din("eyeb", [128, 128], BF16)

    s2_out = c.dout("s2", [D, T], F32)
    mw1_out = c.dout("mw1q", [D, 1024], BF16)
    mw2_out = c.dout("mw2q", [1024, D], BF16)
    mb1_out = c.dout("mb1", [1, 4096], F32)
    mb2_out = c.dout("mb2", [1, 1024], F32)

    xnb = c.sb("xnb", [128, 8 * T], BF16)
    woutb = c.sb("woutb", [128, 16 * 1024], BF16)
    xcq = c.sb("xcq", [128, 16 * 512], BF16)
    xsq = c.sb("xsq", [128, 16 * 512], BF16)
    zq = c.sb("zq", [128, 16 * 512], BF16)
    mfb = c.sb("mfb", [16, T], BF16)
    me3 = c.sb("me3", [16, 3], F32)
    ae3 = c.sb("ae3", [16, 3], F32)
    mcar = c.sb("mcar", [16, 1], F32)
    tcar = c.sb("tcar", [16, 1], F32)
    Cmb = c.sb("Cmb", [16, ED], BF16)
    halo = c.sb("halo", [128, 32], F32)
    cw_c = c.sb("cw_c", [128, 48], F32)
    dv_c = c.sb("dv_c", [128, 16], F32)
    hsa = c.sb("hsa", [128, 32], F32)
    hst = c.sb("hst", [128, 8], F32)
    gw2c = c.sb("gw2c", [128, 64], F32)
    gb2r = c.sb("gb2r", [1, 8], F32)
    grow = c.sb("grow", [1, 8], F32)
    gcolf = c.sb("gcolf", [128, 8], F32)
    gcol8 = c.sb("gcol8", [8, 1], F32)
    gI = c.sb("gI", [128, 8 * 128], BF16)
    eyeb = c.sb("eyebs", [128, 128], BF16)
    tb1 = c.sb("tb1", [8, 4096], F32)
    tb2 = c.sb("tb2", [8, 1024], F32)
    b1s = c.sb("b1s", [1, 4096], F32)
    b2s = c.sb("b2s", [1, 1024], F32)
    ones11 = c.sb("ones11", [1, 1], F32)
    mbrow = c.sb("mbrow", [1, 2 * 512], F32)
    wstg = c.sb("wstg", [128, 2 * 1024], F32)
    ysb = c.sb("ysb", [128, 512], F32)
    gsb = c.sb("gsb", [128, 512], F32)
    dsb = c.sb("dsb", [128, 512], F32)
    s2sb = c.sb("s2sb", [128, 8 * 512], F32)
    mlo = s2sb[0:16, 0:2048]
    apr = s2sb[0:16, 2048:4096]
    Cms = wstg[0:16, 0:2048]
    mfix = wstg[0:16, 0:2048]
    twst = s2sb[:, 0:1024]
    wbst = s2sb[:, 1024:2048]
    twb = zq[:, 0:1024]
    mwsb = zq[:, 1024:2048]

    pb = [c.ps(f"pb{i}", [128, 512], F32) for i in range(8)]

    for s in ["ld", "wl0", "wl1", "lc0", "lc1", "ls0", "ls1", "tc0", "tc1",
              "wb0", "wb1", "ms0", "ms1", "pe", "actc", "dvec", "st", "s20", "s21"]:
        c.sem(s)

    with nc.Block() as block:
        @block.sync
        def _(sync):
            for i in range(8):
                c.dma(sync, "ld", xnb[:, i * T:(i + 1) * T], xn_in[i * 128:(i + 1) * 128, :])
            c.dma(sync, "ld", mlo[0:16, :], ml_in)
            c.dma(sync, "ld", apr[0:16, :], ap_in)
            for dst, srcap in [(me3[0:16, :], me3_in), (ae3[0:16, :], ae3_in),
                               (halo[:, :], halo_in), (cw_c[:, :], convw_col),
                               (Cms, Cm_in), (dv_c[:, :], dv_in),
                               (hsa[:, :], hs_in), (gw2c[:, :], gw2_in),
                               (gb2r[0:1, :], gb2_in), (tb1[0:8, :], tb1_in),
                               (tb2[0:8, :], tb2_in), (b1s[0:1, :], b1_in),
                               (b2s[0:1, :], b2_in), (eyeb[:, :], eyeb_in)]:
                c.dma(sync, "ld", dst, srcap)

        @block.gpsimd
        def _(gpsimd):
            gpsimd.memset(ones11[0:1, :], 1.0)
            gpsimd.memset(mcar[0:16, :], 0.0)

        @block.vector
        def _(vector):
            v = Chain(c, vector, "dvec")
            c.wait(vector, "ld")
            v.tensor_copy(out=Cmb[0:16, :], in_=Cms)
            for j in range(3):
                v.tensor_tensor(out=tcar[0:16, :], in0=ae3[0:16, j:j + 1],
                                in1=mcar[0:16, :], op=ALU.mult)
                v.tensor_tensor(out=mcar[0:16, :], in0=tcar[0:16, :],
                                in1=me3[0:16, j:j + 1], op=ALU.add)
            v.scalar_tensor_tensor(out=mfix, in0=apr,
                                   scalar=mcar[0:16, :], in1=mlo,
                                   op0=ALU.mult, op1=ALU.add)
            v.tensor_copy(out=mfb[0:16, :], in_=mfix)
            v.tensor_tensor(out=hst[:, :], in0=hsa[:, 0:8], in1=hsa[:, 8:16], op=ALU.add)
            v.tensor_tensor(out=hst[:, :], in0=hst[:, :], in1=hsa[:, 16:24], op=ALU.add)
            v.tensor_tensor(out=hst[:, :], in0=hst[:, :], in1=hsa[:, 24:32], op=ALU.add)

    with nc.Block() as block:
        wc = {}
        for j in range(16):
            @block.sync
            def _(sync, j=j):
                sm = "wl0" if j % 2 == 0 else "wl1"
                if j >= 2:
                    sync.wait_ge(c.sems["dvec"][0], wc[j - 2])
                c.dma(sync, sm, wstg[:, (j % 2) * 1024:(j % 2) * 1024 + 1024],
                      wout_t[j * 128:(j + 1) * 128, :])
                wc[("l", j)] = (sm, c.val(sm))

            @block.vector
            def _(vector, j=j):
                v = Chain(c, vector, "dvec")
                sm, tv = wc[("l", j)]
                vector.wait_ge(c.sems[sm][0], tv)
                v.tensor_copy(out=woutb[:, j * 1024:(j + 1) * 1024],
                              in_=wstg[:, (j % 2) * 1024:(j % 2) * 1024 + 1024])
                wc[j] = c.val("dvec")

    with nc.Block() as block:
        q_ld = {}
        y_pe = {}
        y_ev = {}
        z_dv = {}
        s_pe = {}
        s_dv = {}
        s_st = {}
        for qt in range(4):
            @block.sync
            def _(sync, qt=qt):
                smc = "lc0" if qt % 2 == 0 else "lc1"
                sms = "ls0" if qt % 2 == 0 else "ls1"
                if qt >= 2:
                    sync.wait_ge(c.sems["dvec"][0], z_dv[(qt - 2, 15)])
                c.dma(sync, smc, xcq[:, :].rearrange("p (m s) -> p m s", m=16),
                      xc_in.rearrange("(m p) t -> p m t", p=128)[:, :, qt * 512:(qt + 1) * 512])
                c.dma(sync, sms, xsq[:, :].rearrange("p (m s) -> p m s", m=16),
                      xs_in.rearrange("(m p) t -> p m t", p=128)[:, :, qt * 512:(qt + 1) * 512])
                q_ld[qt] = [(smc, c.val(smc)), (sms, c.val(sms))]

            for m in range(16):
                @block.tensor
                def _(tensor, qt=qt, m=m):
                    if m == 0:
                        for sm, tv in q_ld[qt]:
                            tensor.wait_ge(c.sems[sm][0], tv)
                    g = qt * 16 + m
                    if g >= 2:
                        tensor.wait_ge(c.sems["actc"][0], y_ev[g - 2])
                    c.inc(tensor.matmul(pb[m % 2][:, :], Cmb[0:16, m * 128:(m + 1) * 128],
                                        mfb[0:16, qt * 512:(qt + 1) * 512],
                                        start=True, stop=True), "pe", 1)
                    y_pe[g] = c.val("pe")

                @block.vector
                def _(vector, qt=qt, m=m):
                    v = Chain(c, vector, "dvec")
                    g = qt * 16 + m
                    vector.wait_ge(c.sems["pe"][0], y_pe[g])
                    if qt == 0 and m == 0:
                        for mm_ in range(16):
                            v.scalar_tensor_tensor(
                                out=xcq[:, mm_ * 512:mm_ * 512 + 1],
                                in0=halo[:, mm_:mm_ + 1], scalar=cw_c[:, 0 * 16 + mm_:0 * 16 + mm_ + 1],
                                in1=xcq[:, mm_ * 512:mm_ * 512 + 1], op0=ALU.mult, op1=ALU.add)
                    if qt == 3 and m == 0:
                        for mm_ in range(16):
                            v.scalar_tensor_tensor(
                                out=xcq[:, mm_ * 512 + 511:mm_ * 512 + 512],
                                in0=halo[:, 16 + mm_:16 + mm_ + 1],
                                scalar=cw_c[:, 2 * 16 + mm_:2 * 16 + mm_ + 1],
                                in1=xcq[:, mm_ * 512 + 511:mm_ * 512 + 512],
                                op0=ALU.mult, op1=ALU.add)
                    v.scalar_tensor_tensor(out=ysb[:, :], in0=xsq[:, m * 512:(m + 1) * 512],
                                           scalar=dv_c[:, m:m + 1], in1=pb[m % 2][:, :],
                                           op0=ALU.mult, op1=ALU.add)
                    z_dv[("y", qt, m)] = c.val("dvec")

                @block.scalar
                def _(scalar, qt=qt, m=m):
                    a = Chain(c, scalar, "actc")
                    g = qt * 16 + m
                    scalar.wait_ge(c.sems["dvec"][0], z_dv[("y", qt, m)])
                    a.activation(out=gsb[:, :], in_=ysb[:, :], func=AF.Sigmoid)
                    y_ev[g] = c.val("actc")

                @block.vector
                def _(vector, qt=qt, m=m):
                    v = Chain(c, vector, "dvec")
                    g = qt * 16 + m
                    vector.wait_ge(c.sems["actc"][0], y_ev[g])
                    if qt >= 1 and m == 0:
                        vector.wait_ge(c.sems["pe"][0], s_pe[(qt - 1, 7)])
                    v.tensor_tensor(out=dsb[:, :], in0=xcq[:, m * 512:(m + 1) * 512],
                                    in1=ysb[:, :], op=ALU.subtract)
                    v.tensor_tensor(out=dsb[:, :], in0=gsb[:, :], in1=dsb[:, :], op=ALU.mult)
                    v.tensor_tensor(out=zq[:, m * 512:(m + 1) * 512], in0=dsb[:, :],
                                    in1=ysb[:, :], op=ALU.add)
                    z_dv[(qt, m)] = c.val("dvec")

            for j in range(8):
                @block.tensor
                def _(tensor, qt=qt, j=j):
                    if j == 0:
                        tensor.wait_ge(c.sems["dvec"][0], z_dv[(qt, 15)])
                        if qt == 0:
                            tensor.wait_ge(c.sems["dvec"][0], wc[15])
                    if qt >= 1:
                        tensor.wait_ge(c.sems["dvec"][0], s_dv[(qt - 1, j)])
                    for k in range(16):
                        c.inc(tensor.matmul(
                            pb[j][:, :], woutb[:, k * 1024 + j * 128:k * 1024 + (j + 1) * 128],
                            zq[:, k * 512:(k + 1) * 512],
                            start=(k == 0), stop=(k == 15)), "pe", 1)
                    s_pe[(qt, j)] = c.val("pe")

                @block.vector
                def _(vector, qt=qt, j=j):
                    v = Chain(c, vector, "dvec")
                    vector.wait_ge(c.sems["pe"][0], s_pe[(qt, j)])
                    if qt >= 2:
                        sm, tv = s_st[(qt - 2, j)]
                        vector.wait_ge(c.sems[sm][0], tv)
                    v.tensor_tensor(out=s2sb[:, j * 512:(j + 1) * 512], in0=pb[j][:, :],
                                    in1=xnb[:, j * T + qt * 512:j * T + (qt + 1) * 512],
                                    op=ALU.add)
                    s_dv[(qt, j)] = c.val("dvec")

                @block.sync
                def _(sync, qt=qt, j=j):
                    sync.wait_ge(c.sems["dvec"][0], s_dv[(qt, j)])
                    sm = "s20" if qt % 2 == 0 else "s21"
                    c.dma(sync, sm, s2_out[j * 128:(j + 1) * 128, qt * 512:(qt + 1) * 512],
                          s2sb[:, j * 512:(j + 1) * 512])
                    s_st[(qt, j)] = (sm, c.val(sm))

    with nc.Block() as block:
        f4 = {}

        @block.tensor
        def _(tensor):
            for mc in range(8):
                c.inc(tensor.matmul(pb[0][0:1, 0:8], hst[:, mc:mc + 1], gw2c[:, mc * 8:(mc + 1) * 8],
                                    start=(mc == 0), stop=(mc == 7)), "pe", 1)
            f4["g_pe"] = c.val("pe")

        @block.vector
        def _(vector):
            v = Chain(c, vector, "dvec")
            vector.wait_ge(c.sems["pe"][0], f4["g_pe"])
            v.tensor_scalar(out=grow[0:1, :], in0=pb[0][0:1, 0:8], scalar1=1.0 / 8192.0,
                            scalar2=None, op0=ALU.mult)
            v.tensor_tensor(out=grow[0:1, :], in0=grow[0:1, :], in1=gb2r[0:1, :], op=ALU.add)
            f4["g_dv"] = c.val("dvec")

        @block.tensor
        def _(tensor):
            tensor.wait_ge(c.sems["dvec"][0], f4["g_dv"])
            c.inc(tensor.matmul(pb[1][:, 0:8], ones11[0:1, 0:1].to_broadcast([1, 128]),
                                grow[0:1, :], start=True, stop=True), "pe", 1)
            c.inc(tensor.transpose(pb[2][0:8, 0:1], grow[0:1, :], ones11[0:1, 0:1]), "pe", 1)
            f4["b_pe"] = c.val("pe")

        @block.vector
        def _(vector):
            v = Chain(c, vector, "dvec")
            vector.wait_ge(c.sems["pe"][0], f4["b_pe"])
            v.tensor_copy(out=gcolf[:, :], in_=pb[1][:, 0:8])
            v.tensor_copy(out=gcol8[0:8, :], in_=pb[2][0:8, 0:1])
            for e in range(8):
                v.tensor_scalar(out=gI[:, e * 128:(e + 1) * 128], in0=eyeb[:, :],
                                scalar1=gcolf[:, e:e + 1], scalar2=None, op0=ALU.mult)
            f4["gI_dv"] = c.val("dvec")

        nb = 0
        for w2, nch, tbx, bx, out_ap in [(0, 8, tb1, b1s, mb1_out), (1, 2, tb2, b2s, mb2_out)]:
            for fc in range(nch):
                @block.tensor
                def _(tensor, w2=w2, fc=fc, tbx=tbx, nb=nb):
                    if nb == 0:
                        tensor.wait_ge(c.sems["dvec"][0], f4["gI_dv"])
                    if nb >= 2:
                        f4_dv = f4[("mb_dv", nb - 2)]
                        tensor.wait_ge(c.sems["dvec"][0], f4_dv)
                    bank = pb[3 + nb % 2]
                    c.inc(tensor.matmul(bank[0:1, :], gcol8[0:8, :],
                                        tbx[0:8, fc * 512:(fc + 1) * 512],
                                        start=True, stop=True), "pe", 1)
                    f4[("mb_pe", nb)] = c.val("pe")

                @block.vector
                def _(vector, w2=w2, fc=fc, bx=bx, nb=nb):
                    v = Chain(c, vector, "dvec")
                    vector.wait_ge(c.sems["pe"][0], f4[("mb_pe", nb)])
                    if nb >= 2:
                        sm, tv = f4[("mbst", nb - 2)]
                        vector.wait_ge(c.sems[sm][0], tv)
                    bsl = bx[0:1, fc * 512:(fc + 1) * 512]
                    v.tensor_tensor(out=mbrow[0:1, (nb % 2) * 512:(nb % 2) * 512 + 512],
                                    in0=pb[3 + nb % 2][0:1, 0:512], in1=bsl, op=ALU.add)
                    f4[("mb_dv", nb)] = c.val("dvec")

                @block.sync
                def _(sync, fc=fc, out_ap=out_ap, nb=nb):
                    sync.wait_ge(c.sems["dvec"][0], f4[("mb_dv", nb)])
                    sm = "ms0" if nb % 2 == 0 else "ms1"
                    c.dma(sync, sm, out_ap[0:1, fc * 512:(fc + 1) * 512],
                          mbrow[0:1, (nb % 2) * 512:(nb % 2) * 512 + 512])
                    f4[("mbst", nb)] = (sm, c.val(sm))
                nb += 1

        t_ld = {}
        t_cast = {}
        m_pe = {}
        m_dv = {}
        m_st = {}
        ti = 0
        for w in range(2):
            tw_in = tw1_in if w == 0 else tw2_in
            wq_in = w1q_in if w == 0 else w2q_in
            mw_out = mw1_out if w == 0 else mw2_out
            for ht in range(8):
                for fc in range(2):
                    ci0 = ti * 8
                    for e in range(8):
                        ci = ci0 + e
                        @block.sync
                        def _(sync, tw_in=tw_in, wq_in=wq_in, ht=ht, fc=fc, ti=ti,
                              e=e, ci=ci):
                            if e == 0:
                                smb = "wb0" if ti % 2 == 0 else "wb1"
                                if ti >= 2:
                                    sync.wait_ge(c.sems["dvec"][0], m_dv[ti - 2])
                                c.dma(sync, smb, wbst[:, (ti % 2) * 512:(ti % 2) * 512 + 512],
                                      wq_in[ht * 128:(ht + 1) * 128, fc * 512:(fc + 1) * 512])
                                t_ld[("b", ti)] = (smb, c.val(smb))
                            sm = "tc0" if ci % 2 == 0 else "tc1"
                            if ci == 0:
                                c.wait(sync, "s20")
                                c.wait(sync, "s21")
                            if ci >= 2:
                                sync.wait_ge(c.sems["dvec"][0], t_cast[ci - 2])
                            c.dma(sync, sm, twst[:, (ci % 2) * 512:(ci % 2) * 512 + 512],
                                  tw_in[e, ht * 128:(ht + 1) * 128, fc * 512:(fc + 1) * 512])
                            t_ld[ci] = (sm, c.val(sm))

                        @block.vector
                        def _(vector, ci=ci):
                            v = Chain(c, vector, "dvec")
                            sm, tv = t_ld[ci]
                            vector.wait_ge(c.sems[sm][0], tv)
                            if ci == 0:
                                vector.wait_ge(c.sems["pe"][0], s_pe[(3, 7)])
                            if ci >= 2:
                                vector.wait_ge(c.sems["pe"][0], m_pe.get(ci - 2, 0))
                            v.tensor_copy(out=twb[:, (ci % 2) * 512:(ci % 2) * 512 + 512],
                                          in_=twst[:, (ci % 2) * 512:(ci % 2) * 512 + 512])
                            t_cast[ci] = c.val("dvec")

                        @block.tensor
                        def _(tensor, e=e, ci=ci, ti=ti):
                            tensor.wait_ge(c.sems["dvec"][0], t_cast[ci])
                            if e == 0:
                                if ti == 0:
                                    tensor.wait_ge(c.sems["dvec"][0], f4["gI_dv"])
                                if ti >= 2:
                                    tensor.wait_ge(c.sems["dvec"][0], m_dv[ti - 2])
                            c.inc(tensor.matmul(pb[4 + ti % 2][:, :], gI[:, e * 128:(e + 1) * 128],
                                                twb[:, (ci % 2) * 512:(ci % 2) * 512 + 512],
                                                start=(e == 0), stop=(e == 7)), "pe", 1)
                            m_pe[ci] = c.val("pe")

                    @block.vector
                    def _(vector, ti=ti, ci0=ci0):
                        v = Chain(c, vector, "dvec")
                        vector.wait_ge(c.sems["pe"][0], m_pe[ci0 + 7])
                        sm, tv = t_ld[("b", ti)]
                        vector.wait_ge(c.sems[sm][0], tv)
                        if ti >= 2:
                            sm2, tv2 = m_st[ti - 2]
                            vector.wait_ge(c.sems[sm2][0], tv2)
                        v.tensor_tensor(out=mwsb[:, (ti % 2) * 512:(ti % 2) * 512 + 512],
                                        in0=pb[4 + ti % 2][:, :],
                                        in1=wbst[:, (ti % 2) * 512:(ti % 2) * 512 + 512],
                                        op=ALU.add)
                        m_dv[ti] = c.val("dvec")

                    @block.sync
                    def _(sync, mw_out=mw_out, ht=ht, fc=fc, ti=ti):
                        sync.wait_ge(c.sems["dvec"][0], m_dv[ti])
                        sm = "ms0" if ti % 2 == 0 else "ms1"
                        c.dma(sync, sm, mw_out[ht * 128:(ht + 1) * 128, fc * 512:(fc + 1) * 512],
                              mwsb[:, (ti % 2) * 512:(ti % 2) * 512 + 512])
                        m_st[ti] = (sm, c.val(sm))
                    ti += 1

    with nc.Block() as block:
        @block.sync
        def _(sync):
            for sm in ["st", "s20", "s21", "ms0", "ms1"]:
                c.wait(sync, sm)

    c.close()
    return c


def build_l3g():
    c = Ctx("l3g")
    nc = c.nc
    DT = 8
    NCH = 4

    x_in = c.din("x2", [D, T], F32)
    s2_in = c.din("s2", [D, T], F32)
    xn2_in = c.din("xn2", [D, T], BF16)
    mw1_in = c.din("mw1t", [D, 4096], BF16)
    mw2_in = c.din("mw2t", [4096, D], BF16)
    mb1_in = c.din("mb1_col", [128, 32], F32)
    mb2_in = c.din("mb2_col", [128, 8], F32)

    x3_out = c.dout("x3", [D, T], F32)
    xn3_out = c.dout("xn3", [D, T], BF16)

    xs = c.sb("xs", [128, DT * T], F32)
    xnb = c.sb("xnb", [128, DT * T], BF16)
    u2q = c.sb("u2q", [128, 32 * 512], BF16)
    s2q = c.sb("s2q", [128, 8 * 512], F32)
    w1r = c.sb("w1r", [128, 2 * 1024], BF16)
    w2r = c.sb("w2r", [128, 2 * 1024], BF16)
    g2t = c.sb("g2t", [128, 512], F32)
    sq0 = c.sb("sq0", [128, 512], F32)
    sq1 = c.sb("sq1", [128, 512], F32)
    mb1c = c.sb("mb1c", [128, 32], F32)
    mb2c = c.sb("mb2c", [128, 8], F32)
    r1b = s2q[:, 0:2048]
    r2b = s2q[:, 2048:4096]
    rows4 = c.sb("rows4", [1, 2048], F32)
    murow = rows4[0:1, 0:512]
    varrow = rows4[0:1, 512:1024]
    rstdrow = rows4[0:1, 1024:1536]
    r2row = rows4[0:1, 1536:2048]
    ones_col_f32 = c.sb("ones_col_f32", [128, 1], F32)
    ones_row_f32 = c.sb("ones_row_f32", [1, 128], F32)
    epscol = c.sb("epscol", [1, 1], F32)
    nhcol = c.sb("nhcol", [1, 1], F32)

    pb = [c.ps(f"pb{i}", [128, 512], F32) for i in range(8)]

    for s in ["ld", "ls0", "ls1", "w10", "w11", "w20", "w21", "pe", "actc", "dvec",
              "st", "sx0", "sx1"]:
        c.sem(s)

    def xt(i):
        return xs[:, i * T:(i + 1) * T]

    def xnbt(i):
        return xnb[:, i * T:(i + 1) * T]

    with nc.Block() as block:
        @block.sync
        def _(sync):
            for i in range(DT):
                c.dma(sync, "ld", xt(i), x_in[i * 128:(i + 1) * 128, :])
                c.dma(sync, "ld", xnbt(i), xn2_in[i * 128:(i + 1) * 128, :])
            c.dma(sync, "ld", mb1c[:, :], mb1_in)
            c.dma(sync, "ld", mb2c[:, :], mb2_in)

        @block.gpsimd
        def _(gpsimd):
            gpsimd.memset(ones_col_f32[:, :], 1.0)
            gpsimd.memset(ones_row_f32[:, :], 1.0)
            gpsimd.memset(epscol[:, :], LN_EPS)
            gpsimd.memset(nhcol[:, :], -0.5)

    with nc.Block() as block:
        s_ld = {}
        w1l = {}
        w2l = {}
        u_pe = {}
        u_ev = {}
        d_pe = {}
        d_ev = {}
        x_dv = {}
        x_st = {}
        for qt in range(4):
            @block.sync
            def _(sync, qt=qt):
                sm = "ls0" if qt % 2 == 0 else "ls1"
                if qt >= 2:
                    sync.wait_ge(c.sems["dvec"][0], x_dv[(qt - 2, 7)])
                c.dma(sync, sm, s2q[:, 0:4096].rearrange("p (j s) -> p j s", j=8),
                      s2_in.rearrange("(j p) t -> p j t", p=128)[:, :, qt * 512:(qt + 1) * 512])
                s_ld[qt] = (sm, c.val(sm))

            for ft in range(32):
                g = qt * 32 + ft
                @block.sync
                def _(sync, qt=qt, ft=ft, g=g):
                    sm = "w10" if g % 2 == 0 else "w11"
                    if g >= 2:
                        sync.wait_ge(c.sems["pe"][0], u_pe[g - 2])
                    c.dma(sync, sm, w1r[:, (g % 2) * 1024:(g % 2) * 1024 + 1024].rearrange(
                        "p (k m) -> p k m", k=8),
                        mw1_in[:, ft * 128:(ft + 1) * 128].rearrange("(k p) m -> p k m", p=128))
                    w1l[g] = (sm, c.val(sm))

                @block.tensor
                def _(tensor, qt=qt, ft=ft, g=g):
                    sm, tv = w1l[g]
                    tensor.wait_ge(c.sems[sm][0], tv)
                    if g >= 2:
                        tensor.wait_ge(c.sems["actc"][0], u_ev[g - 2])
                    for k in range(8):
                        c.inc(tensor.matmul(
                            pb[g % 2][:, :],
                            w1r[:, (g % 2) * 1024 + k * 128:(g % 2) * 1024 + (k + 1) * 128],
                            xnbt(k)[:, qt * 512:(qt + 1) * 512],
                            start=(k == 0), stop=(k == 7)), "pe", 1)
                    u_pe[g] = c.val("pe")

                @block.scalar
                def _(scalar, qt=qt, ft=ft, g=g):
                    a = Chain(c, scalar, "actc")
                    scalar.wait_ge(c.sems["pe"][0], u_pe[g])
                    if qt >= 1 and ft == 0:
                        scalar.wait_ge(c.sems["pe"][0], d_pe[(qt - 1, 31)])
                    a.activation(out=u2q[:, ft * 512:(ft + 1) * 512], in_=pb[g % 2][:, :],
                                 func=AF.Gelu_apprx_tanh, bias=mb1c[:, ft:ft + 1])
                    u_ev[g] = c.val("actc")

            for k in range(32):
                g = qt * 32 + k
                @block.sync
                def _(sync, qt=qt, k=k, g=g):
                    sm = "w20" if g % 2 == 0 else "w21"
                    if g >= 2:
                        sync.wait_ge(c.sems["pe"][0], d_pe[(qt, k - 2) if k >= 2 else (qt - 1, 30 + k)])
                    c.dma(sync, sm, w2r[:, (g % 2) * 1024:(g % 2) * 1024 + 1024],
                          mw2_in[k * 128:(k + 1) * 128, :])
                    w2l[g] = (sm, c.val(sm))

                @block.tensor
                def _(tensor, qt=qt, k=k, g=g):
                    sm, tv = w2l[g]
                    tensor.wait_ge(c.sems[sm][0], tv)
                    if k == 0:
                        tensor.wait_ge(c.sems["actc"][0], u_ev[qt * 32 + 31])
                        if qt >= 1:
                            tensor.wait_ge(c.sems["dvec"][0], x_dv[(qt - 1, 7)])
                    for j in range(8):
                        c.inc(tensor.matmul(
                            pb[j][:, :],
                            w2r[:, (g % 2) * 1024 + j * 128:(g % 2) * 1024 + (j + 1) * 128],
                            u2q[:, k * 512:(k + 1) * 512],
                            start=(k == 0), stop=(k == 31)), "pe", 1)
                    d_pe[(qt, k)] = c.val("pe")

            for j in range(8):
                @block.scalar
                def _(scalar, qt=qt, j=j):
                    a = Chain(c, scalar, "actc")
                    scalar.wait_ge(c.sems["pe"][0], d_pe[(qt, 31)])
                    if j >= 1:
                        scalar.wait_ge(c.sems["dvec"][0], x_dv[(qt, j - 1)])
                    a.activation(out=g2t[:, :], in_=pb[j][:, :],
                                 func=AF.Silu, bias=mb2c[:, j:j + 1])
                    d_ev[(qt, j)] = c.val("actc")

                @block.vector
                def _(vector, qt=qt, j=j):
                    v = Chain(c, vector, "dvec")
                    vector.wait_ge(c.sems["actc"][0], d_ev[(qt, j)])
                    if qt == 0 and j == 0:
                        c.wait(vector, "ld")
                    sm, tv = s_ld[qt]
                    vector.wait_ge(c.sems[sm][0], tv)
                    if qt >= 2:
                        sm2, tv2 = x_st[(qt - 2, j)]
                        vector.wait_ge(c.sems[sm2][0], tv2)
                    cs = slice(qt * 512, (qt + 1) * 512)
                    v.tensor_tensor(out=sq0[:, :], in0=g2t[:, :],
                                    in1=s2q[:, j * 512:(j + 1) * 512], op=ALU.mult)
                    v.tensor_tensor(out=xt(j)[:, cs], in0=xt(j)[:, cs], in1=sq0[:, :],
                                    op=ALU.add)
                    x_dv[(qt, j)] = c.val("dvec")

                @block.sync
                def _(sync, qt=qt, j=j):
                    sync.wait_ge(c.sems["dvec"][0], x_dv[(qt, j)])
                    sm = "sx0" if qt % 2 == 0 else "sx1"
                    c.dma(sync, sm, x3_out[j * 128:(j + 1) * 128, qt * 512:(qt + 1) * 512],
                          xt(j)[:, qt * 512:(qt + 1) * 512])
                    x_st[(qt, j)] = (sm, c.val(sm))

    ps_row = pb[4][0:1, 0:512]
    ps_row2 = pb[5][0:1, 0:512]
    prm = dict(sq0=sq0, sq1=sq1, murow=murow, varrow=varrow, rstdrow=rstdrow,
               r2row=r2row, r1b=r1b, r2b=r2b, ones_col_f32=ones_col_f32,
               ones_row_f32=ones_row_f32, epscol=epscol, nhcol=nhcol,
               ps_row=ps_row, ps_row2=ps_row2, bcA=pb[2], bcB=pb[3])

    def apply_ln(v):
        for i in range(DT):
            for ch in range(NCH):
                cs = slice(ch * 512, (ch + 1) * 512)
                v.tensor_tensor(out=sq0[:, :], in0=xt(i)[:, cs], in1=r1b[:, cs], op=ALU.mult)
                v.tensor_tensor(out=sq1[:, :], in0=sq0[:, :], in1=r2b[:, cs], op=ALU.add)
                v.tensor_copy(out=xnbt(i)[:, cs], in_=sq1[:, :])

    emit_ln_feat(c, nc, xt, DT, T, prm, apply_ln)

    with nc.Block() as block:
        @block.sync
        def _(sync):
            c.wait(sync, "dvec")
            for i in range(DT):
                c.dma(sync, "st", xn3_out[i * 128:(i + 1) * 128, :], xnbt(i))

    with nc.Block() as block:
        @block.sync
        def _(sync):
            for sm in ["st", "sx0", "sx1"]:
                c.wait(sync, sm)

    c.close()
    return c


def build_l3a():
    c = Ctx("l3a")
    nc = c.nc

    xn3_in = c.din("xn3", [D, T], BF16)
    wk_in = c.din("wk_t", [D, D], F32)
    wv_in = c.din("wv_t", [D, D], F32)
    wg_in = c.din("wg_t", [D, D], F32)
    cos_in = c.din("cosrep", [T, 512], F32)
    sin_in = c.din("sinrep", [T, 512], F32)
    ql_in = c.din("qlat_t", [64, 1024], F32)
    tri_in = c.din("tri2", [128, 128], F32)
    bo_in = c.din("blockones", [128, 2], F32)
    boT_in = c.din("blockonesT", [2, 128], F32)
    eyeb_in = c.din("eyeb", [128, 128], BF16)

    gf_out = c.dout("gates_f", [D, T], BF16)
    lp_out = c.dout("lpart", [128, 512], F32)

    xnb = c.sb("xnb", [128, 8 * T], BF16)
    kfeat = c.sb("kfeat", [128, 8 * T], BF16)
    vtok = c.sb("vtok", [128, 16 * 1024], BF16)
    gatesf = c.sb("gatesf", [128, 8 * T], BF16)
    wbb = c.sb("wbb", [128, 8 * 1024], BF16)
    wst = c.sb("wst", [128, 2 * 1024], F32)
    ktk = c.sb("ktk", [128, 2 * 1024], BF16)
    csr = c.sb("csr", [128, 2 * 512], F32)
    snr = c.sb("snr", [128, 2 * 512], F32)
    rp1 = c.sb("rp1", [128, 512], F32)
    rp2 = c.sb("rp2", [128, 512], F32)
    rp3 = c.sb("rp3", [128, 512], F32)
    rp4 = c.sb("rp4", [128, 512], F32)
    qls = c.sb("qls", [64, 1024], F32)
    qlatb = c.sb("qlatb", [128, 1024], BF16)
    tri2 = c.sb("tri2_s", [128, 128], F32)
    blockones = c.sb("blockones_s", [128, 2], F32)
    blockonesT = c.sb("blockonesT_s", [2, 128], F32)
    eyeb = c.sb("eyeb_s", [128, 128], BF16)
    sclc = c.sb("sclc", [128, 1], F32)
    escr = c.sb("escr", [128, 512], F32)
    rec2 = c.sb("rec2", [2, 512], F32)
    msk = c.sb("msk", [128, 128], F32)
    escr2 = c.sb("escr2", [128, 128], F32)
    mx = c.sb("mx", [128, 1], F32)
    nmx = c.sb("nmx", [128, 1], F32)
    smr = c.sb("smr", [128, 1], F32)
    rc = c.sb("rc", [128, 1], F32)
    pbf = c.sb("pbf", [128, 128], BF16)
    pT = c.sb("pT", [128, 128], BF16)
    lpart = c.sb("lpart_s", [128, 512], F32)

    pb = [c.ps(f"pb{i}", [128, 512], F32) for i in range(6)]
    psTf = c.ps("psT", [128, 1024], BF16)
    psT2f = c.ps("psT2", [128, 1024], BF16)
    psT = psTf[:, 0:128]
    psT2 = psT2f[:, 0:128]

    for s in ["ld", "wl0", "wl1", "cl0", "cl1", "pe", "actc", "dvec", "st"]:
        c.sem(s)

    def xnbt(i):
        return xnb[:, i * T:(i + 1) * T]

    with nc.Block() as block:
        @block.sync
        def _(sync):
            for i in range(8):
                c.dma(sync, "ld", xnbt(i), xn3_in[i * 128:(i + 1) * 128, :])
            for dst, srcap in [(qls[0:64, :], ql_in), (tri2[:, :], tri_in),
                               (blockones[:, :], bo_in), (blockonesT[0:2, :], boT_in),
                               (eyeb[:, :], eyeb_in)]:
                c.dma(sync, "ld", dst, srcap)

        @block.gpsimd
        def _(gpsimd):
            gpsimd.memset(sclc[:, :], 0.125)

        @block.vector
        def _(vector):
            v = Chain(c, vector, "dvec")
            c.wait(vector, "ld")
            v.tensor_copy(out=qlatb[0:64, :], in_=qls[0:64, :])
            v.tensor_copy(out=qlatb[64:128, :], in_=qls[0:64, :])

    def stream_w(w_ap, tag, extra_wait=None):
        with nc.Block() as block:
            wc = {}
            for j in range(8):
                @block.sync
                def _(sync, j=j):
                    sm = "wl0" if j % 2 == 0 else "wl1"
                    if j >= 2:
                        sync.wait_ge(c.sems["dvec"][0], wc[j - 2])
                    c.dma(sync, sm, wst[:, (j % 2) * 1024:(j % 2) * 1024 + 1024],
                          w_ap[j * 128:(j + 1) * 128, :])
                    wc[("l", j)] = (sm, c.val(sm))

                @block.vector
                def _(vector, j=j):
                    v = Chain(c, vector, "dvec")
                    sm, tv = wc[("l", j)]
                    vector.wait_ge(c.sems[sm][0], tv)
                    if j == 0 and extra_wait is not None:
                        vector.wait_ge(c.sems["pe"][0], extra_wait())
                    v.tensor_copy(out=wbb[:, j * 1024:(j + 1) * 1024],
                                  in_=wst[:, (j % 2) * 1024:(j % 2) * 1024 + 1024])
                    wc[j] = c.val("dvec")
        return c.val("dvec")

    wk_done = stream_w(wk_in, "wk")
    with nc.Block() as block:
        k_pe = {}
        k_ev = {}
        rope_dv = {}
        tr_pe = {}
        kev = {}
        cs_ld = {}
        for tt in range(16):
            @block.sync
            def _(sync, tt=tt):
                sm = "cl0" if tt % 2 == 0 else "cl1"
                if tt >= 2:
                    sync.wait_ge(c.sems["dvec"][0], rope_dv[tt - 2])
                c.dma(sync, sm, csr[:, (tt % 2) * 512:(tt % 2) * 512 + 512],
                      cos_in[tt * 128:(tt + 1) * 128, :])
                c.dma(sync, sm, snr[:, (tt % 2) * 512:(tt % 2) * 512 + 512],
                      sin_in[tt * 128:(tt + 1) * 128, :])
                cs_ld[tt] = (sm, c.val(sm))

            for vc in range(2):
                @block.tensor
                def _(tensor, tt=tt, vc=vc):
                    g = tt * 2 + vc
                    if g == 0:
                        tensor.wait_ge(c.sems["dvec"][0], wk_done)
                    if g >= 2:
                        tensor.wait_ge(c.sems["actc"][0], k_ev[g - 2])
                    for k in range(8):
                        c.inc(tensor.matmul(
                            pb[g % 2][:, :],
                            xnbt(k)[:, tt * 128:(tt + 1) * 128],
                            wbb[:, k * 1024 + vc * 512:k * 1024 + (vc + 1) * 512],
                            start=(k == 0), stop=(k == 7)), "pe", 1)
                    k_pe[g] = c.val("pe")

                @block.scalar
                def _(scalar, tt=tt, vc=vc):
                    a = Chain(c, scalar, "actc")
                    g = tt * 2 + vc
                    scalar.wait_ge(c.sems["pe"][0], k_pe[g])
                    if tt >= 2 and vc == 0:
                        scalar.wait_ge(c.sems["pe"][0], tr_pe[tt - 2])
                    a.activation(out=ktk[:, (tt % 2) * 1024 + vc * 512:(tt % 2) * 1024 + (vc + 1) * 512],
                                 in_=pb[g % 2][:, :], func=AF.Copy)
                    k_ev[g] = c.val("actc")

            @block.vector
            def _(vector, tt=tt):
                v = Chain(c, vector, "dvec")
                vector.wait_ge(c.sems["actc"][0], k_ev[tt * 2 + 1])
                sm, tv = cs_ld[tt]
                vector.wait_ge(c.sems[sm][0], tv)
                kt = ktk[:, (tt % 2) * 1024:(tt % 2) * 1024 + 1024]
                a = kt[:, 0:1024:2]
                b = kt[:, 1:1024:2]
                cc = csr[:, (tt % 2) * 512:(tt % 2) * 512 + 512]
                ss = snr[:, (tt % 2) * 512:(tt % 2) * 512 + 512]
                v.tensor_tensor(out=rp1[:, :], in0=a, in1=cc, op=ALU.mult)
                v.tensor_tensor(out=rp2[:, :], in0=b, in1=ss, op=ALU.mult)
                v.tensor_tensor(out=rp3[:, :], in0=a, in1=ss, op=ALU.mult)
                v.tensor_tensor(out=rp4[:, :], in0=b, in1=cc, op=ALU.mult)
                v.tensor_tensor(out=a, in0=rp1[:, :], in1=rp2[:, :], op=ALU.subtract)
                v.tensor_tensor(out=b, in0=rp3[:, :], in1=rp4[:, :], op=ALU.add)
                rope_dv[tt] = c.val("dvec")

            for ht in range(8):
                @block.tensor
                def _(tensor, tt=tt, ht=ht):
                    if ht == 0:
                        tensor.wait_ge(c.sems["dvec"][0], rope_dv[tt])
                    gh = tt * 8 + ht
                    if gh >= 2:
                        tensor.wait_ge(c.sems["dvec"][0], kev[(gh - 2)])
                    kt = ktk[:, (tt % 2) * 1024:(tt % 2) * 1024 + 1024]
                    ps = psT if ht % 2 == 0 else psT2
                    c.inc(tensor.transpose(ps[:, :], kt[:, ht * 128:(ht + 1) * 128],
                                           eyeb[:, :]), "pe", 1)
                    tr_pe[(tt, ht)] = c.val("pe")
                    if ht == 7:
                        tr_pe[tt] = c.val("pe")

                @block.vector
                def _(vector, tt=tt, ht=ht):
                    v = Chain(c, vector, "dvec")
                    gh = tt * 8 + ht
                    ps = psT if ht % 2 == 0 else psT2
                    vector.wait_ge(c.sems["pe"][0], tr_pe[(tt, ht)])
                    v.tensor_copy(out=kfeat[:, ht * T + tt * 128:ht * T + (tt + 1) * 128],
                                  in_=ps[:, :])
                    kev[gh] = c.val("dvec")
                    if ht == 7:
                        kev[tt] = c.val("dvec")

    def last_k_pe():
        return tr_pe[15]
    wv_done = stream_w(wv_in, "wv", extra_wait=last_k_pe)
    with nc.Block() as block:
        v_pe = {}
        v_ev = {}
        for tt in range(16):
            for vc in range(2):
                @block.tensor
                def _(tensor, tt=tt, vc=vc):
                    g = tt * 2 + vc
                    if g == 0:
                        tensor.wait_ge(c.sems["dvec"][0], wv_done)
                    if g >= 2:
                        tensor.wait_ge(c.sems["actc"][0], v_ev[g - 2])
                    for k in range(8):
                        c.inc(tensor.matmul(
                            pb[g % 2][:, :],
                            xnbt(k)[:, tt * 128:(tt + 1) * 128],
                            wbb[:, k * 1024 + vc * 512:k * 1024 + (vc + 1) * 512],
                            start=(k == 0), stop=(k == 7)), "pe", 1)
                    v_pe[g] = c.val("pe")

                @block.scalar
                def _(scalar, tt=tt, vc=vc):
                    a = Chain(c, scalar, "actc")
                    g = tt * 2 + vc
                    scalar.wait_ge(c.sems["pe"][0], v_pe[g])
                    a.activation(out=vtok[:, tt * 1024 + vc * 512:tt * 1024 + (vc + 1) * 512],
                                 in_=pb[g % 2][:, :], func=AF.Copy)
                    v_ev[g] = c.val("actc")

    def last_v_pe():
        return v_pe[31]
    wg_done = stream_w(wg_in, "wg", extra_wait=last_v_pe)
    with nc.Block() as block:
        g_pe = {}
        g_ev = {}
        g_sum = {}
        g_dv = {}
        for gt in range(8):
            for ch in range(4):
                g = gt * 4 + ch
                @block.tensor
                def _(tensor, gt=gt, ch=ch, g=g):
                    if g == 0:
                        tensor.wait_ge(c.sems["dvec"][0], wg_done)
                    if g >= 2:
                        tensor.wait_ge(c.sems["actc"][0], g_ev[g - 2])
                    for k in range(8):
                        c.inc(tensor.matmul(
                            pb[g % 2][:, :],
                            wbb[:, k * 1024 + gt * 128:k * 1024 + (gt + 1) * 128],
                            xnbt(k)[:, ch * 512:(ch + 1) * 512],
                            start=(k == 0), stop=(k == 7)), "pe", 1)
                    g_pe[g] = c.val("pe")

                @block.scalar
                def _(scalar, g=g):
                    a = Chain(c, scalar, "actc")
                    scalar.wait_ge(c.sems["pe"][0], g_pe[g])
                    if g >= 1:
                        scalar.wait_ge(c.sems["dvec"][0], g_dv[g - 1])
                    a.activation(out=escr[:, :], in_=pb[g % 2][:, :], func=AF.Exp,
                                 scale=sclc[:, 0:1])
                    g_ev[g] = c.val("actc")

                @block.tensor
                def _(tensor, g=g):
                    tensor.wait_ge(c.sems["actc"][0], g_ev[g])
                    if g >= 1:
                        tensor.wait_ge(c.sems["dvec"][0], g_dv[g - 1])
                    c.inc(tensor.matmul(pb[2][0:2, :], blockones[:, :], escr[:, :],
                                        start=True, stop=True), "pe", 1)
                    g_sum[g] = c.val("pe")

                @block.vector
                def _(vector, g=g):
                    v = Chain(c, vector, "dvec")
                    vector.wait_ge(c.sems["pe"][0], g_sum[g])
                    v.reciprocal(out=rec2[0:2, :], in_=pb[2][0:2, :])
                    g_dv[("r", g)] = c.val("dvec")

                @block.tensor
                def _(tensor, g=g):
                    tensor.wait_ge(c.sems["dvec"][0], g_dv[("r", g)])
                    c.inc(tensor.matmul(pb[3][:, :], blockonesT[0:2, :], rec2[0:2, :],
                                        start=True, stop=True), "pe", 1)
                    g_sum[("b", g)] = c.val("pe")

                @block.vector
                def _(vector, gt=gt, ch=ch, g=g):
                    v = Chain(c, vector, "dvec")
                    vector.wait_ge(c.sems["pe"][0], g_sum[("b", g)])
                    v.tensor_tensor(out=gatesf[:, gt * T + ch * 512:gt * T + (ch + 1) * 512],
                                    in0=escr[:, :], in1=pb[3][:, :], op=ALU.mult)
                    g_dv[g] = c.val("dvec")

        @block.sync
        def _(sync):
            c.wait(sync, "dvec")
            for gt in range(8):
                c.dma(sync, "st", gf_out[gt * 128:(gt + 1) * 128, :],
                      gatesf[:, gt * T:(gt + 1) * T])

    with nc.Block() as block:
        l_pe = {}
        l_dv = {}
        l_ev = {}
        a_pe = {}
        for hp in range(8):
            for n in range(16):
                gi = hp * 16 + n
                @block.tensor
                def _(tensor, hp=hp, n=n, gi=gi):
                    if gi >= 2:
                        tensor.wait_ge(c.sems["dvec"][0], l_dv[("m", gi - 2)])
                    if gi == 0:
                        tensor.wait_ge(c.sems["pe"][0], g_sum[("b", 31)])
                    c.inc(tensor.matmul(pb[4 + gi % 2][0:64, 0:128],
                                        qlatb[0:64, (2 * hp) * 64:(2 * hp) * 64 + 64],
                                        kfeat[0:64, hp * T + n * 128:hp * T + (n + 1) * 128],
                                        start=True, stop=True, tile_position=(0, 0)), "pe", 1)
                    c.inc(tensor.matmul(pb[4 + gi % 2][64:128, 0:128],
                                        qlatb[64:128, (2 * hp + 1) * 64:(2 * hp + 1) * 64 + 64],
                                        kfeat[64:128, hp * T + n * 128:hp * T + (n + 1) * 128],
                                        start=True, stop=True, tile_position=(64, 64)), "pe", 1)
                    l_pe[gi] = c.val("pe")

                @block.vector
                def _(vector, gi=gi):
                    v = Chain(c, vector, "dvec")
                    vector.wait_ge(c.sems["pe"][0], l_pe[gi])
                    v.tensor_tensor(out=msk[:, :], in0=pb[4 + gi % 2][:, 0:128],
                                    in1=tri2[:, :], op=ALU.add)
                    v.reduce_max(out=mx[:, 0:1], in_=msk[:, :], axis=AX.X)
                    v.tensor_scalar(out=nmx[:, 0:1], in0=mx[:, 0:1], scalar1=-0.125,
                                    scalar2=None, op0=ALU.mult)
                    l_dv[("m", gi)] = c.val("dvec")

                @block.scalar
                def _(scalar, gi=gi):
                    a = Chain(c, scalar, "actc")
                    scalar.wait_ge(c.sems["dvec"][0], l_dv[("m", gi)])
                    a.activation(out=escr2[:, :], in_=msk[:, :], func=AF.Exp,
                                 scale=sclc[:, 0:1], bias=nmx[:, 0:1])
                    l_ev[gi] = c.val("actc")

                @block.vector
                def _(vector, gi=gi):
                    v = Chain(c, vector, "dvec")
                    vector.wait_ge(c.sems["actc"][0], l_ev[gi])
                    if gi >= 1:
                        vector.wait_ge(c.sems["pe"][0], a_pe.get(("t", gi - 1), 0))
                    v.reduce_sum(out=smr[:, 0:1], in_=escr2[:, :], axis=AX.X)
                    v.reciprocal(out=rc[:, 0:1], in_=smr[:, 0:1])
                    v.tensor_scalar(out=pbf[:, :], in0=escr2[:, :], scalar1=rc[:, 0:1],
                                    scalar2=None, op0=ALU.mult)
                    l_dv[("p", gi)] = c.val("dvec")

                @block.tensor
                def _(tensor, gi=gi):
                    tensor.wait_ge(c.sems["dvec"][0], l_dv[("p", gi)])
                    c.inc(tensor.transpose(psT[:, :], pbf[:, :], eyeb[:, :]), "pe", 1)
                    a_pe[("t", gi)] = c.val("pe")

                @block.vector
                def _(vector, gi=gi):
                    v = Chain(c, vector, "dvec")
                    vector.wait_ge(c.sems["pe"][0], a_pe[("t", gi)])
                    if gi >= 1:
                        vector.wait_ge(c.sems["pe"][0], a_pe.get(gi - 1, 0))
                    v.tensor_copy(out=pT[:, :], in_=psT[:, :])
                    l_dv[("pt", gi)] = c.val("dvec")

                @block.tensor
                def _(tensor, hp=hp, n=n, gi=gi):
                    tensor.wait_ge(c.sems["dvec"][0], l_dv[("pt", gi)])
                    if n == 0 and hp >= 2:
                        tensor.wait_ge(c.sems["dvec"][0], l_dv[("ev", hp - 2)])
                    c.inc(tensor.matmul(pb[hp % 2][0:64, 0:64], pT[:, 0:64],
                                        vtok[:, n * 1024 + (2 * hp) * 64:n * 1024 + (2 * hp) * 64 + 64],
                                        start=(n == 0), stop=(n == 15),
                                        tile_position=(0, 0)), "pe", 1)
                    c.inc(tensor.matmul(pb[hp % 2][64:128, 0:64], pT[:, 64:128],
                                        vtok[:, n * 1024 + (2 * hp + 1) * 64:n * 1024 + (2 * hp + 1) * 64 + 64],
                                        start=(n == 0), stop=(n == 15),
                                        tile_position=(0, 64)), "pe", 1)
                    a_pe[gi] = c.val("pe")

            @block.vector
            def _(vector, hp=hp):
                v = Chain(c, vector, "dvec")
                vector.wait_ge(c.sems["pe"][0], a_pe[hp * 16 + 15])
                v.tensor_copy(out=lpart[:, hp * 64:(hp + 1) * 64], in_=pb[hp % 2][0:128, 0:64])
                l_dv[("ev", hp)] = c.val("dvec")

        @block.sync
        def _(sync):
            c.wait(sync, "dvec")
            c.dma(sync, "st", lp_out, lpart[:, :])

    with nc.Block() as block:
        @block.sync
        def _(sync):
            c.wait(sync, "st")

    c.close()
    return c


def build_l3b():
    c = Ctx("l3b")
    nc = c.nc

    x3_in = c.din("x3", [D, T], F32)
    xn3_in = c.din("xn3", [D, T], BF16)
    gf_in = c.din("gates_f", [D, T], BF16)
    lp_in = c.din("lparts", [128, 2048], F32)
    wp_in = c.din("wp_t", [D, D], F32)
    w1_in = c.din("w1qt", [D, 4096], BF16)
    b1_in = c.din("b1_col", [128, 32], F32)
    w2_in = c.din("w2qt", [4096, D], BF16)
    b2_in = c.din("b2_col", [128, 8], F32)

    y_out = c.dout("y", [D, T], F32)

    xs = c.sb("xs", [128, 8 * T], F32)
    xnb = c.sb("xnb", [128, 8 * T], BF16)
    gf = c.sb("gf", [128, 8 * T], BF16)
    wpb = c.sb("wpb", [128, 8 * 1024], BF16)
    s3q = c.sb("s3q", [128, 8 * 512], F32)
    u3q = c.sb("u3q", [128, 32 * 512], BF16)
    attnb = c.sb("attnb", [128, 512], BF16)
    wr = c.sb("wr", [128, 2 * 1024], BF16)
    w1r = wr
    w2r = wr
    g3t = c.sb("g3t", [128, 512], F32)
    sq0 = c.sb("sq0", [128, 512], F32)
    b1c = c.sb("b1c", [128, 32], F32)
    b2c = c.sb("b2c", [128, 8], F32)
    wtq = u3q[:, 0:4096]
    lps = s3q[:, 0:2048]
    wst = s3q[:, 2048:4096]
    attns = g3t[:, 0:512]

    pb = [c.ps(f"pb{i}", [128, 512], F32) for i in range(8)]

    for s in ["ld", "wl0", "wl1", "w10", "w11", "w20", "w21", "pe", "actc", "dvec",
              "sy0", "sy1"]:
        c.sem(s)

    def xt(i):
        return xs[:, i * T:(i + 1) * T]

    def xnbt(i):
        return xnb[:, i * T:(i + 1) * T]

    with nc.Block() as block:
        @block.sync
        def _(sync):
            for i in range(8):
                c.dma(sync, "ld", xt(i), x3_in[i * 128:(i + 1) * 128, :])
                c.dma(sync, "ld", xnbt(i), xn3_in[i * 128:(i + 1) * 128, :])
                c.dma(sync, "ld", gf[:, i * T:(i + 1) * T], gf_in[i * 128:(i + 1) * 128, :])
            c.dma(sync, "ld", lps[:, :], lp_in)
            c.dma(sync, "ld", b1c[:, :], b1_in)
            c.dma(sync, "ld", b2c[:, :], b2_in)

        @block.vector
        def _(vector):
            v = Chain(c, vector, "dvec")
            c.wait(vector, "ld")
            v.tensor_tensor(out=attns[:, :], in0=lps[:, 0:512], in1=lps[:, 512:1024],
                            op=ALU.add)
            v.tensor_tensor(out=attns[:, :], in0=attns[:, :], in1=lps[:, 1024:1536],
                            op=ALU.add)
            v.tensor_tensor(out=attns[:, :], in0=attns[:, :], in1=lps[:, 1536:2048],
                            op=ALU.add)
            v.tensor_copy(out=attnb[:, :], in_=attns[:, :])
    attn_done = c.val("dvec")

    with nc.Block() as block:
        wc = {}
        for j in range(8):
            @block.sync
            def _(sync, j=j):
                sm = "wl0" if j % 2 == 0 else "wl1"
                if j >= 2:
                    sync.wait_ge(c.sems["dvec"][0], wc[j - 2])
                c.dma(sync, sm, wst[:, (j % 2) * 1024:(j % 2) * 1024 + 1024],
                      wp_in[j * 128:(j + 1) * 128, :])
                wc[("l", j)] = (sm, c.val(sm))

            @block.vector
            def _(vector, j=j):
                v = Chain(c, vector, "dvec")
                sm, tv = wc[("l", j)]
                vector.wait_ge(c.sems[sm][0], tv)
                v.tensor_copy(out=wpb[:, j * 1024:(j + 1) * 1024],
                              in_=wst[:, (j % 2) * 1024:(j % 2) * 1024 + 1024])
                wc[j] = c.val("dvec")
    wp_done = c.val("dvec")

    with nc.Block() as block:
        w_pe = {}
        w_ev = {}
        s_pe = {}
        s_ev = {}
        u_pe = {}
        u_ev = {}
        d_pe = {}
        d_ev = {}
        x_dv = {}
        y_st = {}
        w1l = {}
        w2l = {}
        for qt in range(4):
            for ft in range(8):
                g = qt * 8 + ft
                @block.tensor
                def _(tensor, qt=qt, ft=ft, g=g):
                    if g == 0:
                        tensor.wait_ge(c.sems["dvec"][0], attn_done)
                    if qt >= 1 and ft < 2:
                        tensor.wait_ge(c.sems["actc"][0], d_ev[(qt - 1, ft)])
                    if g >= 2:
                        tensor.wait_ge(c.sems["actc"][0], w_ev[g - 2])
                    cs = slice(qt * 512, (qt + 1) * 512)
                    c.inc(tensor.matmul(pb[g % 2][0:64, 0:512],
                                        attnb[0:64, ft * 64:(ft + 1) * 64],
                                        gf[0:64, ft * T + qt * 512:ft * T + (qt + 1) * 512],
                                        start=True, stop=True, tile_position=(0, 0)), "pe", 1)
                    c.inc(tensor.matmul(pb[g % 2][64:128, 0:512],
                                        attnb[64:128, ft * 64:(ft + 1) * 64],
                                        gf[64:128, ft * T + qt * 512:ft * T + (qt + 1) * 512],
                                        start=True, stop=True, tile_position=(64, 64)), "pe", 1)
                    w_pe[g] = c.val("pe")

                @block.scalar
                def _(scalar, qt=qt, ft=ft, g=g):
                    a = Chain(c, scalar, "actc")
                    scalar.wait_ge(c.sems["pe"][0], w_pe[g])
                    if qt >= 1 and ft == 0:
                        scalar.wait_ge(c.sems["pe"][0], d_pe[(qt - 1, 31)])
                    a.activation(out=wtq[:, ft * 512:(ft + 1) * 512], in_=pb[g % 2][:, :],
                                 func=AF.Copy)
                    w_ev[g] = c.val("actc")

            for j in range(8):
                @block.tensor
                def _(tensor, qt=qt, j=j):
                    if j == 0:
                        tensor.wait_ge(c.sems["actc"][0], w_ev[qt * 8 + 7])
                        if qt == 0:
                            tensor.wait_ge(c.sems["dvec"][0], wp_done)
                    if qt >= 1:
                        tensor.wait_ge(c.sems["actc"][0], s_ev[(qt - 1, j)])
                    for k in range(8):
                        c.inc(tensor.matmul(
                            pb[j][:, :], wpb[:, k * 1024 + j * 128:k * 1024 + (j + 1) * 128],
                            wtq[:, k * 512:(k + 1) * 512],
                            start=(k == 0), stop=(k == 7)), "pe", 1)
                    s_pe[(qt, j)] = c.val("pe")

                @block.scalar
                def _(scalar, qt=qt, j=j):
                    a = Chain(c, scalar, "actc")
                    scalar.wait_ge(c.sems["pe"][0], s_pe[(qt, j)])
                    if qt >= 1:
                        scalar.wait_ge(c.sems["dvec"][0], x_dv[(qt - 1, j)])
                    a.activation(out=s3q[:, j * 512:(j + 1) * 512], in_=pb[j][:, :],
                                 func=AF.Copy)
                    s_ev[(qt, j)] = c.val("actc")

            for ft in range(32):
                g = qt * 32 + ft
                @block.sync
                def _(sync, qt=qt, ft=ft, g=g):
                    sm = "w10" if g % 2 == 0 else "w11"
                    if ft < 2 and qt >= 1:
                        sync.wait_ge(c.sems["pe"][0], d_pe[(qt - 1, 30 + ft)])
                    if g >= 2:
                        sync.wait_ge(c.sems["pe"][0], u_pe[g - 2])
                    c.dma(sync, sm, w1r[:, (g % 2) * 1024:(g % 2) * 1024 + 1024].rearrange(
                        "p (k m) -> p k m", k=8),
                        w1_in[:, ft * 128:(ft + 1) * 128].rearrange("(k p) m -> p k m", p=128))
                    w1l[g] = (sm, c.val(sm))

                @block.tensor
                def _(tensor, qt=qt, ft=ft, g=g):
                    sm, tv = w1l[g]
                    tensor.wait_ge(c.sems[sm][0], tv)
                    if ft == 0:
                        tensor.wait_ge(c.sems["actc"][0], s_ev[(qt, 7)])
                    if g >= 2:
                        tensor.wait_ge(c.sems["actc"][0], u_ev[g - 2])
                    if qt >= 1 and ft == 0:
                        tensor.wait_ge(c.sems["pe"][0], d_pe[(qt - 1, 31)])
                    for k in range(8):
                        c.inc(tensor.matmul(
                            pb[g % 2][:, :],
                            w1r[:, (g % 2) * 1024 + k * 128:(g % 2) * 1024 + (k + 1) * 128],
                            xnbt(k)[:, qt * 512:(qt + 1) * 512],
                            start=(k == 0), stop=(k == 7)), "pe", 1)
                    u_pe[g] = c.val("pe")

                @block.scalar
                def _(scalar, ft=ft, g=g):
                    a = Chain(c, scalar, "actc")
                    scalar.wait_ge(c.sems["pe"][0], u_pe[g])
                    a.activation(out=u3q[:, ft * 512:(ft + 1) * 512], in_=pb[g % 2][:, :],
                                 func=AF.Relu, bias=b1c[:, ft:ft + 1])
                    u_ev[g] = c.val("actc")

            for k in range(32):
                g = qt * 32 + k
                @block.sync
                def _(sync, k=k, g=g, qt=qt):
                    sm = "w20" if g % 2 == 0 else "w21"
                    if k < 2:
                        sync.wait_ge(c.sems["pe"][0], u_pe[qt * 32 + 30 + k])
                    else:
                        sync.wait_ge(c.sems["pe"][0], d_pe[(qt, k - 2)])
                    c.dma(sync, sm, w2r[:, (g % 2) * 1024:(g % 2) * 1024 + 1024],
                          w2_in[k * 128:(k + 1) * 128, :])
                    w2l[g] = (sm, c.val(sm))

                @block.tensor
                def _(tensor, qt=qt, k=k, g=g):
                    sm, tv = w2l[g]
                    tensor.wait_ge(c.sems[sm][0], tv)
                    if k == 0:
                        tensor.wait_ge(c.sems["actc"][0], u_ev[qt * 32 + 31])
                        if qt >= 1:
                            tensor.wait_ge(c.sems["dvec"][0], x_dv[(qt - 1, 7)])
                    for j in range(8):
                        c.inc(tensor.matmul(
                            pb[j][:, :],
                            w2r[:, (g % 2) * 1024 + j * 128:(g % 2) * 1024 + (j + 1) * 128],
                            u3q[:, k * 512:(k + 1) * 512],
                            start=(k == 0), stop=(k == 31)), "pe", 1)
                    d_pe[(qt, k)] = c.val("pe")

            for j in range(8):
                @block.scalar
                def _(scalar, qt=qt, j=j):
                    a = Chain(c, scalar, "actc")
                    scalar.wait_ge(c.sems["pe"][0], d_pe[(qt, 31)])
                    if j >= 1:
                        scalar.wait_ge(c.sems["dvec"][0], x_dv[(qt, j - 1)])
                    a.activation(out=g3t[:, :], in_=pb[j][:, :],
                                 func=AF.Silu, bias=b2c[:, j:j + 1])
                    d_ev[(qt, j)] = c.val("actc")

                @block.vector
                def _(vector, qt=qt, j=j):
                    v = Chain(c, vector, "dvec")
                    vector.wait_ge(c.sems["actc"][0], d_ev[(qt, j)])
                    if qt >= 2:
                        sm, tv = y_st[(qt - 2, j)]
                        vector.wait_ge(c.sems[sm][0], tv)
                    cs = slice(qt * 512, (qt + 1) * 512)
                    v.tensor_tensor(out=sq0[:, :], in0=g3t[:, :],
                                    in1=s3q[:, j * 512:(j + 1) * 512], op=ALU.mult)
                    v.tensor_tensor(out=xt(j)[:, cs], in0=xt(j)[:, cs], in1=sq0[:, :],
                                    op=ALU.add)
                    x_dv[(qt, j)] = c.val("dvec")

                @block.sync
                def _(sync, qt=qt, j=j):
                    sync.wait_ge(c.sems["dvec"][0], x_dv[(qt, j)])
                    sm = "sy0" if qt % 2 == 0 else "sy1"
                    c.dma(sync, sm, y_out[j * 128:(j + 1) * 128, qt * 512:(qt + 1) * 512],
                          xt(j)[:, qt * 512:(qt + 1) * 512])
                    y_st[(qt, j)] = (sm, c.val(sm))

    with nc.Block() as block:
        @block.sync
        def _(sync):
            c.wait(sync, "sy0")
            c.wait(sync, "sy1")

    c.close()
    return c


_CACHE = {}


def _get(name, builder):
    if name not in _CACHE:
        _CACHE[name] = builder()
    return _CACHE[name]


def run_prep(inputs):
    c = _get("prep", build_prep)
    wg = to_np(inputs["L1_ffn_w_gate"], NB_F32)
    wu = to_np(inputs["L1_ffn_w_up"], NB_F32)
    wd = to_np(inputs["L1_ffn_w_down"], NB_F32)
    w1 = to_np(inputs["L3_ffn_W1"], NB_F32)
    w2 = to_np(inputs["L3_ffn_W2"], NB_F32)
    w1t = np_t(w1)
    w2t = np_t(w2)

    wg_pad = np.zeros((8 * 384, 1024), NB_F32)
    wg_pad[0:2816] = wg
    wu_pad = np.zeros((8 * 384, 1024), NB_F32)
    wu_pad[0:2816] = wu

    in_maps = []
    for core in range(NCORES):
        in_maps.append({
            "wg_sl": wg_pad[core * 384:(core + 1) * 384],
            "wu_sl": wu_pad[core * 384:(core + 1) * 384],
            "wd_sl": wd[core * 128:(core + 1) * 128],
            "w1t": w1t, "w2t": w2t,
            "w1t_sl": w1t[core * 128:(core + 1) * 128],
            "w2t_sl": w2t[core * 512:(core + 1) * 512],
        })
    res = run_bass_kernel_spmd(c.nc, in_maps, core_ids=list(range(NCORES)))
    r = res.results
    wgq = np.concatenate([to_np(r[i]["wgq"], NB_BF16) for i in range(8)], axis=0)[0:2816]
    wuq = np.concatenate([to_np(r[i]["wuq"], NB_BF16) for i in range(8)], axis=0)[0:2816]
    wdq = np.concatenate([to_np(r[i]["wdq"], NB_BF16) for i in range(8)], axis=0)
    w1q = np.concatenate([to_np(r[i]["w1q_sl"], NB_BF16) for i in range(8)], axis=0)
    w2q = np.concatenate([to_np(r[i]["w2q_sl"], NB_BF16) for i in range(8)], axis=0)
    return {"wgq": wgq, "wuq": wuq, "wdq": wdq, "w1qt": w1q, "w2qt": w2q}


def _core_tokens(core):
    b = core // 4
    t0 = (core % 4) * T
    return b, t0


def run_l0(inputs, x_np):
    c = _get("l0", build_l0)
    wqkv = to_np(inputs["L0_attn_w_qkv"], NB_F32).reshape(1024, 3072)
    bq = to_np(inputs["L0_attn_b_qkv"], NB_F32).reshape(3072)
    bq_col = np.ascontiguousarray(bq.reshape(24, 128).T)
    bv_row = np.ascontiguousarray(bq[2048:3072].reshape(1, 1024))
    lng = to_np(inputs["L0_ffn_ln_gamma"], NB_F32)
    lnb = to_np(inputs["L0_ffn_ln_beta"], NB_F32)
    lng_col = np.ascontiguousarray(lng.reshape(8, 128).T)
    lnb_col = np.ascontiguousarray(lnb.reshape(8, 128).T)
    rw = to_np(inputs["L0_ffn_router_w"], NB_F32)
    rw_col = np.ascontiguousarray(rw.T.reshape(8, 128, 8).transpose(1, 0, 2).reshape(128, 64))
    eye = np.eye(128, dtype=NB_F32)
    gup_t = np.ascontiguousarray(np.transpose(to_np(inputs["L0_ffn_gate_up_w"], NB_F32), (0, 2, 1)))
    down_t = np.ascontiguousarray(np.transpose(to_np(inputs["L0_ffn_down_w"], NB_F32), (0, 2, 1)))
    sgup_t = np_t(to_np(inputs["L0_ffn_shared_gate_up_w"], NB_F32))
    sdwn_t = np_t(to_np(inputs["L0_ffn_shared_down_w"], NB_F32))

    in_maps = []
    for core in range(NCORES):
        b, t0 = _core_tokens(core)
        in_maps.append({
            "x_feat": np.ascontiguousarray(x_np[b, t0:t0 + T].T),
            "wqkv": wqkv, "bqkv_col": bq_col, "bv_row": bv_row,
            "lng_col": lng_col, "lnb_col": lnb_col, "rw_col": rw_col, "eye": eye,
            "gup_t": gup_t, "down_t": down_t, "sgup_t": sgup_t, "sdwn_t": sdwn_t,
        })
    res = run_bass_kernel_spmd(c.nc, in_maps, core_ids=list(range(NCORES)))
    return res.results


def _head_of_core(core, pair, half):
    flat = 4 * core + 2 * pair + half
    return flat // 16, flat % 16


def run_attn(q_cores, k_cores, v_cores):
    c = _get("attn", build_attn)
    tri = np.where(np.arange(128)[None, :] < np.arange(128)[:, None], -1e9, 0.0).astype(NB_F32)
    eye = np.eye(128, dtype=NB_F32)

    def q_bh(b, h):
        return np.concatenate([q_cores[4 * b + i][64 * h:64 * h + 64, :] for i in range(4)], axis=1)

    def k_bh(b, h):
        return np.concatenate([k_cores[4 * b + i][64 * h:64 * h + 64, :] for i in range(4)], axis=1)

    def v_bh(b, h):
        return np.concatenate([v_cores[4 * b + i][:, 64 * h:64 * h + 64] for i in range(4)], axis=0)

    in_maps = []
    for core in range(NCORES):
        qh = np.empty((2, 128, S), NB_BF16)
        kh = np.empty((2, 128, S), NB_BF16)
        vh = np.empty((2, S, 128), NB_BF16)
        for pair in range(2):
            for half in range(2):
                b, h = _head_of_core(core, pair, half)
                qh[pair, 64 * half:64 * half + 64] = q_bh(b, h)
                kh[pair, 64 * half:64 * half + 64] = k_bh(b, h)
                vh[pair, :, 64 * half:64 * half + 64] = v_bh(b, h)
        in_maps.append({"qh": qh, "kh": kh, "vh": vh, "tri": tri, "eyef": eye})
    res = run_bass_kernel_spmd(c.nc, in_maps, core_ids=list(range(NCORES)))
    ans = {}
    for core in range(NCORES):
        a = to_np(res.results[core]["ansh"], NB_BF16)
        for pair in range(2):
            for half in range(2):
                b, h = _head_of_core(core, pair, half)
                ans[(b, h)] = a[pair, 64 * half:64 * half + 64, :]
    return ans


def run_l1(inputs, x_np, ans0, g0_cores, wq):
    c = _get("l1", build_l1)
    wo0 = to_np(inputs["L0_attn_w_out"], NB_F32).reshape(1024, 1024)
    bo0 = to_np(inputs["L0_attn_b_out"], NB_F32)
    bo0_col = np.ascontiguousarray(bo0.reshape(8, 128).T)
    wqkv1 = np.ascontiguousarray(np.concatenate(
        [to_np(inputs["L1_attn_Wq"], NB_F32), to_np(inputs["L1_attn_Wk"], NB_F32),
         to_np(inputs["L1_attn_Wv"], NB_F32)], axis=1))
    bqk = np.concatenate([to_np(inputs["L1_attn_bq"], NB_F32),
                          to_np(inputs["L1_attn_bk"], NB_F32)])
    bqk1_col = np.ascontiguousarray(bqk.reshape(16, 128).T)
    bv1_row = np.ascontiguousarray(to_np(inputs["L1_attn_bv"], NB_F32).reshape(1, 1024))
    rep = lambda a, n: np.ascontiguousarray(np.broadcast_to(np.asarray(a, NB_F32).reshape(1, -1), (128, n)))
    com = {
        "wo0": wo0, "bo0_col": bo0_col, "wqkv1": wqkv1, "bqk1_col": bqk1_col,
        "bv1_row": bv1_row,
        "dpw1": to_np(inputs["L1_attn_dp_w1"], NB_F32),
        "dpb1_row": np.ascontiguousarray(to_np(inputs["L1_attn_dp_b1"], NB_F32).reshape(1, 512)),
        "lng1_rep": rep(inputs["L1_attn_ln_g"], 512),
        "lnb1_rep": rep(inputs["L1_attn_ln_b"], 512),
        "dpw2_rep": rep(np.asarray(inputs["L1_attn_dp_w2"])[:, 0], 512),
        "dpb2_rep": rep(np.asarray(inputs["L1_attn_dp_b2"]), 1),
        "gw1_rep": rep(np.asarray(inputs["L1_attn_g_w1"])[0], 16),
        "gb1_rep": rep(np.asarray(inputs["L1_attn_g_b1"]), 16),
        "gw2_rep": rep(np.asarray(inputs["L1_attn_g_w2"])[:, 0], 16),
        "gb2_rep": rep(np.asarray(inputs["L1_attn_g_b2"]), 1),
        "wgq_t": np_t(wq["wgq"]), "wuq_t": np_t(wq["wuq"]), "wdq_t": np_t(wq["wdq"]),
        "eyef": np.eye(128, dtype=NB_F32),
        "eyeb": np.eye(128, dtype=NB_F32).astype(NB_BF16),
    }
    in_maps = []
    for core in range(NCORES):
        b, t0 = _core_tokens(core)
        a0 = np.empty((1024, T), NB_BF16)
        for h in range(16):
            a0[h * 64:(h + 1) * 64] = ans0[(b, h)][:, t0:t0 + T]
        m = dict(com)
        m["x_feat"] = np.ascontiguousarray(x_np[b, t0:t0 + T].T)
        m["ans0"] = a0
        m["g0"] = g0_cores[core]
        in_maps.append(m)
    res = run_bass_kernel_spmd(c.nc, in_maps, core_ids=list(range(NCORES)))
    return res.results


def kernel(**inputs):
    x = np.asarray(inputs["L0_attn_x"], np.float32)
    wq = run_prep(inputs)
    l0 = run_l0(inputs, x)
    q0 = [to_np(l0[i]["q0"], NB_BF16) for i in range(8)]
    k0 = [to_np(l0[i]["k0"], NB_BF16) for i in range(8)]
    v0 = [to_np(l0[i]["v0"], NB_BF16) for i in range(8)]
    g0 = [to_np(l0[i]["g0"], NB_F32) for i in range(8)]
    ans0 = run_attn(q0, k0, v0)
    l1 = run_l1(inputs, x, ans0, g0, wq)
    q1 = [to_np(l1[i]["q1"], NB_BF16) for i in range(8)]
    k1 = [to_np(l1[i]["k1"], NB_BF16) for i in range(8)]
    v1 = [to_np(l1[i]["v1g"], NB_BF16) for i in range(8)]
    x1 = [to_np(l1[i]["x1"], NB_F32) for i in range(8)]
    g1 = [to_np(l1[i]["g1"], NB_F32) for i in range(8)]
    ans1 = run_attn(q1, k1, v1)
    e = run_l2a(inputs, x1, ans1, g1)
    e_out = [{k: np.asarray(to_np(e[i][k],
                 NB_BF16 if k in ("xn2", "xconv", "xssm") else NB_F32))
              for k in ["x2", "xn2", "xconv", "xssm", "mloc", "aprod", "pedge", "hsum"]}
             for i in range(8)]
    f = run_l2b(inputs, e_out)
    mw = [{"mw1q": to_np(f[i]["mw1q"], NB_BF16), "mw2q": to_np(f[i]["mw2q"], NB_BF16)}
          for i in range(8)]
    mb = [{"mb1": to_np(f[i]["mb1"], NB_F32), "mb2": to_np(f[i]["mb2"], NB_F32)}
          for i in range(8)]
    s2 = [to_np(f[i]["s2"], NB_F32) for i in range(8)]
    x2 = [e_out[i]["x2"] for i in range(8)]
    xn2 = [e_out[i]["xn2"] for i in range(8)]
    g = run_l3g(inputs, x2, s2, xn2, mw, mb)
    x3 = [to_np(g[i]["x3"], NB_F32) for i in range(8)]
    xn3 = [to_np(g[i]["xn3"], NB_BF16) for i in range(8)]
    a3 = run_l3a(inputs, xn3)
    gfc = [to_np(a3[i]["gates_f"], NB_BF16) for i in range(8)]
    lpc = [to_np(a3[i]["lpart"], NB_F32) for i in range(8)]
    yres = run_l3b(inputs, x3, xn3, gfc, lpc, wq)
    out = np.empty((B, S, D), np.float32)
    for core in range(8):
        b, t0 = _core_tokens(core)
        out[b, t0:t0 + T, :] = to_np(yres[core]["y"], NB_F32).T
    return out


def run_l2a(inputs, x1_cores, ans1, g1_cores):
    c = _get("l2a", build_l2a)
    wo1 = np.ascontiguousarray(np.asarray(inputs["L1_attn_Wo"], np.float32).reshape(D, D))
    bo1_col = np.ascontiguousarray(np.asarray(inputs["L1_attn_bo"], np.float32).reshape(8, 128).T)
    win_t = np.ascontiguousarray(np.asarray(inputs["L2_attn_W_in"], np.float32).T)
    wdt_row = np.ascontiguousarray(np.asarray(inputs["L2_attn_W_dt"], np.float32).reshape(1, 2048))
    A_ = np.ascontiguousarray(np.asarray(inputs["L2_attn_A"], np.float32))
    Bm_ = np.ascontiguousarray(np.asarray(inputs["L2_attn_Bm"], np.float32))
    cw = np.asarray(inputs["L2_attn_conv_w"], np.float32)
    convw_col = np.zeros((128, 48), np.float32)
    for j in range(3):
        for m in range(16):
            convw_col[:, j * 16 + m] = cw[m * 128:(m + 1) * 128, 0, j]
    gw1_t = np.ascontiguousarray(np.asarray(inputs["L2_ffn_gate_w1"], np.float32).T)
    gb1_col = np.ascontiguousarray(np.asarray(inputs["L2_ffn_gate_b1"], np.float32).reshape(8, 128).T)

    in_maps = []
    for core in range(NCORES):
        b, t0 = _core_tokens(core)
        a1 = np.empty((D, T), NB_BF16)
        for h in range(16):
            a1[h * 64:(h + 1) * 64] = ans1[(b, h)][:, t0:t0 + T]
        in_maps.append(dict(x1_feat=x1_cores[core], ans1=a1, wo1=wo1, bo1_col=bo1_col,
                            g1=g1_cores[core], win_t=win_t, wdt_row=wdt_row, A_=A_, Bm_=Bm_,
                            convw_col=convw_col, gw1_t=gw1_t, gb1_col=gb1_col))
    res = run_bass_kernel_spmd(c.nc, in_maps, core_ids=list(range(NCORES)))
    return res.results


def run_l2b(inputs, e_out):
    c = _get("l2b", build_l2b)
    cw = np.asarray(inputs["L2_attn_conv_w"], np.float32)
    convw_col = np.zeros((128, 48), np.float32)
    for j in range(3):
        for m in range(16):
            convw_col[:, j * 16 + m] = cw[m * 128:(m + 1) * 128, 0, j]
    Cm_ = np.ascontiguousarray(np.asarray(inputs["L2_attn_Cm"], np.float32))
    dv = np.asarray(inputs["L2_attn_Dv"], np.float32)
    dv_col = np.ascontiguousarray(dv.reshape(16, 128).T)
    wout_t = np.ascontiguousarray(np.asarray(inputs["L2_attn_W_out"], np.float32).T)
    gw2t = np.asarray(inputs["L2_ffn_gate_w2"], np.float32).T
    gw2_col = np.ascontiguousarray(gw2t.reshape(8, 128, 8).transpose(1, 0, 2).reshape(128, 64))
    gb2_row = np.asarray(inputs["L2_ffn_gate_b2"], np.float32).reshape(1, 8)
    Tb1_ = np.ascontiguousarray(np.asarray(inputs["L2_ffn_Tb1"], np.float32))
    b1_row = np.asarray(inputs["L2_ffn_b1"], np.float32).reshape(1, 4096)
    Tb2_ = np.ascontiguousarray(np.asarray(inputs["L2_ffn_Tb2"], np.float32))
    b2_row = np.asarray(inputs["L2_ffn_b2"], np.float32).reshape(1, 1024)
    TW1 = np.asarray(inputs["L2_ffn_TW1"], np.float32)
    TW2 = np.asarray(inputs["L2_ffn_TW2"], np.float32)
    W1 = np.asarray(inputs["L2_ffn_W1"], np.float32)
    W2 = np.asarray(inputs["L2_ffn_W2"], np.float32)
    eyeb = np.eye(128, dtype=NB_F32).astype(NB_BF16)

    in_maps = []
    for core in range(NCORES):
        gb, qi = divmod(core, 4)
        me3 = np.zeros((16, 3), np.float32)
        ae3 = np.zeros((16, 3), np.float32)
        ae3[:, :] = 1.0
        me3[:, :] = 0.0
        preds = list(range(gb * 4, core))[-3:]
        off = 3 - len(preds)
        for j, p in enumerate(preds):
            me3[:, off + j] = e_out[p]["mloc"][:, -1]
            ae3[:, off + j] = e_out[p]["aprod"][:, -1]
        halo = np.zeros((2048, 2), np.float32)
        if qi > 0:
            halo[:, 0] = e_out[core - 1]["pedge"][:, 1]
        if qi < 3:
            halo[:, 1] = e_out[core + 1]["pedge"][:, 0]
        halo_col = np.zeros((128, 32), np.float32)
        for m in range(16):
            halo_col[:, m] = halo[m * 128:(m + 1) * 128, 0]
            halo_col[:, 16 + m] = halo[m * 128:(m + 1) * 128, 1]
        hsums_all = np.concatenate([e_out[gb * 4 + j]["hsum"] for j in range(4)], axis=1)
        sl = slice(qi * 1024, (qi + 1) * 1024)
        in_maps.append(dict(
            xn2=e_out[core]["xn2"].astype(NB_BF16),
            xconv=e_out[core]["xconv"].astype(NB_BF16),
            xssm=e_out[core]["xssm"].astype(NB_BF16),
            mloc=e_out[core]["mloc"], aprod=e_out[core]["aprod"],
            mend3=me3, aend3=ae3, halo_col=halo_col, convw_col=convw_col,
            Cm_=Cm_, dv_col=dv_col, wout_t=wout_t, hsums_all=hsums_all,
            gw2_col=gw2_col, gb2_row=gb2_row,
            tw1q=np.ascontiguousarray(TW1[:, sl, :].transpose(0, 2, 1)),
            tw2q=np.ascontiguousarray(TW2[:, :, sl].transpose(0, 2, 1)),
            w1tq=np.ascontiguousarray(W1[sl, :].T),
            w2tq=np.ascontiguousarray(W2[:, sl].T),
            Tb1_=Tb1_, b1_row=b1_row, Tb2_=Tb2_, b2_row=b2_row, eyeb=eyeb))
    res = run_bass_kernel_spmd(c.nc, in_maps, core_ids=list(range(NCORES)))
    return res.results


def run_l3g(inputs, x2_cores, s2_cores, xn2_cores, mw, mb):
    c = _get("l3g", build_l3g)
    in_maps = []
    for core in range(NCORES):
        gb = core // 4
        mw1t = np.concatenate([mw[gb * 4 + qi]["mw1q"] for qi in range(4)], axis=1)
        mw2t = np.concatenate([mw[gb * 4 + qi]["mw2q"] for qi in range(4)], axis=0)
        in_maps.append(dict(
            x2=x2_cores[core], s2=s2_cores[core], xn2=xn2_cores[core].astype(NB_BF16),
            mw1t=np.ascontiguousarray(mw1t.astype(NB_BF16)),
            mw2t=np.ascontiguousarray(mw2t.astype(NB_BF16)),
            mb1_col=np.ascontiguousarray(mb[core]["mb1"].reshape(32, 128).T),
            mb2_col=np.ascontiguousarray(mb[core]["mb2"].reshape(8, 128).T)))
    res = run_bass_kernel_spmd(c.nc, in_maps, core_ids=list(range(NCORES)))
    return res.results


def run_l3a(inputs, xn3_cores):
    c = _get("l3a", build_l3a)
    wk_t = np.ascontiguousarray(np.asarray(inputs["L3_attn_W_k"], np.float32).T)
    wv_t = np.ascontiguousarray(np.asarray(inputs["L3_attn_W_v"], np.float32).T)
    wg_t = np.ascontiguousarray(np.asarray(inputs["L3_attn_W_g"], np.float32).T)
    lq = np.asarray(inputs["L3_attn_latent_queries"], np.float32)
    qlat_t = np.ascontiguousarray(lq[0].transpose(2, 1, 0).reshape(64, 1024))
    tri2 = np.zeros((128, 128), np.float32)
    for r in range(128):
        tri2[r, (r % 64) + 1:] = -1e9
    blockones = np.zeros((128, 2), np.float32)
    blockones[0:64, 0] = 1.0
    blockones[64:128, 1] = 1.0
    blockonesT = np.ascontiguousarray(blockones.T)
    eyeb = np.eye(128, dtype=NB_F32).astype(NB_BF16)
    inv = 1.0 / (10000.0 ** (np.arange(0, 64, 2, dtype=np.float64) / 64.0))

    in_maps = []
    for core in range(NCORES):
        b, t0 = _core_tokens(core)
        ang = (np.arange(t0, t0 + T, dtype=np.float64)[:, None] * inv[None, :])
        cosr = np.cos(ang).astype(np.float32)
        sinr = np.sin(ang).astype(np.float32)
        cosrep = np.ascontiguousarray(np.tile(cosr, (1, 16)))
        sinrep = np.ascontiguousarray(np.tile(sinr, (1, 16)))
        in_maps.append(dict(xn3=xn3_cores[core].astype(NB_BF16), wk_t=wk_t, wv_t=wv_t,
                            wg_t=wg_t, cosrep=cosrep, sinrep=sinrep, qlat_t=qlat_t,
                            tri2=tri2, blockones=blockones, blockonesT=blockonesT,
                            eyeb=eyeb))
    res = run_bass_kernel_spmd(c.nc, in_maps, core_ids=list(range(NCORES)))
    return res.results


def run_l3b(inputs, x3_cores, xn3_cores, gf_cores, lp_cores, wq):
    c = _get("l3b", build_l3b)
    wp_t = np.ascontiguousarray(np.asarray(inputs["L3_attn_W_p"], np.float32).T)
    b1_col = np.ascontiguousarray(np.asarray(inputs["L3_ffn_b1"], np.float32).reshape(32, 128).T)
    b2_col = np.ascontiguousarray(np.asarray(inputs["L3_ffn_b2"], np.float32).reshape(8, 128).T)
    w1qt = np.ascontiguousarray(wq["w1qt"].astype(NB_BF16))
    w2qt = np.ascontiguousarray(wq["w2qt"].astype(NB_BF16))
    in_maps = []
    for core in range(NCORES):
        gb = core // 4
        lparts = np.concatenate([lp_cores[gb * 4 + j] for j in range(4)], axis=1)
        in_maps.append(dict(x3=x3_cores[core], xn3=xn3_cores[core].astype(NB_BF16),
                            gates_f=gf_cores[core].astype(NB_BF16),
                            lparts=np.ascontiguousarray(lparts), wp_t=wp_t,
                            w1qt=w1qt, b1_col=b1_col, w2qt=w2qt, b2_col=b2_col))
    res = run_bass_kernel_spmd(c.nc, in_maps, core_ids=list(range(NCORES)))
    return res.results


# version 02
# speedup vs baseline: 1.1110x; 1.1110x over previous
import numpy as np
import ml_dtypes
from contextlib import ExitStack

import concourse.bass as bass
import concourse.mybir as mybir
from concourse.bass_utils import run_bass_kernel_spmd

F32 = mybir.dt.float32
BF16 = mybir.dt.bfloat16
AF = mybir.ActivationFunctionType
ALU = mybir.AluOpType
AX = mybir.AxisListType

NCORES = 8
B, S, D = 2, 8192, 1024
L0_NEXP = 8
L0_NQ = 4
L0_SKIP_QKV = False
T = 2048
NB_F32 = np.float32
NB_BF16 = ml_dtypes.bfloat16
LN_EPS = 1e-5
PARANOID = False


class Ctx:
    def __init__(self, name):
        self.nc = bass.Bass("TRN2", target_bir_lowering=False)
        self.name = name
        self.stack = ExitStack()
        self.sems = {}
        self.inputs = {}
        self.outputs = {}

    def din(self, name, shape, dtype=F32):
        h = self.nc.dram_tensor(name, list(shape), dtype, kind="ExternalInput")
        self.inputs[name] = (tuple(shape), dtype)
        return h.ap()

    def dout(self, name, shape, dtype=F32):
        h = self.nc.dram_tensor(name, list(shape), dtype, kind="ExternalOutput")
        self.outputs[name] = (tuple(shape), dtype)
        return h.ap()

    def sb(self, name, shape, dtype=F32):
        return self.stack.enter_context(self.nc.sbuf_tensor(name, list(shape), dtype))

    def ps(self, name, shape, dtype=F32):
        return self.stack.enter_context(self.nc.psum_tensor(name, list(shape), dtype))

    def sem(self, name):
        if name not in self.sems:
            h = self.stack.enter_context(self.nc.semaphore(name))
            self.sems[name] = [h, 0]
        return self.sems[name][0]

    def inc(self, ins, name, amount=1):
        s = self.sems[name]
        ins.then_inc(s[0], amount)
        s[1] += amount
        return s[1]

    def val(self, name):
        return self.sems[name][1]

    def wait(self, eng, name, value=None):
        s = self.sems[name]
        eng.wait_ge(s[0], s[1] if value is None else value)

    def close(self):
        self.stack.close()

    def dma(self, eng, sem, out, in_):
        if PARANOID:
            self.wait(eng, sem)
        return self.inc(eng.dma_start(out=out, in_=in_), sem, 16)


class Chain:

    def __init__(self, c, eng, sem):
        self._c = c
        self._eng = eng
        self._sem = sem
        c.sem(sem)

    def __getattr__(self, name):
        if name.startswith("_"):
            raise AttributeError(name)
        f = getattr(self._eng, name)
        c, eng, sem = self._c, self._eng, self._sem

        def wrapped(*a, **kw):
            c.wait(eng, sem)
            ins = f(*a, **kw)
            c.inc(ins, sem)
            return ins

        return wrapped

    def wait_ge(self, semh, val):
        self._eng.wait_ge(semh, val)

    def raw(self):
        return self._eng


def cdiv(a, b):
    return (a + b - 1) // b


def np_t(a):
    return np.ascontiguousarray(np.swapaxes(np.asarray(a), -1, -2))


def to_np(a, dtype):
    arr = np.asarray(a)
    if arr.dtype == np.uint16 and dtype == NB_BF16:
        arr = arr.view(NB_BF16)
    return np.ascontiguousarray(arr.astype(dtype, copy=False))


def emit_fp4_quant(vector, dst_bf, src_f32, scr, nrows, ncols):
    nb = ncols // 16
    x3 = src_f32.rearrange("p (b k) -> p b k", k=16)
    mx = scr["mx"][0:nrows, 0:nb]
    vector.tensor_reduce(out=mx, in_=x3, axis=AX.X, op=ALU.max, apply_absolute_value=True)
    vector.tensor_scalar(out=mx, in0=mx, scalar1=6e-12, scalar2=None, op0=ALU.max)
    rinv = scr["rinv"][0:nrows, 0:nb]
    vector.reciprocal(out=rinv, in_=mx)
    vector.tensor_scalar(out=rinv, in0=rinv, scalar1=6.0, scalar2=None, op0=ALU.mult)
    xn = scr["xn"][0:nrows, 0:ncols]
    rb = rinv.rearrange("p (b o) -> p b o", o=1).broadcast_to((nrows, nb, 16))
    vector.tensor_tensor(out=xn.rearrange("p (b k) -> p b k", k=16), in0=x3, in1=rb, op=ALU.mult)
    v = scr["v"][0:nrows, 0:ncols]
    t = scr["t"][0:nrows, 0:ncols]
    iscr = scr["iscr"][0:nrows, 0:ncols]
    vector.tensor_scalar(out=t, in0=xn, scalar1=-1.0, scalar2=None, op0=ALU.mult)
    vector.tensor_tensor(out=v, in0=xn, in1=t, op=ALU.max)
    u = scr["u"][0:nrows, 0:ncols]
    vector.tensor_scalar(out=u, in0=v, scalar1=2.0, scalar2=2.0, op0=ALU.min, op1=ALU.mult)
    vector.tensor_copy(out=iscr, in_=u)
    vector.tensor_copy(out=u, in_=iscr)
    w = scr["w"][0:nrows, 0:ncols]
    vector.tensor_scalar(out=w, in0=v, scalar1=2.0, scalar2=4.0, op0=ALU.max, op1=ALU.min)
    vector.tensor_copy(out=iscr, in_=w)
    vector.tensor_copy(out=w, in_=iscr)
    hi = t
    vector.tensor_scalar(out=hi, in0=v, scalar1=5.0, scalar2=None, op0=ALU.is_gt)
    vector.scalar_tensor_tensor(out=u, in0=u, scalar=0.5, in1=w, op0=ALU.mult, op1=ALU.add)
    vector.scalar_tensor_tensor(out=u, in0=hi, scalar=2.0, in1=u, op0=ALU.mult, op1=ALU.add)
    vector.tensor_scalar(out=u, in0=u, scalar1=-2.0, scalar2=1.0 / 6.0, op0=ALU.add, op1=ALU.mult)
    sgp = w
    vector.tensor_scalar(out=sgp, in0=xn, scalar1=0.0, scalar2=None, op0=ALU.is_lt)
    vector.scalar_tensor_tensor(out=sgp, in0=xn, scalar=0.0, in1=sgp, op0=ALU.is_gt, op1=ALU.subtract)
    vector.tensor_tensor(out=u, in0=u, in1=sgp, op=ALU.mult)
    mxb = mx.rearrange("p (b o) -> p b o", o=1).broadcast_to((nrows, nb, 16))
    vector.tensor_tensor(out=dst_bf.rearrange("p (b k) -> p b k", k=16),
                         in0=u.rearrange("p (b k) -> p b k", k=16), in1=mxb, op=ALU.mult)


def build_prep():
    c = Ctx("prep")
    nc = c.nc

    wg = c.din("wg_sl", [384, 1024], F32)
    wu = c.din("wu_sl", [384, 1024], F32)
    wd = c.din("wd_sl", [128, 2816], F32)
    w1t = c.din("w1t", [1024, 4096], F32)
    w2t = c.din("w2t", [4096, 1024], F32)
    w1t_sl = c.din("w1t_sl", [128, 4096], F32)
    w2t_sl = c.din("w2t_sl", [512, 1024], F32)

    blob_in = c.din("blob_in", [4864, 2048], F32)
    blob_out = c.dout("blob_out", [4864, 2048], BF16)
    wgq = c.dout("wgq", [384, 1024], BF16)
    wuq = c.dout("wuq", [384, 1024], BF16)
    wdq = c.dout("wdq", [128, 2816], BF16)
    w1q_sl = c.dout("w1q_sl", [128, 4096], BF16)
    w2q_sl = c.dout("w2q_sl", [512, 1024], BF16)

    sin = c.sb("sin", [128, 8960], F32)
    sout = c.sb("sout", [128, 8960], BF16)
    scr = {k: c.sb("scr_" + k, [128, 2816], F32)
           for k in ["xn", "v", "u", "t", "w"]}
    scr["mx"] = c.sb("scr_mx", [128, 176], F32)
    scr["rinv"] = c.sb("scr_rinv", [128, 176], F32)
    scr["iscr"] = c.sb("scr_iscr", [128, 2816], mybir.dt.int32)
    bufA = c.sb("bufA", [128, 4096], F32)
    bufB = c.sb("bufB", [128, 4096], F32)
    qout = c.sb("qout", [128, 4096], BF16)
    qscr = c.sb("qscr", [128, 4096], F32)
    acc1 = c.sb("acc1", [128, 1], F32)
    acc2 = c.sb("acc2", [128, 1], F32)
    redcol = c.sb("redcol", [128, 1], F32)
    ones_col = c.sb("ones_col", [128, 1], F32)
    ones_row = c.sb("ones_row", [1, 128], F32)
    thr1 = c.sb("thr1", [128, 2], F32)
    thr2 = c.sb("thr2", [128, 2], F32)
    grow = c.sb("grow", [1, 2], F32)
    ps_s = c.ps("ps_s", [1, 1], F32)
    ps_b = c.ps("ps_b", [128, 1], F32)

    for s in ["ld", "la0", "la1", "pe", "dve", "st", "cl0", "cl1", "cst"]:
        c.sem(s)

    with nc.Block() as block:
        @block.sync
        def _(sync):
            c.dma(sync, "ld", sin[:, 0:3 * 1024].rearrange("p (a d) -> p a d", a=3), wg.rearrange("(a p) d -> p a d", p=128))
            c.dma(sync, "ld", sin[:, 3 * 1024:6 * 1024].rearrange("p (a d) -> p a d", a=3), wu.rearrange("(a p) d -> p a d", p=128))
            c.dma(sync, "ld", sin[:, 6 * 1024:6 * 1024 + 2816], wd)

        @block.gpsimd
        def _(gpsimd):
            gpsimd.memset(ones_col[:, :], 1.0)
            gpsimd.memset(ones_row[:, :], 1.0)

        @block.vector
        def _(vector):
            c.wait(vector, "ld")
            v = Chain(c, vector, "dvec")
            for j in range(6):
                emit_fp4_quant(v, sout[:, j * 1024:(j + 1) * 1024],
                               sin[:, j * 1024:(j + 1) * 1024], scr, 128, 1024)
            emit_fp4_quant(v, sout[:, 6 * 1024:6 * 1024 + 2816],
                           sin[:, 6 * 1024:6 * 1024 + 2816], scr, 128, 2816)

    with nc.Block() as block:
        @block.sync
        def _(sync):
            c.dma(sync, "st", wgq.rearrange("(a p) d -> p a d", p=128), sout[:, 0:3 * 1024].rearrange("p (a d) -> p a d", a=3))
            c.dma(sync, "st", wuq.rearrange("(a p) d -> p a d", p=128), sout[:, 3 * 1024:6 * 1024].rearrange("p (a d) -> p a d", a=3))
            c.dma(sync, "st", wdq, sout[:, 6 * 1024:6 * 1024 + 2816])

    def abs_sum_pass(src_dram, ntiles, ncols, acc):
        ld_base = {sm: c.val(sm) for sm in ("la0", "la1")}
        iter_done = []
        with nc.Block() as block:
            @block.vector
            def _(vector):
                v = Chain(c, vector, "dvec")
                for i in range(ntiles):
                    sm = "la0" if i % 2 == 0 else "la1"
                    vector.wait_ge(c.sems[sm][0], ld_base[sm] + 16 * (i // 2 + 1))
                    buf = (bufA if i % 2 == 0 else bufB)
                    v.tensor_reduce(out=redcol[:, :], in_=buf[:, 0:ncols], axis=AX.X,
                                    op=ALU.add, apply_absolute_value=True)
                    if i == 0:
                        v.tensor_copy(out=acc[:, :], in_=redcol[:, :])
                    else:
                        v.tensor_tensor(out=acc[:, :], in0=acc[:, :], in1=redcol[:, :], op=ALU.add)
                    iter_done.append(c.val("dvec"))

            @block.sync
            def _(sync):
                for i in range(ntiles):
                    if i >= 2:
                        sync.wait_ge(c.sems["dvec"][0], iter_done[i - 2])
                    c.dma(sync, "la0" if i % 2 == 0 else "la1",
                          (bufA if i % 2 == 0 else bufB)[:, 0:ncols], src_dram[i * 128:(i + 1) * 128, :])

    abs_sum_pass(w1t, 8, 4096, acc1)
    abs_sum_pass(w2t, 32, 1024, acc2)

    def gamma_thr(acc, numel, thr, col):
        pe_base = c.val("pe")
        grow_done = []
        with nc.Block() as block:
            @block.vector
            def _(vector):
                v = Chain(c, vector, "dvec")
                vector.wait_ge(c.sems["pe"][0], pe_base + 1)
                v.tensor_scalar(out=grow[0:1, col:col + 1], in0=ps_s[0:1, 0:1],
                                scalar1=1.0 / numel, scalar2=None, op0=ALU.mult)
                grow_done.append(c.val("dvec"))
                vector.wait_ge(c.sems["pe"][0], pe_base + 2)
                v.tensor_scalar(out=thr[:, 0:1], in0=ps_b[:, :], scalar1=1e-5, scalar2=0.5,
                                op0=ALU.add, op1=ALU.mult)
                v.tensor_scalar(out=thr[:, 1:2], in0=thr[:, 0:1], scalar1=-1.0, scalar2=None,
                                op0=ALU.mult)

            @block.tensor
            def _(tensor):
                c.inc(tensor.matmul(ps_s[:, :], ones_col[:, :], acc[:, :], start=True, stop=True), "pe", 1)
                tensor.wait_ge(c.sems["dvec"][0], grow_done[0])
                c.inc(tensor.matmul(ps_b[:, :], ones_row[:, :], grow[0:1, col:col + 1], start=True, stop=True), "pe", 1)

    gamma_thr(acc1, 4096 * 1024, thr1, 0)
    gamma_thr(acc2, 4096 * 1024, thr2, 1)

    def tern_quant(src_dram, dst_dram, ntiles, ncols, thr):
        ld_base = {sm: c.val(sm) for sm in ("la0", "la1")}
        st_base = c.val("st")
        iter_done = []
        with nc.Block() as block:
            @block.vector
            def _(vector):
                v = Chain(c, vector, "dvec")
                for i in range(ntiles):
                    sm = "la0" if i % 2 == 0 else "la1"
                    vector.wait_ge(c.sems[sm][0], ld_base[sm] + 16 * (i // 2 + 1))
                    vector.wait_ge(c.sems["st"][0], st_base + 16 * i)
                    buf = (bufA if i % 2 == 0 else bufB)
                    v.tensor_scalar(out=qscr[:, 0:ncols], in0=buf[:, 0:ncols],
                                    scalar1=thr[:, 1:2], scalar2=None, op0=ALU.is_lt)
                    v.scalar_tensor_tensor(out=qout[:, 0:ncols], in0=buf[:, 0:ncols],
                                           scalar=thr[:, 0:1], in1=qscr[:, 0:ncols],
                                           op0=ALU.is_gt, op1=ALU.subtract)
                    iter_done.append(c.val("dvec"))

            @block.sync
            def _(sync):
                for i in range(ntiles):
                    if i >= 2:
                        sync.wait_ge(c.sems["dvec"][0], iter_done[i - 2])
                    c.dma(sync, "la0" if i % 2 == 0 else "la1",
                          (bufA if i % 2 == 0 else bufB)[:, 0:ncols], src_dram[i * 128:(i + 1) * 128, :])

            @block.scalar
            def _(scalar):
                for i in range(ntiles):
                    scalar.wait_ge(c.sems["dvec"][0], iter_done[i])
                    c.dma(scalar, "st", dst_dram[i * 128:(i + 1) * 128, :], qout[:, 0:ncols])

    tern_quant(w1t_sl, w1q_sl, 1, 4096, thr1)
    tern_quant(w2t_sl, w2q_sl, 4, 1024, thr2)

    with nc.Block() as block:
        @block.sync
        def _(sync):
            c.wait(sync, "st")

    cbi = c.sb("cbi", [128, 2 * 2048], F32)
    cbo = c.sb("cbo", [128, 2 * 2048], BF16)
    with nc.Block() as block:
        cb_ld = {}
        cb_cv = {}
        cb_st = {}
        for i in range(38):
            @block.sync
            def _(sync, i=i):
                sm = "cl0" if i % 2 == 0 else "cl1"
                if i >= 2:
                    sm2, tv2 = cb_st[i - 2]
                    sync.wait_ge(c.sems[sm2][0], tv2)
                c.dma(sync, sm, cbi[:, (i % 2) * 2048:(i % 2) * 2048 + 2048],
                      blob_in[i * 128:(i + 1) * 128, :])
                cb_ld[i] = (sm, c.val(sm))

            @block.vector
            def _(vector, i=i):
                v = Chain(c, vector, "dve")
                sm, tv = cb_ld[i]
                vector.wait_ge(c.sems[sm][0], tv)
                v.tensor_copy(out=cbo[:, (i % 2) * 2048:(i % 2) * 2048 + 2048],
                              in_=cbi[:, (i % 2) * 2048:(i % 2) * 2048 + 2048])
                cb_cv[i] = c.val("dve")

            @block.sync
            def _(sync, i=i):
                sync.wait_ge(c.sems["dve"][0], cb_cv[i])
                c.dma(sync, "cst", blob_out[i * 128:(i + 1) * 128, :],
                      cbo[:, (i % 2) * 2048:(i % 2) * 2048 + 2048])
                cb_st[i] = ("cst", c.val("cst"))

    with nc.Block() as block:
        @block.sync
        def _(sync):
            c.wait(sync, "cst")

    c.close()
    return c


def emit_ln_feat(c, nc, tiles_f32, DT, Tlen, prm, apply_fn):
    NCH = Tlen // 512
    prm["_bc_evac"] = {}
    prm["_rows_done"] = {}
    prm["_bc_pe"] = {}
    with nc.Block() as block:
        for ch in range(NCH):
            pe_ch = c.val("pe")
            sq_done = []

            @block.scalar
            def _(scalar, ch=ch, pe_ch=pe_ch, sq_done=sq_done):
                a = Chain(c, scalar, "actc")
                for i in range(DT):
                    if i >= 2:
                        scalar.wait_ge(c.sems["pe"][0], pe_ch + 2 * (i - 2) + 2)
                    elif ch > 0:
                        scalar.wait_ge(c.sems["pe"][0], pe_ch)
                    a.activation(out=(prm["sq0"] if i % 2 == 0 else prm["sq1"])[:, :],
                                 in_=tiles_f32(i)[:, ch * 512:(ch + 1) * 512], func=AF.Square)
                    sq_done.append(c.val("actc"))

            @block.tensor
            def _(tensor, ch=ch, sq_done=sq_done):
                if ch > 0:
                    tensor.wait_ge(c.sems["dvec"][0], prm["_rows_done"][ch - 1])
                for i in range(DT):
                    c.inc(tensor.matmul(prm["ps_row"][:, :], prm["ones_col_f32"][:, :],
                                        tiles_f32(i)[:, ch * 512:(ch + 1) * 512],
                                        start=(i == 0), stop=(i == DT - 1)), "pe", 1)
                    tensor.wait_ge(c.sems["actc"][0], sq_done[i])
                    c.inc(tensor.matmul(prm["ps_row2"][:, :], prm["ones_col_f32"][:, :],
                                        (prm["sq0"] if i % 2 == 0 else prm["sq1"])[:, :],
                                        start=(i == 0), stop=(i == DT - 1)), "pe", 1)

            rows_done = []

            @block.vector
            def _(vector, ch=ch, pe_ch=pe_ch, rows_done=rows_done):
                v = Chain(c, vector, "dvec")
                vector.wait_ge(c.sems["pe"][0], pe_ch + 2 * DT)
                v.tensor_scalar(out=prm["murow"][0:1, :], in0=prm["ps_row"][:, :],
                                scalar1=1.0 / (DT * 128), scalar2=None, op0=ALU.mult)
                v.tensor_scalar(out=prm["varrow"][0:1, :], in0=prm["ps_row2"][:, :],
                                scalar1=1.0 / (DT * 128), scalar2=None, op0=ALU.mult)
                v.tensor_tensor(out=prm["r2row"][0:1, :], in0=prm["murow"][0:1, :],
                                in1=prm["murow"][0:1, :], op=ALU.mult)
                v.tensor_tensor(out=prm["varrow"][0:1, :], in0=prm["varrow"][0:1, :],
                                in1=prm["r2row"][0:1, :], op=ALU.subtract)
                rows_done.append(c.val("dvec"))
                prm["_rows_done"][ch] = c.val("dvec")

            rstd_done = []

            @block.scalar
            def _(scalar, ch=ch, rows_done=rows_done, rstd_done=rstd_done):
                a = Chain(c, scalar, "actc")
                scalar.wait_ge(c.sems["dvec"][0], rows_done[0])
                if ch > 0:
                    scalar.wait_ge(c.sems["pe"][0], prm["_bc_pe"][ch - 1])
                a.activation(out=prm["rstdrow"][0:1, :], in_=prm["varrow"][0:1, :],
                             func=AF.Ln, bias=prm["epscol"][0:1, 0:1])
                a.activation(out=prm["rstdrow"][0:1, :], in_=prm["rstdrow"][0:1, :],
                             func=AF.Exp, scale=prm["nhcol"][0:1, 0:1])
                rstd_done.append(c.val("actc"))

            r2_done = []

            @block.vector
            def _(vector, ch=ch, rstd_done=rstd_done, r2_done=r2_done):
                v = Chain(c, vector, "dvec")
                vector.wait_ge(c.sems["actc"][0], rstd_done[0])
                v.scalar_tensor_tensor(out=prm["r2row"][0:1, :], in0=prm["murow"][0:1, :],
                                       scalar=-1.0, in1=prm["rstdrow"][0:1, :],
                                       op0=ALU.mult, op1=ALU.mult)
                r2_done.append(c.val("dvec"))

            bc_pe = []

            @block.tensor
            def _(tensor, ch=ch, r2_done=r2_done):
                tensor.wait_ge(c.sems["dvec"][0], r2_done[0])
                if ch > 0:
                    tensor.wait_ge(c.sems["dvec"][0], prm["_bc_evac"][ch - 1])
                c.inc(tensor.matmul(prm["bcA"][:, :], prm["ones_row_f32"][:, :],
                                    prm["rstdrow"][0:1, :], start=True, stop=True), "pe", 1)
                c.inc(tensor.matmul(prm["bcB"][:, :], prm["ones_row_f32"][:, :],
                                    prm["r2row"][0:1, :], start=True, stop=True), "pe", 1)
                bc_pe.append(c.val("pe"))
                prm["_bc_pe"][ch] = c.val("pe")

            @block.vector
            def _(vector, ch=ch, bc_pe=bc_pe):
                v = Chain(c, vector, "dvec")
                cs = slice(ch * 512, (ch + 1) * 512)
                vector.wait_ge(c.sems["pe"][0], bc_pe[0])
                v.tensor_copy(out=prm["r1b"][:, cs], in_=prm["bcA"][:, :])
                v.tensor_copy(out=prm["r2b"][:, cs], in_=prm["bcB"][:, :])
                prm["_bc_evac"][ch] = c.val("dvec")

        @block.vector
        def _(vector):
            v = Chain(c, vector, "dvec")
            apply_fn(v)


def build_l0():
    c = Ctx("l0")
    nc = c.nc
    DT = 8
    NCH = 4

    x_in = c.din("x_feat", [D, T], F32)
    wqkv = c.din("wqkv", [D, 3072], F32)
    bq_col_in = c.din("bqkv_col", [128, 24], F32)
    bv_in = c.din("bv_row", [1, 1024], F32)
    lng_in = c.din("lng_col", [128, 8], F32)
    lnb_in = c.din("lnb_col", [128, 8], F32)
    rw_in = c.din("rw_col", [128, 64], F32)
    eye_in = c.din("eye", [128, 128], F32)
    gup = c.din("gup_t", [8, D, 6144], BF16)
    dwn = c.din("down_t", [8, 3072, D], BF16)
    sgup = c.din("sgup_t", [D, 2048], BF16)
    sdwn = c.din("sdwn_t", [1024, D], BF16)

    q_out = c.dout("q0", [1024, T], BF16)
    k_out = c.dout("k0", [1024, T], BF16)
    v_out = c.dout("v0", [T, 1024], BF16)
    g_out = c.dout("g0", [D, T], F32)
    xn_out = c.dout("xn0", [D, T], BF16)
    comb_out = c.dout("comb_dbg", [8, T], BF16)
    fs_out = c.dout("facc_sh", [D, 512], F32)

    xs = c.sb("xs", [128, DT * T], F32)
    nrm = c.sb("nrm", [128, DT * T], BF16)
    rows4 = c.sb("rows4", [1, 2048], F32)
    murow = rows4[0:1, 0:512]
    varrow = rows4[0:1, 512:1024]
    rstdrow = rows4[0:1, 1024:1536]
    r2row = rows4[0:1, 1536:2048]
    ones_col_f32 = c.sb("ones_col_f32", [128, 1], F32)
    ones_row_f32 = c.sb("ones_row_f32", [1, 128], F32)
    ones_row_bf = c.sb("ones_row_bf", [1, 128], BF16)
    epscol = c.sb("epscol", [1, 1], F32)
    nhcol = c.sb("nhcol", [1, 1], F32)
    bq_col = c.sb("bq_col", [128, 24], F32)
    bv_row = c.sb("bv_rowb", [1, 1024], BF16)
    lng_c = c.sb("lng_c", [128, 8], F32)
    lnb_c = c.sb("lnb_c", [128, 8], F32)
    rwf = c.sb("rwf", [128, 64], F32)
    normf = c.sb("normf", [128, 1024], F32)
    idb = c.sb("idb", [128, 128], BF16)
    m1c = c.sb("m1c", [128, 1], F32)
    m2c = c.sb("m2c", [128, 1], F32)
    sumc = c.sb("sumc", [128, 1], F32)
    d12c = c.sb("d12c", [128, 1], F32)
    g1col = c.sb("g1col", [128, 1], F32)
    g2col = c.sb("g2col", [128, 1], F32)
    probs8 = c.sb("probs8", [128, 8], F32)
    tmp8 = c.sb("tmp8", [128, 8], F32)
    eq1 = c.sb("eq1", [128, 8], F32)
    eq2 = c.sb("eq2", [128, 8], F32)
    combt = c.sb("combt", [128, 8], BF16)
    comb_feat = c.sb("comb_feat", [8, T], BF16)
    comb_row_cur = c.sb("comb_row_cur", [1, T], BF16)

    wfA = c.sb("wfA", [128, 4096], F32)
    wfB = c.sb("wfB", [128, 4096], F32)
    wbA = c.sb("wbA", [128, 4096], BF16)
    wbB = c.sb("wbB", [128, 4096], BF16)
    sg = c.sb("sg", [128, 512], BF16)
    r1b = wfA[:, 0:2048]
    r2b = wfA[:, 2048:4096]
    tA = wfB[:, 0:2048]
    gstage = [wfA[:, 0:2048], wfB[:, 0:2048]]
    qk_stage = [nrm[:, 0:2048], nrm[:, 2048:4096]]
    vstage = [nrm[:, 4096:4608], nrm[:, 4608:5120]]
    sq0 = c.sb("sq0", [128, 512], F32)
    sq1 = c.sb("sq1", [128, 512], F32)

    pb = [c.ps(f"pb{i}", [128, 512], F32) for i in range(6)]

    for s in ["ld", "wl0", "wl1", "dl0", "dl1", "dl2", "pe", "actc", "dvec",
              "st", "st0", "st1", "sv0", "sv1", "sg0", "sg1", "cld"]:
        c.sem(s)

    def xt(i):
        return xs[:, i * T:(i + 1) * T]

    def nrmt(i):
        return nrm[:, i * T:(i + 1) * T]

    prm = dict(sq0=sq0, sq1=sq1, murow=murow, varrow=varrow, rstdrow=rstdrow,
               r2row=r2row, r1b=r1b, r2b=r2b, ones_col_f32=ones_col_f32,
               ones_row_f32=ones_row_f32, epscol=epscol, nhcol=nhcol)

    stackP = ExitStack()
    ps_row = stackP.enter_context(nc.psum_tensor("ps_row", [1, 512], F32))
    ps_row2 = stackP.enter_context(nc.psum_tensor("ps_row2", [1, 512], F32))
    prm["ps_row"] = ps_row
    prm["ps_row2"] = ps_row2
    prm["bcA"] = pb[4]
    prm["bcB"] = pb[5]

    xnb_stack = ExitStack()
    xnb = xnb_stack.enter_context(nc.sbuf_tensor("xnb", [128, DT * T], BF16))

    def xnbt(i):
        return xnb[:, i * T:(i + 1) * T]

    with nc.Block() as block:
        @block.sync
        def _(sync):
            for i in range(DT):
                c.dma(sync, "ld", xt(i), x_in[i * 128:(i + 1) * 128, :])
            c.dma(sync, "ld", bq_col[:, :], bq_col_in)
            c.dma(sync, "ld", lng_c[:, :], lng_in)
            c.dma(sync, "ld", lnb_c[:, :], lnb_in)
            c.dma(sync, "ld", rwf[:, :], rw_in)
            c.dma(sync, "ld", wfA[:, 64:192], eye_in)
            c.dma(sync, "ld", wfA[0:1, 192:1216], bv_in)

        @block.gpsimd
        def _(gpsimd):
            gpsimd.memset(ones_col_f32[:, :], 1.0)
            gpsimd.memset(ones_row_f32[:, :], 1.0)
            gpsimd.memset(ones_row_bf[:, :], 1.0)
            gpsimd.memset(epscol[:, :], LN_EPS)
            gpsimd.memset(nhcol[:, :], -0.5)

        @block.vector
        def _(vector):
            v = Chain(c, vector, "dvec")
            c.wait(vector, "ld")
            v.tensor_copy(out=idb[:, :], in_=wfA[:, 64:192])
            v.tensor_copy(out=bv_row[0:1, :], in_=wfA[0:1, 192:1216])

    def apply_ln1(v):
        for i in range(DT):
            v.tensor_tensor(out=xt(i), in0=xt(i), in1=r1b[:, :], op=ALU.mult)
            v.tensor_tensor(out=xt(i), in0=xt(i), in1=r2b[:, :], op=ALU.add)
            v.tensor_copy(out=xnbt(i), in_=xt(i))

    emit_ln_feat(c, nc, xt, DT, T, prm, apply_ln1)

    with nc.Block() as block:
        @block.sync
        def _(sync):
            c.wait(sync, "dvec")
            for i in range(DT):
                c.dma(sync, "st", xn_out[i * 128:(i + 1) * 128, :], xnbt(i))

    with nc.Block() as block:
        if L0_SKIP_QKV:
            pass
        cast_done = {}
        mm_done = {}
        mm_done_v = {}
        qk_pe = {}
        evac_done = {}
        store_done = {}
        qkv_wl = {}
        for blk in range(0 if L0_SKIP_QKV else 6):
            @block.sync
            def _(sync, blk=blk):
                if blk >= 2:
                    sync.wait_ge(c.sems["dvec"][0], cast_done[blk - 2])
                sm = "wl0" if blk % 2 == 0 else "wl1"
                c.dma(sync, sm, (wfA if blk % 2 == 0 else wfB)[:, :].rearrange("p (k m) -> p k m", k=8), wqkv[:, blk * 512:(blk + 1) * 512].rearrange("(k p) m -> p k m", p=128))
                qkv_wl[blk] = (sm, c.val(sm))

            @block.vector
            def _(vector, blk=blk):
                v = Chain(c, vector, "dvec")
                vector.wait_ge(c.sems[qkv_wl[blk][0]][0], qkv_wl[blk][1])
                if blk >= 2:
                    vector.wait_ge(c.sems["pe"][0], mm_done[blk - 2])
                v.tensor_copy(out=(wbA if blk % 2 == 0 else wbB)[:, :],
                              in_=(wfA if blk % 2 == 0 else wfB)[:, :])
                cast_done[blk] = c.val("dvec")

            wb_cur = wbA if blk % 2 == 0 else wbB
            if blk < 4:
                for m in range(4):
                    mg = blk * 4 + m
                    stg = qk_stage[mg % 2]

                    @block.tensor
                    def _(tensor, blk=blk, m=m, mg=mg, wb_cur=wb_cur):
                        tensor.wait_ge(c.sems["dvec"][0], cast_done[blk])
                        for ch in range(NCH):
                            bidx = (mg * 4 + ch) % 4
                            key = (mg * 4 + ch) - 4
                            if key >= 0:
                                tensor.wait_ge(c.sems["actc"][0], evac_done[key])
                            for k in range(8):
                                c.inc(tensor.matmul(
                                    pb[bidx][:, :],
                                    wb_cur[:, k * 512 + m * 128:k * 512 + (m + 1) * 128],
                                    xnbt(k)[:, ch * 512:(ch + 1) * 512],
                                    start=(k == 0), stop=(k == 7)), "pe", 1)
                            qk_pe[(mg, ch)] = c.val("pe")
                        mm_done[blk] = c.val("pe")

                    @block.scalar
                    def _(scalar, mg=mg, stg=stg):
                        a = Chain(c, scalar, "actc")
                        if mg >= 2:
                            scalar.wait_ge(c.sems[store_done[mg - 2][0]][0], store_done[mg - 2][1])
                        for ch in range(NCH):
                            bidx = (mg * 4 + ch) % 4
                            scalar.wait_ge(c.sems["pe"][0], qk_pe[(mg, ch)])
                            a.activation(out=stg[:, ch * 512:(ch + 1) * 512], in_=pb[bidx][:, :],
                                         func=AF.Identity, bias=bq_col[:, mg:mg + 1])
                            evac_done[(mg * 4 + ch)] = c.val("actc")

                    @block.sync
                    def _(sync, mg=mg, stg=stg):
                        sync.wait_ge(c.sems["actc"][0], evac_done[mg * 4 + 3])
                        dst = q_out if mg < 8 else k_out
                        r = mg if mg < 8 else mg - 8
                        sm = "st0" if mg % 2 == 0 else "st1"
                        c.dma(sync, sm, dst[r * 128:(r + 1) * 128, :], stg[:, :])
                        store_done[mg] = (sm, c.val(sm))
            else:
                for tt in range(16):
                    stg = vstage[tt % 2]

                    @block.tensor
                    def _(tensor, blk=blk, tt=tt, wb_cur=wb_cur):
                        tensor.wait_ge(c.sems["dvec"][0], cast_done[blk])
                        bidx = tt % 4
                        key = ("v", blk, tt - 4)
                        if tt >= 4:
                            tensor.wait_ge(c.sems["actc"][0], evac_done[key])
                        elif blk == 5:
                            tensor.wait_ge(c.sems["actc"][0], evac_done[("v", 4, 12 + tt)])
                        else:
                            tensor.wait_ge(c.sems["actc"][0], evac_done[60 + tt])
                        for k in range(8):
                            c.inc(tensor.matmul(
                                pb[bidx][:, :],
                                xnbt(k)[:, tt * 128:(tt + 1) * 128],
                                wb_cur[:, k * 512:(k + 1) * 512],
                                start=(k == 0), stop=False), "pe", 1)
                        c.inc(tensor.matmul(
                            pb[bidx][:, :], ones_row_bf[:, :],
                            bv_row[0:1, (blk - 4) * 512:(blk - 3) * 512],
                            start=False, stop=True), "pe", 1)
                        mm_done_v[(blk, tt)] = c.val("pe")

                    @block.scalar
                    def _(scalar, blk=blk, tt=tt, stg=stg):
                        a = Chain(c, scalar, "actc")
                        if tt >= 2:
                            sd = store_done[("v", blk, tt - 2)]
                            scalar.wait_ge(c.sems[sd[0]][0], sd[1])
                        elif blk == 5:
                            sd = store_done[("v", 4, 14 + tt)]
                            scalar.wait_ge(c.sems[sd[0]][0], sd[1])
                        scalar.wait_ge(c.sems["pe"][0], mm_done_v[(blk, tt)])
                        a.activation(out=stg[:, :], in_=pb[tt % 4][:, :], func=AF.Copy)
                        evac_done[("v", blk, tt)] = c.val("actc")

                    @block.sync
                    def _(sync, blk=blk, tt=tt, stg=stg):
                        sync.wait_ge(c.sems["actc"][0], evac_done[("v", blk, tt)])
                        sm = "sv0" if tt % 2 == 0 else "sv1"
                        c.dma(sync, sm, v_out[tt * 128:(tt + 1) * 128, (blk - 4) * 512:(blk - 3) * 512], stg[:, :])
                        store_done[("v", blk, tt)] = (sm, c.val(sm))

    st_parities = ["st0", "st1", "sv0", "sv1"]
    st_after_b3 = [(sm, c.val(sm)) for sm in st_parities]

    def apply_ln2(v):
        for sm, tv in st_after_b3:
            v.wait_ge(c.sems[sm][0], tv)
        for i in range(DT):
            v.tensor_tensor(out=tA[:, :], in0=xt(i), in1=r1b[:, :], op=ALU.mult)
            v.tensor_tensor(out=tA[:, :], in0=tA[:, :], in1=r2b[:, :], op=ALU.add)
            v.tensor_scalar(out=nrmt(i), in0=tA[:, :], scalar1=lng_c[:, i:i + 1],
                            scalar2=lnb_c[:, i:i + 1], op0=ALU.mult, op1=ALU.add)

    emit_ln_feat(c, nc, xt, DT, T, prm, apply_ln2)

    stackP.close()
    stackR = ExitStack()
    ps_t = stackR.enter_context(nc.psum_tensor("ps_t", [8, 128], BF16))

    with nc.Block() as block:
        exp_done = {}
        tr_done = {}
        ev_done = {}
        rt_pe = {}
        m1_done = {}
        d12_done = {}
        sig_done = {}
        comb_done = {}
        nf_done = {}
        for tt in range(16):
            @block.vector
            def _(vector, tt=tt):
                v = Chain(c, vector, "dvec")
                cs = slice(tt * 128, (tt + 1) * 128)
                if tt > 0:
                    vector.wait_ge(c.sems["pe"][0], rt_pe[tt - 1])
                for i in range(DT):
                    nf = normf[:, i * 128:(i + 1) * 128]
                    v.tensor_tensor(out=nf, in0=xt(i)[:, cs], in1=r1b[:, cs], op=ALU.mult)
                    v.tensor_tensor(out=nf, in0=nf, in1=r2b[:, cs], op=ALU.add)
                    v.tensor_scalar(out=nf, in0=nf, scalar1=lng_c[:, i:i + 1],
                                    scalar2=lnb_c[:, i:i + 1], op0=ALU.mult, op1=ALU.add)
                nf_done[tt] = c.val("dvec")

            @block.tensor
            def _(tensor, tt=tt):
                tensor.wait_ge(c.sems["dvec"][0], nf_done[tt])
                if tt > 0:
                    tensor.wait_ge(c.sems["actc"][0], exp_done[tt - 1])
                for k in range(8):
                    c.inc(tensor.matmul(pb[4][:, 0:8],
                                        normf[:, k * 128:(k + 1) * 128],
                                        rwf[:, k * 8:(k + 1) * 8],
                                        start=(k == 0), stop=(k == 7)), "pe", 1)
                rt_pe[tt] = c.val("pe")

            @block.vector
            def _(vector, tt=tt):
                v = Chain(c, vector, "dvec")
                vector.wait_ge(c.sems["pe"][0], rt_pe[tt])
                v.reduce_max(out=m1c[:, :], in_=pb[4][:, 0:8], axis=AX.X)
                v.tensor_scalar(out=m1c[:, :], in0=m1c[:, :], scalar1=-1.0, scalar2=None, op0=ALU.mult)
                m1_done[tt] = c.val("dvec")

            @block.scalar
            def _(scalar, tt=tt):
                a = Chain(c, scalar, "actc")
                scalar.wait_ge(c.sems["dvec"][0], m1_done[tt])
                a.activation(out=probs8[:, :], in_=pb[4][:, 0:8], func=AF.Exp, bias=m1c[:, 0:1])
                exp_done[tt] = c.val("actc")

            @block.vector
            def _(vector, tt=tt):
                v = Chain(c, vector, "dvec")
                vector.wait_ge(c.sems["actc"][0], exp_done[tt])
                v.reduce_sum(out=sumc[:, :], in_=probs8[:, :], axis=AX.X)
                v.reciprocal(out=sumc[:, :], in_=sumc[:, :])
                v.tensor_scalar(out=probs8[:, :], in0=probs8[:, :], scalar1=sumc[:, 0:1],
                                scalar2=None, op0=ALU.mult)
                v.reduce_max(out=m1c[:, :], in_=probs8[:, :], axis=AX.X)
                v.tensor_scalar(out=eq1[:, :], in0=probs8[:, :], scalar1=m1c[:, 0:1],
                                scalar2=None, op0=ALU.is_ge)
                v.scalar_tensor_tensor(out=tmp8[:, :], in0=eq1[:, :], scalar=-2.0, in1=probs8[:, :],
                                       op0=ALU.mult, op1=ALU.add)
                v.reduce_max(out=m2c[:, :], in_=tmp8[:, :], axis=AX.X)
                v.tensor_scalar(out=eq2[:, :], in0=tmp8[:, :], scalar1=m2c[:, 0:1],
                                scalar2=None, op0=ALU.is_ge)
                v.tensor_tensor(out=d12c[:, :], in0=m1c[:, :], in1=m2c[:, :], op=ALU.subtract)
                d12_done[tt] = c.val("dvec")

            @block.scalar
            def _(scalar, tt=tt):
                a = Chain(c, scalar, "actc")
                scalar.wait_ge(c.sems["dvec"][0], d12_done[tt])
                a.activation(out=g1col[:, :], in_=d12c[:, :], func=AF.Sigmoid)
                sig_done[tt] = c.val("actc")

            @block.vector
            def _(vector, tt=tt):
                v = Chain(c, vector, "dvec")
                vector.wait_ge(c.sems["actc"][0], sig_done[tt])
                v.tensor_scalar(out=g2col[:, :], in0=g1col[:, :], scalar1=-1.0, scalar2=1.0,
                                op0=ALU.mult, op1=ALU.add)
                v.tensor_scalar(out=eq1[:, :], in0=eq1[:, :], scalar1=g1col[:, 0:1],
                                scalar2=None, op0=ALU.mult)
                v.tensor_scalar(out=eq2[:, :], in0=eq2[:, :], scalar1=g2col[:, 0:1],
                                scalar2=None, op0=ALU.mult)
                v.tensor_tensor(out=combt[:, :], in0=eq1[:, :], in1=eq2[:, :], op=ALU.add)
                comb_done[tt] = c.val("dvec")

            @block.tensor
            def _(tensor, tt=tt):
                tensor.wait_ge(c.sems["dvec"][0], comb_done[tt])
                if tt > 0:
                    tensor.wait_ge(c.sems["dvec"][0], ev_done[tt - 1])
                c.inc(tensor.transpose(out=ps_t[:, :], in_=combt[:, :], identity=idb[:, :]), "pe", 1)
                tr_done[tt] = c.val("pe")

            @block.vector
            def _(vector, tt=tt):
                v = Chain(c, vector, "dvec")
                vector.wait_ge(c.sems["pe"][0], tr_done[tt])
                v.tensor_copy(out=comb_feat[0:8, tt * 128:(tt + 1) * 128], in_=ps_t[:, :])
                ev_done[tt] = c.val("dvec")

    comb_ev_done = c.val("dvec")
    stackR.close()
    pb6 = c.ps("pb6", [128, 512], F32)
    pb7 = c.ps("pb7", [128, 512], F32)
    pbank = pb + [pb6, pb7]
    xnb_stack.close()
    hbuf = c.sb("hbuf", [128, 24 * 512], BF16)
    wfg = [wfA[:, 0:1024], wfA[:, 1024:2048]]
    wfu = [wfA[:, 2048:3072], wfA[:, 3072:4096]]
    dsf = [wfB[:, j * 1024:(j + 1) * 1024] for j in range(3)]
    wbg = [wbA[:, 0:1024], wbA[:, 1024:2048]]
    wbu = [wbA[:, 2048:3072], wbA[:, 3072:4096]]
    dsb = [wbB[:, j * 1024:(j + 1) * 1024] for j in range(3)]
    combB = c.sb("combB", [128, T], BF16)

    def emit_ffn_expert(e, quarter, n_mp, n_ks, gup_ap, dwn_ap, gate_off, comb):
        t0 = quarter * 512
        with nc.Block() as block:
            gu_cast = {}
            g_pe = {}
            u_pe = {}
            silu_done = {}
            h_done = {}
            for mp in range(n_mp):
                @block.sync
                def _(sync, mp=mp):
                    if mp >= 2:
                        sync.wait_ge(c.sems["pe"][0], u_pe[mp - 2])
                    sm = "wl0" if mp % 2 == 0 else "wl1"
                    c.dma(sync, sm, wbg[mp % 2][:, :].rearrange("p (k m) -> p k m", k=8), gup_ap[:, mp * 128:(mp + 1) * 128].rearrange("(k p) m -> p k m", p=128))
                    c.dma(sync, sm, wbu[mp % 2][:, :].rearrange("p (k m) -> p k m", k=8), gup_ap[:, gate_off + mp * 128:gate_off + (mp + 1) * 128].rearrange(
                            "(k p) m -> p k m", p=128))
                    gu_wl[(e, quarter, mp)] = (sm, c.val(sm))

                @block.tensor
                def _(tensor, mp=mp):
                    tensor.wait_ge(c.sems[gu_wl[(e, quarter, mp)][0]][0],
                                   gu_wl[(e, quarter, mp)][1])
                    if mp >= 1:
                        tensor.wait_ge(c.sems["actc"][0], silu_done[mp - 1])
                    for k in range(8):
                        c.inc(tensor.matmul(
                            pbank[0][:, :],
                            wbg[mp % 2][:, k * 128:(k + 1) * 128],
                            nrmt(k)[:, t0:t0 + 512],
                            start=(k == 0), stop=(k == 7)), "pe", 1)
                    g_pe[mp] = c.val("pe")
                    if mp >= 1:
                        tensor.wait_ge(c.sems["dvec"][0], h_done[mp - 1])
                    for k in range(8):
                        c.inc(tensor.matmul(
                            pbank[1][:, :],
                            wbu[mp % 2][:, k * 128:(k + 1) * 128],
                            nrmt(k)[:, t0:t0 + 512],
                            start=(k == 0), stop=(k == 7)), "pe", 1)
                    u_pe[mp] = c.val("pe")

                @block.scalar
                def _(scalar, mp=mp):
                    a = Chain(c, scalar, "actc")
                    scalar.wait_ge(c.sems["pe"][0], g_pe[mp])
                    if mp >= 1:
                        scalar.wait_ge(c.sems["dvec"][0], h_done[mp - 1])
                    a.activation(out=sg[:, :], in_=pbank[0][:, :], func=AF.Silu)
                    silu_done[mp] = c.val("actc")

                @block.vector
                def _(vector, mp=mp):
                    v = Chain(c, vector, "dvec")
                    vector.wait_ge(c.sems["actc"][0], silu_done[mp])
                    vector.wait_ge(c.sems["pe"][0], u_pe[mp])
                    hst = hbuf[:, mp * 512:(mp + 1) * 512]
                    v.tensor_tensor(out=hst, in0=sg[:, :], in1=pbank[1][:, :], op=ALU.mult)
                    if comb is not None:
                        v.tensor_tensor(out=hst, in0=hst, in1=comb[:, t0:t0 + 512], op=ALU.mult)
                    h_done[mp] = c.val("dvec")

            ds_cast = {}
            ds_mm = {}
            for k in range(n_ks):
                @block.sync
                def _(sync, k=k):
                    if k >= 3:
                        sync.wait_ge(c.sems["pe"][0], ds_mm[k - 3])
                    sm = f"dl{k % 3}"
                    c.dma(sync, sm, dsb[k % 3][:, :], dwn_ap[k * 128:(k + 1) * 128, :])
                    ds_wl[(e, quarter, k)] = (sm, c.val(sm))

                @block.tensor
                def _(tensor, k=k):
                    tensor.wait_ge(c.sems[ds_wl[(e, quarter, k)][0]][0],
                                   ds_wl[(e, quarter, k)][1])
                    if k == 0:
                        tensor.wait_ge(c.sems["actc"][0], silu_done[n_mp - 1])
                        tensor.wait_ge(c.sems["dvec"][0], h_done[n_mp - 1])
                    tensor.wait_ge(c.sems["dvec"][0], h_done[k])
                    for m in range(8):
                        c.inc(tensor.matmul(
                            pbank[m][:, :],
                            dsb[k % 3][:, m * 128:(m + 1) * 128],
                            hbuf[:, k * 512:(k + 1) * 512],
                            start=(k == 0), stop=(k == n_ks - 1)), "pe", 1)
                    ds_mm[k] = c.val("pe")

            @block.vector
            def _(vector):
                v = Chain(c, vector, "dvec")
                vector.wait_ge(c.sems["pe"][0], ds_mm[n_ks - 1])
                for m in range(8):
                    dst = xt(m)[:, t0:t0 + 512]
                    if e == "shared":
                        v.tensor_copy(out=dst, in_=pbank[m][:, :])
                    else:
                        v.tensor_tensor(out=dst, in0=dst, in1=pbank[m][:, :], op=ALU.add)

    gu_wl = {}
    ds_wl = {}

    for q in range(L0_NQ):
        emit_ffn_expert("shared", q, 8, 8, sgup, sdwn, 1024, None)

    for e in range(L0_NEXP):
        with nc.Block() as block:
            bc_pe = {}
            bc_ev = {}

            @block.sync
            def _(sync, e=e):
                sync.wait_ge(c.sems["dvec"][0], comb_ev_done)
                c.dma(sync, "cld", comb_row_cur[0:1, :], comb_feat[e:e + 1, :])

            combrow_ld = c.val("cld")

            for chh in range(4):
                @block.tensor
                def _(tensor, chh=chh, combrow_ld=combrow_ld):
                    if chh == 0:
                        tensor.wait_ge(c.sems["cld"][0], combrow_ld)
                    else:
                        tensor.wait_ge(c.sems["dvec"][0], bc_ev[chh - 1])
                    c.inc(tensor.matmul(pbank[4][:, :], ones_row_bf[:, :],
                                        comb_row_cur[0:1, chh * 512:(chh + 1) * 512],
                                        start=True, stop=True), "pe", 1)
                    bc_pe[chh] = c.val("pe")

                @block.vector
                def _(vector, chh=chh):
                    v = Chain(c, vector, "dvec")
                    vector.wait_ge(c.sems["pe"][0], bc_pe[chh])
                    v.tensor_copy(out=combB[:, chh * 512:(chh + 1) * 512], in_=pbank[4][:, :])
                    bc_ev[chh] = c.val("dvec")

        for q in range(L0_NQ):
            emit_ffn_expert(e, q, 24, 24, gup[e], dwn[e], 3072, combB)

    with nc.Block() as block:
        sil_done = {}
        gst_done = {}
        for i in range(DT):
            @block.scalar
            def _(scalar, i=i):
                a = Chain(c, scalar, "actc")
                if i >= 2:
                    scalar.wait_ge(c.sems[gst_done[i - 2][0]][0], gst_done[i - 2][1])
                a.activation(out=gstage[i % 2][:, :], in_=xt(i), func=AF.Silu)
                sil_done[i] = c.val("actc")

            @block.sync
            def _(sync, i=i):
                sync.wait_ge(c.sems["actc"][0], sil_done[i])
                sm = "sg0" if i % 2 == 0 else "sg1"
                c.dma(sync, sm, g_out[i * 128:(i + 1) * 128, :], gstage[i % 2][:, :])
                gst_done[i] = (sm, c.val(sm))

    with nc.Block() as block:
        @block.sync
        def _(sync):
            for sm in ["st", "st0", "st1", "sv0", "sv1", "sg0", "sg1"]:
                c.wait(sync, sm)

    c.close()
    return c


def build_attn():
    c = Ctx("attn")
    nc = c.nc
    NQ = S // 512
    NT = S // 128
    NPAIR = 2

    q_in = c.din("qh", [NPAIR, 128, S], BF16)
    k_in = c.din("kh", [NPAIR, 128, S], BF16)
    v_in = c.din("vh", [NPAIR, S, 128], BF16)
    tri_in = c.din("tri", [128, 128], F32)
    eye_in = c.din("eyef", [128, 128], F32)
    ans_out = c.dout("ansh", [NPAIR, 128, S], BF16)

    qb_ = [c.sb(f"qbuf{j}", [128, S], BF16) for j in range(2)]
    kb_ = [c.sb(f"kbuf{j}", [128, S], BF16) for j in range(2)]
    vb_ = [c.sb(f"vbuf{j}", [128, S], BF16) for j in range(2)]
    ansb = [c.sb(f"ansb{j}", [128, S], BF16) for j in range(2)]
    prob = [c.sb(f"prob{j}", [128, 512], BF16) for j in range(4)]
    tri = c.sb("tri_s", [128, 128], F32)
    eyef = c.sb("eyef_s", [128, 128], F32)
    onescol_bf = c.sb("onescol_bf", [128, 1], BF16)
    onesrow = c.sb("onesrow", [1, 128], F32)
    sel0 = c.sb("sel0", [1, 128], F32)
    sel1 = c.sb("sel1", [1, 128], F32)
    scl8 = c.sb("scl8", [128, 1], F32)
    nrm2 = c.sb("nrm2", [128, 128], F32)
    cmax = c.sb("cmax", [128, 4], F32)
    red4 = c.sb("red4", [4, 1], F32)
    brow = c.sb("brow", [1, 4], F32)
    bias_h = c.sb("bias_h", [128, 2], F32)
    recr = c.sb("recr", [1, 1024], F32)
    recipB = c.sb("recipB", [128, 512], F32)
    qq = [c.sb(f"qqs{j}", [128, 512], BF16) for j in range(2)]

    psA = [c.ps(f"psA{j}", [128, 512], F32) for j in range(2)]
    psB = [c.ps(f"psB{j}", [128, 512], F32) for j in range(2)]
    psC = c.ps("psC", [128, 512], F32)
    psD = c.ps("psD", [128, 512], F32)
    psE = c.ps("psE", [128, 512], F32)
    ps_n = c.ps("ps_n", [128, 128], F32)

    for s in ["lq0", "lq1", "pe", "actc", "dvec", "st0", "st1", "ld"]:
        c.sem(s)

    with nc.Block() as block:
        @block.sync
        def _(sync):
            c.dma(sync, "ld", tri[:, :], tri_in)
            c.dma(sync, "ld", eyef[:, :], eye_in)

        @block.gpsimd
        def _(gpsimd):
            gpsimd.memset(onescol_bf[:, :], 1.0)
            gpsimd.memset(onesrow[:, :], 1.0)
            gpsimd.memset(scl8[:, :], 0.125)
            gpsimd.memset(sel0[:, :], 0.0)
            gpsimd.memset(sel1[:, :], 0.0)
            gpsimd.memset(sel0[0:1, 0:64], 1.0)
            gpsimd.memset(sel1[0:1, 64:128], 1.0)

    ld_pair = {}
    store_done = {}

    def emit_loads(block, p):
        sm = f"lq{p % 2}"

        @block.sync
        def _(sync, p=p, sm=sm):
            c.dma(sync, sm, qb_[p % 2][:, :].rearrange("p (a s) -> p a s", a=16),
                  q_in[p].rearrange("p (a s) -> p a s", a=16))
            c.dma(sync, sm, kb_[p % 2][:, :].rearrange("p (a s) -> p a s", a=16),
                  k_in[p].rearrange("p (a s) -> p a s", a=16))
            c.dma(sync, sm, vb_[p % 2][:, :].rearrange("p (t d) -> p t d", d=128),
                  v_in[p].rearrange("(t p) d -> p t d", p=128))
            ld_pair[p] = c.val(sm)

    with nc.Block() as block:
        emit_loads(block, 0)

    def emit_pair(p):
        buf = p % 2
        Q, K, V, A = qb_[buf], kb_[buf], vb_[buf], ansb[buf]
        with nc.Block() as block:
            if p + 1 < NPAIR:
                emit_loads(block, p + 1)

            nb_dv = {}
            nb_pe = {}
            for si, SRC in enumerate([Q, K]):
                for ch in range(NQ):
                    @block.vector
                    def _(vector, si=si, ch=ch, SRC=SRC):
                        v = Chain(c, vector, "dvec")
                        if si == 0 and ch == 0:
                            vector.wait_ge(c.sems[f"lq{p % 2}"][0], ld_pair[p])
                            if p >= 2:
                                sd = store_done[p - 2]
                                vector.wait_ge(c.sems[sd[0]][0], sd[1])
                        key = (si, ch - 2) if ch >= 2 else ((si - 1, NQ - 2 + ch) if si > 0 else None)
                        if key is not None and key in nb_pe:
                            vector.wait_ge(c.sems["pe"][0], nb_pe[key])
                        v.tensor_tensor(out=qq[ch % 2][:, :], in0=SRC[:, ch * 512:(ch + 1) * 512],
                                        in1=SRC[:, ch * 512:(ch + 1) * 512], op=ALU.mult)
                        nb_dv[(si, ch)] = c.val("dvec")

                    @block.tensor
                    def _(tensor, si=si, ch=ch):
                        tensor.wait_ge(c.sems["dvec"][0], nb_dv[(si, ch)])
                        for t4 in range(4):
                            gt = ch * 4 + t4
                            for h in range(2):
                                col = h * 64 + gt % 64
                                c.inc(tensor.matmul(
                                    ps_n[:, col:col + 1],
                                    qq[ch % 2][h * 64:(h + 1) * 64, t4 * 128:(t4 + 1) * 128],
                                    onescol_bf[h * 64:(h + 1) * 64, :],
                                    start=(gt < 64), stop=True,
                                    tile_position=(h * 64, 0)), "pe", 1)
                        nb_pe[(si, ch)] = c.val("pe")

                @block.vector
                def _(vector, si=si):
                    v = Chain(c, vector, "dvec")
                    vector.wait_ge(c.sems["pe"][0], nb_pe[(si, NQ - 1)])
                    v.tensor_copy(out=nrm2[:, :], in_=ps_n[:, :])
                    v.reduce_max(out=cmax[:, 2 * si:2 * si + 1], in_=nrm2[:, 0:64], axis=AX.X)
                    v.reduce_max(out=cmax[:, 2 * si + 1:2 * si + 2], in_=nrm2[:, 64:128], axis=AX.X)
                    nb_dv[("red", si)] = c.val("dvec")


            @block.tensor
            def _(tensor):
                pass

            cb_pe = {}

            @block.tensor
            def _(tensor):
                tensor.wait_ge(c.sems["dvec"][0], nb_dv[("red", 1)])
                c.inc(tensor.transpose(out=ps_n[0:4, 0:128], in_=cmax[:, :], identity=eyef[:, :]),
                      "pe", 1)
                cb_pe[0] = c.val("pe")

            @block.vector
            def _(vector):
                v = Chain(c, vector, "dvec")
                vector.wait_ge(c.sems["pe"][0], cb_pe[0])
                v.reduce_max(out=red4[:, :], in_=ps_n[0:4, 0:128], axis=AX.X)
                nb_dv["red4"] = c.val("dvec")

            @block.tensor
            def _(tensor):
                tensor.wait_ge(c.sems["dvec"][0], nb_dv["red4"])
                c.inc(tensor.transpose(out=ps_n[0:1, 0:4], in_=red4[:, :], identity=eyef[0:4, 0:4]),
                      "pe", 1)
                cb_pe[1] = c.val("pe")

            @block.vector
            def _(vector):
                v = Chain(c, vector, "dvec")
                vector.wait_ge(c.sems["pe"][0], cb_pe[1])
                v.tensor_copy(out=brow[0:1, :], in_=ps_n[0:1, 0:4])
                v.tensor_tensor(out=brow[0:1, 0:2], in0=brow[0:1, 0:2], in1=brow[0:1, 2:4], op=ALU.add)
                v.tensor_scalar(out=brow[0:1, 0:2], in0=brow[0:1, 0:2], scalar1=-1.0 / 16.0,
                                scalar2=None, op0=ALU.mult)
                nb_dv["brow"] = c.val("dvec")

            @block.tensor
            def _(tensor):
                tensor.wait_ge(c.sems["dvec"][0], nb_dv["brow"])
                for h in range(2):
                    c.inc(tensor.matmul(psE[:, h:h + 1], onesrow[:, :], brow[0:1, h:h + 1],
                                        start=True, stop=True), "pe", 1)
                cb_pe[2] = c.val("pe")

            @block.vector
            def _(vector):
                v = Chain(c, vector, "dvec")
                vector.wait_ge(c.sems["pe"][0], cb_pe[2])
                v.tensor_copy(out=bias_h[:, :], in_=psE[:, 0:2])
                nb_dv["bias"] = c.val("dvec")

            sc_pe = {}
            exp_done = {}
            av_pe = {}
            msk_dv = {}
            ep_dv = {}
            for qb in range(NQ):
                qs = qb * 512
                n_kt = 4 * qb + 4
                for kt in range(n_kt):
                    j = kt - 4 * qb
                    col0 = max(0, j) * 128
                    pa, pb_ = psA[kt % 2], psB[kt % 2]

                    @block.tensor
                    def _(tensor, qb=qb, kt=kt, j=j, col0=col0, pa=pa, pb_=pb_):
                        if (qb, kt) == (0, 0):
                            tensor.wait_ge(c.sems["dvec"][0], nb_dv["bias"])
                        if kt >= 2:
                            tensor.wait_ge(c.sems["actc"][0], exp_done[(qb, kt - 2)])
                        elif qb > 0:
                            pk = 4 * (qb - 1) + 4 - 2 + kt
                            tensor.wait_ge(c.sems["actc"][0], exp_done[(qb - 1, pk)])
                        for h, pp in ((0, pa), (1, pb_)):
                            c.inc(tensor.matmul(
                                pp[:, col0:512],
                                K[h * 64:(h + 1) * 64, kt * 128:(kt + 1) * 128],
                                Q[h * 64:(h + 1) * 64, qs + col0:qs + 512],
                                start=True, stop=True, tile_position=(h * 64, 0)), "pe", 1)
                        sc_pe[(qb, kt)] = c.val("pe")

                    if j >= 0:
                        @block.vector
                        def _(vector, qb=qb, kt=kt, col0=col0, pa=pa, pb_=pb_):
                            v = Chain(c, vector, "dvec")
                            vector.wait_ge(c.sems["pe"][0], sc_pe[(qb, kt)])
                            v.tensor_tensor(out=pa[:, col0:col0 + 128], in0=pa[:, col0:col0 + 128],
                                            in1=tri[:, :], op=ALU.add)
                            v.tensor_tensor(out=pb_[:, col0:col0 + 128], in0=pb_[:, col0:col0 + 128],
                                            in1=tri[:, :], op=ALU.add)
                            msk_dv[(qb, kt)] = c.val("dvec")

                    @block.scalar
                    def _(scalar, qb=qb, kt=kt, j=j, col0=col0, pa=pa, pb_=pb_):
                        a = Chain(c, scalar, "actc")
                        if j >= 0:
                            scalar.wait_ge(c.sems["dvec"][0], msk_dv[(qb, kt)])
                        else:
                            scalar.wait_ge(c.sems["pe"][0], sc_pe[(qb, kt)])
                        if kt >= 2:
                            scalar.wait_ge(c.sems["pe"][0], av_pe[(qb, kt - 2)])
                        elif qb > 0:
                            pk = 4 * (qb - 1) + 4 - 2 + kt
                            scalar.wait_ge(c.sems["pe"][0], av_pe[(qb - 1, pk)])
                        for h, pp in ((0, pa), (1, pb_)):
                            a.activation(out=prob[2 * (kt % 2) + h][:, col0:512],
                                         in_=pp[:, col0:512], func=AF.Exp,
                                         scale=scl8[:, 0:1], bias=bias_h[:, h:h + 1])
                        exp_done[(qb, kt)] = c.val("actc")

                    @block.tensor
                    def _(tensor, qb=qb, kt=kt, col0=col0):
                        tensor.wait_ge(c.sems["actc"][0], exp_done[(qb, kt)])
                        if kt == 0 and qb > 0:
                            tensor.wait_ge(c.sems["dvec"][0], ep_dv[qb - 1])
                        for h in range(2):
                            c.inc(tensor.matmul(
                                psC[h * 64:(h + 1) * 64, col0:512],
                                V[:, kt * 128 + h * 64:kt * 128 + (h + 1) * 64],
                                prob[2 * (kt % 2) + h][:, col0:512],
                                start=(kt == 0), stop=(kt == n_kt - 1),
                                tile_position=(0, h * 64)), "pe", 1)
                        for h in range(2):
                            c.inc(tensor.matmul(
                                psD[h * 64:h * 64 + 1, col0:512],
                                onescol_bf[:, :],
                                prob[2 * (kt % 2) + h][:, col0:512],
                                start=(kt == 0), stop=(kt == n_kt - 1),
                                tile_position=(0, h * 64)), "pe", 1)
                        av_pe[(qb, kt)] = c.val("pe")

                @block.vector
                def _(vector, qb=qb, qs=qs, n_kt=n_kt):
                    v = Chain(c, vector, "dvec")
                    vector.wait_ge(c.sems["pe"][0], av_pe[(qb, n_kt - 1)])
                    v.reciprocal(out=recr[0:1, 0:512], in_=psD[0:1, :])
                    v.reciprocal(out=recr[0:1, 512:1024], in_=psD[64:65, :])
                    ep_dv[("r", qb)] = c.val("dvec")

                @block.tensor
                def _(tensor, qb=qb):
                    tensor.wait_ge(c.sems["dvec"][0], ep_dv[("r", qb)])
                    if qb > 0:
                        tensor.wait_ge(c.sems["dvec"][0], ep_dv[("e", qb - 1)])
                    c.inc(tensor.matmul(psE[:, :], sel0[:, :], recr[0:1, 0:512],
                                        start=True, stop=False), "pe", 1)
                    c.inc(tensor.matmul(psE[:, :], sel1[:, :], recr[0:1, 512:1024],
                                        start=False, stop=True), "pe", 1)
                    ep_dv[("b", qb)] = c.val("pe")

                @block.vector
                def _(vector, qb=qb, qs=qs):
                    v = Chain(c, vector, "dvec")
                    vector.wait_ge(c.sems["pe"][0], ep_dv[("b", qb)])
                    v.tensor_copy(out=recipB[:, :], in_=psE[:, :])
                    ep_dv[("e", qb)] = c.val("dvec")
                    v.tensor_tensor(out=A[:, qs:qs + 512], in0=psC[:, :], in1=recipB[:, :],
                                    op=ALU.mult)
                    ep_dv[qb] = c.val("dvec")

            @block.sync
            def _(sync, p=p):
                sync.wait_ge(c.sems["dvec"][0], ep_dv[NQ - 1])
                sm = f"st{p % 2}"
                c.dma(sync, sm, ans_out[p].rearrange("p (a s) -> p a s", a=16),
                      A[:, :].rearrange("p (a s) -> p a s", a=16))
                store_done[p] = (sm, c.val(sm))

    for p in range(NPAIR):
        emit_pair(p)

    with nc.Block() as block:
        @block.sync
        def _(sync):
            for sm in ["st0", "st1"]:
                c.wait(sync, sm)

    c.close()
    return c


def build_l1():
    c = Ctx("l1")
    nc = c.nc
    DT = 8
    NCH = 4

    x_in = c.din("x_feat", [D, T], F32)
    ans0_in = c.din("ans0", [D, T], BF16)
    wo0 = c.din("wo0", [D, D], F32)
    bo0_col = c.din("bo0_col", [128, 8], F32)
    g0_in = c.din("g0", [D, T], F32)
    wqkv1 = c.din("wqkv1", [D, 3072], F32)
    bqk1_col = c.din("bqk1_col", [128, 16], F32)
    bv1_row = c.din("bv1_row", [1, 1024], F32)
    dpw1 = c.din("dpw1", [D, 512], F32)
    dpb1_row = c.din("dpb1_row", [1, 512], F32)
    lng1_rep = c.din("lng1_rep", [128, 512], F32)
    lnb1_rep = c.din("lnb1_rep", [128, 512], F32)
    dpw2_rep = c.din("dpw2_rep", [128, 512], F32)
    dpb2_rep = c.din("dpb2_rep", [128, 1], F32)
    gw1_rep = c.din("gw1_rep", [128, 16], F32)
    gb1_rep = c.din("gb1_rep", [128, 16], F32)
    gw2_rep = c.din("gw2_rep", [128, 16], F32)
    gb2_rep = c.din("gb2_rep", [128, 1], F32)
    wgq_t = c.din("wgq_t", [D, 2816], BF16)
    wuq_t = c.din("wuq_t", [D, 2816], BF16)
    wdq_t = c.din("wdq_t", [2816, D], BF16)
    eyef_in = c.din("eyef", [128, 128], F32)
    eyeb_in = c.din("eyeb", [128, 128], BF16)

    x1_out = c.dout("x1", [D, T], F32)
    q_out = c.dout("q1", [D, T], BF16)
    k_out = c.dout("k1", [D, T], BF16)
    v_out = c.dout("v1g", [T, D], BF16)
    g_out = c.dout("g1", [D, T], F32)
    xn_out = c.dout("xn1", [D, T], BF16)

    xs = c.sb("xs", [128, DT * T], F32)
    xnb = c.sb("xnb", [128, DT * T], BF16)
    r1bK = c.sb("r1bK", [128, T], F32)
    r2bK = c.sb("r2bK", [128, T], F32)
    R1 = c.sb("R1", [128, 2048], F32)
    R2 = c.sb("R2", [128, 4096], F32)
    R3 = c.sb("R3", [128, 8192], BF16)
    R4 = c.sb("R4", [128, 16384], BF16)
    rows4 = R2[0:1, 0:2048]
    murow = rows4[0:1, 0:512]
    varrow = rows4[0:1, 512:1024]
    rstdrow = rows4[0:1, 1024:1536]
    r2row = rows4[0:1, 1536:2048]
    sq0 = c.sb("sq0", [128, 512], F32)
    sq1 = c.sb("sq1", [128, 512], F32)
    iscr = c.sb("iscr", [128, 512], mybir.dt.int32)
    ones_col_f32 = c.sb("ones_col_f32", [128, 1], F32)
    ones_row_f32 = c.sb("ones_row_f32", [1, 128], F32)
    ones_row_bf = c.sb("ones_row_bf", [1, 128], BF16)
    epscol = c.sb("epscol", [1, 1], F32)
    nhcol = c.sb("nhcol", [1, 1], F32)
    epscol_p = c.sb("epscol_p", [128, 1], F32)
    nhcol_p = c.sb("nhcol_p", [128, 1], F32)
    eyef = c.sb("eyef_s", [128, 128], F32)
    eyeb = c.sb("eyeb_s", [128, 128], BF16)
    bqk_c = c.sb("bqk_c", [128, 16], F32)
    bo0_c = c.sb("bo0_c", [128, 8], F32)
    bv_row = c.sb("bv_rowb", [1, 1024], BF16)
    dpb1_r = c.sb("dpb1_r", [1, 512], BF16)
    lng1_r = c.sb("lng1_r", [128, 512], F32)
    lnb1_r = c.sb("lnb1_r", [128, 512], F32)
    dpw2_r = c.sb("dpw2_r", [128, 512], F32)
    dpb2_r = c.sb("dpb2_r", [128, 1], F32)
    gw1_r = c.sb("gw1_r", [128, 16], F32)
    gb1_r = c.sb("gb1_r", [128, 16], F32)
    gw2_r = c.sb("gw2_r", [128, 16], F32)
    gb2_r = c.sb("gb2_r", [128, 1], F32)
    gateT = c.sb("gateT", [128, 16], F32)
    hsq = R1[:, 512:1024]
    hln = R1[:, 0:512]
    ghb = R1[:, 1024:1536]
    mucol = c.sb("mucol", [128, 4], F32)
    sgc = c.sb("sgc", [128, 512], BF16)
    scr = {"xn": R2[:, 0:512], "v": R2[:, 512:1024], "u": R2[:, 1024:1536],
           "t": R2[:, 1536:2048], "w": R2[:, 2048:2560], "iscr": iscr}
    scr["mx"] = c.sb("scr_mx", [128, 64], F32)
    scr["rinv"] = c.sb("scr_rinv", [128, 64], F32)
    xtokf = R2[:, 2560:3584]
    qkst = [R4[:, 0:2048], R4[:, 2048:4096]]
    vstage = [R4[:, 4096:4608], R4[:, 4608:5120]]
    ansring = [R4[:, 0:4096], R4[:, 4096:8192]]
    hq4 = xnb[:, 0:11264]
    htok = xnb[:, 11264:14080]
    xqf = xnb[:, 14080:15104]
    xtokq = xnb[:, 15104:16128]

    pb = [c.ps(f"pb{i}", [128, 512], F32) for i in range(5)]
    stackP = ExitStack()
    ps_row = stackP.enter_context(nc.psum_tensor("ps_row", [1, 512], F32))
    ps_row2 = stackP.enter_context(nc.psum_tensor("ps_row2", [1, 512], F32))

    for s in ["ld", "lg0", "lg1", "la0", "la1", "wl0", "wl1", "dl0", "dl1", "dl2",
              "dw0", "dw1", "dw2",
              "pe", "actc", "dvec", "st", "st0", "st1", "sv0", "sv1", "sg0", "sg1", "sx"]:
        c.sem(s)

    def xt(i):
        return xs[:, i * T:(i + 1) * T]

    def xnbt(i):
        return xnb[:, i * T:(i + 1) * T]

    prm = dict(sq0=sq0, sq1=sq1, murow=murow, varrow=varrow, rstdrow=rstdrow,
               r2row=r2row, r1b=r1bK, r2b=r2bK, ones_col_f32=ones_col_f32,
               ones_row_f32=ones_row_f32, epscol=epscol, nhcol=nhcol,
               ps_row=ps_row, ps_row2=ps_row2, bcA=pb[3], bcB=pb[4])

    with nc.Block() as block:
        @block.sync
        def _(sync):
            for i in range(DT):
                c.dma(sync, "ld", xt(i), x_in[i * 128:(i + 1) * 128, :])
            for dst, srcap in [(bqk_c, bqk1_col), (bo0_c, bo0_col),
                               (lng1_r, lng1_rep), (lnb1_r, lnb1_rep),
                               (dpw2_r, dpw2_rep), (dpb2_r, dpb2_rep),
                               (gw1_r, gw1_rep), (gb1_r, gb1_rep),
                               (gw2_r, gw2_rep), (gb2_r, gb2_rep),
                               (eyef, eyef_in), (eyeb, eyeb_in)]:
                c.dma(sync, "ld", dst[:, :], srcap)
            c.dma(sync, "ld", R2[0:1, 0:1024], bv1_row)
            c.dma(sync, "ld", R2[0:1, 1024:1536], dpb1_row)

        @block.gpsimd
        def _(gpsimd):
            gpsimd.memset(ones_col_f32[:, :], 1.0)
            gpsimd.memset(ones_row_f32[:, :], 1.0)
            gpsimd.memset(ones_row_bf[:, :], 1.0)
            gpsimd.memset(epscol[:, :], LN_EPS)
            gpsimd.memset(nhcol[:, :], -0.5)
            gpsimd.memset(epscol_p[:, :], LN_EPS)
            gpsimd.memset(nhcol_p[:, :], -0.5)

        @block.scalar
        def _(scalar):
            a = Chain(c, scalar, "actc")
            c.wait(scalar, "ld")
            a.activation(out=bv_row[0:1, :], in_=R2[0:1, 0:1024], func=AF.Copy)
            a.activation(out=dpb1_r[0:1, :], in_=R2[0:1, 1024:1536], func=AF.Copy)

    with nc.Block() as block:
        cast_dv = {}
        for j in range(8):
            @block.sync
            def _(sync, j=j):
                sm = "wl0" if j % 2 == 0 else "wl1"
                if j >= 2:
                    sync.wait_ge(c.sems["dvec"][0], cast_dv[j - 2])
                c.dma(sync, sm, R1[:, (j % 2) * 1024:(j % 2) * 1024 + 1024].rearrange(
                    "p (k m) -> p k m", k=8),
                    wo0[:, j * 128:(j + 1) * 128].rearrange("(k p) m -> p k m", p=128))
                cast_dv[("wl", j)] = (sm, c.val(sm))

            @block.vector
            def _(vector, j=j):
                v = Chain(c, vector, "dvec")
                sm, tv = cast_dv[("wl", j)]
                vector.wait_ge(c.sems[sm][0], tv)
                v.tensor_copy(out=R3[:, j * 1024:(j + 1) * 1024],
                              in_=R1[:, (j % 2) * 1024:(j % 2) * 1024 + 1024])
                cast_dv[j] = c.val("dvec")
    wo0_done = c.val("dvec")

    with nc.Block() as block:
        s0_pe = {}
        s0_ev = {}
        xu_dv = {}
        ans_ld = {}
        g0_ld = {}
        for qt in range(4):
            @block.sync
            def _(sync, qt=qt):
                sma = "la0" if qt % 2 == 0 else "la1"
                if qt >= 2:
                    sync.wait_ge(c.sems["pe"][0], s0_pe[(qt - 2, 7)])
                c.dma(sync, sma, ansring[qt % 2][:, :].rearrange("p (k s) -> p k s", k=8),
                      ans0_in.rearrange("(k p) t -> p k t", p=128)[:, :, qt * 512:(qt + 1) * 512])
                ans_ld[qt] = (sma, c.val(sma))
                for ihalf in range(2):
                    smg = "lg0" if ihalf == 0 else "lg1"
                    if qt >= 1:
                        sync.wait_ge(c.sems["dvec"][0], xu_dv[(qt - 1, ihalf * 4 + 3)])
                    c.dma(sync, smg, R2[:, ihalf * 2048:ihalf * 2048 + 2048].rearrange(
                        "p (k s) -> p k s", k=4),
                        g0_in.rearrange("(k p) t -> p k t", p=128)[:, ihalf * 4:(ihalf + 1) * 4,
                                                                  qt * 512:(qt + 1) * 512])
                    g0_ld[(qt, ihalf)] = (smg, c.val(smg))

            for i in range(DT):
                @block.tensor
                def _(tensor, qt=qt, i=i):
                    if i == 0:
                        if qt == 0:
                            tensor.wait_ge(c.sems["dvec"][0], wo0_done)
                        sm, tv = ans_ld[qt]
                        tensor.wait_ge(c.sems[sm][0], tv)
                    gidx = qt * 8 + i
                    bidx = gidx % 3
                    if gidx >= 3:
                        tensor.wait_ge(c.sems["actc"][0], s0_ev[gidx - 3])
                    for k in range(8):
                        c.inc(tensor.matmul(
                            pb[bidx][:, :],
                            R3[:, i * 1024 + k * 128:i * 1024 + (k + 1) * 128],
                            ansring[qt % 2][:, k * 512:(k + 1) * 512],
                            start=(k == 0), stop=(k == 7)), "pe", 1)
                    s0_pe[(qt, i)] = c.val("pe")

                @block.scalar
                def _(scalar, qt=qt, i=i):
                    a = Chain(c, scalar, "actc")
                    gidx = qt * 8 + i
                    scalar.wait_ge(c.sems["pe"][0], s0_pe[(qt, i)])
                    if gidx >= 1:
                        pq, pi = divmod(gidx - 1, 8)
                        scalar.wait_ge(c.sems["dvec"][0], xu_dv[(pq, pi)])
                    a.activation(out=sq0[:, :], in_=pb[gidx % 3][:, :],
                                 func=AF.Identity, bias=bo0_c[:, i:i + 1])
                    s0_ev[gidx] = c.val("actc")

                @block.vector
                def _(vector, qt=qt, i=i):
                    v = Chain(c, vector, "dvec")
                    gidx = qt * 8 + i
                    sm, tv = g0_ld[(qt, i // 4)]
                    vector.wait_ge(c.sems[sm][0], tv)
                    vector.wait_ge(c.sems["actc"][0], s0_ev[gidx])
                    gsl = R2[:, (i // 4) * 2048 + (i % 4) * 512:(i // 4) * 2048 + (i % 4 + 1) * 512]
                    cs = slice(qt * 512, (qt + 1) * 512)
                    v.tensor_tensor(out=sq1[:, :], in0=sq0[:, :], in1=gsl, op=ALU.mult)
                    v.tensor_tensor(out=xt(i)[:, cs], in0=xt(i)[:, cs], in1=sq1[:, :], op=ALU.add)
                    xu_dv[(qt, i)] = c.val("dvec")

        @block.sync
        def _(sync):
            c.wait(sync, "dvec")
            for i in range(DT):
                c.dma(sync, "sx", x1_out[i * 128:(i + 1) * 128, :], xt(i))

    def apply_ln1(v):
        for i in range(DT):
            for ch in range(NCH):
                cs = slice(ch * 512, (ch + 1) * 512)
                v.tensor_tensor(out=sq0[:, :], in0=xt(i)[:, cs], in1=r1bK[:, cs], op=ALU.mult)
                v.tensor_tensor(out=sq1[:, :], in0=sq0[:, :], in1=r2bK[:, cs], op=ALU.add)
                v.tensor_copy(out=xnbt(i)[:, cs], in_=sq1[:, :])

    emit_ln_feat(c, nc, xt, DT, T, prm, apply_ln1)
    stackP.close()
    ps_tf = c.ps("ps_tf", [128, 128], F32)
    ps_tb = c.ps("ps_tb", [128, 128], BF16)
    ps_tb2 = c.ps("ps_tb2", [128, 128], BF16)

    with nc.Block() as block:
        @block.sync
        def _(sync):
            c.wait(sync, "dvec")
            for i in range(DT):
                c.dma(sync, "st", xn_out[i * 128:(i + 1) * 128, :], xnbt(i))

    with nc.Block() as block:
        cast_dv = {}
        for j in range(4):
            @block.sync
            def _(sync, j=j):
                sm = "wl0" if j % 2 == 0 else "wl1"
                if j >= 2:
                    sync.wait_ge(c.sems["dvec"][0], cast_dv[j - 2])
                c.dma(sync, sm, R1[:, (j % 2) * 1024:(j % 2) * 1024 + 1024].rearrange(
                    "p (k m) -> p k m", k=8),
                    dpw1[:, j * 128:(j + 1) * 128].rearrange("(k p) m -> p k m", p=128))
                cast_dv[("wl", j)] = (sm, c.val(sm))

            @block.vector
            def _(vector, j=j):
                v = Chain(c, vector, "dvec")
                sm, tv = cast_dv[("wl", j)]
                vector.wait_ge(c.sems[sm][0], tv)
                v.tensor_copy(out=R3[:, j * 1024:(j + 1) * 1024],
                              in_=R1[:, (j % 2) * 1024:(j % 2) * 1024 + 1024])
                cast_dv[j] = c.val("dvec")
    dpw1_done = c.val("dvec")

    with nc.Block() as block:
        dr_pe = {}
        dr_dv = {}
        dr_ac = {}
        for tt in range(16):
            @block.tensor
            def _(tensor, tt=tt):
                if tt == 0:
                    tensor.wait_ge(c.sems["dvec"][0], dpw1_done)
                if tt >= 2:
                    tensor.wait_ge(c.sems["dvec"][0], dr_dv[(tt - 2, "done")])
                    tensor.wait_ge(c.sems["actc"][0], dr_ac[(tt - 2, "gate")])
                bidx = tt % 2
                for k in range(8):
                    rhs = bass.AP(R3.tensor, R3.offset + k * 128,
                                  [[R3.ap[0][0], 128], [1024, 4], [1, 128]]) if False else \
                        R3[:, 0:4096].rearrange("p (j m) -> p j m", j=4)[:, :, k * 128:(k + 1) * 128]
                    c.inc(tensor.matmul(
                        pb[bidx][:, :].rearrange("p (j m) -> p j m", j=4),
                        xnbt(k)[:, tt * 128:(tt + 1) * 128],
                        rhs,
                        start=(k == 0), stop=False), "pe", 1)
                c.inc(tensor.matmul(pb[bidx][:, :], ones_row_bf[:, :], dpb1_r[0:1, :],
                                    start=False, stop=True), "pe", 1)
                dr_pe[tt] = c.val("pe")

            @block.scalar
            def _(scalar, tt=tt):
                a = Chain(c, scalar, "actc")
                scalar.wait_ge(c.sems["pe"][0], dr_pe[tt])
                a.activation(out=hsq[:, :], in_=pb[tt % 2][:, :], func=AF.Square)
                dr_ac[(tt, "sq")] = c.val("actc")

            @block.vector
            def _(vector, tt=tt):
                v = Chain(c, vector, "dvec")
                vector.wait_ge(c.sems["actc"][0], dr_ac[(tt, "sq")])
                v.reduce_sum(out=mucol[:, 0:1], in_=pb[tt % 2][:, :], axis=AX.X)
                v.tensor_scalar(out=mucol[:, 0:1], in0=mucol[:, 0:1], scalar1=1.0 / 512,
                                scalar2=None, op0=ALU.mult)
                v.reduce_sum(out=mucol[:, 1:2], in_=hsq[:, :], axis=AX.X)
                v.tensor_scalar(out=mucol[:, 1:2], in0=mucol[:, 1:2], scalar1=1.0 / 512,
                                scalar2=None, op0=ALU.mult)
                v.tensor_tensor(out=mucol[:, 2:3], in0=mucol[:, 0:1], in1=mucol[:, 0:1], op=ALU.mult)
                v.tensor_tensor(out=mucol[:, 1:2], in0=mucol[:, 1:2], in1=mucol[:, 2:3], op=ALU.subtract)
                dr_dv[(tt, "muvar")] = c.val("dvec")

            @block.scalar
            def _(scalar, tt=tt):
                a = Chain(c, scalar, "actc")
                scalar.wait_ge(c.sems["dvec"][0], dr_dv[(tt, "muvar")])
                a.activation(out=mucol[:, 2:3], in_=mucol[:, 1:2], func=AF.Ln, bias=epscol_p[:, 0:1])
                a.activation(out=mucol[:, 2:3], in_=mucol[:, 2:3], func=AF.Exp, scale=nhcol_p[:, 0:1])
                dr_ac[(tt, "rstd")] = c.val("actc")

            @block.vector
            def _(vector, tt=tt):
                v = Chain(c, vector, "dvec")
                vector.wait_ge(c.sems["actc"][0], dr_ac[(tt, "rstd")])
                v.tensor_scalar(out=mucol[:, 3:4], in0=mucol[:, 0:1], scalar1=-1.0,
                                scalar2=None, op0=ALU.mult)
                v.tensor_scalar(out=hln[:, :], in0=pb[tt % 2][:, :], scalar1=mucol[:, 3:4],
                                scalar2=mucol[:, 2:3], op0=ALU.add, op1=ALU.mult)
                v.tensor_tensor(out=hln[:, :], in0=hln[:, :], in1=lng1_r[:, :], op=ALU.mult)
                v.tensor_tensor(out=hln[:, :], in0=hln[:, :], in1=lnb1_r[:, :], op=ALU.add)
                dr_dv[(tt, "done")] = c.val("dvec")

            @block.scalar
            def _(scalar, tt=tt):
                a = Chain(c, scalar, "actc")
                scalar.wait_ge(c.sems["dvec"][0], dr_dv[(tt, "done")])
                a.activation(out=ghb[:, :], in_=hln[:, :], func=AF.Gelu)
                dr_ac[(tt, "gelu")] = c.val("actc")

            @block.vector
            def _(vector, tt=tt):
                v = Chain(c, vector, "dvec")
                vector.wait_ge(c.sems["actc"][0], dr_ac[(tt, "gelu")])
                v.tensor_tensor(out=ghb[:, :], in0=ghb[:, :], in1=dpw2_r[:, :], op=ALU.mult)
                v.reduce_sum(out=mucol[:, 3:4], in_=ghb[:, :], axis=AX.X)
                v.tensor_tensor(out=mucol[:, 3:4], in0=mucol[:, 3:4], in1=dpb2_r[:, 0:1], op=ALU.add)
                v.tensor_scalar(out=hsq[:, 0:16], in0=gw1_r[:, :], scalar1=mucol[:, 3:4],
                                scalar2=None, op0=ALU.mult)
                v.tensor_tensor(out=hsq[:, 0:16], in0=hsq[:, 0:16], in1=gb1_r[:, :], op=ALU.add)
                dr_dv[(tt, "g16")] = c.val("dvec")

            @block.scalar
            def _(scalar, tt=tt):
                a = Chain(c, scalar, "actc")
                scalar.wait_ge(c.sems["dvec"][0], dr_dv[(tt, "g16")])
                a.activation(out=hsq[:, 16:32], in_=hsq[:, 0:16], func=AF.Gelu)
                dr_ac[(tt, "g16g")] = c.val("actc")

            @block.vector
            def _(vector, tt=tt):
                v = Chain(c, vector, "dvec")
                vector.wait_ge(c.sems["actc"][0], dr_ac[(tt, "g16g")])
                v.tensor_tensor(out=hsq[:, 16:32], in0=hsq[:, 16:32], in1=gw2_r[:, :], op=ALU.mult)
                v.reduce_sum(out=mucol[:, 2:3], in_=hsq[:, 16:32], axis=AX.X)
                v.tensor_tensor(out=mucol[:, 2:3], in0=mucol[:, 2:3], in1=gb2_r[:, 0:1], op=ALU.add)
                dr_dv[(tt, "tanhin")] = c.val("dvec")

            @block.scalar
            def _(scalar, tt=tt):
                a = Chain(c, scalar, "actc")
                scalar.wait_ge(c.sems["dvec"][0], dr_dv[(tt, "tanhin")])
                a.activation(out=gateT[:, tt:tt + 1], in_=mucol[:, 2:3], func=AF.Tanh)
                dr_ac[(tt, "gate")] = c.val("actc")

    gate_done = c.val("actc")

    with nc.Block() as block:
        cast_done = {}
        qk_pe = {}
        evac_done = {}
        store_done = {}
        wl_t = {}
        for mc in range(16):
            mg = mc

            @block.sync
            def _(sync, mc=mc):
                sm = "wl0" if mc % 2 == 0 else "wl1"
                if mc >= 2:
                    sync.wait_ge(c.sems["dvec"][0], cast_done[mc - 2])
                c.dma(sync, sm, R1[:, (mc % 2) * 1024:(mc % 2) * 1024 + 1024].rearrange(
                    "p (k m) -> p k m", k=8),
                    wqkv1[:, mc * 128:(mc + 1) * 128].rearrange("(k p) m -> p k m", p=128))
                wl_t[mc] = (sm, c.val(sm))

            @block.vector
            def _(vector, mc=mc):
                v = Chain(c, vector, "dvec")
                sm, tv = wl_t[mc]
                vector.wait_ge(c.sems[sm][0], tv)
                if mc >= 2:
                    vector.wait_ge(c.sems["pe"][0], qk_pe[(mc - 2, 3)])
                v.tensor_copy(out=R3[:, (mc % 2) * 1024:(mc % 2) * 1024 + 1024],
                              in_=R1[:, (mc % 2) * 1024:(mc % 2) * 1024 + 1024])
                cast_done[mc] = c.val("dvec")

            for ch in range(NCH):
                @block.tensor
                def _(tensor, mc=mc, mg=mg, ch=ch):
                    if ch == 0:
                        tensor.wait_ge(c.sems["dvec"][0], cast_done[mc])
                    gidx = mg * 4 + ch
                    bidx = gidx % 3
                    if gidx >= 3:
                        tensor.wait_ge(c.sems["actc"][0], evac_done[gidx - 3])
                    for k in range(8):
                        c.inc(tensor.matmul(
                            pb[bidx][:, :],
                            R3[:, (mc % 2) * 1024 + k * 128:(mc % 2) * 1024 + (k + 1) * 128],
                            xnbt(k)[:, ch * 512:(ch + 1) * 512],
                            start=(k == 0), stop=(k == 7)), "pe", 1)
                    qk_pe[(mc, ch)] = c.val("pe")

                @block.scalar
                def _(scalar, mc=mc, mg=mg, ch=ch):
                    a = Chain(c, scalar, "actc")
                    stg = qkst[mg % 2]
                    gidx = mg * 4 + ch
                    if ch == 0 and mg >= 2:
                        sd = store_done[mg - 2]
                        scalar.wait_ge(c.sems[sd[0]][0], sd[1])
                    scalar.wait_ge(c.sems["pe"][0], qk_pe[(mc, ch)])
                    a.activation(out=stg[:, ch * 512:(ch + 1) * 512], in_=pb[gidx % 3][:, :],
                                 func=AF.Identity, bias=bqk_c[:, mg:mg + 1])
                    evac_done[gidx] = c.val("actc")

            @block.sync
            def _(sync, mg=mg):
                stg = qkst[mg % 2]
                sync.wait_ge(c.sems["actc"][0], evac_done[mg * 4 + 3])
                dst = q_out if mg < 8 else k_out
                r = mg if mg < 8 else mg - 8
                sm = "st0" if mg % 2 == 0 else "st1"
                c.dma(sync, sm, dst[r * 128:(r + 1) * 128, :], stg[:, :])
                store_done[mg] = (sm, c.val(sm))

        vcast = {}
        for j in range(8):
            @block.sync
            def _(sync, j=j):
                sm = "wl0" if j % 2 == 0 else "wl1"
                if j == 0:
                    sync.wait_ge(c.sems["pe"][0], qk_pe[(15, 3)])
                c.dma(sync, sm, R1[:, (j % 2) * 1024:(j % 2) * 1024 + 1024].rearrange(
                    "p (k m) -> p k m", k=8),
                    wqkv1[:, 2048 + j * 128:2048 + (j + 1) * 128].rearrange("(k p) m -> p k m", p=128))
                wl_t[("v", j)] = (sm, c.val(sm))

            @block.vector
            def _(vector, j=j):
                v = Chain(c, vector, "dvec")
                sm, tv = wl_t[("v", j)]
                vector.wait_ge(c.sems[sm][0], tv)
                v.tensor_copy(out=R3[:, j * 1024:(j + 1) * 1024],
                              in_=R1[:, (j % 2) * 1024:(j % 2) * 1024 + 1024])
                vcast[j] = c.val("dvec")

        R3v = R3[:, :].rearrange("p (j x) -> p j x", j=8)
        mm_v = {}
        for tt in range(16):
            for vb in range(2):
                @block.tensor
                def _(tensor, tt=tt, vb=vb):
                    if (tt, vb) == (0, 0):
                        tensor.wait_ge(c.sems["dvec"][0], vcast[7])
                        tensor.wait_ge(c.sems["actc"][0], gate_done)
                    gidx = tt * 2 + vb
                    bidx = gidx % 3
                    if gidx >= 3:
                        tensor.wait_ge(c.sems["actc"][0], evac_done[("v", gidx - 3)])
                    elif gidx < 3:
                        tensor.wait_ge(c.sems["actc"][0], evac_done[61 + gidx])
                    for k in range(8):
                        c.inc(tensor.matmul(
                            pb[bidx][:, :],
                            xnbt(k)[:, tt * 128:(tt + 1) * 128],
                            R3v[:, vb * 4:(vb + 1) * 4, k * 128:(k + 1) * 128],
                            start=(k == 0), stop=False), "pe", 1)
                    c.inc(tensor.matmul(
                        pb[bidx][:, :], ones_row_bf[:, :],
                        bv_row[0:1, vb * 512:(vb + 1) * 512],
                        start=False, stop=True), "pe", 1)
                    mm_v[gidx] = c.val("pe")

                @block.scalar
                def _(scalar, tt=tt, vb=vb):
                    a = Chain(c, scalar, "actc")
                    gidx = tt * 2 + vb
                    stg = vstage[gidx % 2]
                    if gidx >= 2:
                        sd = store_done[("v", gidx - 2)]
                        scalar.wait_ge(c.sems[sd[0]][0], sd[1])
                    scalar.wait_ge(c.sems["pe"][0], mm_v[gidx])
                    a.activation(out=stg[:, :], in_=pb[gidx % 3][:, :], func=AF.Identity,
                                 scale=gateT[:, tt:tt + 1])
                    evac_done[("v", gidx)] = c.val("actc")

                @block.sync
                def _(sync, tt=tt, vb=vb):
                    gidx = tt * 2 + vb
                    sync.wait_ge(c.sems["actc"][0], evac_done[("v", gidx)])
                    sm = "sv0" if gidx % 2 == 0 else "sv1"
                    c.dma(sync, sm, v_out[tt * 128:(tt + 1) * 128, vb * 512:(vb + 1) * 512],
                          vstage[gidx % 2][:, :])
                    store_done[("v", gidx)] = (sm, c.val(sm))

    OC = [(0, 512), (512, 1024), (1024, 1536), (1536, 2048), (2048, 2560), (2560, 2816)]
    xqf4 = xnb[:, 11264:15360]
    hchf = R2[:, 3584:4096]
    g1st = [R1[:, 0:512], R1[:, 512:1024]]
    hqc = c.sb("hqc", [128, 512], BF16)
    wg_slots = [R4[:, 0:4096], R4[:, 4096:8192]]
    wu_slots = [R4[:, 8192:12288], R4[:, 12288:16384]]

    with nc.Block() as block:
        xq_dv = {}
        xq_pe = {}
        wld = {}
        gu_pe = {}
        sil_ac = {}
        h_dv = {}
        hq_pe = {}
        dn_pe = {}
        dn_ld = {}
        g1_ac = {}
        g1_sd = {}
        for grp in range(4):
            for t4 in range(4):
                tt = grp * 4 + t4

                @block.vector
                def _(vector, tt=tt, t4=t4, grp=grp):
                    v = Chain(c, vector, "dvec")
                    if tt == 0:
                        for sm in ["st0", "st1", "sv0", "sv1"]:
                            vector.wait_ge(c.sems[sm][0], c.val(sm))
                    if grp > 0 and t4 == 0:
                        vector.wait_ge(c.sems["pe"][0], dn_pe[(grp - 1, 1, 21)])
                    cs = slice(tt * 128, (tt + 1) * 128)
                    stg = R2[:, 0:1024]
                    for k in range(8):
                        v.tensor_tensor(out=sq0[:, 0:128], in0=xt(k)[:, cs],
                                        in1=r1bK[:, cs], op=ALU.mult)
                        v.tensor_tensor(out=stg[:, k * 128:(k + 1) * 128],
                                        in0=sq0[:, 0:128], in1=r2bK[:, cs], op=ALU.add)
                    xq_dv[(tt, "rc")] = c.val("dvec")

                for k in range(8):
                    @block.tensor
                    def _(tensor, tt=tt, k=k):
                        tensor.wait_ge(c.sems["dvec"][0], xq_dv[(tt, "rc")])
                        if (tt, k) != (0, 0):
                            pk = (tt, k - 1) if k > 0 else (tt - 1, 7)
                            tensor.wait_ge(c.sems["dvec"][0], xq_dv[("ev", pk)])
                        c.inc(tensor.transpose(out=ps_tf[:, :],
                                               in_=R2[:, k * 128:(k + 1) * 128],
                                               identity=eyef[:, :]), "pe", 1)
                        xq_pe[(tt, k)] = c.val("pe")

                    @block.vector
                    def _(vector, tt=tt, k=k):
                        v = Chain(c, vector, "dvec")
                        vector.wait_ge(c.sems["pe"][0], xq_pe[(tt, k)])
                        v.tensor_copy(out=xtokf[:, k * 128:(k + 1) * 128], in_=ps_tf[:, :])
                        xq_dv[("ev", (tt, k))] = c.val("dvec")

                @block.vector
                def _(vector, tt=tt, t4=t4):
                    v = Chain(c, vector, "dvec")
                    emit_fp4_quant(v, xtokq[:, 0:512], xtokf[:, 0:512], scr, 128, 512)
                    emit_fp4_quant(v, xtokq[:, 512:1024], xtokf[:, 512:1024], scr, 128, 512)
                    xq_dv[(tt, "q")] = c.val("dvec")

                for k in range(8):
                    @block.tensor
                    def _(tensor, tt=tt, t4=t4, k=k):
                        tensor.wait_ge(c.sems["dvec"][0], xq_dv[(tt, "q")])
                        if (tt, k) != (0, 0):
                            pk = (tt, k - 1) if k > 0 else (tt - 1, 7)
                            tensor.wait_ge(c.sems["dvec"][0], xq_dv[("evb", pk)])
                        c.inc(tensor.transpose(out=(ps_tb if k % 2 == 0 else ps_tb2)[:, :],
                                               in_=xtokq[:, k * 128:(k + 1) * 128],
                                               identity=eyeb[:, :]), "pe", 1)
                        xq_pe[("b", tt, k)] = c.val("pe")

                    @block.vector
                    def _(vector, tt=tt, t4=t4, k=k):
                        v = Chain(c, vector, "dvec")
                        vector.wait_ge(c.sems["pe"][0], xq_pe[("b", tt, k)])
                        v.tensor_copy(out=xqf4[:, t4 * 1024 + k * 128:t4 * 1024 + (k + 1) * 128],
                                      in_=(ps_tb if k % 2 == 0 else ps_tb2)[:, :])
                        xq_dv[("evb", (tt, k))] = c.val("dvec")

            for oci, (o0, o1) in enumerate(OC):
                ow = o1 - o0
                nkk = ow // 128

                @block.sync
                def _(sync, grp=grp, oci=oci, o0=o0, o1=o1, ow=ow):
                    smg = "dl0" if oci % 2 == 0 else "dl1"
                    gi = grp * 6 + oci
                    if gi >= 2:
                        sync.wait_ge(c.sems["pe"][0], wld[("done", gi - 2)])
                    c.dma(sync, smg, wg_slots[oci % 2][:, 0:ow * 8].rearrange(
                        "p (k s) -> p k s", k=8),
                        wgq_t[:, o0:o1].rearrange("(k p) s -> p k s", p=128))
                    c.dma(sync, smg, wu_slots[oci % 2][:, 0:ow * 8].rearrange(
                        "p (k s) -> p k s", k=8),
                        wuq_t[:, o0:o1].rearrange("(k p) s -> p k s", p=128))
                    wld[(grp, oci)] = (smg, c.val(smg))

                for t4 in range(4):
                    tt = grp * 4 + t4
                    u = oci * 4 + t4

                    @block.tensor
                    def _(tensor, grp=grp, oci=oci, t4=t4, o0=o0, ow=ow, u=u):
                        if t4 == 0:
                            sm, tv = wld[(grp, oci)]
                            tensor.wait_ge(c.sems[sm][0], tv)
                        if oci == 0 and t4 == 0:
                            tensor.wait_ge(c.sems["dvec"][0], xq_dv[("evb", (grp * 4 + 3, 7))])
                            if grp > 0:
                                tensor.wait_ge(c.sems["actc"][0], g1_ac[(grp - 1, 1, 3)])
                        if u >= 1:
                            tensor.wait_ge(c.sems["actc"][0], sil_ac[(grp, u - 1)])
                            tensor.wait_ge(c.sems["dvec"][0], h_dv[(grp, u - 1)])
                        for k in range(8):
                            c.inc(tensor.matmul(
                                pb[0][:, 0:ow],
                                xqf4[:, t4 * 1024 + k * 128:t4 * 1024 + (k + 1) * 128],
                                wg_slots[oci % 2][:, k * ow:(k + 1) * ow],
                                start=(k == 0), stop=(k == 7)), "pe", 1)
                        for k in range(8):
                            c.inc(tensor.matmul(
                                pb[1][:, 0:ow],
                                xqf4[:, t4 * 1024 + k * 128:t4 * 1024 + (k + 1) * 128],
                                wu_slots[oci % 2][:, k * ow:(k + 1) * ow],
                                start=(k == 0), stop=(k == 7)), "pe", 1)
                        gu_pe[(grp, oci, t4)] = c.val("pe")
                        wld[("done", grp * 6 + oci)] = c.val("pe")

                    @block.scalar
                    def _(scalar, grp=grp, oci=oci, t4=t4, ow=ow, u=u):
                        a = Chain(c, scalar, "actc")
                        scalar.wait_ge(c.sems["pe"][0], gu_pe[(grp, oci, t4)])
                        if u >= 1:
                            scalar.wait_ge(c.sems["dvec"][0], h_dv[(grp, u - 1)])
                        a.activation(out=sgc[:, 0:ow], in_=pb[0][:, 0:ow], func=AF.Silu)
                        sil_ac[(grp, u)] = c.val("actc")

                    @block.vector
                    def _(vector, grp=grp, oci=oci, t4=t4, o0=o0, ow=ow, u=u):
                        v = Chain(c, vector, "dvec")
                        vector.wait_ge(c.sems["actc"][0], sil_ac[(grp, u)])
                        v.tensor_tensor(out=hchf[:, 0:ow], in0=sgc[:, 0:ow],
                                        in1=pb[1][:, 0:ow], op=ALU.mult)
                        emit_fp4_quant(v, hqc[:, 0:ow], hchf[:, 0:ow], scr, 128, ow)
                        h_dv[(grp, u)] = c.val("dvec")

                    for jj in range(nkk):
                        kk = o0 // 128 + jj
                        pt = ps_tb if (kk + t4) % 2 == 0 else ps_tb2

                        @block.tensor
                        def _(tensor, grp=grp, t4=t4, u=u, jj=jj, kk=kk, pt=pt):
                            tensor.wait_ge(c.sems["dvec"][0], h_dv[(grp, u)])
                            key = ("tbprev", id(pt))
                            if key in hq_pe:
                                tensor.wait_ge(c.sems["dvec"][0], hq_pe[key])
                            c.inc(tensor.transpose(out=pt[:, :],
                                                   in_=hqc[:, jj * 128:(jj + 1) * 128],
                                                   identity=eyeb[:, :]), "pe", 1)
                            hq_pe[(grp, kk, t4)] = c.val("pe")

                        @block.vector
                        def _(vector, grp=grp, t4=t4, jj=jj, kk=kk, pt=pt):
                            v = Chain(c, vector, "dvec")
                            vector.wait_ge(c.sems["pe"][0], hq_pe[(grp, kk, t4)])
                            v.tensor_copy(out=hq4[:, kk * 512 + t4 * 128:kk * 512 + (t4 + 1) * 128],
                                          in_=pt[:, :])
                            hq_pe[("tbprev", id(pt))] = c.val("dvec")
                            hq_pe[("ev", grp, kk, t4)] = c.val("dvec")

            for pass_ in range(2):
                for k in range(22):
                    @block.sync
                    def _(sync, grp=grp, pass_=pass_, k=k):
                        sm = f"dw{k % 3}"
                        gi = (grp * 2 + pass_) * 22 + k
                        if (pass_, k) > (0, 2) or grp > 0:
                            pk = (pass_, k - 3) if k >= 3 else (pass_ - 1, 19 + k)
                            if pk[0] < 0:
                                if grp > 0:
                                    sync.wait_ge(c.sems["pe"][0], dn_pe[(grp - 1, 1, 19 + k)])
                            else:
                                sync.wait_ge(c.sems["pe"][0], dn_pe[(grp, pk[0], pk[1])])
                        c.dma(sync, sm, R3[:, (k % 3) * 1024:(k % 3 + 1) * 1024],
                              wdq_t[k * 128:(k + 1) * 128, :])
                        dn_ld[(grp, pass_, k)] = (sm, c.val(sm))

                    @block.tensor
                    def _(tensor, grp=grp, pass_=pass_, k=k):
                        sm, tv = dn_ld[(grp, pass_, k)]
                        tensor.wait_ge(c.sems[sm][0], tv)
                        if pass_ == 0:
                            tensor.wait_ge(c.sems["dvec"][0], hq_pe[("ev", grp, k, 3)])
                        if k == 0:
                            if pass_ == 0:
                                tensor.wait_ge(c.sems["actc"][0], sil_ac[(grp, 23)])
                                tensor.wait_ge(c.sems["dvec"][0], h_dv[(grp, 23)])
                            else:
                                tensor.wait_ge(c.sems["actc"][0], g1_ac[(grp, 0, 3)])
                        for mi in range(4):
                            m = pass_ * 4 + mi
                            c.inc(tensor.matmul(
                                pb[mi][:, :],
                                R3[:, (k % 3) * 1024 + m * 128:(k % 3) * 1024 + (m + 1) * 128],
                                hq4[:, k * 512:(k + 1) * 512],
                                start=(k == 0), stop=(k == 21)), "pe", 1)
                        dn_pe[(grp, pass_, k)] = c.val("pe")

                for mi in range(4):
                    m = pass_ * 4 + mi

                    @block.scalar
                    def _(scalar, grp=grp, pass_=pass_, mi=mi, m=m):
                        a = Chain(c, scalar, "actc")
                        scalar.wait_ge(c.sems["pe"][0], dn_pe[(grp, pass_, 21)])
                        gsl = g1st[mi % 2]
                        key = (grp * 2 + pass_) * 4 + mi - 2
                        if key >= 0:
                            sd = g1_sd[key]
                            scalar.wait_ge(c.sems[sd[0]][0], sd[1])
                        a.activation(out=gsl[:, :], in_=pb[mi][:, :], func=AF.Silu)
                        g1_ac[(grp, pass_, mi)] = c.val("actc")

                    @block.sync
                    def _(sync, grp=grp, pass_=pass_, mi=mi, m=m):
                        sync.wait_ge(c.sems["actc"][0], g1_ac[(grp, pass_, mi)])
                        sm = "sg0" if mi % 2 == 0 else "sg1"
                        c.dma(sync, sm, g_out[m * 128:(m + 1) * 128, grp * 512:(grp + 1) * 512],
                              g1st[mi % 2][:, :])
                        g1_sd[(grp * 2 + pass_) * 4 + mi] = (sm, c.val(sm))

    with nc.Block() as block:
        @block.sync
        def _(sync):
            for sm in ["st", "st0", "st1", "sv0", "sv1", "sg0", "sg1", "sx"]:
                c.wait(sync, sm)

    c.close()
    return c


def build_l2a():
    c = Ctx("l2a")
    nc = c.nc
    DT = 8
    NCH = 4
    ED = 2048

    x_in = c.din("x1_feat", [D, T], F32)
    ans_in = c.din("ans1", [D, T], BF16)
    wo1 = c.din("wo1", [D, D], F32)
    bo1_col = c.din("bo1_col", [128, 8], F32)
    g1_in = c.din("g1", [D, T], F32)
    win_t = c.din("win_t", [D, 2 * ED], F32)
    wdt_row_in = c.din("wdt_row", [1, ED], F32)
    A_in = c.din("A_", [ED, 16], F32)
    Bm_in = c.din("Bm_", [ED, 16], F32)
    convw_col = c.din("convw_col", [128, 48], F32)
    gw1_t = c.din("gw1_t", [D, D], F32)
    gb1_col = c.din("gb1_col", [128, 8], F32)

    x2_out = c.dout("x2", [D, T], F32)
    xn_out = c.dout("xn2", [D, T], BF16)
    xc_out = c.dout("xconv", [ED, T], BF16)
    xs_out = c.dout("xssm", [ED, T], BF16)
    ml_out = c.dout("mloc", [16, T], F32)
    ap_out = c.dout("aprod", [16, T], F32)
    pe_out = c.dout("pedge", [ED, 2], F32)
    hs_out = c.dout("hsum", [128, 8], F32)

    xs = c.sb("xs", [128, DT * T], F32)
    xnb = c.sb("xnb", [128, DT * T], BF16)
    R1 = c.sb("R1", [128, 2048], F32)
    R2 = c.sb("R2", [128, 4096], F32)
    R3 = c.sb("R3", [128, 8192], BF16)
    R4 = c.sb("R4", [128, 16384], BF16)
    rows4 = R2[0:1, 0:2048]
    murow = rows4[0:1, 0:512]
    varrow = rows4[0:1, 512:1024]
    rstdrow = rows4[0:1, 1024:1536]
    r2row = rows4[0:1, 1536:2048]
    sq0 = c.sb("sq0", [128, 512], F32)
    sq1 = c.sb("sq1", [128, 512], F32)
    r1b = R1[:, 0:2048]
    r2b = c.sb("r2b", [128, 2048], F32)
    ones_col_f32 = c.sb("ones_col_f32", [128, 1], F32)
    ones_row_f32 = c.sb("ones_row_f32", [1, 128], F32)
    ones_col_bf = c.sb("ones_col_bf", [128, 1], BF16)
    epscol = c.sb("epscol", [1, 1], F32)
    nhcol = c.sb("nhcol", [1, 1], F32)
    bo1_c = c.sb("bo1_c", [128, 8], F32)
    gb1_c = c.sb("gb1_c", [128, 8], F32)
    cw_c = c.sb("cw_c", [128, 48], F32)
    wdt_r = c.sb("wdt_r", [1, ED], F32)
    Ab = c.sb("Ab", [128, 16 * 16], BF16)
    Bb = c.sb("Bb", [128, 16 * 16], BF16)
    pedge_s = c.sb("pedge_s", [128, 2 * 16], F32)
    hsum = c.sb("hsum_s", [128, 8], F32)
    dtrow = xs[0:1, 0:2048]
    a_s = xs[0:16, 2048:4096]
    u_s = xs[0:16, 4096:6144]
    ones16 = xs[0:16, 6144:8192]
    mres = xs[0:16, 8192:10240]
    apres = xs[0:16, 10240:12288]
    hrel = xs[:, 12288:12800]
    pct = R4[:, 0:2048]
    cout = R4[:, 2048:4096]
    dtvb = R4[:, 4096:4608]
    ub = R4[:, 4608:5120]
    cacc = R2[:, 0:2048]

    pb = [c.ps(f"pb{i}", [128, 512], F32) for i in range(4)]
    stackP = ExitStack()
    ps_row = stackP.enter_context(nc.psum_tensor("ps_row", [1, 512], F32))
    ps_row2 = stackP.enter_context(nc.psum_tensor("ps_row2", [1, 512], F32))

    for s in ["ld", "lg0", "lg1", "la0", "la1", "wl0", "wl1", "xr0", "xr1",
              "pe", "actc", "dvec", "st", "sc0", "sc1", "ss0", "ss1", "sx"]:
        c.sem(s)

    def xt(i):
        return xs[:, i * T:(i + 1) * T]

    def xnbt(i):
        return xnb[:, i * T:(i + 1) * T]

    prm = dict(sq0=sq0, sq1=sq1, murow=murow, varrow=varrow, rstdrow=rstdrow,
               r2row=r2row, r1b=r1b, r2b=r2b, ones_col_f32=ones_col_f32,
               ones_row_f32=ones_row_f32, epscol=epscol, nhcol=nhcol,
               ps_row=ps_row, ps_row2=ps_row2, bcA=pb[2], bcB=pb[3])

    with nc.Block() as block:
        @block.sync
        def _(sync):
            for i in range(DT):
                c.dma(sync, "ld", xt(i), x_in[i * 128:(i + 1) * 128, :])
            for dst, srcap in [(bo1_c[:, :], bo1_col), (gb1_c[:, :], gb1_col),
                               (cw_c[:, :], convw_col), (wdt_r[:, :], wdt_row_in)]:
                c.dma(sync, "ld", dst, srcap)
            for m in range(16):
                c.dma(sync, "ld", R2[:, m * 16:(m + 1) * 16], A_in[m * 128:(m + 1) * 128, :])
            for m in range(16):
                c.dma(sync, "ld", R2[:, 256 + m * 16:256 + (m + 1) * 16],
                      Bm_in[m * 128:(m + 1) * 128, :])

        @block.gpsimd
        def _(gpsimd):
            gpsimd.memset(ones_col_f32[:, :], 1.0)
            gpsimd.memset(ones_row_f32[:, :], 1.0)
            gpsimd.memset(ones_col_bf[:, :], 1.0)
            gpsimd.memset(epscol[:, :], LN_EPS)
            gpsimd.memset(nhcol[:, :], -0.5)

        @block.vector
        def _(vector):
            v = Chain(c, vector, "dvec")
            c.wait(vector, "ld")
            v.tensor_copy(out=Ab[:, :], in_=R2[:, 0:256])
            v.tensor_copy(out=Bb[:, :], in_=R2[:, 256:512])

    with nc.Block() as block:
        cast_dv = {}
        for j in range(8):
            @block.sync
            def _(sync, j=j):
                sm = "wl0" if j % 2 == 0 else "wl1"
                if j >= 2:
                    sync.wait_ge(c.sems["dvec"][0], cast_dv[j - 2])
                c.dma(sync, sm, R1[:, (j % 2) * 1024:(j % 2) * 1024 + 1024].rearrange(
                    "p (k m) -> p k m", k=8),
                    wo1[:, j * 128:(j + 1) * 128].rearrange("(k p) m -> p k m", p=128))
                cast_dv[("wl", j)] = (sm, c.val(sm))

            @block.vector
            def _(vector, j=j):
                v = Chain(c, vector, "dvec")
                sm, tv = cast_dv[("wl", j)]
                vector.wait_ge(c.sems[sm][0], tv)
                v.tensor_copy(out=R3[:, j * 1024:(j + 1) * 1024],
                              in_=R1[:, (j % 2) * 1024:(j % 2) * 1024 + 1024])
                cast_dv[j] = c.val("dvec")
    wo1_done = c.val("dvec")

    ansring = [R4[:, 0:4096], R4[:, 4096:8192]]
    with nc.Block() as block:
        s_pe = {}
        s_ev = {}
        xu_dv = {}
        ans_ld = {}
        g_ld = {}
        for qt in range(4):
            @block.sync
            def _(sync, qt=qt):
                sma = "la0" if qt % 2 == 0 else "la1"
                if qt >= 2:
                    sync.wait_ge(c.sems["pe"][0], s_pe[(qt - 2, 7)])
                c.dma(sync, sma, ansring[qt % 2][:, :].rearrange("p (k s) -> p k s", k=8),
                      ans_in.rearrange("(k p) t -> p k t", p=128)[:, :, qt * 512:(qt + 1) * 512])
                ans_ld[qt] = (sma, c.val(sma))
                for ihalf in range(2):
                    smg = "lg0" if ihalf == 0 else "lg1"
                    if qt >= 1:
                        sync.wait_ge(c.sems["dvec"][0], xu_dv[(qt - 1, ihalf * 4 + 3)])
                    c.dma(sync, smg, R2[:, ihalf * 2048:ihalf * 2048 + 2048].rearrange(
                        "p (k s) -> p k s", k=4),
                        g1_in.rearrange("(k p) t -> p k t", p=128)[:, ihalf * 4:(ihalf + 1) * 4,
                                                                  qt * 512:(qt + 1) * 512])
                    g_ld[(qt, ihalf)] = (smg, c.val(smg))

            for i in range(DT):
                @block.tensor
                def _(tensor, qt=qt, i=i):
                    if i == 0:
                        if qt == 0:
                            tensor.wait_ge(c.sems["dvec"][0], wo1_done)
                        sm, tv = ans_ld[qt]
                        tensor.wait_ge(c.sems[sm][0], tv)
                    gidx = qt * 8 + i
                    bidx = gidx % 2
                    if gidx >= 2:
                        tensor.wait_ge(c.sems["actc"][0], s_ev[gidx - 2])
                    for k in range(8):
                        c.inc(tensor.matmul(
                            pb[bidx][:, :],
                            R3[:, i * 1024 + k * 128:i * 1024 + (k + 1) * 128],
                            ansring[qt % 2][:, k * 512:(k + 1) * 512],
                            start=(k == 0), stop=(k == 7)), "pe", 1)
                    s_pe[(qt, i)] = c.val("pe")

                @block.scalar
                def _(scalar, qt=qt, i=i):
                    a = Chain(c, scalar, "actc")
                    gidx = qt * 8 + i
                    scalar.wait_ge(c.sems["pe"][0], s_pe[(qt, i)])
                    if gidx >= 1:
                        pq, pi = divmod(gidx - 1, 8)
                        scalar.wait_ge(c.sems["dvec"][0], xu_dv[(pq, pi)])
                    a.activation(out=sq0[:, :], in_=pb[gidx % 2][:, :],
                                 func=AF.Identity, bias=bo1_c[:, i:i + 1])
                    s_ev[gidx] = c.val("actc")

                @block.vector
                def _(vector, qt=qt, i=i):
                    v = Chain(c, vector, "dvec")
                    gidx = qt * 8 + i
                    sm, tv = g_ld[(qt, i // 4)]
                    vector.wait_ge(c.sems[sm][0], tv)
                    vector.wait_ge(c.sems["actc"][0], s_ev[gidx])
                    gsl = R2[:, (i // 4) * 2048 + (i % 4) * 512:(i // 4) * 2048 + (i % 4 + 1) * 512]
                    cs = slice(qt * 512, (qt + 1) * 512)
                    v.tensor_tensor(out=sq1[:, :], in0=sq0[:, :], in1=gsl, op=ALU.mult)
                    v.tensor_tensor(out=xt(i)[:, cs], in0=xt(i)[:, cs], in1=sq1[:, :], op=ALU.add)
                    xu_dv[(qt, i)] = c.val("dvec")

        @block.sync
        def _(sync):
            c.wait(sync, "dvec")
            for i in range(DT):
                c.dma(sync, "sx", x2_out[i * 128:(i + 1) * 128, :], xt(i))

    def apply_ln(v):
        for i in range(DT):
            for ch in range(NCH):
                cs = slice(ch * 512, (ch + 1) * 512)
                v.tensor_tensor(out=sq0[:, :], in0=xt(i)[:, cs], in1=r1b[:, cs], op=ALU.mult)
                v.tensor_tensor(out=sq1[:, :], in0=sq0[:, :], in1=r2b[:, cs], op=ALU.add)
                v.tensor_copy(out=xnbt(i)[:, cs], in_=sq1[:, :])

    emit_ln_feat(c, nc, xt, DT, T, prm, apply_ln)
    stackP.close()
    ps_dt = [c.ps(f"ps_dt{j}", [1, 512], F32) for j in range(4)]

    with nc.Block() as block:
        @block.sync
        def _(sync):
            c.wait(sync, "dvec")
            for i in range(DT):
                c.dma(sync, "st", xn_out[i * 128:(i + 1) * 128, :], xnbt(i))

    with nc.Block() as block:
        @block.gpsimd
        def _(gpsimd):
            c.wait(gpsimd, "sx")
            gpsimd.memset(ones16, 1.0)

        wl_t = {}
        cast_done = {}
        pj_pe = {}
        pj_ev = {}
        conv_dv = {}
        st_done = {}
        for mc in range(32):
            is_ssm = mc >= 16

            @block.sync
            def _(sync, mc=mc):
                sm = "wl0" if mc % 2 == 0 else "wl1"
                if mc >= 2:
                    sync.wait_ge(c.sems["dvec"][0], cast_done[mc - 2])
                c.dma(sync, sm, R1[:, (mc % 2) * 1024:(mc % 2) * 1024 + 1024].rearrange(
                    "p (k m) -> p k m", k=8),
                    win_t[:, mc * 128:(mc + 1) * 128].rearrange("(k p) m -> p k m", p=128))
                wl_t[mc] = (sm, c.val(sm))

            @block.vector
            def _(vector, mc=mc):
                v = Chain(c, vector, "dvec")
                sm, tv = wl_t[mc]
                vector.wait_ge(c.sems[sm][0], tv)
                if mc >= 2:
                    vector.wait_ge(c.sems["pe"][0], pj_pe[(mc - 2, 3)])
                dst = R3[:, (mc % 2) * 1024:(mc % 2) * 1024 + 1024]
                v.tensor_copy(out=dst, in_=R1[:, (mc % 2) * 1024:(mc % 2) * 1024 + 1024])
                cast_done[mc] = c.val("dvec")

            for ch in range(NCH):
                @block.tensor
                def _(tensor, mc=mc, ch=ch):
                    if ch == 0:
                        tensor.wait_ge(c.sems["dvec"][0], cast_done[mc])
                    gidx = mc * 4 + ch
                    bidx = gidx % 2
                    if gidx >= 2:
                        tensor.wait_ge(c.sems["actc"][0], pj_ev[gidx - 2])
                    for k in range(8):
                        c.inc(tensor.matmul(
                            pb[bidx][:, :],
                            R3[:, (mc % 2) * 1024 + k * 128:(mc % 2) * 1024 + (k + 1) * 128],
                            xnbt(k)[:, ch * 512:(ch + 1) * 512],
                            start=(k == 0), stop=(k == 7)), "pe", 1)
                    pj_pe[(mc, ch)] = c.val("pe")

                @block.scalar
                def _(scalar, mc=mc, ch=ch):
                    a = Chain(c, scalar, "actc")
                    gidx = mc * 4 + ch
                    scalar.wait_ge(c.sems["pe"][0], pj_pe[(mc, ch)])
                    if ch == 0 and mc >= 1:
                        if mc - 1 < 16:
                            scalar.wait_ge(c.sems["dvec"][0], conv_dv[mc - 1])
                        else:
                            scalar.wait_ge(c.sems["pe"][0], pj_pe[("dt", mc - 1)])
                            sd = st_done[("s", mc - 1)]
                            scalar.wait_ge(c.sems[sd[0]][0], sd[1])
                    a.activation(out=pct[:, ch * 512:(ch + 1) * 512], in_=pb[gidx % 2][:, :],
                                 func=AF.Copy)
                    pj_ev[gidx] = c.val("actc")

            if not is_ssm:
                @block.vector
                def _(vector, mc=mc):
                    v = Chain(c, vector, "dvec")
                    vector.wait_ge(c.sems["actc"][0], pj_ev[mc * 4 + 3])
                    if mc >= 1:
                        sd = st_done[("c", mc - 1)]
                        vector.wait_ge(c.sems[sd[0]][0], sd[1])
                    w0 = cw_c[:, 0 * 16 + mc:0 * 16 + mc + 1]
                    w1 = cw_c[:, 1 * 16 + mc:1 * 16 + mc + 1]
                    w2 = cw_c[:, 2 * 16 + mc:2 * 16 + mc + 1]
                    v.tensor_scalar(out=cacc[:, 0:T], in0=pct[:, :], scalar1=w1,
                                    scalar2=None, op0=ALU.mult)
                    v.scalar_tensor_tensor(out=cacc[:, 1:T], in0=pct[:, 0:T - 1], scalar=w0,
                                           in1=cacc[:, 1:T], op0=ALU.mult, op1=ALU.add)
                    v.scalar_tensor_tensor(out=cacc[:, 0:T - 1], in0=pct[:, 1:T], scalar=w2,
                                           in1=cacc[:, 0:T - 1], op0=ALU.mult, op1=ALU.add)
                    v.tensor_copy(out=cout[:, :], in_=cacc[:, 0:T])
                    v.tensor_copy(out=pedge_s[:, 2 * mc:2 * mc + 1], in_=pct[:, 0:1])
                    v.tensor_copy(out=pedge_s[:, 2 * mc + 1:2 * mc + 2], in_=pct[:, T - 1:T])
                    conv_dv[mc] = c.val("dvec")

                @block.sync
                def _(sync, mc=mc):
                    sync.wait_ge(c.sems["dvec"][0], conv_dv[mc])
                    sm = "sc0" if mc % 2 == 0 else "sc1"
                    c.dma(sync, sm, xc_out[mc * 128:(mc + 1) * 128, :], cout[:, :])
                    st_done[("c", mc)] = (sm, c.val(sm))
            else:
                ms = mc - 16

                @block.tensor
                def _(tensor, mc=mc, ms=ms):
                    tensor.wait_ge(c.sems["actc"][0], pj_ev[mc * 4 + 3])
                    for ch in range(NCH):
                        c.inc(tensor.matmul(
                            ps_dt[ch][:, :], ones_col_bf[:, :],
                            pct[:, ch * 512:(ch + 1) * 512],
                            start=(ms == 0), stop=(ms == 15)), "pe", 1)
                    pj_pe[("dt", mc)] = c.val("pe")

                @block.sync
                def _(sync, mc=mc, ms=ms):
                    sync.wait_ge(c.sems["pe"][0], pj_pe[("dt", mc)])
                    sm = "ss0" if mc % 2 == 0 else "ss1"
                    c.dma(sync, sm, xs_out[ms * 128:(ms + 1) * 128, :], pct[:, :])
                    st_done[("s", mc)] = (sm, c.val(sm))

        @block.vector
        def _(vector):
            v = Chain(c, vector, "dvec")
            vector.wait_ge(c.sems["pe"][0], pj_pe[("dt", 31)])
            c.wait(vector, "sx")
            for ch in range(NCH):
                v.tensor_scalar(out=dtrow[0:1, ch * 512:(ch + 1) * 512], in0=ps_dt[ch][:, :],
                                scalar1=1.0 / ED, scalar2=None, op0=ALU.mult)

        @block.sync
        def _(sync):
            c.wait(sync, "dvec")
            for m in range(16):
                c.dma(sync, "st", pe_out[m * 128:(m + 1) * 128, :], pedge_s[:, 2 * m:2 * m + 2])

    with nc.Block() as block:
        xr_ld = {}
        dtv_pe = {}
        dtv_dv = {}
        au_pe = {}
        for ch in range(NCH):
            for m in range(16):
                @block.sync
                def _(sync, ch=ch, m=m):
                    g = ch * 16 + m
                    sm = "xr0" if g % 2 == 0 else "xr1"
                    if g == 0:
                        c.wait(sync, "ss0")
                        c.wait(sync, "ss1")
                    if g >= 2:
                        sync.wait_ge(c.sems["dvec"][0], dtv_dv[g - 2])
                    c.dma(sync, sm, R3[:, (g % 2) * 512:(g % 2) * 512 + 512],
                          xs_out[m * 128:(m + 1) * 128, ch * 512:(ch + 1) * 512])
                    xr_ld[g] = (sm, c.val(sm))

                @block.tensor
                def _(tensor, ch=ch, m=m):
                    g = ch * 16 + m
                    if g >= 1:
                        tensor.wait_ge(c.sems["dvec"][0], dtv_dv[g - 1])
                    c.inc(tensor.matmul(pb[2][:, :], wdt_r[0:1, m * 128:(m + 1) * 128],
                                        dtrow[0:1, ch * 512:(ch + 1) * 512],
                                        start=True, stop=True), "pe", 1)
                    dtv_pe[g] = c.val("pe")

                @block.vector
                def _(vector, ch=ch, m=m):
                    v = Chain(c, vector, "dvec")
                    g = ch * 16 + m
                    sm, tv = xr_ld[g]
                    vector.wait_ge(c.sems[sm][0], tv)
                    vector.wait_ge(c.sems["pe"][0], dtv_pe[g])
                    if g >= 1:
                        vector.wait_ge(c.sems["pe"][0], au_pe[g - 1])
                    v.tensor_copy(out=dtvb[:, :], in_=pb[2][:, :])
                    v.tensor_tensor(out=ub[:, :], in0=R3[:, (g % 2) * 512:(g % 2) * 512 + 512],
                                    in1=dtvb[:, :], op=ALU.mult)
                    dtv_dv[g] = c.val("dvec")

                @block.tensor
                def _(tensor, ch=ch, m=m):
                    g = ch * 16 + m
                    tensor.wait_ge(c.sems["dvec"][0], dtv_dv[g])
                    if m == 0 and ch >= 1:
                        tensor.wait_ge(c.sems["dvec"][0], dtv_dv[("evac", ch - 1)])
                    c.inc(tensor.matmul(pb[0][0:16, :], Ab[:, m * 16:(m + 1) * 16],
                                        dtvb[:, :], start=(m == 0), stop=(m == 15)), "pe", 1)
                    c.inc(tensor.matmul(pb[1][0:16, :], Bb[:, m * 16:(m + 1) * 16],
                                        ub[:, :], start=(m == 0), stop=(m == 15)), "pe", 1)
                    au_pe[g] = c.val("pe")

            @block.vector
            def _(vector, ch=ch):
                v = Chain(c, vector, "dvec")
                vector.wait_ge(c.sems["pe"][0], au_pe[ch * 16 + 15])
                cs = slice(ch * 512, (ch + 1) * 512)
                v.tensor_copy(out=a_s[:, cs], in_=pb[0][0:16, :])
                v.tensor_copy(out=u_s[:, cs], in_=pb[1][0:16, :])
                dtv_dv[("evac", ch)] = c.val("dvec")

        @block.vector
        def _(vector):
            v = Chain(c, vector, "dvec")
            v.tensor_tensor_scan(out=mres, data0=a_s, data1=u_s,
                                 initial=0.0, op0=ALU.mult, op1=ALU.add)
            v.tensor_tensor_scan(out=apres, data0=a_s, data1=ones16,
                                 initial=1.0, op0=ALU.mult, op1=ALU.mult)

        @block.sync
        def _(sync):
            c.wait(sync, "dvec")
            c.dma(sync, "st", ml_out[0:16, :], mres)
            c.dma(sync, "st", ap_out[0:16, :], apres)

    with nc.Block() as block:
        wl_t2 = {}
        cast2 = {}
        gp_pe = {}
        gp_ev = {}
        hs_dv = {}
        for mc in range(8):
            @block.sync
            def _(sync, mc=mc):
                sm = "wl0" if mc % 2 == 0 else "wl1"
                if mc >= 2:
                    sync.wait_ge(c.sems["dvec"][0], cast2[mc - 2])
                c.dma(sync, sm, R1[:, (mc % 2) * 1024:(mc % 2) * 1024 + 1024].rearrange(
                    "p (k m) -> p k m", k=8),
                    gw1_t[:, mc * 128:(mc + 1) * 128].rearrange("(k p) m -> p k m", p=128))
                wl_t2[mc] = (sm, c.val(sm))

            @block.vector
            def _(vector, mc=mc):
                v = Chain(c, vector, "dvec")
                sm, tv = wl_t2[mc]
                vector.wait_ge(c.sems[sm][0], tv)
                if mc >= 2:
                    vector.wait_ge(c.sems["pe"][0], gp_pe[(mc - 2, 3)])
                v.tensor_copy(out=R3[:, (mc % 2) * 1024:(mc % 2) * 1024 + 1024],
                              in_=R1[:, (mc % 2) * 1024:(mc % 2) * 1024 + 1024])
                cast2[mc] = c.val("dvec")

            for ch in range(NCH):
                @block.tensor
                def _(tensor, mc=mc, ch=ch):
                    if ch == 0:
                        tensor.wait_ge(c.sems["dvec"][0], cast2[mc])
                    gidx = mc * 4 + ch
                    bidx = gidx % 2
                    if gidx >= 2:
                        tensor.wait_ge(c.sems["actc"][0], gp_ev[gidx - 2])
                    for k in range(8):
                        c.inc(tensor.matmul(
                            pb[bidx][:, :],
                            R3[:, (mc % 2) * 1024 + k * 128:(mc % 2) * 1024 + (k + 1) * 128],
                            xnbt(k)[:, ch * 512:(ch + 1) * 512],
                            start=(k == 0), stop=(k == 7)), "pe", 1)
                    gp_pe[(mc, ch)] = c.val("pe")

                @block.scalar
                def _(scalar, mc=mc, ch=ch):
                    a = Chain(c, scalar, "actc")
                    gidx = mc * 4 + ch
                    scalar.wait_ge(c.sems["pe"][0], gp_pe[(mc, ch)])
                    if gidx >= 1:
                        scalar.wait_ge(c.sems["dvec"][0], hs_dv[gidx - 1])
                    a.activation(out=hrel, in_=pb[gidx % 2][:, :],
                                 func=AF.Relu, bias=gb1_c[:, mc:mc + 1])
                    gp_ev[gidx] = c.val("actc")

                @block.vector
                def _(vector, mc=mc, ch=ch):
                    v = Chain(c, vector, "dvec")
                    gidx = mc * 4 + ch
                    vector.wait_ge(c.sems["actc"][0], gp_ev[gidx])
                    v.reduce_sum(out=sq0[:, 0:1], in_=hrel, axis=AX.X)
                    if ch == 0:
                        v.tensor_copy(out=hsum[:, mc:mc + 1], in_=sq0[:, 0:1])
                    else:
                        v.tensor_tensor(out=hsum[:, mc:mc + 1], in0=hsum[:, mc:mc + 1],
                                        in1=sq0[:, 0:1], op=ALU.add)
                    hs_dv[gidx] = c.val("dvec")

        @block.sync
        def _(sync):
            c.wait(sync, "dvec")
            c.dma(sync, "st", hs_out, hsum[:, :])

    with nc.Block() as block:
        @block.sync
        def _(sync):
            for sm in ["st", "sc0", "sc1", "ss0", "ss1", "sx"]:
                c.wait(sync, sm)

    c.close()
    return c


def build_l2b():
    c = Ctx("l2b")
    nc = c.nc
    NCH = 4
    ED = 2048

    xn_in = c.din("xn2", [D, T], BF16)
    xc_in = c.din("xconv", [ED, T], BF16)
    xs_in = c.din("xssm", [ED, T], BF16)
    ml_in = c.din("mloc", [16, T], F32)
    ap_in = c.din("aprod", [16, T], F32)
    me3_in = c.din("mend3", [16, 3], F32)
    ae3_in = c.din("aend3", [16, 3], F32)
    halo_in = c.din("halo_col", [128, 32], F32)
    convw_col = c.din("convw_col", [128, 48], F32)
    Cm_in = c.din("Cm_", [16, ED], F32)
    dv_in = c.din("dv_col", [128, 16], F32)
    wout_t = c.din("wout_t", [ED, D], F32)
    hs_in = c.din("hsums_all", [128, 32], F32)
    gw2_in = c.din("gw2_col", [128, 64], F32)
    gb2_in = c.din("gb2_row", [1, 8], F32)
    tw1_in = c.din("tw1q", [8, D, 1024], F32)
    tw2_in = c.din("tw2q", [8, 1024, D], F32)
    w1q_in = c.din("w1tq", [D, 1024], F32)
    w2q_in = c.din("w2tq", [1024, D], F32)
    tb1_in = c.din("Tb1_", [8, 4096], F32)
    b1_in = c.din("b1_row", [1, 4096], F32)
    tb2_in = c.din("Tb2_", [8, 1024], F32)
    b2_in = c.din("b2_row", [1, 1024], F32)
    eyeb_in = c.din("eyeb", [128, 128], BF16)

    s2_out = c.dout("s2", [D, T], F32)
    mw1_out = c.dout("mw1q", [D, 1024], BF16)
    mw2_out = c.dout("mw2q", [1024, D], BF16)
    mb1_out = c.dout("mb1", [1, 4096], F32)
    mb2_out = c.dout("mb2", [1, 1024], F32)

    xnb = c.sb("xnb", [128, 8 * T], BF16)
    woutb = c.sb("woutb", [128, 16 * 1024], BF16)
    xcq = c.sb("xcq", [128, 16 * 512], BF16)
    xsq = c.sb("xsq", [128, 16 * 512], BF16)
    zq = c.sb("zq", [128, 16 * 512], BF16)
    mfb = c.sb("mfb", [16, T], BF16)
    me3 = c.sb("me3", [16, 3], F32)
    ae3 = c.sb("ae3", [16, 3], F32)
    mcar = c.sb("mcar", [16, 1], F32)
    tcar = c.sb("tcar", [16, 1], F32)
    Cmb = c.sb("Cmb", [16, ED], BF16)
    halo = c.sb("halo", [128, 32], F32)
    cw_c = c.sb("cw_c", [128, 48], F32)
    dv_c = c.sb("dv_c", [128, 16], F32)
    hsa = c.sb("hsa", [128, 32], F32)
    hst = c.sb("hst", [128, 8], F32)
    gw2c = c.sb("gw2c", [128, 64], F32)
    gb2r = c.sb("gb2r", [1, 8], F32)
    grow = c.sb("grow", [1, 8], F32)
    gcolf = c.sb("gcolf", [128, 8], F32)
    gcol8 = c.sb("gcol8", [8, 1], F32)
    gI = c.sb("gI", [128, 8 * 128], BF16)
    eyeb = c.sb("eyebs", [128, 128], BF16)
    tb1 = c.sb("tb1", [8, 4096], F32)
    tb2 = c.sb("tb2", [8, 1024], F32)
    b1s = c.sb("b1s", [1, 4096], F32)
    b2s = c.sb("b2s", [1, 1024], F32)
    ones11 = c.sb("ones11", [1, 1], F32)
    mbrow = c.sb("mbrow", [1, 2 * 512], F32)
    wstg = c.sb("wstg", [128, 2 * 1024], F32)
    ysb = c.sb("ysb", [128, 512], F32)
    gsb = c.sb("gsb", [128, 512], F32)
    dsb = c.sb("dsb", [128, 512], F32)
    s2sb = c.sb("s2sb", [128, 8 * 512], F32)
    mlo = s2sb[0:16, 0:2048]
    apr = s2sb[0:16, 2048:4096]
    Cms = wstg[0:16, 0:2048]
    mfix = wstg[0:16, 0:2048]
    twst = s2sb[:, 0:1024]
    wbst = s2sb[:, 1024:2048]
    twb = zq[:, 0:1024]
    mwsb = zq[:, 1024:2048]

    pb = [c.ps(f"pb{i}", [128, 512], F32) for i in range(8)]

    for s in ["ld", "wl0", "wl1", "lc0", "lc1", "ls0", "ls1", "tc0", "tc1",
              "wb0", "wb1", "ms0", "ms1", "pe", "actc", "dvec", "st", "s20", "s21"]:
        c.sem(s)

    with nc.Block() as block:
        @block.sync
        def _(sync):
            for i in range(8):
                c.dma(sync, "ld", xnb[:, i * T:(i + 1) * T], xn_in[i * 128:(i + 1) * 128, :])
            c.dma(sync, "ld", mlo[0:16, :], ml_in)
            c.dma(sync, "ld", apr[0:16, :], ap_in)
            for dst, srcap in [(me3[0:16, :], me3_in), (ae3[0:16, :], ae3_in),
                               (halo[:, :], halo_in), (cw_c[:, :], convw_col),
                               (Cms, Cm_in), (dv_c[:, :], dv_in),
                               (hsa[:, :], hs_in), (gw2c[:, :], gw2_in),
                               (gb2r[0:1, :], gb2_in), (tb1[0:8, :], tb1_in),
                               (tb2[0:8, :], tb2_in), (b1s[0:1, :], b1_in),
                               (b2s[0:1, :], b2_in), (eyeb[:, :], eyeb_in)]:
                c.dma(sync, "ld", dst, srcap)

        @block.gpsimd
        def _(gpsimd):
            gpsimd.memset(ones11[0:1, :], 1.0)
            gpsimd.memset(mcar[0:16, :], 0.0)

        @block.vector
        def _(vector):
            v = Chain(c, vector, "dvec")
            c.wait(vector, "ld")
            v.tensor_copy(out=Cmb[0:16, :], in_=Cms)
            for j in range(3):
                v.tensor_tensor(out=tcar[0:16, :], in0=ae3[0:16, j:j + 1],
                                in1=mcar[0:16, :], op=ALU.mult)
                v.tensor_tensor(out=mcar[0:16, :], in0=tcar[0:16, :],
                                in1=me3[0:16, j:j + 1], op=ALU.add)
            v.scalar_tensor_tensor(out=mfix, in0=apr,
                                   scalar=mcar[0:16, :], in1=mlo,
                                   op0=ALU.mult, op1=ALU.add)
            v.tensor_copy(out=mfb[0:16, :], in_=mfix)
            v.tensor_tensor(out=hst[:, :], in0=hsa[:, 0:8], in1=hsa[:, 8:16], op=ALU.add)
            v.tensor_tensor(out=hst[:, :], in0=hst[:, :], in1=hsa[:, 16:24], op=ALU.add)
            v.tensor_tensor(out=hst[:, :], in0=hst[:, :], in1=hsa[:, 24:32], op=ALU.add)

    with nc.Block() as block:
        wc = {}
        for j in range(16):
            @block.sync
            def _(sync, j=j):
                sm = "wl0" if j % 2 == 0 else "wl1"
                if j >= 2:
                    sync.wait_ge(c.sems["dvec"][0], wc[j - 2])
                c.dma(sync, sm, wstg[:, (j % 2) * 1024:(j % 2) * 1024 + 1024],
                      wout_t[j * 128:(j + 1) * 128, :])
                wc[("l", j)] = (sm, c.val(sm))

            @block.vector
            def _(vector, j=j):
                v = Chain(c, vector, "dvec")
                sm, tv = wc[("l", j)]
                vector.wait_ge(c.sems[sm][0], tv)
                v.tensor_copy(out=woutb[:, j * 1024:(j + 1) * 1024],
                              in_=wstg[:, (j % 2) * 1024:(j % 2) * 1024 + 1024])
                wc[j] = c.val("dvec")

    with nc.Block() as block:
        q_ld = {}
        y_pe = {}
        y_ev = {}
        z_dv = {}
        s_pe = {}
        s_dv = {}
        s_st = {}
        for qt in range(4):
            @block.sync
            def _(sync, qt=qt):
                smc = "lc0" if qt % 2 == 0 else "lc1"
                sms = "ls0" if qt % 2 == 0 else "ls1"
                if qt >= 2:
                    sync.wait_ge(c.sems["dvec"][0], z_dv[(qt - 2, 15)])
                c.dma(sync, smc, xcq[:, :].rearrange("p (m s) -> p m s", m=16),
                      xc_in.rearrange("(m p) t -> p m t", p=128)[:, :, qt * 512:(qt + 1) * 512])
                c.dma(sync, sms, xsq[:, :].rearrange("p (m s) -> p m s", m=16),
                      xs_in.rearrange("(m p) t -> p m t", p=128)[:, :, qt * 512:(qt + 1) * 512])
                q_ld[qt] = [(smc, c.val(smc)), (sms, c.val(sms))]

            for m in range(16):
                @block.tensor
                def _(tensor, qt=qt, m=m):
                    if m == 0:
                        for sm, tv in q_ld[qt]:
                            tensor.wait_ge(c.sems[sm][0], tv)
                    g = qt * 16 + m
                    if g >= 2:
                        tensor.wait_ge(c.sems["actc"][0], y_ev[g - 2])
                    c.inc(tensor.matmul(pb[m % 2][:, :], Cmb[0:16, m * 128:(m + 1) * 128],
                                        mfb[0:16, qt * 512:(qt + 1) * 512],
                                        start=True, stop=True), "pe", 1)
                    y_pe[g] = c.val("pe")

                @block.vector
                def _(vector, qt=qt, m=m):
                    v = Chain(c, vector, "dvec")
                    g = qt * 16 + m
                    vector.wait_ge(c.sems["pe"][0], y_pe[g])
                    if qt == 0 and m == 0:
                        for mm_ in range(16):
                            v.scalar_tensor_tensor(
                                out=xcq[:, mm_ * 512:mm_ * 512 + 1],
                                in0=halo[:, mm_:mm_ + 1], scalar=cw_c[:, 0 * 16 + mm_:0 * 16 + mm_ + 1],
                                in1=xcq[:, mm_ * 512:mm_ * 512 + 1], op0=ALU.mult, op1=ALU.add)
                    if qt == 3 and m == 0:
                        for mm_ in range(16):
                            v.scalar_tensor_tensor(
                                out=xcq[:, mm_ * 512 + 511:mm_ * 512 + 512],
                                in0=halo[:, 16 + mm_:16 + mm_ + 1],
                                scalar=cw_c[:, 2 * 16 + mm_:2 * 16 + mm_ + 1],
                                in1=xcq[:, mm_ * 512 + 511:mm_ * 512 + 512],
                                op0=ALU.mult, op1=ALU.add)
                    v.scalar_tensor_tensor(out=ysb[:, :], in0=xsq[:, m * 512:(m + 1) * 512],
                                           scalar=dv_c[:, m:m + 1], in1=pb[m % 2][:, :],
                                           op0=ALU.mult, op1=ALU.add)
                    z_dv[("y", qt, m)] = c.val("dvec")

                @block.scalar
                def _(scalar, qt=qt, m=m):
                    a = Chain(c, scalar, "actc")
                    g = qt * 16 + m
                    scalar.wait_ge(c.sems["dvec"][0], z_dv[("y", qt, m)])
                    a.activation(out=gsb[:, :], in_=ysb[:, :], func=AF.Sigmoid)
                    y_ev[g] = c.val("actc")

                @block.vector
                def _(vector, qt=qt, m=m):
                    v = Chain(c, vector, "dvec")
                    g = qt * 16 + m
                    vector.wait_ge(c.sems["actc"][0], y_ev[g])
                    if qt >= 1 and m == 0:
                        vector.wait_ge(c.sems["pe"][0], s_pe[(qt - 1, 7)])
                    v.tensor_tensor(out=dsb[:, :], in0=xcq[:, m * 512:(m + 1) * 512],
                                    in1=ysb[:, :], op=ALU.subtract)
                    v.tensor_tensor(out=dsb[:, :], in0=gsb[:, :], in1=dsb[:, :], op=ALU.mult)
                    v.tensor_tensor(out=zq[:, m * 512:(m + 1) * 512], in0=dsb[:, :],
                                    in1=ysb[:, :], op=ALU.add)
                    z_dv[(qt, m)] = c.val("dvec")

            for j in range(8):
                @block.tensor
                def _(tensor, qt=qt, j=j):
                    if j == 0:
                        tensor.wait_ge(c.sems["dvec"][0], z_dv[(qt, 15)])
                        if qt == 0:
                            tensor.wait_ge(c.sems["dvec"][0], wc[15])
                    if qt >= 1:
                        tensor.wait_ge(c.sems["dvec"][0], s_dv[(qt - 1, j)])
                    for k in range(16):
                        c.inc(tensor.matmul(
                            pb[j][:, :], woutb[:, k * 1024 + j * 128:k * 1024 + (j + 1) * 128],
                            zq[:, k * 512:(k + 1) * 512],
                            start=(k == 0), stop=(k == 15)), "pe", 1)
                    s_pe[(qt, j)] = c.val("pe")

                @block.vector
                def _(vector, qt=qt, j=j):
                    v = Chain(c, vector, "dvec")
                    vector.wait_ge(c.sems["pe"][0], s_pe[(qt, j)])
                    if qt >= 2:
                        sm, tv = s_st[(qt - 2, j)]
                        vector.wait_ge(c.sems[sm][0], tv)
                    v.tensor_tensor(out=s2sb[:, j * 512:(j + 1) * 512], in0=pb[j][:, :],
                                    in1=xnb[:, j * T + qt * 512:j * T + (qt + 1) * 512],
                                    op=ALU.add)
                    s_dv[(qt, j)] = c.val("dvec")

                @block.sync
                def _(sync, qt=qt, j=j):
                    sync.wait_ge(c.sems["dvec"][0], s_dv[(qt, j)])
                    sm = "s20" if qt % 2 == 0 else "s21"
                    c.dma(sync, sm, s2_out[j * 128:(j + 1) * 128, qt * 512:(qt + 1) * 512],
                          s2sb[:, j * 512:(j + 1) * 512])
                    s_st[(qt, j)] = (sm, c.val(sm))

    with nc.Block() as block:
        f4 = {}

        @block.tensor
        def _(tensor):
            for mc in range(8):
                c.inc(tensor.matmul(pb[0][0:1, 0:8], hst[:, mc:mc + 1], gw2c[:, mc * 8:(mc + 1) * 8],
                                    start=(mc == 0), stop=(mc == 7)), "pe", 1)
            f4["g_pe"] = c.val("pe")

        @block.vector
        def _(vector):
            v = Chain(c, vector, "dvec")
            vector.wait_ge(c.sems["pe"][0], f4["g_pe"])
            v.tensor_scalar(out=grow[0:1, :], in0=pb[0][0:1, 0:8], scalar1=1.0 / 8192.0,
                            scalar2=None, op0=ALU.mult)
            v.tensor_tensor(out=grow[0:1, :], in0=grow[0:1, :], in1=gb2r[0:1, :], op=ALU.add)
            f4["g_dv"] = c.val("dvec")

        @block.tensor
        def _(tensor):
            tensor.wait_ge(c.sems["dvec"][0], f4["g_dv"])
            c.inc(tensor.matmul(pb[1][:, 0:8], ones11[0:1, 0:1].to_broadcast([1, 128]),
                                grow[0:1, :], start=True, stop=True), "pe", 1)
            c.inc(tensor.transpose(pb[2][0:8, 0:1], grow[0:1, :], ones11[0:1, 0:1]), "pe", 1)
            f4["b_pe"] = c.val("pe")

        @block.vector
        def _(vector):
            v = Chain(c, vector, "dvec")
            vector.wait_ge(c.sems["pe"][0], f4["b_pe"])
            v.tensor_copy(out=gcolf[:, :], in_=pb[1][:, 0:8])
            v.tensor_copy(out=gcol8[0:8, :], in_=pb[2][0:8, 0:1])
            for e in range(8):
                v.tensor_scalar(out=gI[:, e * 128:(e + 1) * 128], in0=eyeb[:, :],
                                scalar1=gcolf[:, e:e + 1], scalar2=None, op0=ALU.mult)
            f4["gI_dv"] = c.val("dvec")

        nb = 0
        for w2, nch, tbx, bx, out_ap in [(0, 8, tb1, b1s, mb1_out), (1, 2, tb2, b2s, mb2_out)]:
            for fc in range(nch):
                @block.tensor
                def _(tensor, w2=w2, fc=fc, tbx=tbx, nb=nb):
                    if nb == 0:
                        tensor.wait_ge(c.sems["dvec"][0], f4["gI_dv"])
                    if nb >= 2:
                        f4_dv = f4[("mb_dv", nb - 2)]
                        tensor.wait_ge(c.sems["dvec"][0], f4_dv)
                    bank = pb[3 + nb % 2]
                    c.inc(tensor.matmul(bank[0:1, :], gcol8[0:8, :],
                                        tbx[0:8, fc * 512:(fc + 1) * 512],
                                        start=True, stop=True), "pe", 1)
                    f4[("mb_pe", nb)] = c.val("pe")

                @block.vector
                def _(vector, w2=w2, fc=fc, bx=bx, nb=nb):
                    v = Chain(c, vector, "dvec")
                    vector.wait_ge(c.sems["pe"][0], f4[("mb_pe", nb)])
                    if nb >= 2:
                        sm, tv = f4[("mbst", nb - 2)]
                        vector.wait_ge(c.sems[sm][0], tv)
                    bsl = bx[0:1, fc * 512:(fc + 1) * 512]
                    v.tensor_tensor(out=mbrow[0:1, (nb % 2) * 512:(nb % 2) * 512 + 512],
                                    in0=pb[3 + nb % 2][0:1, 0:512], in1=bsl, op=ALU.add)
                    f4[("mb_dv", nb)] = c.val("dvec")

                @block.sync
                def _(sync, fc=fc, out_ap=out_ap, nb=nb):
                    sync.wait_ge(c.sems["dvec"][0], f4[("mb_dv", nb)])
                    sm = "ms0" if nb % 2 == 0 else "ms1"
                    c.dma(sync, sm, out_ap[0:1, fc * 512:(fc + 1) * 512],
                          mbrow[0:1, (nb % 2) * 512:(nb % 2) * 512 + 512])
                    f4[("mbst", nb)] = (sm, c.val(sm))
                nb += 1

        t_ld = {}
        t_cast = {}
        m_pe = {}
        m_dv = {}
        m_st = {}
        ti = 0
        for w in range(2):
            tw_in = tw1_in if w == 0 else tw2_in
            wq_in = w1q_in if w == 0 else w2q_in
            mw_out = mw1_out if w == 0 else mw2_out
            for ht in range(8):
                for fc in range(2):
                    ci0 = ti * 8
                    for e in range(8):
                        ci = ci0 + e
                        @block.sync
                        def _(sync, tw_in=tw_in, wq_in=wq_in, ht=ht, fc=fc, ti=ti,
                              e=e, ci=ci):
                            if e == 0:
                                smb = "wb0" if ti % 2 == 0 else "wb1"
                                if ti >= 2:
                                    sync.wait_ge(c.sems["dvec"][0], m_dv[ti - 2])
                                c.dma(sync, smb, wbst[:, (ti % 2) * 512:(ti % 2) * 512 + 512],
                                      wq_in[ht * 128:(ht + 1) * 128, fc * 512:(fc + 1) * 512])
                                t_ld[("b", ti)] = (smb, c.val(smb))
                            sm = "tc0" if ci % 2 == 0 else "tc1"
                            if ci == 0:
                                c.wait(sync, "s20")
                                c.wait(sync, "s21")
                            if ci >= 2:
                                sync.wait_ge(c.sems["dvec"][0], t_cast[ci - 2])
                            c.dma(sync, sm, twst[:, (ci % 2) * 512:(ci % 2) * 512 + 512],
                                  tw_in[e, ht * 128:(ht + 1) * 128, fc * 512:(fc + 1) * 512])
                            t_ld[ci] = (sm, c.val(sm))

                        @block.vector
                        def _(vector, ci=ci):
                            v = Chain(c, vector, "dvec")
                            sm, tv = t_ld[ci]
                            vector.wait_ge(c.sems[sm][0], tv)
                            if ci == 0:
                                vector.wait_ge(c.sems["pe"][0], s_pe[(3, 7)])
                            if ci >= 2:
                                vector.wait_ge(c.sems["pe"][0], m_pe.get(ci - 2, 0))
                            v.tensor_copy(out=twb[:, (ci % 2) * 512:(ci % 2) * 512 + 512],
                                          in_=twst[:, (ci % 2) * 512:(ci % 2) * 512 + 512])
                            t_cast[ci] = c.val("dvec")

                        @block.tensor
                        def _(tensor, e=e, ci=ci, ti=ti):
                            tensor.wait_ge(c.sems["dvec"][0], t_cast[ci])
                            if e == 0:
                                if ti == 0:
                                    tensor.wait_ge(c.sems["dvec"][0], f4["gI_dv"])
                                if ti >= 2:
                                    tensor.wait_ge(c.sems["dvec"][0], m_dv[ti - 2])
                            c.inc(tensor.matmul(pb[4 + ti % 2][:, :], gI[:, e * 128:(e + 1) * 128],
                                                twb[:, (ci % 2) * 512:(ci % 2) * 512 + 512],
                                                start=(e == 0), stop=(e == 7)), "pe", 1)
                            m_pe[ci] = c.val("pe")

                    @block.vector
                    def _(vector, ti=ti, ci0=ci0):
                        v = Chain(c, vector, "dvec")
                        vector.wait_ge(c.sems["pe"][0], m_pe[ci0 + 7])
                        sm, tv = t_ld[("b", ti)]
                        vector.wait_ge(c.sems[sm][0], tv)
                        if ti >= 2:
                            sm2, tv2 = m_st[ti - 2]
                            vector.wait_ge(c.sems[sm2][0], tv2)
                        v.tensor_tensor(out=mwsb[:, (ti % 2) * 512:(ti % 2) * 512 + 512],
                                        in0=pb[4 + ti % 2][:, :],
                                        in1=wbst[:, (ti % 2) * 512:(ti % 2) * 512 + 512],
                                        op=ALU.add)
                        m_dv[ti] = c.val("dvec")

                    @block.sync
                    def _(sync, mw_out=mw_out, ht=ht, fc=fc, ti=ti):
                        sync.wait_ge(c.sems["dvec"][0], m_dv[ti])
                        sm = "ms0" if ti % 2 == 0 else "ms1"
                        c.dma(sync, sm, mw_out[ht * 128:(ht + 1) * 128, fc * 512:(fc + 1) * 512],
                              mwsb[:, (ti % 2) * 512:(ti % 2) * 512 + 512])
                        m_st[ti] = (sm, c.val(sm))
                    ti += 1

    with nc.Block() as block:
        @block.sync
        def _(sync):
            for sm in ["st", "s20", "s21", "ms0", "ms1"]:
                c.wait(sync, sm)

    c.close()
    return c


def build_l3g():
    c = Ctx("l3g")
    nc = c.nc
    DT = 8
    NCH = 4

    x_in = c.din("x2", [D, T], F32)
    s2_in = c.din("s2", [D, T], F32)
    xn2_in = c.din("xn2", [D, T], BF16)
    mw1_in = c.din("mw1t", [D, 4096], BF16)
    mw2_in = c.din("mw2t", [4096, D], BF16)
    mb1_in = c.din("mb1_col", [128, 32], F32)
    mb2_in = c.din("mb2_col", [128, 8], F32)

    x3_out = c.dout("x3", [D, T], F32)
    xn3_out = c.dout("xn3", [D, T], BF16)

    xs = c.sb("xs", [128, DT * T], F32)
    xnb = c.sb("xnb", [128, DT * T], BF16)
    u2q = c.sb("u2q", [128, 32 * 512], BF16)
    s2q = c.sb("s2q", [128, 8 * 512], F32)
    w1r = c.sb("w1r", [128, 2 * 1024], BF16)
    w2r = c.sb("w2r", [128, 2 * 1024], BF16)
    g2t = c.sb("g2t", [128, 512], F32)
    sq0 = c.sb("sq0", [128, 512], F32)
    sq1 = c.sb("sq1", [128, 512], F32)
    mb1c = c.sb("mb1c", [128, 32], F32)
    mb2c = c.sb("mb2c", [128, 8], F32)
    r1b = s2q[:, 0:2048]
    r2b = s2q[:, 2048:4096]
    rows4 = c.sb("rows4", [1, 2048], F32)
    murow = rows4[0:1, 0:512]
    varrow = rows4[0:1, 512:1024]
    rstdrow = rows4[0:1, 1024:1536]
    r2row = rows4[0:1, 1536:2048]
    ones_col_f32 = c.sb("ones_col_f32", [128, 1], F32)
    ones_row_f32 = c.sb("ones_row_f32", [1, 128], F32)
    epscol = c.sb("epscol", [1, 1], F32)
    nhcol = c.sb("nhcol", [1, 1], F32)

    pb = [c.ps(f"pb{i}", [128, 512], F32) for i in range(8)]

    for s in ["ld", "ls0", "ls1", "w10", "w11", "w20", "w21", "pe", "actc", "dvec",
              "st", "sx0", "sx1"]:
        c.sem(s)

    def xt(i):
        return xs[:, i * T:(i + 1) * T]

    def xnbt(i):
        return xnb[:, i * T:(i + 1) * T]

    with nc.Block() as block:
        @block.sync
        def _(sync):
            for i in range(DT):
                c.dma(sync, "ld", xt(i), x_in[i * 128:(i + 1) * 128, :])
                c.dma(sync, "ld", xnbt(i), xn2_in[i * 128:(i + 1) * 128, :])
            c.dma(sync, "ld", mb1c[:, :], mb1_in)
            c.dma(sync, "ld", mb2c[:, :], mb2_in)

        @block.gpsimd
        def _(gpsimd):
            gpsimd.memset(ones_col_f32[:, :], 1.0)
            gpsimd.memset(ones_row_f32[:, :], 1.0)
            gpsimd.memset(epscol[:, :], LN_EPS)
            gpsimd.memset(nhcol[:, :], -0.5)

    with nc.Block() as block:
        s_ld = {}
        w1l = {}
        w2l = {}
        u_pe = {}
        u_ev = {}
        d_pe = {}
        d_ev = {}
        x_dv = {}
        x_st = {}
        for qt in range(4):
            @block.sync
            def _(sync, qt=qt):
                sm = "ls0" if qt % 2 == 0 else "ls1"
                if qt >= 2:
                    sync.wait_ge(c.sems["dvec"][0], x_dv[(qt - 2, 7)])
                c.dma(sync, sm, s2q[:, 0:4096].rearrange("p (j s) -> p j s", j=8),
                      s2_in.rearrange("(j p) t -> p j t", p=128)[:, :, qt * 512:(qt + 1) * 512])
                s_ld[qt] = (sm, c.val(sm))

            for ft in range(32):
                g = qt * 32 + ft
                @block.sync
                def _(sync, qt=qt, ft=ft, g=g):
                    sm = "w10" if g % 2 == 0 else "w11"
                    if g >= 2:
                        sync.wait_ge(c.sems["pe"][0], u_pe[g - 2])
                    c.dma(sync, sm, w1r[:, (g % 2) * 1024:(g % 2) * 1024 + 1024].rearrange(
                        "p (k m) -> p k m", k=8),
                        mw1_in[:, ft * 128:(ft + 1) * 128].rearrange("(k p) m -> p k m", p=128))
                    w1l[g] = (sm, c.val(sm))

                @block.tensor
                def _(tensor, qt=qt, ft=ft, g=g):
                    sm, tv = w1l[g]
                    tensor.wait_ge(c.sems[sm][0], tv)
                    if g >= 2:
                        tensor.wait_ge(c.sems["actc"][0], u_ev[g - 2])
                    for k in range(8):
                        c.inc(tensor.matmul(
                            pb[g % 2][:, :],
                            w1r[:, (g % 2) * 1024 + k * 128:(g % 2) * 1024 + (k + 1) * 128],
                            xnbt(k)[:, qt * 512:(qt + 1) * 512],
                            start=(k == 0), stop=(k == 7)), "pe", 1)
                    u_pe[g] = c.val("pe")

                @block.scalar
                def _(scalar, qt=qt, ft=ft, g=g):
                    a = Chain(c, scalar, "actc")
                    scalar.wait_ge(c.sems["pe"][0], u_pe[g])
                    if qt >= 1 and ft == 0:
                        scalar.wait_ge(c.sems["pe"][0], d_pe[(qt - 1, 31)])
                    a.activation(out=u2q[:, ft * 512:(ft + 1) * 512], in_=pb[g % 2][:, :],
                                 func=AF.Gelu_apprx_tanh, bias=mb1c[:, ft:ft + 1])
                    u_ev[g] = c.val("actc")

            for k in range(32):
                g = qt * 32 + k
                @block.sync
                def _(sync, qt=qt, k=k, g=g):
                    sm = "w20" if g % 2 == 0 else "w21"
                    if g >= 2:
                        sync.wait_ge(c.sems["pe"][0], d_pe[(qt, k - 2) if k >= 2 else (qt - 1, 30 + k)])
                    c.dma(sync, sm, w2r[:, (g % 2) * 1024:(g % 2) * 1024 + 1024],
                          mw2_in[k * 128:(k + 1) * 128, :])
                    w2l[g] = (sm, c.val(sm))

                @block.tensor
                def _(tensor, qt=qt, k=k, g=g):
                    sm, tv = w2l[g]
                    tensor.wait_ge(c.sems[sm][0], tv)
                    if k == 0:
                        tensor.wait_ge(c.sems["actc"][0], u_ev[qt * 32 + 31])
                        if qt >= 1:
                            tensor.wait_ge(c.sems["dvec"][0], x_dv[(qt - 1, 7)])
                    for j in range(8):
                        c.inc(tensor.matmul(
                            pb[j][:, :],
                            w2r[:, (g % 2) * 1024 + j * 128:(g % 2) * 1024 + (j + 1) * 128],
                            u2q[:, k * 512:(k + 1) * 512],
                            start=(k == 0), stop=(k == 31)), "pe", 1)
                    d_pe[(qt, k)] = c.val("pe")

            for j in range(8):
                @block.scalar
                def _(scalar, qt=qt, j=j):
                    a = Chain(c, scalar, "actc")
                    scalar.wait_ge(c.sems["pe"][0], d_pe[(qt, 31)])
                    if j >= 1:
                        scalar.wait_ge(c.sems["dvec"][0], x_dv[(qt, j - 1)])
                    a.activation(out=g2t[:, :], in_=pb[j][:, :],
                                 func=AF.Silu, bias=mb2c[:, j:j + 1])
                    d_ev[(qt, j)] = c.val("actc")

                @block.vector
                def _(vector, qt=qt, j=j):
                    v = Chain(c, vector, "dvec")
                    vector.wait_ge(c.sems["actc"][0], d_ev[(qt, j)])
                    if qt == 0 and j == 0:
                        c.wait(vector, "ld")
                    sm, tv = s_ld[qt]
                    vector.wait_ge(c.sems[sm][0], tv)
                    if qt >= 2:
                        sm2, tv2 = x_st[(qt - 2, j)]
                        vector.wait_ge(c.sems[sm2][0], tv2)
                    cs = slice(qt * 512, (qt + 1) * 512)
                    v.tensor_tensor(out=sq0[:, :], in0=g2t[:, :],
                                    in1=s2q[:, j * 512:(j + 1) * 512], op=ALU.mult)
                    v.tensor_tensor(out=xt(j)[:, cs], in0=xt(j)[:, cs], in1=sq0[:, :],
                                    op=ALU.add)
                    x_dv[(qt, j)] = c.val("dvec")

                @block.sync
                def _(sync, qt=qt, j=j):
                    sync.wait_ge(c.sems["dvec"][0], x_dv[(qt, j)])
                    sm = "sx0" if qt % 2 == 0 else "sx1"
                    c.dma(sync, sm, x3_out[j * 128:(j + 1) * 128, qt * 512:(qt + 1) * 512],
                          xt(j)[:, qt * 512:(qt + 1) * 512])
                    x_st[(qt, j)] = (sm, c.val(sm))

    ps_row = pb[4][0:1, 0:512]
    ps_row2 = pb[5][0:1, 0:512]
    prm = dict(sq0=sq0, sq1=sq1, murow=murow, varrow=varrow, rstdrow=rstdrow,
               r2row=r2row, r1b=r1b, r2b=r2b, ones_col_f32=ones_col_f32,
               ones_row_f32=ones_row_f32, epscol=epscol, nhcol=nhcol,
               ps_row=ps_row, ps_row2=ps_row2, bcA=pb[2], bcB=pb[3])

    def apply_ln(v):
        for i in range(DT):
            for ch in range(NCH):
                cs = slice(ch * 512, (ch + 1) * 512)
                v.tensor_tensor(out=sq0[:, :], in0=xt(i)[:, cs], in1=r1b[:, cs], op=ALU.mult)
                v.tensor_tensor(out=sq1[:, :], in0=sq0[:, :], in1=r2b[:, cs], op=ALU.add)
                v.tensor_copy(out=xnbt(i)[:, cs], in_=sq1[:, :])

    emit_ln_feat(c, nc, xt, DT, T, prm, apply_ln)

    with nc.Block() as block:
        @block.sync
        def _(sync):
            c.wait(sync, "dvec")
            for i in range(DT):
                c.dma(sync, "st", xn3_out[i * 128:(i + 1) * 128, :], xnbt(i))

    with nc.Block() as block:
        @block.sync
        def _(sync):
            for sm in ["st", "sx0", "sx1"]:
                c.wait(sync, sm)

    c.close()
    return c


def build_l3a():
    c = Ctx("l3a")
    nc = c.nc

    xn3_in = c.din("xn3", [D, T], BF16)
    wk_in = c.din("wk_t", [D, D], F32)
    wv_in = c.din("wv_t", [D, D], F32)
    wg_in = c.din("wg_t", [D, D], F32)
    cos_in = c.din("cosrep", [T, 512], F32)
    sin_in = c.din("sinrep", [T, 512], F32)
    ql_in = c.din("qlat_t", [64, 1024], F32)
    tri_in = c.din("tri2", [128, 128], F32)
    bo_in = c.din("blockones", [128, 2], F32)
    boT_in = c.din("blockonesT", [2, 128], F32)
    eyeb_in = c.din("eyeb", [128, 128], BF16)

    gf_out = c.dout("gates_f", [D, T], BF16)
    lp_out = c.dout("lpart", [128, 512], F32)

    xnb = c.sb("xnb", [128, 8 * T], BF16)
    kfeat = c.sb("kfeat", [128, 8 * T], BF16)
    vtok = c.sb("vtok", [128, 16 * 1024], BF16)
    gatesf = c.sb("gatesf", [128, 8 * T], BF16)
    wbb = c.sb("wbb", [128, 8 * 1024], BF16)
    wst = c.sb("wst", [128, 2 * 1024], F32)
    ktk = c.sb("ktk", [128, 2 * 1024], BF16)
    csr = c.sb("csr", [128, 2 * 512], F32)
    snr = c.sb("snr", [128, 2 * 512], F32)
    rp1 = c.sb("rp1", [128, 512], F32)
    rp2 = c.sb("rp2", [128, 512], F32)
    rp3 = c.sb("rp3", [128, 512], F32)
    rp4 = c.sb("rp4", [128, 512], F32)
    qls = c.sb("qls", [64, 1024], F32)
    qlatb = c.sb("qlatb", [128, 1024], BF16)
    tri2 = c.sb("tri2_s", [128, 128], F32)
    blockones = c.sb("blockones_s", [128, 2], F32)
    blockonesT = c.sb("blockonesT_s", [2, 128], F32)
    eyeb = c.sb("eyeb_s", [128, 128], BF16)
    sclc = c.sb("sclc", [128, 1], F32)
    escr = c.sb("escr", [128, 512], F32)
    rec2 = c.sb("rec2", [2, 512], F32)
    msk = c.sb("msk", [128, 128], F32)
    escr2 = c.sb("escr2", [128, 128], F32)
    mx = c.sb("mx", [128, 1], F32)
    nmx = c.sb("nmx", [128, 1], F32)
    smr = c.sb("smr", [128, 1], F32)
    rc = c.sb("rc", [128, 1], F32)
    pbf = c.sb("pbf", [128, 128], BF16)
    pT = c.sb("pT", [128, 128], BF16)
    lpart = c.sb("lpart_s", [128, 512], F32)

    pb = [c.ps(f"pb{i}", [128, 512], F32) for i in range(6)]
    psTf = c.ps("psT", [128, 1024], BF16)
    psT2f = c.ps("psT2", [128, 1024], BF16)
    psT = psTf[:, 0:128]
    psT2 = psT2f[:, 0:128]

    for s in ["ld", "wl0", "wl1", "cl0", "cl1", "pe", "actc", "dvec", "st"]:
        c.sem(s)

    def xnbt(i):
        return xnb[:, i * T:(i + 1) * T]

    with nc.Block() as block:
        @block.sync
        def _(sync):
            for i in range(8):
                c.dma(sync, "ld", xnbt(i), xn3_in[i * 128:(i + 1) * 128, :])
            for dst, srcap in [(qls[0:64, :], ql_in), (tri2[:, :], tri_in),
                               (blockones[:, :], bo_in), (blockonesT[0:2, :], boT_in),
                               (eyeb[:, :], eyeb_in)]:
                c.dma(sync, "ld", dst, srcap)

        @block.gpsimd
        def _(gpsimd):
            gpsimd.memset(sclc[:, :], 0.125)

        @block.vector
        def _(vector):
            v = Chain(c, vector, "dvec")
            c.wait(vector, "ld")
            v.tensor_copy(out=qlatb[0:64, :], in_=qls[0:64, :])
            v.tensor_copy(out=qlatb[64:128, :], in_=qls[0:64, :])

    def stream_w(w_ap, tag, extra_wait=None):
        with nc.Block() as block:
            wc = {}
            for j in range(8):
                @block.sync
                def _(sync, j=j):
                    sm = "wl0" if j % 2 == 0 else "wl1"
                    if j >= 2:
                        sync.wait_ge(c.sems["dvec"][0], wc[j - 2])
                    c.dma(sync, sm, wst[:, (j % 2) * 1024:(j % 2) * 1024 + 1024],
                          w_ap[j * 128:(j + 1) * 128, :])
                    wc[("l", j)] = (sm, c.val(sm))

                @block.vector
                def _(vector, j=j):
                    v = Chain(c, vector, "dvec")
                    sm, tv = wc[("l", j)]
                    vector.wait_ge(c.sems[sm][0], tv)
                    if j == 0 and extra_wait is not None:
                        vector.wait_ge(c.sems["pe"][0], extra_wait())
                    v.tensor_copy(out=wbb[:, j * 1024:(j + 1) * 1024],
                                  in_=wst[:, (j % 2) * 1024:(j % 2) * 1024 + 1024])
                    wc[j] = c.val("dvec")
        return c.val("dvec")

    wk_done = stream_w(wk_in, "wk")
    with nc.Block() as block:
        k_pe = {}
        k_ev = {}
        rope_dv = {}
        tr_pe = {}
        kev = {}
        cs_ld = {}
        for tt in range(16):
            @block.sync
            def _(sync, tt=tt):
                sm = "cl0" if tt % 2 == 0 else "cl1"
                if tt >= 2:
                    sync.wait_ge(c.sems["dvec"][0], rope_dv[tt - 2])
                c.dma(sync, sm, csr[:, (tt % 2) * 512:(tt % 2) * 512 + 512],
                      cos_in[tt * 128:(tt + 1) * 128, :])
                c.dma(sync, sm, snr[:, (tt % 2) * 512:(tt % 2) * 512 + 512],
                      sin_in[tt * 128:(tt + 1) * 128, :])
                cs_ld[tt] = (sm, c.val(sm))

            for vc in range(2):
                @block.tensor
                def _(tensor, tt=tt, vc=vc):
                    g = tt * 2 + vc
                    if g == 0:
                        tensor.wait_ge(c.sems["dvec"][0], wk_done)
                    if g >= 2:
                        tensor.wait_ge(c.sems["actc"][0], k_ev[g - 2])
                    for k in range(8):
                        c.inc(tensor.matmul(
                            pb[g % 2][:, :],
                            xnbt(k)[:, tt * 128:(tt + 1) * 128],
                            wbb[:, k * 1024 + vc * 512:k * 1024 + (vc + 1) * 512],
                            start=(k == 0), stop=(k == 7)), "pe", 1)
                    k_pe[g] = c.val("pe")

                @block.scalar
                def _(scalar, tt=tt, vc=vc):
                    a = Chain(c, scalar, "actc")
                    g = tt * 2 + vc
                    scalar.wait_ge(c.sems["pe"][0], k_pe[g])
                    if tt >= 2 and vc == 0:
                        scalar.wait_ge(c.sems["pe"][0], tr_pe[tt - 2])
                    a.activation(out=ktk[:, (tt % 2) * 1024 + vc * 512:(tt % 2) * 1024 + (vc + 1) * 512],
                                 in_=pb[g % 2][:, :], func=AF.Copy)
                    k_ev[g] = c.val("actc")

            @block.vector
            def _(vector, tt=tt):
                v = Chain(c, vector, "dvec")
                vector.wait_ge(c.sems["actc"][0], k_ev[tt * 2 + 1])
                sm, tv = cs_ld[tt]
                vector.wait_ge(c.sems[sm][0], tv)
                kt = ktk[:, (tt % 2) * 1024:(tt % 2) * 1024 + 1024]
                a = kt[:, 0:1024:2]
                b = kt[:, 1:1024:2]
                cc = csr[:, (tt % 2) * 512:(tt % 2) * 512 + 512]
                ss = snr[:, (tt % 2) * 512:(tt % 2) * 512 + 512]
                v.tensor_tensor(out=rp1[:, :], in0=a, in1=cc, op=ALU.mult)
                v.tensor_tensor(out=rp2[:, :], in0=b, in1=ss, op=ALU.mult)
                v.tensor_tensor(out=rp3[:, :], in0=a, in1=ss, op=ALU.mult)
                v.tensor_tensor(out=rp4[:, :], in0=b, in1=cc, op=ALU.mult)
                v.tensor_tensor(out=a, in0=rp1[:, :], in1=rp2[:, :], op=ALU.subtract)
                v.tensor_tensor(out=b, in0=rp3[:, :], in1=rp4[:, :], op=ALU.add)
                rope_dv[tt] = c.val("dvec")

            for ht in range(8):
                @block.tensor
                def _(tensor, tt=tt, ht=ht):
                    if ht == 0:
                        tensor.wait_ge(c.sems["dvec"][0], rope_dv[tt])
                    gh = tt * 8 + ht
                    if gh >= 2:
                        tensor.wait_ge(c.sems["dvec"][0], kev[(gh - 2)])
                    kt = ktk[:, (tt % 2) * 1024:(tt % 2) * 1024 + 1024]
                    ps = psT if ht % 2 == 0 else psT2
                    c.inc(tensor.transpose(ps[:, :], kt[:, ht * 128:(ht + 1) * 128],
                                           eyeb[:, :]), "pe", 1)
                    tr_pe[(tt, ht)] = c.val("pe")
                    if ht == 7:
                        tr_pe[tt] = c.val("pe")

                @block.vector
                def _(vector, tt=tt, ht=ht):
                    v = Chain(c, vector, "dvec")
                    gh = tt * 8 + ht
                    ps = psT if ht % 2 == 0 else psT2
                    vector.wait_ge(c.sems["pe"][0], tr_pe[(tt, ht)])
                    v.tensor_copy(out=kfeat[:, ht * T + tt * 128:ht * T + (tt + 1) * 128],
                                  in_=ps[:, :])
                    kev[gh] = c.val("dvec")
                    if ht == 7:
                        kev[tt] = c.val("dvec")

    def last_k_pe():
        return tr_pe[15]
    wv_done = stream_w(wv_in, "wv", extra_wait=last_k_pe)
    with nc.Block() as block:
        v_pe = {}
        v_ev = {}
        for tt in range(16):
            for vc in range(2):
                @block.tensor
                def _(tensor, tt=tt, vc=vc):
                    g = tt * 2 + vc
                    if g == 0:
                        tensor.wait_ge(c.sems["dvec"][0], wv_done)
                    if g >= 2:
                        tensor.wait_ge(c.sems["actc"][0], v_ev[g - 2])
                    for k in range(8):
                        c.inc(tensor.matmul(
                            pb[g % 2][:, :],
                            xnbt(k)[:, tt * 128:(tt + 1) * 128],
                            wbb[:, k * 1024 + vc * 512:k * 1024 + (vc + 1) * 512],
                            start=(k == 0), stop=(k == 7)), "pe", 1)
                    v_pe[g] = c.val("pe")

                @block.scalar
                def _(scalar, tt=tt, vc=vc):
                    a = Chain(c, scalar, "actc")
                    g = tt * 2 + vc
                    scalar.wait_ge(c.sems["pe"][0], v_pe[g])
                    a.activation(out=vtok[:, tt * 1024 + vc * 512:tt * 1024 + (vc + 1) * 512],
                                 in_=pb[g % 2][:, :], func=AF.Copy)
                    v_ev[g] = c.val("actc")

    def last_v_pe():
        return v_pe[31]
    wg_done = stream_w(wg_in, "wg", extra_wait=last_v_pe)
    with nc.Block() as block:
        g_pe = {}
        g_ev = {}
        g_sum = {}
        g_dv = {}
        for gt in range(8):
            for ch in range(4):
                g = gt * 4 + ch
                @block.tensor
                def _(tensor, gt=gt, ch=ch, g=g):
                    if g == 0:
                        tensor.wait_ge(c.sems["dvec"][0], wg_done)
                    if g >= 2:
                        tensor.wait_ge(c.sems["actc"][0], g_ev[g - 2])
                    for k in range(8):
                        c.inc(tensor.matmul(
                            pb[g % 2][:, :],
                            wbb[:, k * 1024 + gt * 128:k * 1024 + (gt + 1) * 128],
                            xnbt(k)[:, ch * 512:(ch + 1) * 512],
                            start=(k == 0), stop=(k == 7)), "pe", 1)
                    g_pe[g] = c.val("pe")

                @block.scalar
                def _(scalar, g=g):
                    a = Chain(c, scalar, "actc")
                    scalar.wait_ge(c.sems["pe"][0], g_pe[g])
                    if g >= 1:
                        scalar.wait_ge(c.sems["dvec"][0], g_dv[g - 1])
                    a.activation(out=escr[:, :], in_=pb[g % 2][:, :], func=AF.Exp,
                                 scale=sclc[:, 0:1])
                    g_ev[g] = c.val("actc")

                @block.tensor
                def _(tensor, g=g):
                    tensor.wait_ge(c.sems["actc"][0], g_ev[g])
                    if g >= 1:
                        tensor.wait_ge(c.sems["dvec"][0], g_dv[g - 1])
                    c.inc(tensor.matmul(pb[2][0:2, :], blockones[:, :], escr[:, :],
                                        start=True, stop=True), "pe", 1)
                    g_sum[g] = c.val("pe")

                @block.vector
                def _(vector, g=g):
                    v = Chain(c, vector, "dvec")
                    vector.wait_ge(c.sems["pe"][0], g_sum[g])
                    v.reciprocal(out=rec2[0:2, :], in_=pb[2][0:2, :])
                    g_dv[("r", g)] = c.val("dvec")

                @block.tensor
                def _(tensor, g=g):
                    tensor.wait_ge(c.sems["dvec"][0], g_dv[("r", g)])
                    c.inc(tensor.matmul(pb[3][:, :], blockonesT[0:2, :], rec2[0:2, :],
                                        start=True, stop=True), "pe", 1)
                    g_sum[("b", g)] = c.val("pe")

                @block.vector
                def _(vector, gt=gt, ch=ch, g=g):
                    v = Chain(c, vector, "dvec")
                    vector.wait_ge(c.sems["pe"][0], g_sum[("b", g)])
                    v.tensor_tensor(out=gatesf[:, gt * T + ch * 512:gt * T + (ch + 1) * 512],
                                    in0=escr[:, :], in1=pb[3][:, :], op=ALU.mult)
                    g_dv[g] = c.val("dvec")

        @block.sync
        def _(sync):
            c.wait(sync, "dvec")
            for gt in range(8):
                c.dma(sync, "st", gf_out[gt * 128:(gt + 1) * 128, :],
                      gatesf[:, gt * T:(gt + 1) * T])

    with nc.Block() as block:
        l_pe = {}
        l_dv = {}
        l_ev = {}
        a_pe = {}
        for hp in range(8):
            for n in range(16):
                gi = hp * 16 + n
                @block.tensor
                def _(tensor, hp=hp, n=n, gi=gi):
                    if gi >= 2:
                        tensor.wait_ge(c.sems["dvec"][0], l_dv[("m", gi - 2)])
                    if gi == 0:
                        tensor.wait_ge(c.sems["pe"][0], g_sum[("b", 31)])
                    c.inc(tensor.matmul(pb[4 + gi % 2][0:64, 0:128],
                                        qlatb[0:64, (2 * hp) * 64:(2 * hp) * 64 + 64],
                                        kfeat[0:64, hp * T + n * 128:hp * T + (n + 1) * 128],
                                        start=True, stop=True, tile_position=(0, 0)), "pe", 1)
                    c.inc(tensor.matmul(pb[4 + gi % 2][64:128, 0:128],
                                        qlatb[64:128, (2 * hp + 1) * 64:(2 * hp + 1) * 64 + 64],
                                        kfeat[64:128, hp * T + n * 128:hp * T + (n + 1) * 128],
                                        start=True, stop=True, tile_position=(64, 64)), "pe", 1)
                    l_pe[gi] = c.val("pe")

                @block.vector
                def _(vector, gi=gi):
                    v = Chain(c, vector, "dvec")
                    vector.wait_ge(c.sems["pe"][0], l_pe[gi])
                    v.tensor_tensor(out=msk[:, :], in0=pb[4 + gi % 2][:, 0:128],
                                    in1=tri2[:, :], op=ALU.add)
                    v.reduce_max(out=mx[:, 0:1], in_=msk[:, :], axis=AX.X)
                    v.tensor_scalar(out=nmx[:, 0:1], in0=mx[:, 0:1], scalar1=-0.125,
                                    scalar2=None, op0=ALU.mult)
                    l_dv[("m", gi)] = c.val("dvec")

                @block.scalar
                def _(scalar, gi=gi):
                    a = Chain(c, scalar, "actc")
                    scalar.wait_ge(c.sems["dvec"][0], l_dv[("m", gi)])
                    a.activation(out=escr2[:, :], in_=msk[:, :], func=AF.Exp,
                                 scale=sclc[:, 0:1], bias=nmx[:, 0:1])
                    l_ev[gi] = c.val("actc")

                @block.vector
                def _(vector, gi=gi):
                    v = Chain(c, vector, "dvec")
                    vector.wait_ge(c.sems["actc"][0], l_ev[gi])
                    if gi >= 1:
                        vector.wait_ge(c.sems["pe"][0], a_pe.get(("t", gi - 1), 0))
                    v.reduce_sum(out=smr[:, 0:1], in_=escr2[:, :], axis=AX.X)
                    v.reciprocal(out=rc[:, 0:1], in_=smr[:, 0:1])
                    v.tensor_scalar(out=pbf[:, :], in0=escr2[:, :], scalar1=rc[:, 0:1],
                                    scalar2=None, op0=ALU.mult)
                    l_dv[("p", gi)] = c.val("dvec")

                @block.tensor
                def _(tensor, gi=gi):
                    tensor.wait_ge(c.sems["dvec"][0], l_dv[("p", gi)])
                    c.inc(tensor.transpose(psT[:, :], pbf[:, :], eyeb[:, :]), "pe", 1)
                    a_pe[("t", gi)] = c.val("pe")

                @block.vector
                def _(vector, gi=gi):
                    v = Chain(c, vector, "dvec")
                    vector.wait_ge(c.sems["pe"][0], a_pe[("t", gi)])
                    if gi >= 1:
                        vector.wait_ge(c.sems["pe"][0], a_pe.get(gi - 1, 0))
                    v.tensor_copy(out=pT[:, :], in_=psT[:, :])
                    l_dv[("pt", gi)] = c.val("dvec")

                @block.tensor
                def _(tensor, hp=hp, n=n, gi=gi):
                    tensor.wait_ge(c.sems["dvec"][0], l_dv[("pt", gi)])
                    if n == 0 and hp >= 2:
                        tensor.wait_ge(c.sems["dvec"][0], l_dv[("ev", hp - 2)])
                    c.inc(tensor.matmul(pb[hp % 2][0:64, 0:64], pT[:, 0:64],
                                        vtok[:, n * 1024 + (2 * hp) * 64:n * 1024 + (2 * hp) * 64 + 64],
                                        start=(n == 0), stop=(n == 15),
                                        tile_position=(0, 0)), "pe", 1)
                    c.inc(tensor.matmul(pb[hp % 2][64:128, 0:64], pT[:, 64:128],
                                        vtok[:, n * 1024 + (2 * hp + 1) * 64:n * 1024 + (2 * hp + 1) * 64 + 64],
                                        start=(n == 0), stop=(n == 15),
                                        tile_position=(0, 64)), "pe", 1)
                    a_pe[gi] = c.val("pe")

            @block.vector
            def _(vector, hp=hp):
                v = Chain(c, vector, "dvec")
                vector.wait_ge(c.sems["pe"][0], a_pe[hp * 16 + 15])
                v.tensor_copy(out=lpart[:, hp * 64:(hp + 1) * 64], in_=pb[hp % 2][0:128, 0:64])
                l_dv[("ev", hp)] = c.val("dvec")

        @block.sync
        def _(sync):
            c.wait(sync, "dvec")
            c.dma(sync, "st", lp_out, lpart[:, :])

    with nc.Block() as block:
        @block.sync
        def _(sync):
            c.wait(sync, "st")

    c.close()
    return c


def build_l3b():
    c = Ctx("l3b")
    nc = c.nc

    x3_in = c.din("x3", [D, T], F32)
    xn3_in = c.din("xn3", [D, T], BF16)
    gf_in = c.din("gates_f", [D, T], BF16)
    lp_in = c.din("lparts", [128, 2048], F32)
    wp_in = c.din("wp_t", [D, D], F32)
    w1_in = c.din("w1qt", [D, 4096], BF16)
    b1_in = c.din("b1_col", [128, 32], F32)
    w2_in = c.din("w2qt", [4096, D], BF16)
    b2_in = c.din("b2_col", [128, 8], F32)

    y_out = c.dout("y", [D, T], F32)

    xs = c.sb("xs", [128, 8 * T], F32)
    xnb = c.sb("xnb", [128, 8 * T], BF16)
    gf = c.sb("gf", [128, 8 * T], BF16)
    wpb = c.sb("wpb", [128, 8 * 1024], BF16)
    s3q = c.sb("s3q", [128, 8 * 512], F32)
    u3q = c.sb("u3q", [128, 32 * 512], BF16)
    attnb = c.sb("attnb", [128, 512], BF16)
    wr = c.sb("wr", [128, 2 * 1024], BF16)
    w1r = wr
    w2r = wr
    g3t = c.sb("g3t", [128, 512], F32)
    sq0 = c.sb("sq0", [128, 512], F32)
    b1c = c.sb("b1c", [128, 32], F32)
    b2c = c.sb("b2c", [128, 8], F32)
    wtq = u3q[:, 0:4096]
    lps = s3q[:, 0:2048]
    wst = s3q[:, 2048:4096]
    attns = g3t[:, 0:512]

    pb = [c.ps(f"pb{i}", [128, 512], F32) for i in range(8)]

    for s in ["ld", "wl0", "wl1", "w10", "w11", "w20", "w21", "pe", "actc", "dvec",
              "sy0", "sy1"]:
        c.sem(s)

    def xt(i):
        return xs[:, i * T:(i + 1) * T]

    def xnbt(i):
        return xnb[:, i * T:(i + 1) * T]

    with nc.Block() as block:
        @block.sync
        def _(sync):
            for i in range(8):
                c.dma(sync, "ld", xt(i), x3_in[i * 128:(i + 1) * 128, :])
                c.dma(sync, "ld", xnbt(i), xn3_in[i * 128:(i + 1) * 128, :])
                c.dma(sync, "ld", gf[:, i * T:(i + 1) * T], gf_in[i * 128:(i + 1) * 128, :])
            c.dma(sync, "ld", lps[:, :], lp_in)
            c.dma(sync, "ld", b1c[:, :], b1_in)
            c.dma(sync, "ld", b2c[:, :], b2_in)

        @block.vector
        def _(vector):
            v = Chain(c, vector, "dvec")
            c.wait(vector, "ld")
            v.tensor_tensor(out=attns[:, :], in0=lps[:, 0:512], in1=lps[:, 512:1024],
                            op=ALU.add)
            v.tensor_tensor(out=attns[:, :], in0=attns[:, :], in1=lps[:, 1024:1536],
                            op=ALU.add)
            v.tensor_tensor(out=attns[:, :], in0=attns[:, :], in1=lps[:, 1536:2048],
                            op=ALU.add)
            v.tensor_copy(out=attnb[:, :], in_=attns[:, :])
    attn_done = c.val("dvec")

    with nc.Block() as block:
        wc = {}
        for j in range(8):
            @block.sync
            def _(sync, j=j):
                sm = "wl0" if j % 2 == 0 else "wl1"
                if j >= 2:
                    sync.wait_ge(c.sems["dvec"][0], wc[j - 2])
                c.dma(sync, sm, wst[:, (j % 2) * 1024:(j % 2) * 1024 + 1024],
                      wp_in[j * 128:(j + 1) * 128, :])
                wc[("l", j)] = (sm, c.val(sm))

            @block.vector
            def _(vector, j=j):
                v = Chain(c, vector, "dvec")
                sm, tv = wc[("l", j)]
                vector.wait_ge(c.sems[sm][0], tv)
                v.tensor_copy(out=wpb[:, j * 1024:(j + 1) * 1024],
                              in_=wst[:, (j % 2) * 1024:(j % 2) * 1024 + 1024])
                wc[j] = c.val("dvec")
    wp_done = c.val("dvec")

    with nc.Block() as block:
        w_pe = {}
        w_ev = {}
        s_pe = {}
        s_ev = {}
        u_pe = {}
        u_ev = {}
        d_pe = {}
        d_ev = {}
        x_dv = {}
        y_st = {}
        w1l = {}
        w2l = {}
        for qt in range(4):
            for ft in range(8):
                g = qt * 8 + ft
                @block.tensor
                def _(tensor, qt=qt, ft=ft, g=g):
                    if g == 0:
                        tensor.wait_ge(c.sems["dvec"][0], attn_done)
                    if qt >= 1 and ft < 2:
                        tensor.wait_ge(c.sems["actc"][0], d_ev[(qt - 1, ft)])
                    if g >= 2:
                        tensor.wait_ge(c.sems["actc"][0], w_ev[g - 2])
                    cs = slice(qt * 512, (qt + 1) * 512)
                    c.inc(tensor.matmul(pb[g % 2][0:64, 0:512],
                                        attnb[0:64, ft * 64:(ft + 1) * 64],
                                        gf[0:64, ft * T + qt * 512:ft * T + (qt + 1) * 512],
                                        start=True, stop=True, tile_position=(0, 0)), "pe", 1)
                    c.inc(tensor.matmul(pb[g % 2][64:128, 0:512],
                                        attnb[64:128, ft * 64:(ft + 1) * 64],
                                        gf[64:128, ft * T + qt * 512:ft * T + (qt + 1) * 512],
                                        start=True, stop=True, tile_position=(64, 64)), "pe", 1)
                    w_pe[g] = c.val("pe")

                @block.scalar
                def _(scalar, qt=qt, ft=ft, g=g):
                    a = Chain(c, scalar, "actc")
                    scalar.wait_ge(c.sems["pe"][0], w_pe[g])
                    if qt >= 1 and ft == 0:
                        scalar.wait_ge(c.sems["pe"][0], d_pe[(qt - 1, 31)])
                    a.activation(out=wtq[:, ft * 512:(ft + 1) * 512], in_=pb[g % 2][:, :],
                                 func=AF.Copy)
                    w_ev[g] = c.val("actc")

            for j in range(8):
                @block.tensor
                def _(tensor, qt=qt, j=j):
                    if j == 0:
                        tensor.wait_ge(c.sems["actc"][0], w_ev[qt * 8 + 7])
                        if qt == 0:
                            tensor.wait_ge(c.sems["dvec"][0], wp_done)
                    if qt >= 1:
                        tensor.wait_ge(c.sems["actc"][0], s_ev[(qt - 1, j)])
                    for k in range(8):
                        c.inc(tensor.matmul(
                            pb[j][:, :], wpb[:, k * 1024 + j * 128:k * 1024 + (j + 1) * 128],
                            wtq[:, k * 512:(k + 1) * 512],
                            start=(k == 0), stop=(k == 7)), "pe", 1)
                    s_pe[(qt, j)] = c.val("pe")

                @block.scalar
                def _(scalar, qt=qt, j=j):
                    a = Chain(c, scalar, "actc")
                    scalar.wait_ge(c.sems["pe"][0], s_pe[(qt, j)])
                    if qt >= 1:
                        scalar.wait_ge(c.sems["dvec"][0], x_dv[(qt - 1, j)])
                    a.activation(out=s3q[:, j * 512:(j + 1) * 512], in_=pb[j][:, :],
                                 func=AF.Copy)
                    s_ev[(qt, j)] = c.val("actc")

            for ft in range(32):
                g = qt * 32 + ft
                @block.sync
                def _(sync, qt=qt, ft=ft, g=g):
                    sm = "w10" if g % 2 == 0 else "w11"
                    if ft < 2 and qt >= 1:
                        sync.wait_ge(c.sems["pe"][0], d_pe[(qt - 1, 30 + ft)])
                    if g >= 2:
                        sync.wait_ge(c.sems["pe"][0], u_pe[g - 2])
                    c.dma(sync, sm, w1r[:, (g % 2) * 1024:(g % 2) * 1024 + 1024].rearrange(
                        "p (k m) -> p k m", k=8),
                        w1_in[:, ft * 128:(ft + 1) * 128].rearrange("(k p) m -> p k m", p=128))
                    w1l[g] = (sm, c.val(sm))

                @block.tensor
                def _(tensor, qt=qt, ft=ft, g=g):
                    sm, tv = w1l[g]
                    tensor.wait_ge(c.sems[sm][0], tv)
                    if ft == 0:
                        tensor.wait_ge(c.sems["actc"][0], s_ev[(qt, 7)])
                    if g >= 2:
                        tensor.wait_ge(c.sems["actc"][0], u_ev[g - 2])
                    if qt >= 1 and ft == 0:
                        tensor.wait_ge(c.sems["pe"][0], d_pe[(qt - 1, 31)])
                    for k in range(8):
                        c.inc(tensor.matmul(
                            pb[g % 2][:, :],
                            w1r[:, (g % 2) * 1024 + k * 128:(g % 2) * 1024 + (k + 1) * 128],
                            xnbt(k)[:, qt * 512:(qt + 1) * 512],
                            start=(k == 0), stop=(k == 7)), "pe", 1)
                    u_pe[g] = c.val("pe")

                @block.scalar
                def _(scalar, ft=ft, g=g):
                    a = Chain(c, scalar, "actc")
                    scalar.wait_ge(c.sems["pe"][0], u_pe[g])
                    a.activation(out=u3q[:, ft * 512:(ft + 1) * 512], in_=pb[g % 2][:, :],
                                 func=AF.Relu, bias=b1c[:, ft:ft + 1])
                    u_ev[g] = c.val("actc")

            for k in range(32):
                g = qt * 32 + k
                @block.sync
                def _(sync, k=k, g=g, qt=qt):
                    sm = "w20" if g % 2 == 0 else "w21"
                    if k < 2:
                        sync.wait_ge(c.sems["pe"][0], u_pe[qt * 32 + 30 + k])
                    else:
                        sync.wait_ge(c.sems["pe"][0], d_pe[(qt, k - 2)])
                    c.dma(sync, sm, w2r[:, (g % 2) * 1024:(g % 2) * 1024 + 1024],
                          w2_in[k * 128:(k + 1) * 128, :])
                    w2l[g] = (sm, c.val(sm))

                @block.tensor
                def _(tensor, qt=qt, k=k, g=g):
                    sm, tv = w2l[g]
                    tensor.wait_ge(c.sems[sm][0], tv)
                    if k == 0:
                        tensor.wait_ge(c.sems["actc"][0], u_ev[qt * 32 + 31])
                        if qt >= 1:
                            tensor.wait_ge(c.sems["dvec"][0], x_dv[(qt - 1, 7)])
                    for j in range(8):
                        c.inc(tensor.matmul(
                            pb[j][:, :],
                            w2r[:, (g % 2) * 1024 + j * 128:(g % 2) * 1024 + (j + 1) * 128],
                            u3q[:, k * 512:(k + 1) * 512],
                            start=(k == 0), stop=(k == 31)), "pe", 1)
                    d_pe[(qt, k)] = c.val("pe")

            for j in range(8):
                @block.scalar
                def _(scalar, qt=qt, j=j):
                    a = Chain(c, scalar, "actc")
                    scalar.wait_ge(c.sems["pe"][0], d_pe[(qt, 31)])
                    if j >= 1:
                        scalar.wait_ge(c.sems["dvec"][0], x_dv[(qt, j - 1)])
                    a.activation(out=g3t[:, :], in_=pb[j][:, :],
                                 func=AF.Silu, bias=b2c[:, j:j + 1])
                    d_ev[(qt, j)] = c.val("actc")

                @block.vector
                def _(vector, qt=qt, j=j):
                    v = Chain(c, vector, "dvec")
                    vector.wait_ge(c.sems["actc"][0], d_ev[(qt, j)])
                    if qt >= 2:
                        sm, tv = y_st[(qt - 2, j)]
                        vector.wait_ge(c.sems[sm][0], tv)
                    cs = slice(qt * 512, (qt + 1) * 512)
                    v.tensor_tensor(out=sq0[:, :], in0=g3t[:, :],
                                    in1=s3q[:, j * 512:(j + 1) * 512], op=ALU.mult)
                    v.tensor_tensor(out=xt(j)[:, cs], in0=xt(j)[:, cs], in1=sq0[:, :],
                                    op=ALU.add)
                    x_dv[(qt, j)] = c.val("dvec")

                @block.sync
                def _(sync, qt=qt, j=j):
                    sync.wait_ge(c.sems["dvec"][0], x_dv[(qt, j)])
                    sm = "sy0" if qt % 2 == 0 else "sy1"
                    c.dma(sync, sm, y_out[j * 128:(j + 1) * 128, qt * 512:(qt + 1) * 512],
                          xt(j)[:, qt * 512:(qt + 1) * 512])
                    y_st[(qt, j)] = (sm, c.val(sm))

    with nc.Block() as block:
        @block.sync
        def _(sync):
            c.wait(sync, "sy0")
            c.wait(sync, "sy1")

    c.close()
    return c


_CACHE = {}


def _get(name, builder):
    if name not in _CACHE:
        _CACHE[name] = builder()
    return _CACHE[name]


def run_prep(inputs):
    c = _get("prep", build_prep)
    wg = to_np(inputs["L1_ffn_w_gate"], NB_F32)
    wu = to_np(inputs["L1_ffn_w_up"], NB_F32)
    wd = to_np(inputs["L1_ffn_w_down"], NB_F32)
    w1 = to_np(inputs["L3_ffn_W1"], NB_F32)
    w2 = to_np(inputs["L3_ffn_W2"], NB_F32)
    w1t = np_t(w1)
    w2t = np_t(w2)

    wg_pad = np.zeros((8 * 384, 1024), NB_F32)
    wg_pad[0:2816] = wg
    wu_pad = np.zeros((8 * 384, 1024), NB_F32)
    wu_pad[0:2816] = wu

    gup_t = np.ascontiguousarray(np.transpose(to_np(inputs["L0_ffn_gate_up_w"], NB_F32), (0, 2, 1)))
    down_t = np.ascontiguousarray(np.transpose(to_np(inputs["L0_ffn_down_w"], NB_F32), (0, 2, 1)))
    sgup_t = np_t(to_np(inputs["L0_ffn_shared_gate_up_w"], NB_F32))
    sdwn_t = np_t(to_np(inputs["L0_ffn_shared_down_w"], NB_F32))
    blob = np.zeros((38912, 2048), np.float32)
    blob[0:24576] = gup_t.reshape(24576, 2048)
    blob[24576:36864] = down_t.reshape(12288, 2048)
    blob[36864:37888] = sgup_t.reshape(1024, 2048)
    blob[37888:38400] = sdwn_t.reshape(512, 2048)
    in_maps = []
    for core in range(NCORES):
        in_maps.append({
            "wg_sl": wg_pad[core * 384:(core + 1) * 384],
            "wu_sl": wu_pad[core * 384:(core + 1) * 384],
            "wd_sl": wd[core * 128:(core + 1) * 128],
            "w1t": w1t, "w2t": w2t,
            "w1t_sl": w1t[core * 128:(core + 1) * 128],
            "w2t_sl": w2t[core * 512:(core + 1) * 512],
            "blob_in": blob[core * 4864:(core + 1) * 4864],
        })
    res = run_bass_kernel_spmd(c.nc, in_maps, core_ids=list(range(NCORES)))
    r = res.results
    wgq = np.concatenate([to_np(r[i]["wgq"], NB_BF16) for i in range(8)], axis=0)[0:2816]
    wuq = np.concatenate([to_np(r[i]["wuq"], NB_BF16) for i in range(8)], axis=0)[0:2816]
    wdq = np.concatenate([to_np(r[i]["wdq"], NB_BF16) for i in range(8)], axis=0)
    w1q = np.concatenate([to_np(r[i]["w1q_sl"], NB_BF16) for i in range(8)], axis=0)
    w2q = np.concatenate([to_np(r[i]["w2q_sl"], NB_BF16) for i in range(8)], axis=0)
    blob = np.concatenate([to_np(r[i]["blob_out"], NB_BF16) for i in range(8)], axis=0)
    gub = blob[0:24576].reshape(8, 1024, 6144)
    dwb = blob[24576:36864].reshape(8, 3072, 1024)
    sgub = blob[36864:37888].reshape(1024, 2048)
    sdwb = blob[37888:38400].reshape(1024, 1024)
    return {"wgq": wgq, "wuq": wuq, "wdq": wdq, "w1qt": w1q, "w2qt": w2q,
            "gub": np.ascontiguousarray(gub), "dwb": np.ascontiguousarray(dwb),
            "sgub": np.ascontiguousarray(sgub), "sdwb": np.ascontiguousarray(sdwb)}


def _core_tokens(core):
    b = core // 4
    t0 = (core % 4) * T
    return b, t0


def run_l0(inputs, x_np, wq):
    c = _get("l0", build_l0)
    wqkv = to_np(inputs["L0_attn_w_qkv"], NB_F32).reshape(1024, 3072)
    bq = to_np(inputs["L0_attn_b_qkv"], NB_F32).reshape(3072)
    bq_col = np.ascontiguousarray(bq.reshape(24, 128).T)
    bv_row = np.ascontiguousarray(bq[2048:3072].reshape(1, 1024))
    lng = to_np(inputs["L0_ffn_ln_gamma"], NB_F32)
    lnb = to_np(inputs["L0_ffn_ln_beta"], NB_F32)
    lng_col = np.ascontiguousarray(lng.reshape(8, 128).T)
    lnb_col = np.ascontiguousarray(lnb.reshape(8, 128).T)
    rw = to_np(inputs["L0_ffn_router_w"], NB_F32)
    rw_col = np.ascontiguousarray(rw.T.reshape(8, 128, 8).transpose(1, 0, 2).reshape(128, 64))
    eye = np.eye(128, dtype=NB_F32)
    in_maps = []
    for core in range(NCORES):
        b, t0 = _core_tokens(core)
        in_maps.append({
            "x_feat": np.ascontiguousarray(x_np[b, t0:t0 + T].T),
            "wqkv": wqkv, "bqkv_col": bq_col, "bv_row": bv_row,
            "lng_col": lng_col, "lnb_col": lnb_col, "rw_col": rw_col, "eye": eye,
            "gup_t": wq["gub"], "down_t": wq["dwb"], "sgup_t": wq["sgub"],
            "sdwn_t": wq["sdwb"],
        })
    res = run_bass_kernel_spmd(c.nc, in_maps, core_ids=list(range(NCORES)))
    return res.results


def _head_of_core(core, pair, half):
    flat = 4 * core + 2 * pair + half
    return flat // 16, flat % 16


def run_attn(q_cores, k_cores, v_cores):
    c = _get("attn", build_attn)
    tri = np.where(np.arange(128)[None, :] < np.arange(128)[:, None], -1e9, 0.0).astype(NB_F32)
    eye = np.eye(128, dtype=NB_F32)

    def q_bh(b, h):
        return np.concatenate([q_cores[4 * b + i][64 * h:64 * h + 64, :] for i in range(4)], axis=1)

    def k_bh(b, h):
        return np.concatenate([k_cores[4 * b + i][64 * h:64 * h + 64, :] for i in range(4)], axis=1)

    def v_bh(b, h):
        return np.concatenate([v_cores[4 * b + i][:, 64 * h:64 * h + 64] for i in range(4)], axis=0)

    in_maps = []
    for core in range(NCORES):
        qh = np.empty((2, 128, S), NB_BF16)
        kh = np.empty((2, 128, S), NB_BF16)
        vh = np.empty((2, S, 128), NB_BF16)
        for pair in range(2):
            for half in range(2):
                b, h = _head_of_core(core, pair, half)
                qh[pair, 64 * half:64 * half + 64] = q_bh(b, h)
                kh[pair, 64 * half:64 * half + 64] = k_bh(b, h)
                vh[pair, :, 64 * half:64 * half + 64] = v_bh(b, h)
        in_maps.append({"qh": qh, "kh": kh, "vh": vh, "tri": tri, "eyef": eye})
    res = run_bass_kernel_spmd(c.nc, in_maps, core_ids=list(range(NCORES)))
    ans = {}
    for core in range(NCORES):
        a = to_np(res.results[core]["ansh"], NB_BF16)
        for pair in range(2):
            for half in range(2):
                b, h = _head_of_core(core, pair, half)
                ans[(b, h)] = a[pair, 64 * half:64 * half + 64, :]
    return ans


def run_l1(inputs, x_np, ans0, g0_cores, wq):
    c = _get("l1", build_l1)
    wo0 = to_np(inputs["L0_attn_w_out"], NB_F32).reshape(1024, 1024)
    bo0 = to_np(inputs["L0_attn_b_out"], NB_F32)
    bo0_col = np.ascontiguousarray(bo0.reshape(8, 128).T)
    wqkv1 = np.ascontiguousarray(np.concatenate(
        [to_np(inputs["L1_attn_Wq"], NB_F32), to_np(inputs["L1_attn_Wk"], NB_F32),
         to_np(inputs["L1_attn_Wv"], NB_F32)], axis=1))
    bqk = np.concatenate([to_np(inputs["L1_attn_bq"], NB_F32),
                          to_np(inputs["L1_attn_bk"], NB_F32)])
    bqk1_col = np.ascontiguousarray(bqk.reshape(16, 128).T)
    bv1_row = np.ascontiguousarray(to_np(inputs["L1_attn_bv"], NB_F32).reshape(1, 1024))
    rep = lambda a, n: np.ascontiguousarray(np.broadcast_to(np.asarray(a, NB_F32).reshape(1, -1), (128, n)))
    com = {
        "wo0": wo0, "bo0_col": bo0_col, "wqkv1": wqkv1, "bqk1_col": bqk1_col,
        "bv1_row": bv1_row,
        "dpw1": to_np(inputs["L1_attn_dp_w1"], NB_F32),
        "dpb1_row": np.ascontiguousarray(to_np(inputs["L1_attn_dp_b1"], NB_F32).reshape(1, 512)),
        "lng1_rep": rep(inputs["L1_attn_ln_g"], 512),
        "lnb1_rep": rep(inputs["L1_attn_ln_b"], 512),
        "dpw2_rep": rep(np.asarray(inputs["L1_attn_dp_w2"])[:, 0], 512),
        "dpb2_rep": rep(np.asarray(inputs["L1_attn_dp_b2"]), 1),
        "gw1_rep": rep(np.asarray(inputs["L1_attn_g_w1"])[0], 16),
        "gb1_rep": rep(np.asarray(inputs["L1_attn_g_b1"]), 16),
        "gw2_rep": rep(np.asarray(inputs["L1_attn_g_w2"])[:, 0], 16),
        "gb2_rep": rep(np.asarray(inputs["L1_attn_g_b2"]), 1),
        "wgq_t": np_t(wq["wgq"]), "wuq_t": np_t(wq["wuq"]), "wdq_t": np_t(wq["wdq"]),
        "eyef": np.eye(128, dtype=NB_F32),
        "eyeb": np.eye(128, dtype=NB_F32).astype(NB_BF16),
    }
    in_maps = []
    for core in range(NCORES):
        b, t0 = _core_tokens(core)
        a0 = np.empty((1024, T), NB_BF16)
        for h in range(16):
            a0[h * 64:(h + 1) * 64] = ans0[(b, h)][:, t0:t0 + T]
        m = dict(com)
        m["x_feat"] = np.ascontiguousarray(x_np[b, t0:t0 + T].T)
        m["ans0"] = a0
        m["g0"] = g0_cores[core]
        in_maps.append(m)
    res = run_bass_kernel_spmd(c.nc, in_maps, core_ids=list(range(NCORES)))
    return res.results


def kernel(**inputs):
    x = np.asarray(inputs["L0_attn_x"], np.float32)
    wq = run_prep(inputs)
    l0 = run_l0(inputs, x, wq)
    q0 = [to_np(l0[i]["q0"], NB_BF16) for i in range(8)]
    k0 = [to_np(l0[i]["k0"], NB_BF16) for i in range(8)]
    v0 = [to_np(l0[i]["v0"], NB_BF16) for i in range(8)]
    g0 = [to_np(l0[i]["g0"], NB_F32) for i in range(8)]
    ans0 = run_attn(q0, k0, v0)
    l1 = run_l1(inputs, x, ans0, g0, wq)
    q1 = [to_np(l1[i]["q1"], NB_BF16) for i in range(8)]
    k1 = [to_np(l1[i]["k1"], NB_BF16) for i in range(8)]
    v1 = [to_np(l1[i]["v1g"], NB_BF16) for i in range(8)]
    x1 = [to_np(l1[i]["x1"], NB_F32) for i in range(8)]
    g1 = [to_np(l1[i]["g1"], NB_F32) for i in range(8)]
    ans1 = run_attn(q1, k1, v1)
    e = run_l2a(inputs, x1, ans1, g1)
    e_out = [{k: np.asarray(to_np(e[i][k],
                 NB_BF16 if k in ("xn2", "xconv", "xssm") else NB_F32))
              for k in ["x2", "xn2", "xconv", "xssm", "mloc", "aprod", "pedge", "hsum"]}
             for i in range(8)]
    f = run_l2b(inputs, e_out)
    mw = [{"mw1q": to_np(f[i]["mw1q"], NB_BF16), "mw2q": to_np(f[i]["mw2q"], NB_BF16)}
          for i in range(8)]
    mb = [{"mb1": to_np(f[i]["mb1"], NB_F32), "mb2": to_np(f[i]["mb2"], NB_F32)}
          for i in range(8)]
    s2 = [to_np(f[i]["s2"], NB_F32) for i in range(8)]
    x2 = [e_out[i]["x2"] for i in range(8)]
    xn2 = [e_out[i]["xn2"] for i in range(8)]
    g = run_l3g(inputs, x2, s2, xn2, mw, mb)
    x3 = [to_np(g[i]["x3"], NB_F32) for i in range(8)]
    xn3 = [to_np(g[i]["xn3"], NB_BF16) for i in range(8)]
    a3 = run_l3a(inputs, xn3)
    gfc = [to_np(a3[i]["gates_f"], NB_BF16) for i in range(8)]
    lpc = [to_np(a3[i]["lpart"], NB_F32) for i in range(8)]
    yres = run_l3b(inputs, x3, xn3, gfc, lpc, wq)
    out = np.empty((B, S, D), np.float32)
    for core in range(8):
        b, t0 = _core_tokens(core)
        out[b, t0:t0 + T, :] = to_np(yres[core]["y"], NB_F32).T
    return out


def run_l2a(inputs, x1_cores, ans1, g1_cores):
    c = _get("l2a", build_l2a)
    wo1 = np.ascontiguousarray(np.asarray(inputs["L1_attn_Wo"], np.float32).reshape(D, D))
    bo1_col = np.ascontiguousarray(np.asarray(inputs["L1_attn_bo"], np.float32).reshape(8, 128).T)
    win_t = np.ascontiguousarray(np.asarray(inputs["L2_attn_W_in"], np.float32).T)
    wdt_row = np.ascontiguousarray(np.asarray(inputs["L2_attn_W_dt"], np.float32).reshape(1, 2048))
    A_ = np.ascontiguousarray(np.asarray(inputs["L2_attn_A"], np.float32))
    Bm_ = np.ascontiguousarray(np.asarray(inputs["L2_attn_Bm"], np.float32))
    cw = np.asarray(inputs["L2_attn_conv_w"], np.float32)
    convw_col = np.zeros((128, 48), np.float32)
    for j in range(3):
        for m in range(16):
            convw_col[:, j * 16 + m] = cw[m * 128:(m + 1) * 128, 0, j]
    gw1_t = np.ascontiguousarray(np.asarray(inputs["L2_ffn_gate_w1"], np.float32).T)
    gb1_col = np.ascontiguousarray(np.asarray(inputs["L2_ffn_gate_b1"], np.float32).reshape(8, 128).T)

    in_maps = []
    for core in range(NCORES):
        b, t0 = _core_tokens(core)
        a1 = np.empty((D, T), NB_BF16)
        for h in range(16):
            a1[h * 64:(h + 1) * 64] = ans1[(b, h)][:, t0:t0 + T]
        in_maps.append(dict(x1_feat=x1_cores[core], ans1=a1, wo1=wo1, bo1_col=bo1_col,
                            g1=g1_cores[core], win_t=win_t, wdt_row=wdt_row, A_=A_, Bm_=Bm_,
                            convw_col=convw_col, gw1_t=gw1_t, gb1_col=gb1_col))
    res = run_bass_kernel_spmd(c.nc, in_maps, core_ids=list(range(NCORES)))
    return res.results


def run_l2b(inputs, e_out):
    c = _get("l2b", build_l2b)
    cw = np.asarray(inputs["L2_attn_conv_w"], np.float32)
    convw_col = np.zeros((128, 48), np.float32)
    for j in range(3):
        for m in range(16):
            convw_col[:, j * 16 + m] = cw[m * 128:(m + 1) * 128, 0, j]
    Cm_ = np.ascontiguousarray(np.asarray(inputs["L2_attn_Cm"], np.float32))
    dv = np.asarray(inputs["L2_attn_Dv"], np.float32)
    dv_col = np.ascontiguousarray(dv.reshape(16, 128).T)
    wout_t = np.ascontiguousarray(np.asarray(inputs["L2_attn_W_out"], np.float32).T)
    gw2t = np.asarray(inputs["L2_ffn_gate_w2"], np.float32).T
    gw2_col = np.ascontiguousarray(gw2t.reshape(8, 128, 8).transpose(1, 0, 2).reshape(128, 64))
    gb2_row = np.asarray(inputs["L2_ffn_gate_b2"], np.float32).reshape(1, 8)
    Tb1_ = np.ascontiguousarray(np.asarray(inputs["L2_ffn_Tb1"], np.float32))
    b1_row = np.asarray(inputs["L2_ffn_b1"], np.float32).reshape(1, 4096)
    Tb2_ = np.ascontiguousarray(np.asarray(inputs["L2_ffn_Tb2"], np.float32))
    b2_row = np.asarray(inputs["L2_ffn_b2"], np.float32).reshape(1, 1024)
    TW1 = np.asarray(inputs["L2_ffn_TW1"], np.float32)
    TW2 = np.asarray(inputs["L2_ffn_TW2"], np.float32)
    W1 = np.asarray(inputs["L2_ffn_W1"], np.float32)
    W2 = np.asarray(inputs["L2_ffn_W2"], np.float32)
    eyeb = np.eye(128, dtype=NB_F32).astype(NB_BF16)

    in_maps = []
    for core in range(NCORES):
        gb, qi = divmod(core, 4)
        me3 = np.zeros((16, 3), np.float32)
        ae3 = np.zeros((16, 3), np.float32)
        ae3[:, :] = 1.0
        me3[:, :] = 0.0
        preds = list(range(gb * 4, core))[-3:]
        off = 3 - len(preds)
        for j, p in enumerate(preds):
            me3[:, off + j] = e_out[p]["mloc"][:, -1]
            ae3[:, off + j] = e_out[p]["aprod"][:, -1]
        halo = np.zeros((2048, 2), np.float32)
        if qi > 0:
            halo[:, 0] = e_out[core - 1]["pedge"][:, 1]
        if qi < 3:
            halo[:, 1] = e_out[core + 1]["pedge"][:, 0]
        halo_col = np.zeros((128, 32), np.float32)
        for m in range(16):
            halo_col[:, m] = halo[m * 128:(m + 1) * 128, 0]
            halo_col[:, 16 + m] = halo[m * 128:(m + 1) * 128, 1]
        hsums_all = np.concatenate([e_out[gb * 4 + j]["hsum"] for j in range(4)], axis=1)
        sl = slice(qi * 1024, (qi + 1) * 1024)
        in_maps.append(dict(
            xn2=e_out[core]["xn2"].astype(NB_BF16),
            xconv=e_out[core]["xconv"].astype(NB_BF16),
            xssm=e_out[core]["xssm"].astype(NB_BF16),
            mloc=e_out[core]["mloc"], aprod=e_out[core]["aprod"],
            mend3=me3, aend3=ae3, halo_col=halo_col, convw_col=convw_col,
            Cm_=Cm_, dv_col=dv_col, wout_t=wout_t, hsums_all=hsums_all,
            gw2_col=gw2_col, gb2_row=gb2_row,
            tw1q=np.ascontiguousarray(TW1[:, sl, :].transpose(0, 2, 1)),
            tw2q=np.ascontiguousarray(TW2[:, :, sl].transpose(0, 2, 1)),
            w1tq=np.ascontiguousarray(W1[sl, :].T),
            w2tq=np.ascontiguousarray(W2[:, sl].T),
            Tb1_=Tb1_, b1_row=b1_row, Tb2_=Tb2_, b2_row=b2_row, eyeb=eyeb))
    res = run_bass_kernel_spmd(c.nc, in_maps, core_ids=list(range(NCORES)))
    return res.results


def run_l3g(inputs, x2_cores, s2_cores, xn2_cores, mw, mb):
    c = _get("l3g", build_l3g)
    in_maps = []
    for core in range(NCORES):
        gb = core // 4
        mw1t = np.concatenate([mw[gb * 4 + qi]["mw1q"] for qi in range(4)], axis=1)
        mw2t = np.concatenate([mw[gb * 4 + qi]["mw2q"] for qi in range(4)], axis=0)
        in_maps.append(dict(
            x2=x2_cores[core], s2=s2_cores[core], xn2=xn2_cores[core].astype(NB_BF16),
            mw1t=np.ascontiguousarray(mw1t.astype(NB_BF16)),
            mw2t=np.ascontiguousarray(mw2t.astype(NB_BF16)),
            mb1_col=np.ascontiguousarray(mb[core]["mb1"].reshape(32, 128).T),
            mb2_col=np.ascontiguousarray(mb[core]["mb2"].reshape(8, 128).T)))
    res = run_bass_kernel_spmd(c.nc, in_maps, core_ids=list(range(NCORES)))
    return res.results


def run_l3a(inputs, xn3_cores):
    c = _get("l3a", build_l3a)
    wk_t = np.ascontiguousarray(np.asarray(inputs["L3_attn_W_k"], np.float32).T)
    wv_t = np.ascontiguousarray(np.asarray(inputs["L3_attn_W_v"], np.float32).T)
    wg_t = np.ascontiguousarray(np.asarray(inputs["L3_attn_W_g"], np.float32).T)
    lq = np.asarray(inputs["L3_attn_latent_queries"], np.float32)
    qlat_t = np.ascontiguousarray(lq[0].transpose(2, 1, 0).reshape(64, 1024))
    tri2 = np.zeros((128, 128), np.float32)
    for r in range(128):
        tri2[r, (r % 64) + 1:] = -1e9
    blockones = np.zeros((128, 2), np.float32)
    blockones[0:64, 0] = 1.0
    blockones[64:128, 1] = 1.0
    blockonesT = np.ascontiguousarray(blockones.T)
    eyeb = np.eye(128, dtype=NB_F32).astype(NB_BF16)
    inv = 1.0 / (10000.0 ** (np.arange(0, 64, 2, dtype=np.float64) / 64.0))

    in_maps = []
    for core in range(NCORES):
        b, t0 = _core_tokens(core)
        ang = (np.arange(t0, t0 + T, dtype=np.float64)[:, None] * inv[None, :])
        cosr = np.cos(ang).astype(np.float32)
        sinr = np.sin(ang).astype(np.float32)
        cosrep = np.ascontiguousarray(np.tile(cosr, (1, 16)))
        sinrep = np.ascontiguousarray(np.tile(sinr, (1, 16)))
        in_maps.append(dict(xn3=xn3_cores[core].astype(NB_BF16), wk_t=wk_t, wv_t=wv_t,
                            wg_t=wg_t, cosrep=cosrep, sinrep=sinrep, qlat_t=qlat_t,
                            tri2=tri2, blockones=blockones, blockonesT=blockonesT,
                            eyeb=eyeb))
    res = run_bass_kernel_spmd(c.nc, in_maps, core_ids=list(range(NCORES)))
    return res.results


def run_l3b(inputs, x3_cores, xn3_cores, gf_cores, lp_cores, wq):
    c = _get("l3b", build_l3b)
    wp_t = np.ascontiguousarray(np.asarray(inputs["L3_attn_W_p"], np.float32).T)
    b1_col = np.ascontiguousarray(np.asarray(inputs["L3_ffn_b1"], np.float32).reshape(32, 128).T)
    b2_col = np.ascontiguousarray(np.asarray(inputs["L3_ffn_b2"], np.float32).reshape(8, 128).T)
    w1qt = np.ascontiguousarray(wq["w1qt"].astype(NB_BF16))
    w2qt = np.ascontiguousarray(wq["w2qt"].astype(NB_BF16))
    in_maps = []
    for core in range(NCORES):
        gb = core // 4
        lparts = np.concatenate([lp_cores[gb * 4 + j] for j in range(4)], axis=1)
        in_maps.append(dict(x3=x3_cores[core], xn3=xn3_cores[core].astype(NB_BF16),
                            gates_f=gf_cores[core].astype(NB_BF16),
                            lparts=np.ascontiguousarray(lparts), wp_t=wp_t,
                            w1qt=w1qt, b1_col=b1_col, w2qt=w2qt, b2_col=b2_col))
    res = run_bass_kernel_spmd(c.nc, in_maps, core_ids=list(range(NCORES)))
    return res.results
